# version 02
# baseline (speedup 1.0000x reference)
import math
import numpy as np
import ml_dtypes
import concourse.bass as bass
import concourse.mybir as mybir
from concourse.bass_utils import run_bass_kernel_spmd
from contextlib import ExitStack

F32 = mybir.dt.float32
BF16 = mybir.dt.bfloat16
AF = mybir.ActivationFunctionType
ALU = mybir.AluOpType
AX = mybir.AxisListType

B, S, D, NH, HD, KF, LORA = 2, 8192, 1024, 16, 64, 4, 64
NS = S // KF
E, HID = 4, 4096
NC = 8
TQ = S // 4
EPS = 1e-5
ENGINES = ("sync", "tensor", "vector", "scalar", "gpsimd")



def _rope_tables():
    inv = 1.0 / (10000.0 ** (np.arange(0, HD, 2, dtype=np.float64) / HD))
    pos = np.arange(S, dtype=np.float64)
    fr = pos[:, None] * inv[None, :]
    emb = np.concatenate([fr, fr], axis=1)
    cos = np.cos(emb).T.astype(np.float32)
    sin = np.sin(emb).T.astype(np.float32)
    return cos, sin


def _hadamard():
    H4 = np.array([[1, 1, 1, 1], [1, -1, 1, -1], [1, 1, -1, -1], [1, -1, -1, 1]],
                  dtype=np.float32)
    CDMA = np.tile(H4, (1, HD // 4))
    return CDMA


def _phaseT():
    CDMA = _hadamard()
    reps = CDMA.T
    return np.tile(reps, (1, S // 4)).astype(np.float32)


def host_consts():
    c = {}
    cos, sin = _rope_tables()
    phase = _phaseT()
    ss = sin.copy()
    ss[:32, :] *= -1.0
    var_scale = math.sqrt(KF)
    c["cosP_q"] = np.tile(cos * phase, (2, 1))
    c["ssP_q"] = np.tile(ss * phase, (2, 1))
    c["cosP_k"] = np.tile(cos * phase / var_scale, (2, 1))
    c["ssP_k"] = np.tile(ss * phase / var_scale, (2, 1))
    c["phase_h"] = np.tile(phase / var_scale, (2, 1))
    c["phase2"] = (phase[:, 0:1] * 0.0 + phase[:, 0:4])
    c["phase_o"] = np.tile(phase * var_scale, (2, 1))
    P = np.zeros((128, 128), dtype=np.float32)
    for h in range(2):
        for d in range(64):
            P[h * 64 + (d + 32) % 64, h * 64 + d] = 1.0
    c["rot32"] = P
    sl = np.arange(128)[:, None]
    ql = np.arange(512)[None, :]
    c["mask_l0"] = np.where(sl < (ql // 4), 0.0, -1e30).astype(np.float32)
    masks = []
    for d in range(4):
        masks.append(np.where(128 * d + sl <= ql, 0.0, -1e30).astype(np.float32))
    c["mask_l1"] = np.stack(masks)
    c["ident64_bf"] = np.eye(64).astype(ml_dtypes.bfloat16)
    c["ident128_f"] = np.eye(128).astype(np.float32)
    c["ident4_f"] = np.eye(4).astype(np.float32)
    c["ones_col_bf"] = np.ones((128, 1), dtype=ml_dtypes.bfloat16)
    c["ones_row_f"] = np.ones((1, 128), dtype=np.float32)
    c["eps"] = np.full((1, 1), EPS, dtype=np.float32)
    em = np.zeros((4, 512), dtype=np.float32)
    for e in range(4):
        em[e, 128 * e:128 * (e + 1)] = 1.0
    c["emat"] = em
    return c


CONSTS = host_consts()


ECHO_SEM = {"tensor": "PE_E", "vector": "DVE_E", "scalar": "ACT_E", "gpsimd": "GP_E"}
DMA_TAGS = {"LD": 4, "LDG": 4, "ST": 4}
CC_SEMS = ["CC0", "CC1", "CC2"]
ALL_SEMS = (["PE", "ACT", "DVE", "GP"] + list(ECHO_SEM.values())
            + [f"{t}{i}" for t, n in DMA_TAGS.items() for i in range(n)]
            + CC_SEMS)
RG = [[0, 1, 2, 3], [4, 5, 6, 7]]


class Emitter:

    def __init__(self):
        self.ops = {e: [] for e in ENGINES}
        self.cnt = {}
        self.hwm = {e: {} for e in ENGINES}
        self.echoed = {e: (0, 0) for e in ENGINES}
        self.dma_idx = {t: 0 for t in DMA_TAGS}
        self.dma_cnt = {}

    def _emit_wait(self, eng, sem, val):
        if self.hwm[eng].get(sem, 0) < val:
            self.ops[eng].append(("w", sem, val))
            self.hwm[eng][sem] = val

    def _wait(self, eng, tok):
        if tok is None:
            return
        sem, val, peng = tok
        if sem in DMA_TAGS:
            for i in range(DMA_TAGS[sem]):
                ps = f"{sem}{i}"
                c = self.dma_cnt.get(ps, 0)
                if c:
                    self._emit_wait(eng, ps, c)
            return
        if peng == eng or peng in ("__none__", "sync", "tensor"):
            self._emit_wait(eng, sem, val)
            return
        cov, esem_val = self.echoed[peng]
        if cov < val:
            esem = ECHO_SEM[peng]
            self._emit_wait(peng, sem, val)
            self.cnt[esem] = self.cnt.get(esem, 0) + 1
            self.ops[peng].append(("o", lambda e: e.nop(), (esem, self.cnt[esem], peng), 1))
            self.echoed[peng] = (val, self.cnt[esem])
            esem_val = self.cnt[esem]
        self._emit_wait(eng, ECHO_SEM[peng], esem_val)

    def o(self, eng, fn, after=(), tag=None, inc=1):
        for tok in after:
            self._wait(eng, tok)
        tokout = None
        if tag is not None and tag in DMA_TAGS:
            i = self.dma_idx[tag] % DMA_TAGS[tag]
            self.dma_idx[tag] += 1
            ps = f"{tag}{i}"
            prev = self.dma_cnt.get(ps, 0)
            if prev:
                self._emit_wait(eng, ps, prev)
            self.dma_cnt[ps] = prev + 16
            tokout = (tag, self.dma_idx[tag], eng)
            self.ops[eng].append(("o", fn, (ps, self.dma_cnt[ps], eng), 16))
            return tokout
        if tag is not None:
            self.cnt[tag] = self.cnt.get(tag, 0) + inc
            tokout = (tag, self.cnt[tag], eng)
        self.ops[eng].append(("o", fn, tokout, inc))
        return tokout

    def dma_barrier(self, tag):
        return (tag, self.dma_idx.get(tag, 0), "__dma__")

    def engine_tok(self, eng):
        sem = {"tensor": "PE", "vector": "DVE", "scalar": "ACT", "gpsimd": "GP"}[eng]
        v = self.cnt.get(sem, 0)
        return (sem, v, eng) if v else None

    def phase_barrier(self):
        toks = [self.dma_barrier(t) for t in DMA_TAGS]
        toks += [self.engine_tok(e) for e in ("tensor", "vector", "scalar", "gpsimd")]
        return [t for t in toks if t is not None]

    def echo(self, toks, via, tag):
        return self.o(via, lambda e: e.nop(), after=toks, tag=tag)

    def barrier_tok(self, tag):
        if tag not in self.cnt:
            return None
        return (tag, self.cnt[tag], "__none__")

    def build(self, block, sems):
        for e in ENGINES:
            ops = self.ops[e]
            if not ops:
                continue
            dec = getattr(block, e)

            def mk(ops_list):
                def _(eng):
                    for op in ops_list:
                        if op[0] == "w":
                            eng.wait_ge(sems[op[1]], op[2])
                        else:
                            _, fn, tokout, inc = op
                            inst = fn(eng)
                            if tokout is not None:
                                inst.then_inc(sems[tokout[0]], inc)
                return _
            dec(mk(ops))


def emit_ln_stats(em, sb, ps, xsrc_fn, W, deps, tagpfx, ln_scr, raw_mean=False):
    xbf = ln_scr["xbf"]
    rows = ln_scr["rows"]
    pa, pb = ps["stat_a"], ps["stat_b"]
    casts = []
    for c in range(8):
        xa = xsrc_fn(c)
        casts.append(em.o("vector", lambda v, xa=xa, c=c: v.tensor_copy(xbf[:, c * W:(c + 1) * W], xa),
                          after=deps, tag="DVE"))
    mean_mms = []
    for c in range(8):
        mean_mms.append(em.o("tensor", lambda t, c=c: t.matmul(
            pa[0:1, 0:W], sb["ones_col_bf"][:], xbf[:, c * W:(c + 1) * W],
            start=(c == 0), stop=(c == 7)), after=[casts[c]], tag="PE"))
    mm_s = None
    sqs = []
    for c in range(8):
        xa = xsrc_fn(c)
        sq = em.o("scalar", lambda s, xa=xa, c=c: s.activation(
            xbf[:, c * W:(c + 1) * W], xa, AF.Square),
            after=[mean_mms[c]] + list(deps), tag="ACT")
        sqs.append(sq)
    for c in range(8):
        mm_s = em.o("tensor", lambda t, c=c: t.matmul(
            pb[0:1, 0:W], sb["ones_col_bf"][:], xbf[:, c * W:(c + 1) * W],
            start=(c == 0), stop=(c == 7)), after=[sqs[c]], tag="PE")
    mm_m = mean_mms[-1]
    m_row, ms_row, t_row, rstd_row = rows
    d1 = em.o("vector", lambda v: v.tensor_scalar(m_row[0:1, 0:W], pa[0:1, 0:W], 1.0 / D, None, ALU.mult),
              after=[mm_m], tag="DVE")
    d2 = em.o("vector", lambda v: v.tensor_scalar(ms_row[0:1, 0:W], pb[0:1, 0:W], 1.0 / D, None, ALU.mult),
              after=[mm_s], tag="DVE")
    d3 = em.o("vector", lambda v: v.tensor_mul(t_row[0:1, 0:W], m_row[0:1, 0:W], m_row[0:1, 0:W]),
              after=[d1], tag="DVE")
    d4 = em.o("vector", lambda v: v.tensor_sub(ms_row[0:1, 0:W], ms_row[0:1, 0:W], t_row[0:1, 0:W]),
              after=[d2, d3], tag="DVE")
    a1 = em.o("scalar", lambda s: s.activation(ms_row[0:1, 0:W], ms_row[0:1, 0:W], AF.Ln,
                                               bias=sb["eps"][0:1, 0:1]),
              after=[d4], tag="ACT")
    a2 = em.o("scalar", lambda s: s.activation(rstd_row[0:1, 0:W], ms_row[0:1, 0:W], AF.Exp, scale=-0.5),
              after=[a1], tag="ACT")
    if raw_mean:
        bc_src, bc_deps = m_row, [d1, d2]
    else:
        d5 = em.o("vector", lambda v: v.tensor_mul(t_row[0:1, 0:W], m_row[0:1, 0:W], rstd_row[0:1, 0:W]),
                  after=[a2], tag="DVE")
        bc_src, bc_deps = t_row, [d5, d2]
    bm_r = em.o("tensor", lambda t: t.matmul(pa[:, 0:W], sb["ones_row_f"][:], rstd_row[0:1, 0:W],
                                             start=True, stop=True),
                after=[a2, d1], tag="PE")
    bm_mr = em.o("tensor", lambda t: t.matmul(pb[:, 0:W], sb["ones_row_f"][:], bc_src[0:1, 0:W],
                                              start=True, stop=True),
                 after=bc_deps, tag="PE")
    bc_r, bc_mr = ln_scr["bc_r"], ln_scr["bc_mr"]
    e1 = em.o("vector", lambda v: v.tensor_copy(bc_r[:, 0:W], pa[:, 0:W]), after=[bm_r], tag="DVE")
    e2 = em.o("vector", lambda v: v.tensor_copy(bc_mr[:, 0:W], pb[:, 0:W]), after=[bm_mr], tag="DVE")
    return (e1, e2), bc_r, bc_mr



def emit_phase2(nc, em, G, bar, cc_tok):
    T = TQ + 4
    xT = G["xqT"]
    wrT, wkT, wvT = G["wrT"], G["wkT"], G["wvT"]
    mu_r, mu_k = G["mu_r"], G["mu_k"]
    ones_col_bf, ones_row_f, eps = G["ones_col_bf"], G["ones_row_f"], G["eps"]
    cc1o, bmask = G["cc1o"], G["bmask"]
    x2T, h3T = G["x2scr"], G["cc2i"]
    with ExitStack() as stk:
        en = stk.enter_context

        def sbuf(name, shape, dt):
            return nc.alloc_sbuf_tensor("m_" + name, list(shape), dt)

        xb = sbuf("xb", [128, 8 * T], F32)
        h2 = sbuf("h2", [128, 8 * 516], BF16)
        wr_sb = sbuf("wr_sb", [128, 8 * D], BF16)
        wk_sb = sbuf("wk_sb", [128, 8 * D], BF16)
        wv_sb = sbuf("wv_sb", [128, 8 * D], BF16)
        mur_sb = sbuf("mur_sb", [128, 8], F32)
        muk_sb = sbuf("muk_sb", [128, 8], F32)
        omr_sb = sbuf("omr_sb", [128, 8], F32)
        omk_sb = sbuf("omk_sb", [128, 8], F32)
        bmask_sb = sbuf("bmask_sb", [128, 1], F32)
        W = 512
        y0t = sbuf("y0t", [128, 8 * W], F32)
        y0b = sbuf("y0b", [128, 8 * 4], F32)
        xrb = sbuf("xrb", [128, 8 * W], BF16)
        xkb = sbuf("xkb", [128, 8 * W], BF16)
        rfb = sbuf("rfb", [128, 8 * W], BF16)
        kact = sbuf("kact", [128, 8 * W], BF16)
        krelu = sbuf("krelu", [128, 2 * W], F32)
        tmix = krelu
        tmp = sbuf("tmp", [128, W], F32)

        ln_scr = {
            "xbf": sbuf("ln_xbf", [128, 8 * W], BF16),
            "rows": [sbuf(f"ln_r{i}", [1, W], F32) for i in range(4)],
            "bc_r": sbuf("ln_bcr", [128, W], F32),
            "bc_mr": sbuf("ln_bcmr", [128, W], F32),
        }
        h3t = ln_scr["xbf"]
        consts = {
            "ones_col_bf": sbuf("c_ones_col", [128, 1], BF16),
            "ones_row_f": sbuf("c_ones_row", [1, 128], F32),
            "eps": sbuf("c_eps", [1, 1], F32),
        }
        p0 = en(nc.psum_tensor("m_p0", [128, W], F32))
        p1 = en(nc.psum_tensor("m_p1", [128, W], F32))
        pa = en(nc.psum_tensor("m_pa", [128, W], F32))
        pb = en(nc.psum_tensor("m_pb", [128, W], F32))
        ps = {"stat_a": pa, "stat_b": pb}

        def dma_in(dst, src, q="sync", slow=False, after=()):
            return em.o(q, lambda e, dst=dst, src=src: e.dma_start(
                out=dst, in_=src, allow_slow_non_contiguous=slow),
                after=after, tag="LD" if q == "sync" else "LDG", inc=16)

        ld_x = dma_in(xb.ap().rearrange("p (c t) -> p c t", c=8),
                      xT.rearrange("(c p) t -> p c t", p=128), after=bar)
        ld_wr = dma_in(wr_sb.ap().rearrange("p (c o) -> p c o", c=8),
                       wrT.rearrange("(c p) o -> p c o", p=128), q="gpsimd", after=bar)
        ld_wk = dma_in(wk_sb.ap().rearrange("p (c o) -> p c o", c=8),
                       wkT.rearrange("(c p) o -> p c o", p=128), q="gpsimd", after=bar)
        ld_wv = dma_in(wv_sb.ap().rearrange("p (c o) -> p c o", c=8),
                       wvT.rearrange("(c p) o -> p c o", p=128), q="gpsimd", after=bar)
        ld_mur = dma_in(mur_sb[:], mu_r, after=bar)
        ld_muk = dma_in(muk_sb[:], mu_k, after=bar)
        ld_bm = dma_in(bmask_sb[:], bmask, after=bar)
        ld_oc = dma_in(consts["ones_col_bf"][:], ones_col_bf, after=bar)
        ld_or = dma_in(consts["ones_row_f"][:], ones_row_f, after=bar)
        ld_eps = dma_in(consts["eps"][:], eps, after=bar)
        ld_y0b = dma_in(y0b.ap().rearrange("p (c j) -> p c j", j=4)[:, :, 0:1],
                        cc1o[:, 0:1].rearrange("(c p) t -> p c t", p=128),
                        slow=True, after=bar + [cc_tok])
        yb_toks = [ld_y0b]
        for j in range(1, 4):
            yb_toks.append(em.o("vector", lambda v, j=j: v.tensor_copy(
                y0b[:, j::4], y0b[:, 0::4]), after=[ld_y0b] + bar, tag="DVE"))
        yb_mask = em.o("vector", lambda v: v.tensor_scalar(
            y0b[:], y0b[:], bmask_sb[:, 0:1], None, ALU.mult),
            after=yb_toks + [ld_bm], tag="DVE")

        om1 = em.o("vector", lambda v: v.tensor_scalar(omr_sb[:], mur_sb[:], -1.0, 1.0, ALU.mult, ALU.add),
                   after=[ld_mur], tag="DVE")
        om2 = em.o("vector", lambda v: v.tensor_scalar(omk_sb[:], muk_sb[:], -1.0, 1.0, ALU.mult, ALU.add),
                   after=[ld_muk], tag="DVE")

        bank_free = {0: None, 1: None}

        def proj_tile(dst_write, rhs_fn, w_sb, Wd, t0, lora=None, deps=(),
                      extra_evac_dep=()):
            toks = []
            tl = None
            if lora is not None:
                a_sb, b_sb, plp = lora
                for c in range(8):
                    tl = em.o("tensor", lambda t, c=c: t.matmul(
                        plp[:, 0:Wd], a_sb[:, c * LORA:(c + 1) * LORA], rhs_fn(c),
                        start=(c == 0), stop=(c == 7)), after=deps, tag="PE")
                tl = em.o("vector", lambda v: v.tensor_copy(tlora[:, 0:Wd], plp[:, 0:Wd]),
                          after=[tl], tag="DVE")
            for oc in range(8):
                pp = p0 if oc % 2 == 0 else p1
                mm = None
                prev = bank_free[oc % 2]
                dep0 = list(deps) + ([prev] if prev else [])
                for c in range(8):
                    mm = em.o("tensor", lambda t, c=c, oc=oc, pp=pp: t.matmul(
                        pp[:, 0:Wd], w_sb[:, c * D + oc * 128: c * D + (oc + 1) * 128],
                        rhs_fn(c), start=(c == 0), stop=(c == 7 and lora is None)),
                        after=dep0 if c == 0 else (), tag="PE")
                if lora is not None:
                    mm = em.o("tensor", lambda t, oc=oc, pp=pp: t.matmul(
                        pp[:, 0:Wd], bo_sb[:, oc * 128:(oc + 1) * 128], tlora[:, 0:Wd],
                        start=False, stop=True), after=[tl], tag="PE")
                ev = dst_write(oc, pp, [mm] + list(extra_evac_dep))
                bank_free[oc % 2] = ev
                toks.append(ev)
            return toks

        def x_chunk(c, off, Wd):
            return xb[:, c * T + off: c * T + off + Wd]

        def h2_chunk(c, off, Wd):
            return h2[:, c * 516 + off: c * 516 + off + Wd]

        def do_tile(t0, Wd, ysrc_chunk, deps_extra=(), h2_off=4):
            deps = [ld_x] + list(deps_extra)
            y_toks = []
            prev = None
            for oc in range(8):
                tk = em.o("vector", lambda v, oc=oc: v.tensor_add(
                    x_chunk(oc, t0, Wd), ysrc_chunk(oc), x_chunk(oc, t0, Wd)),
                    after=deps + ([prev] if prev else []), tag="DVE")
                prev = tk
                y_toks.append(tk)
            (l1, l2), bc_r, bc_mr = emit_ln_stats(
                em, consts, ps, lambda c: x_chunk(c, t0, Wd), Wd,
                y_toks + [ld_oc, ld_or, ld_eps], "", ln_scr)
            h_toks = []
            prev_hh = None
            for c in range(8):
                hm = em.o("vector", lambda v, c=c: v.tensor_mul(
                    tmp[:, 0:Wd], x_chunk(c, t0, Wd), bc_r[:, 0:Wd]),
                    after=[l1] + ([prev_hh] if prev_hh else []), tag="DVE")
                hh = em.o("vector", lambda v, c=c: v.tensor_sub(
                    h2_chunk(c, h2_off, Wd), tmp[:, 0:Wd], bc_mr[:, 0:Wd]),
                    after=[l2, hm], tag="DVE")
                prev_hh = hh
                h_toks.append(hh)
            return h_toks, y_toks

        bt, _ = do_tile(TQ, 4, lambda c: y0b[:, c * 4:(c + 1) * 4],
                        deps_extra=[yb_mask], h2_off=0)

        ffn_weight_deps = [ld_wr, ld_wk, ld_wv, om1, om2]
        prev_ffn_reads = []
        prev_h3_out = []
        prev_y_toks = []
        for tt in range(4):
            t0 = tt * W
            ldt = em.o("sync", lambda e, t0=t0: e.dma_start(
                out=y0t.ap().rearrange("p (c t) -> p c t", c=8),
                in_=cc1o[:, 8 + t0:8 + t0 + W].rearrange("(c p) t -> p c t", p=128)),
                after=prev_y_toks + [cc_tok] + bar, tag="LD", inc=16)
            h_toks, prev_y_toks = do_tile(t0, W, lambda c: y0t[:, c * W:(c + 1) * W],
                                          deps_extra=[ldt] + bt + prev_h3_out)
            mix_toks = []
            for c in range(8):
                for (mu_s, om_s, dstb) in ((mur_sb, omr_sb, xrb), (muk_sb, omk_sb, xkb)):
                    m1 = em.o("vector", lambda v, c=c, om_s=om_s: v.tensor_scalar(
                        tmix[:, 0:W], h2_chunk(c, 3, W), om_s[:, c:c + 1], None, ALU.mult),
                        after=h_toks + (bt if tt == 0 else []) + prev_ffn_reads + prev_h3_out
                        + (mix_toks[-1:] if mix_toks else []), tag="DVE")
                    m2 = em.o("vector", lambda v, c=c, mu_s=mu_s, dstb=dstb: v.scalar_tensor_tensor(
                        dstb[:, c * W:(c + 1) * W], h2_chunk(c, 4, W), mu_s[:, c:c + 1],
                        tmix[:, 0:W], ALU.mult, ALU.add), after=[m1], tag="DVE")
                    mix_toks.append(m2)
            for c in range(8):
                em.o("vector", lambda v, c=c: v.tensor_copy(h2_chunk(c, 3, 1), h2_chunk(c, 3 + W, 1)),
                     after=mix_toks, tag="DVE")

            def evac_r(oc, pp, after):
                return em.o("scalar", lambda s, oc=oc, pp=pp: s.activation(
                    rfb[:, oc * W:(oc + 1) * W], pp[:, 0:W], AF.Sigmoid),
                    after=after, tag="ACT")

            r_toks = proj_tile(evac_r, lambda c: xrb[:, c * W:(c + 1) * W],
                               wr_sb, W, t0, deps=mix_toks + ffn_weight_deps)

            krelu_last = {0: None, 1: None}

            def evac_k(oc, pp, after):
                prev_sq = krelu_last.get(oc % 2)
                t1 = em.o("vector", lambda v, oc=oc, pp=pp: v.tensor_scalar(
                    krelu[:, (oc % 2) * W:(oc % 2 + 1) * W], pp[:, 0:W], 0.0, None, ALU.max),
                    after=list(after) + ([prev_sq] if prev_sq else []), tag="DVE")
                sq = em.o("scalar", lambda s, oc=oc: s.activation(
                    kact[:, oc * W:(oc + 1) * W], krelu[:, (oc % 2) * W:(oc % 2 + 1) * W], AF.Square),
                    after=[t1], tag="ACT")
                krelu_last[oc % 2] = sq
                return sq

            k_toks = proj_tile(evac_k, lambda c: xkb[:, c * W:(c + 1) * W],
                               wk_sb, W, t0, deps=mix_toks + ffn_weight_deps)

            fprev = [None]

            def evac_f(oc, pp, after):
                t1 = em.o("vector", lambda v, oc=oc, pp=pp: v.tensor_mul(
                    tmp[:, 0:W], pp[:, 0:W], rfb[:, oc * W:(oc + 1) * W]),
                    after=list(after) + r_toks + ([fprev[0]] if fprev[0] else []), tag="DVE")
                fadd = em.o("vector", lambda v, oc=oc, t0=t0: v.tensor_add(
                    x_chunk(oc, t0, W), x_chunk(oc, t0, W), tmp[:, 0:W]),
                    after=[t1], tag="DVE")
                fprev[0] = fadd
                return fadd

            f_toks = proj_tile(evac_f, lambda c: kact[:, c * W:(c + 1) * W],
                               wv_sb, W, t0, deps=k_toks)
            prev_ffn_reads = f_toks

            (l1, l2), bc_r, bc_mr = emit_ln_stats(
                em, consts, ps, lambda c: x_chunk(c, t0, W), W, f_toks, "", ln_scr)
            out_toks = []
            prev_hh = None
            for c in range(8):
                hm = em.o("vector", lambda v, c=c, t0=t0: v.tensor_mul(
                    tmp[:, 0:W], x_chunk(c, t0, W), bc_r[:, 0:W]),
                    after=[l1] + prev_h3_out + ([prev_hh] if prev_hh else []), tag="DVE")
                hh = em.o("vector", lambda v, c=c: v.tensor_sub(
                    h3t[:, c * W:(c + 1) * W], tmp[:, 0:W], bc_mr[:, 0:W]),
                    after=[l2, hm], tag="DVE")
                prev_hh = hh
                out_toks.append(hh)
            ek1 = em.echo(out_toks, "scalar", "ACT")
            st1 = em.o("gpsimd", lambda e, t0=t0: e.dma_start(
                out=h3T[:, t0:t0 + W].rearrange("(c p) t -> p c t", p=128),
                in_=h3t.ap().rearrange("p (c t) -> p c t", c=8)),
                after=[ek1], tag="ST", inc=16)
            ek2 = em.echo(f_toks, "scalar", "ACT")
            st2 = em.o("gpsimd", lambda e, t0=t0: e.dma_start(
                out=x2T[:, t0:t0 + W].rearrange("(c p) t -> p c t", p=128),
                in_=xb.ap().rearrange("p (c t) -> p c t", c=8)[:, :, t0:t0 + W]),
                after=[ek2], tag="ST", inc=16)
            prev_h3_out = [st1]



def emit_phase4(nc, em, G, bar, cc_tok):
    T = TQ
    W = 512
    BIG = 30000.0
    x2T = G["x2scr"]
    cc3o = G["cc3o"]
    ob, w1, b1c, w2, b2c = G["ob"], G["w1"], G["b1c"], G["w2"], G["b2c"]
    wr, brc, id4, id128, emat = G["wr"], G["brc"], G["id4"], G["id128"], G["emat"]
    ones_col_bf, ones_row_f, eps = G["ones_col_bf"], G["ones_row_f"], G["eps"]
    x4T = G["x4T"]
    x3scr = G["x3scr"]

    with ExitStack() as stk:
        en = stk.enter_context

        def alloc(name, shape, dt):
            return nc.alloc_sbuf_tensor("f_" + name, list(shape), dt)

        xf = alloc("xf", [128, 8 * T], BF16)
        gT = alloc("gT", [4, T], F32)
        b1_sb = alloc("b1_sb", [128, E * 32], F32)
        b2_sb = alloc("b2_sb", [128, 32], F32)
        gel_sb = alloc("gel_sb", [128, E * 32], BF16)
        c_sb = alloc("c_sb", [128, 32], F32)
        cT_sb = alloc("cT_sb", [4, 8 * 128], F32)
        csum_sb = alloc("csum_sb", [128, 8], F32)
        rs_sb = alloc("rs_sb", [128, 16], F32)
        id4_sb = alloc("id4_sb", [4, 4], F32)
        id128_sb = alloc("id128_sb", [128, 128], F32)
        emat_sb = alloc("emat_sb", [4, 512], F32)
        wr_sb = alloc("wr_sb", [128, 8 * 4], F32)
        brc_sb = alloc("brc_sb", [4, 1], F32)
        consts = {
            "ones_col_bf": alloc("c_ones_col", [128, 1], BF16),
            "ones_row_f": alloc("c_ones_row", [1, 128], F32),
            "eps": alloc("c_eps", [1, 1], F32),
        }
        p0 = en(nc.psum_tensor("f_p0", [128, W], F32))
        p1 = en(nc.psum_tensor("f_p1", [128, W], F32))
        pa = en(nc.psum_tensor("f_pa", [128, W], F32))
        pb = en(nc.psum_tensor("f_pb", [128, W], F32))
        pc = en(nc.psum_tensor("f_pc", [128, 16], F32))
        pt = en(nc.psum_tensor("f_pt", [128, 128], F32))
        ps = {"stat_a": pa, "stat_b": pb}

        def dma_in(dst, src, q="sync", slow=False, after=()):
            return em.o(q, lambda e, dst=dst, src=src: e.dma_start(
                out=dst, in_=src, allow_slow_non_contiguous=slow),
                after=after, tag="LD" if q == "sync" else "LDG", inc=16)

        ob_sb = alloc("ob_sb", [128, 8], F32)
        ld_ob = dma_in(ob_sb[:], ob, after=bar)
        ld_wr = dma_in(wr_sb.ap().rearrange("p (c o) -> p c o", c=8),
                       wr.rearrange("(c p) o -> p c o", p=128), after=bar)
        ld_br = dma_in(brc_sb[:], brc, after=bar)
        ld_id4 = dma_in(id4_sb[:], id4, after=bar)
        ld_id128 = dma_in(id128_sb[:], id128, after=bar)
        ld_em = dma_in(emat_sb[:], emat, after=bar)
        ld_b1 = dma_in(b1_sb[:], b1c, after=bar)
        ld_b2 = dma_in(b2_sb[:], b2c, after=bar)
        ld_oc = dma_in(consts["ones_col_bf"][:], ones_col_bf, after=bar)
        ld_or = dma_in(consts["ones_row_f"][:], ones_row_f, after=bar)
        ld_eps = dma_in(consts["eps"][:], eps, after=bar)
        const_deps = [ld_oc, ld_or, ld_eps]
        gel_tok = em.o("scalar", lambda sc: sc.activation(gel_sb[:], b1_sb[:], AF.Gelu),
                       after=[ld_b1], tag="ACT")

        markA = nc.sbuf_base
        x3t = alloc("x3t", [128, 8 * W], F32)
        y1t = alloc("y1t", [128, 8 * W], F32)
        ut = alloc("ut", [128, 8 * W], F32)
        lg_sb = alloc("lg_sb", [4, W], F32)
        lgtm = alloc("lgtm", [128, 16], F32)
        m1b = alloc("m1b", [128, 4], F32)
        dbuf = alloc("dbuf", [128, 16], F32)
        sgb = alloc("sgb", [128, 16], F32)
        q1b = alloc("q1b", [128, 16], F32)
        zb = alloc("zb", [128, 16], F32)
        m2b = alloc("m2b", [128, 4], F32)
        d2b = alloc("d2b", [128, 16], F32)
        nrb = alloc("nrb", [128, 16], F32)
        selb = alloc("selb", [128, 16], F32)
        eb = alloc("eb", [128, 16], F32)
        g0b = alloc("g0b", [128, 16], F32)
        nsb = alloc("nsb", [128, 4], F32)
        rcb = alloc("rcb", [128, 4], F32)
        gtm = alloc("gtm", [128, 16], F32)
        ln_scr = {
            "xbf": alloc("ln_xbf", [128, 8 * W], BF16),
            "rows": [alloc(f"ln_r{i}", [1, W], F32) for i in range(4)],
            "bc_r": alloc("ln_bcr", [128, W], F32),
            "bc_mr": alloc("ln_bcmr", [128, W], F32),
        }
        gate_toks = []
        xf_toks = []
        x3st_toks = []
        prev_tile_toks = []
        bank_free = {}
        for tt in range(4):
            t0 = tt * W
            bar_tile = [t for t in (em.engine_tok("tensor"), em.engine_tok("vector"),
                                    em.engine_tok("scalar")) if t] + \
                ([em.dma_barrier("ST")] if em.dma_idx.get("ST") else []) + \
                (bar if tt == 0 else [])
            ldx = em.o("sync", lambda e, t0=t0: e.dma_start(
                out=x3t.ap().rearrange("p (c t) -> p c t", c=8),
                in_=x2T[:, t0:t0 + W].rearrange("(c p) t -> p c t", p=128)),
                after=bar_tile, tag="LD", inc=16)
            ldo = em.o("sync", lambda e, t0=t0: e.dma_start(
                out=y1t.ap().rearrange("p (c t) -> p c t", c=8),
                in_=cc3o[:, t0:t0 + W].rearrange("(c p) t -> p c t", p=128)),
                after=bar_tile + [cc_tok], tag="LD", inc=16)
            proj_toks = []
            prev_pj = None
            for oc in range(8):
                ev = em.o("vector", lambda v, oc=oc: v.scalar_tensor_tensor(
                    x3t[:, oc * W:(oc + 1) * W], y1t[:, oc * W:(oc + 1) * W],
                    ob_sb[:, oc:oc + 1],
                    x3t[:, oc * W:(oc + 1) * W], ALU.add, ALU.add),
                    after=[ldo, ldx, ld_ob] + prev_tile_toks +
                    ([prev_pj] if prev_pj else []), tag="DVE")
                prev_pj = ev
                proj_toks.append(ev)
            st_x3 = em.o("gpsimd", lambda e, t0=t0: e.dma_start(
                out=x3scr[:, t0:t0 + W].rearrange("(c p) t -> p c t", p=128),
                in_=x3t.ap().rearrange("p (c t) -> p c t", c=8)),
                after=[em.echo(proj_toks, "scalar", "ACT")], tag="ST", inc=16)
            x3st_toks.append(st_x3)
            (l1, l2), bc_r, bc_mr = emit_ln_stats(
                em, consts, ps, lambda c: x3t[:, c * W:(c + 1) * W], W,
                proj_toks + const_deps, "", ln_scr)
            u_toks = []
            for c in range(8):
                um = em.o("vector", lambda v, c=c: v.tensor_mul(
                    ut[:, c * W:(c + 1) * W], x3t[:, c * W:(c + 1) * W], bc_r[:, 0:W]),
                    after=[l1], tag="DVE")
                ux = em.o("vector", lambda v, c=c, t0=t0: v.tensor_sub(
                    xf[:, c * T + t0: c * T + t0 + W], ut[:, c * W:(c + 1) * W], bc_mr[:, 0:W]),
                    after=[l2, um], tag="DVE")
                xf_toks.append(ux)
                uc = em.o("vector", lambda v, c=c, t0=t0: v.tensor_copy(
                    ut[:, c * W:(c + 1) * W], xf[:, c * T + t0: c * T + t0 + W]),
                    after=[ux], tag="DVE")
                u_toks.append(uc)
            mm = None
            for c in range(8):
                mm = em.o("tensor", lambda t, c=c: t.matmul(
                    pa[0:4, 0:W], wr_sb[:, c * 4:(c + 1) * 4], ut[:, c * W:(c + 1) * W],
                    start=(c == 0), stop=(c == 7)),
                    after=[u_toks[c], ld_wr] if c == 0 else [u_toks[c]], tag="PE")
            lg = em.o("vector", lambda v: v.tensor_scalar(
                lg_sb[0:4, 0:W], pa[0:4, 0:W], brc_sb[0:4, 0:1], None, ALU.add),
                after=[mm, ld_br], tag="DVE")
            gs = [slice(g * 4, (g + 1) * 4) for g in range(4)]
            tr_toks = []
            for g in range(4):
                tr = em.o("tensor", lambda t, g=g: t.transpose(
                    pt[0:128, g * 4:(g + 1) * 4], lg_sb[0:4, g * 128:(g + 1) * 128],
                    id4_sb[:]),
                    after=[lg, ld_id4] + (tr_toks[-1:] if tr_toks else []) +
                    (gate_toks[-1:] if gate_toks else []), tag="PE")
                tr_toks.append(em.o("vector", lambda v, g=g: v.tensor_copy(
                    lgtm[:, gs[g]], pt[0:128, g * 4:(g + 1) * 4]),
                    after=[tr], tag="DVE"))
            t_m1 = [em.o("vector", lambda v, g=g: v.reduce_max(
                m1b[:, g:g + 1], lgtm[:, gs[g]], axis=AX.X), after=[tr_toks[g]], tag="DVE")
                for g in range(4)]
            t_d = [em.o("vector", lambda v, g=g: v.tensor_scalar(
                dbuf[:, gs[g]], lgtm[:, gs[g]], m1b[:, g:g + 1], None, ALU.subtract),
                after=[t_m1[g]], tag="DVE") for g in range(4)]
            t_sg = [em.o("scalar", lambda sc, g=g: sc.sign(
                sgb[:, gs[g]], dbuf[:, gs[g]], bias=0.0), after=[t_d[g]], tag="ACT")
                for g in range(4)]
            t_q1 = [em.o("vector", lambda v, g=g: v.tensor_scalar(
                q1b[:, gs[g]], sgb[:, gs[g]], 1.0, -BIG, ALU.add, ALU.mult),
                after=[t_sg[g]], tag="DVE") for g in range(4)]
            t_z = [em.o("vector", lambda v, g=g: v.tensor_add(
                zb[:, gs[g]], dbuf[:, gs[g]], q1b[:, gs[g]]),
                after=[t_q1[g], t_d[g]], tag="DVE") for g in range(4)]
            t_m2 = [em.o("vector", lambda v, g=g: v.reduce_max(
                m2b[:, g:g + 1], zb[:, gs[g]], axis=AX.X), after=[t_z[g]], tag="DVE")
                for g in range(4)]
            t_d2 = [em.o("vector", lambda v, g=g: v.tensor_scalar(
                d2b[:, gs[g]], dbuf[:, gs[g]], m2b[:, g:g + 1], None, ALU.subtract),
                after=[t_m2[g]], tag="DVE") for g in range(4)]
            t_s2 = [em.o("scalar", lambda sc, g=g: sc.sign(
                d2b[:, gs[g]], d2b[:, gs[g]], bias=0.0), after=[t_d2[g]], tag="ACT")
                for g in range(4)]
            t_nr = [em.o("vector", lambda v, g=g: v.tensor_scalar(
                nrb[:, gs[g]], d2b[:, gs[g]], -1.0, 0.0, ALU.mult, ALU.max),
                after=[t_s2[g]], tag="DVE") for g in range(4)]
            t_sel = [em.o("vector", lambda v, g=g: v.tensor_scalar(
                selb[:, gs[g]], nrb[:, gs[g]], -1.0, 1.0, ALU.mult, ALU.add),
                after=[t_nr[g]], tag="DVE") for g in range(4)]
            t_e = [em.o("scalar", lambda sc, g=g: sc.activation(
                eb[:, gs[g]], dbuf[:, gs[g]], AF.Exp), after=[t_d[g]], tag="ACT")
                for g in range(4)]
            t_g0 = [em.o("vector", lambda v, g=g: v.tensor_mul(
                g0b[:, gs[g]], eb[:, gs[g]], selb[:, gs[g]]),
                after=[t_e[g], t_sel[g]], tag="DVE") for g in range(4)]
            t_ns = [em.o("vector", lambda v, g=g: v.reduce_sum(
                nsb[:, g:g + 1], g0b[:, gs[g]], axis=AX.X), after=[t_g0[g]], tag="DVE")
                for g in range(4)]
            t_rc = [em.o("vector", lambda v, g=g: v.reciprocal(
                rcb[:, g:g + 1], nsb[:, g:g + 1]), after=[t_ns[g]], tag="DVE")
                for g in range(4)]
            t_gg = [em.o("vector", lambda v, g=g: v.tensor_scalar(
                gtm[:, gs[g]], g0b[:, gs[g]], rcb[:, g:g + 1], None, ALU.mult),
                after=[t_rc[g]], tag="DVE") for g in range(4)]
            for g in range(4):
                trb = em.o("tensor", lambda t, g=g: t.transpose(
                    pt[0:4, 0:128], gtm[:, gs[g]], id128_sb[:]),
                    after=[t_gg[g], ld_id128, tr_toks[-1]] +
                    (gate_toks[-1:] if gate_toks else []), tag="PE")
                gate_toks.append(em.o("vector", lambda v, g=g, t0=t0: v.tensor_copy(
                    gT[0:4, t0 + g * 128: t0 + (g + 1) * 128], pt[0:4, 0:128]),
                    after=[trb], tag="DVE"))
            prev_tile_toks = proj_toks + [lg]
        barA = em.phase_barrier()
        nc.sbuf_base = markA

        markB = nc.sbuf_base
        w1b = [alloc(f"w1b{i}", [128, 8 * 512], BF16) for i in range(2)]
        w2b = [alloc(f"w2b{i}", [128, 32 * 128], BF16) for i in range(2)]
        h1 = alloc("h1", [128, 32 * 1024], BF16)
        bct = alloc("bct", [128, 2 * W], F32)
        yt = alloc("yt", [128, 2 * W], F32)
        HB = 512
        acc_toks = []
        c_toks = []
        war_yt = {0: None, 1: None}
        ld_prev = {0: None, 1: None}
        ld2_prev = {0: None, 1: None}
        bct_prev = [None]
        for e in range(E):
            for half in range(2):
                h0 = half * 1024
                st1_toks = []
                for hb in range(8):
                    buf = w1b[hb % 2]
                    war = ld_prev[hb % 2]
                    ldw = em.o("gpsimd", lambda g, e=e, hb=hb, buf=buf: g.dma_start(
                        out=buf.ap().rearrange("p (c t) -> p c t", c=8),
                        in_=w1[e, :, hb * HB:(hb + 1) * HB].rearrange("(c p) t -> p c t", p=128)),
                        after=([war] if war else []) + barA, tag="LDG", inc=16)
                    hb_toks = []
                    for hc in range(4):
                        for tg in range(2):
                            slot1 = (hc * 2 + tg) % 2
                            pp = p0 if slot1 == 0 else p1
                            mm = None
                            for c in range(8):
                                mm = em.o("tensor", lambda t, c=c, hc=hc, tg=tg, buf=buf, pp=pp, h0=h0: t.matmul(
                                    pp[:, 0:W], buf[:, c * HB + hc * 128:c * HB + (hc + 1) * 128],
                                    xf[:, c * T + h0 + tg * W: c * T + h0 + (tg + 1) * W],
                                    start=(c == 0), stop=(c == 7)),
                                    after=[ldw] + barA +
                                    ([bank_free[slot1]] if bank_free.get(slot1) else [])
                                    if c == 0 else (), tag="PE")
                            evg = em.o("scalar", lambda sc, hb=hb, hc=hc, tg=tg, pp=pp, e=e: sc.activation(
                                h1[:, (hb * 4 + hc) * 1024 + tg * W: (hb * 4 + hc) * 1024 + (tg + 1) * W],
                                pp[:, 0:W], AF.Gelu,
                                bias=b1_sb[:, e * 32 + hb * 4 + hc: e * 32 + hb * 4 + hc + 1]),
                                after=[mm, ld_b1], tag="ACT")
                            bank_free[slot1] = evg
                            hb_toks.append(evg)
                            st1_toks.append(evg)
                    ld_prev[hb % 2] = em.echo(hb_toks, "scalar", "ACT")
                st1_bar = em.echo(st1_toks, "scalar", "ACT")
                btoks = []
                for tg in range(2):
                    bmm = em.o("tensor", lambda t, e=e, h0=h0, tg=tg: t.matmul(
                        pa[:, 0:W], emat_sb[:, e * 128:(e + 1) * 128],
                        gT[0:4, h0 + tg * W: h0 + (tg + 1) * W],
                        start=True, stop=True),
                        after=[ld_em] + barA + ([bct_prev[0]] if bct_prev[0] else [])
                        + btoks[-1:], tag="PE")
                    btoks.append(em.o("vector", lambda v, tg=tg: v.tensor_copy(
                        bct[:, tg * W:(tg + 1) * W], pa[:, 0:W]),
                        after=[bmm], tag="DVE"))
                bct_prev[0] = em.echo(btoks, "scalar", "ACT")
                for dc in range(8):
                    buf = w2b[dc % 2]
                    war = ld2_prev[dc % 2]
                    ldw2 = em.o("gpsimd", lambda g, e=e, dc=dc, buf=buf: g.dma_start(
                        out=buf.ap().rearrange("p (c t) -> p c t", c=32),
                        in_=w2[e, :, dc * 128:(dc + 1) * 128].rearrange("(c p) t -> p c t", p=128)),
                        after=([war] if war else []) + barA, tag="LDG", inc=16)
                    dc_reads = []
                    for tg in range(2):
                        pp = p0 if tg == 0 else p1
                        mm = None
                        for hc in range(32):
                            mm = em.o("tensor", lambda t, hc=hc, tg=tg, buf=buf, pp=pp: t.matmul(
                                pp[:, 0:W], buf[:, hc * 128:(hc + 1) * 128],
                                h1[:, hc * 1024 + tg * W: hc * 1024 + (tg + 1) * W],
                                start=(hc == 0), stop=(hc == 31)),
                                after=[ldw2, st1_bar] +
                                ([bank_free[tg]] if bank_free.get(tg) else [])
                                if hc == 0 else (), tag="PE")
                        slot = tg
                        tv = em.o("vector", lambda v, pp=pp, slot=slot: v.tensor_mul(
                            yt[:, slot * W:(slot + 1) * W], pp[:, 0:W], bct[:, slot * W:(slot + 1) * W]),
                            after=[mm, bct_prev[0]] + ([war_yt[slot]] if war_yt[slot] else []),
                            tag="DVE")
                        bank_free[slot] = tv
                        tv_e = em.echo([tv], "scalar", "ACT")
                        acc = em.o("gpsimd", lambda g, dc=dc, h0=h0, tg=tg, slot=slot: g.dma_start(
                            out=x3scr[dc * 128:(dc + 1) * 128, h0 + tg * W: h0 + (tg + 1) * W],
                            in_=yt[:, slot * W:(slot + 1) * W], accum_op=ALU.add),
                            after=[tv_e] + x3st_toks, tag="ST", inc=16)
                        acc_toks.append(acc)
                        war_yt[slot] = acc
                        dc_reads.append(tv)
                    if half == 0:
                        mmc = None
                        for hc in range(32):
                            mmc = em.o("tensor", lambda t, hc=hc, dc=dc, e=e, buf=buf: t.matmul(
                                pc[:, 4 + e:4 + e + 1], buf[:, hc * 128:(hc + 1) * 128],
                                gel_sb[:, e * 32 + hc:e * 32 + hc + 1],
                                start=(hc == 0), stop=(hc == 31)),
                                after=[ldw2, gel_tok] + (c_toks[-1:] if c_toks else [])
                                if hc == 0 else (), tag="PE")
                        c_toks.append(em.o("vector", lambda v, dc=dc, e=e: v.tensor_copy(
                            c_sb[:, dc * 4 + e:dc * 4 + e + 1], pc[:, 4 + e:4 + e + 1]),
                            after=[mmc], tag="DVE"))
                        dc_reads.append(c_toks[-1])
                    ld2_prev[dc % 2] = em.echo(dc_reads, "scalar", "ACT")
        barB = em.phase_barrier()
        nc.sbuf_base = markB

        x3c = alloc("x3c", [128, 8 * W], F32)
        cbar = em.echo(c_toks, "vector", "DVE")
        ct_toks = []
        for dc in range(8):
            trc = em.o("tensor", lambda t, dc=dc: t.transpose(
                pt[0:4, 0:128], c_sb[:, dc * 4:(dc + 1) * 4], id128_sb[:]),
                after=[cbar, ld_id128] + barB + (ct_toks[-2:-1] if ct_toks else []) +
                (gate_toks[-1:] if gate_toks else []), tag="PE")
            ct_toks.append(em.o("vector", lambda v, dc=dc: v.tensor_copy(
                cT_sb[0:4, dc * 128:(dc + 1) * 128], pt[0:4, 0:128]),
                after=[trc], tag="DVE"))
            rs1 = em.o("vector", lambda v, dc=dc: v.reduce_sum(
                rs_sb[:, dc:dc + 1], c_sb[:, dc * 4:(dc + 1) * 4], axis=AX.X),
                after=[cbar], tag="DVE")
            rs2 = em.o("vector", lambda v, dc=dc: v.reduce_sum(
                rs_sb[:, 8 + dc:8 + dc + 1], b2_sb[:, dc * 4:(dc + 1) * 4], axis=AX.X),
                after=[ld_b2], tag="DVE")
            ct_toks.append(em.o("vector", lambda v, dc=dc: v.tensor_add(
                csum_sb[:, dc:dc + 1], rs_sb[:, dc:dc + 1], rs_sb[:, 8 + dc:8 + dc + 1]),
                after=[rs1, rs2, rs1], tag="DVE"))
        prev_out = []
        for tt in range(4):
            t0 = tt * W
            ldx3 = em.o("sync", lambda e, t0=t0: e.dma_start(
                out=x3c.ap().rearrange("p (c t) -> p c t", c=8),
                in_=x3scr[:, t0:t0 + W].rearrange("(c p) t -> p c t", p=128)),
                after=barB + prev_out + [em.dma_barrier("ST")], tag="LD", inc=16)
            out_toks = []
            prev_o1 = None
            for dc in range(8):
                mmt = em.o("tensor", lambda t, dc=dc, t0=t0: t.matmul(
                    pb[:, 0:W], cT_sb[0:4, dc * 128:(dc + 1) * 128], gT[0:4, t0:t0 + W],
                    start=True, stop=True),
                    after=ct_toks + [ldx3] + ([prev_o1] if prev_o1 else []), tag="PE")
                o1_ = em.o("vector", lambda v, dc=dc: v.scalar_tensor_tensor(
                    x3c[:, dc * W:(dc + 1) * W], pb[:, 0:W], -1.0,
                    x3c[:, dc * W:(dc + 1) * W], ALU.mult, ALU.add),
                    after=[mmt, ldx3], tag="DVE")
                o2_ = em.o("vector", lambda v, dc=dc: v.tensor_scalar(
                    x3c[:, dc * W:(dc + 1) * W], x3c[:, dc * W:(dc + 1) * W],
                    csum_sb[:, dc:dc + 1], None, ALU.add),
                    after=[o1_], tag="DVE")
                prev_o1 = o1_
                out_toks.append(o2_)
            ek = em.echo(out_toks, "scalar", "ACT")
            sto = em.o("gpsimd", lambda e, t0=t0: e.dma_start(
                out=x4T[:, t0:t0 + W].rearrange("(c p) t -> p c t", p=128),
                in_=x3c.ap().rearrange("p (c t) -> p c t", c=8)),
                after=[ek], tag="ST", inc=16)
            prev_out = [sto]



def emit_phase3(nc, em, G, bar, cc_tok, phases=(1, 2), nt_limit=None):
    W = 512
    NT = S // W
    cc2o = G["cc2o"]
    qw, kw, vw = G["qw"], G["kw"], G["vw"]
    qb, kb, vb = G["qb"], G["kb"], G["vb"]
    masks = G["masks"]
    id64 = G["idt"]
    ones_row_f = G["ones_row_f"]
    out1T = G["out1d"]

    with ExitStack() as stk:
        en = stk.enter_context

        def alloc(name, shape, dt):
            return nc.alloc_sbuf_tensor("a_" + name, list(shape), dt)

        q_sb = alloc("q_sb", [128, 2 * S], BF16)
        k_sb = alloc("k_sb", [128, 2 * S], BF16)
        v_tm = alloc("v_tm", [128, 64 * 4 * 65], BF16)
        ht = [alloc(f"ht{i}", [128, 8 * W], BF16) for i in range(2)]
        vtmp = alloc("vtmp", [128, 2 * W], BF16)
        qw_sb = alloc("qw_sb", [128, 8 * 256], BF16)
        kw_sb = alloc("kw_sb", [128, 8 * 256], BF16)
        vw_sb = alloc("vw_sb", [128, 8 * 256], BF16)
        qb_sb = alloc("qb_sb", [128, 2], F32)
        kb_sb = alloc("kb_sb", [128, 2], F32)
        vb_sb = alloc("vb_sb", [128, 2], F32)
        mask_sb = alloc("mask_sb", [128, 4 * W], F32)
        id_sb = alloc("id_sb", [128, 128], BF16)
        or_sb = alloc("or_sb", [1, 128], F32)
        w_sb = [alloc(f"w_sb{i}", [128, W], BF16) for i in range(2)]
        row_sb = alloc("row_sb", [1, W], F32)
        rc_sb = alloc("rc_sb", [1, W], F32)
        bc_sb = alloc("bc_sb", [64, W], F32)
        out_sb = [alloc(f"out_sb{i}", [64, W], BF16) for i in range(2)]
        p_s = [en(nc.psum_tensor(f"a_p_s{i}", [128, W], F32)) for i in range(2)]
        p_o = en(nc.psum_tensor("a_p_o", [128, W], F32))
        p_b = en(nc.psum_tensor("a_p_b", [64, W], F32))
        pt_bf2 = [en(nc.psum_tensor(f"a_pt_bf{i}", [128, 256], BF16)) for i in range(2)]

        def dma_in(dst, src, q="sync", after=()):
            return em.o(q, lambda e, dst=dst, src=src: e.dma_start(out=dst, in_=src),
                        after=after, tag="LD" if q == "sync" else "LDG", inc=16)

        ld_qw = dma_in(qw_sb.ap().rearrange("p (c o) -> p c o", c=8),
                       qw.rearrange("(c p) o -> p c o", p=128), q="gpsimd", after=bar)
        ld_kw = dma_in(kw_sb.ap().rearrange("p (c o) -> p c o", c=8),
                       kw.rearrange("(c p) o -> p c o", p=128), q="gpsimd", after=bar)
        ld_vw = dma_in(vw_sb.ap().rearrange("p (c o) -> p c o", c=8),
                       vw.rearrange("(c p) o -> p c o", p=128), q="gpsimd", after=bar)
        ld_qb = dma_in(qb_sb[:], qb, after=bar)
        ld_kb = dma_in(kb_sb[:], kb, after=bar)
        ld_vb = dma_in(vb_sb[:], vb, after=bar)
        ld_m = dma_in(mask_sb[:], masks, after=bar)
        ld_id = dma_in(id_sb[:], id64, after=bar)
        ld_or = dma_in(or_sb[:], ones_row_f, after=bar)

        ones_tok = em.o("vector", lambda v: v.memset(v_tm[:], 1.0), after=bar,
                        tag="DVE") \
            if 1 in phases else None

        qkv_toks = []
        ht_war = {0: None, 1: None}
        pbank_free = {0: None, 1: None}
        ptbf_free = {0: None, 1: None}
        for tt in range(nt_limit if nt_limit else NT):
            hb = ht[tt % 2]
            war = ht_war[tt % 2]
            ldh = em.o("sync", lambda e, tt=tt, hb=hb: e.dma_start(
                out=hb.ap().rearrange("p (c t) -> p c t", c=8),
                in_=cc2o[(tt // 4) * D:((tt // 4) + 1) * D,
                         (tt % 4) * W:((tt % 4) + 1) * W].rearrange(
                    "(c p) t -> p c t", p=128)),
                after=([war] if war else []) + [cc_tok] + bar, tag="LD", inc=16)
            tile_reads = []
            for (wsb, bsb, ldwt, ldbt, dst) in (
                    (qw_sb, qb_sb, ld_qw, ld_qb, q_sb),
                    (kw_sb, kb_sb, ld_kw, ld_kb, k_sb),
                    (vw_sb, vb_sb, ld_vw, ld_vb, None)):
                for oc in range(2):
                    pp = p_s[oc]
                    mm = None
                    for c in range(8):
                        mm = em.o("tensor", lambda t, c=c, oc=oc, wsb=wsb, hb=hb, pp=pp: t.matmul(
                            pp[:, 0:W], wsb[:, c * 256 + oc * 128: c * 256 + (oc + 1) * 128],
                            hb[:, c * W:(c + 1) * W], start=(c == 0), stop=(c == 7)),
                            after=[ldwt, ldh] +
                            ([pbank_free[oc]] if pbank_free.get(oc) else [])
                            if c == 0 else (), tag="PE")
                    if dst is not None:
                        ev = em.o("scalar", lambda sc, oc=oc, tt=tt, pp=pp, dst=dst, bsb=bsb: sc.activation(
                            dst[:, oc * S + tt * W: oc * S + (tt + 1) * W], pp[:, 0:W],
                            AF.Identity, bias=bsb[:, oc:oc + 1]),
                            after=[mm, ldbt], tag="ACT")
                    else:
                        ev = em.o("scalar", lambda sc, oc=oc, pp=pp, bsb=bsb: sc.activation(
                            vtmp[:, oc * W:(oc + 1) * W], pp[:, 0:W],
                            AF.Identity, bias=bsb[:, oc:oc + 1]),
                            after=[mm, ldbt] + [t for t in ptbf_free.values() if t],
                            tag="ACT")
                    pbank_free[oc] = ev
                    tile_reads.append(ev)
                    qkv_toks.append(ev)
            tr_evs = []
            for sc_i in (range(4) if 1 in phases else ()):
                pbank = pt_bf2[sc_i % 2]
                war_b = ptbf_free.get(sc_i % 2)
                trvs = []
                for oc in range(2):
                    if "notr" in phases:
                        trv = em.o("tensor", lambda t: t.nop(),
                                   after=[tile_reads[-1], tile_reads[-2], ld_id] +
                                   ([war_b] if war_b else []), tag="PE")
                    else:
                        trv = em.o("tensor", lambda t, oc=oc, sc_i=sc_i, pbank=pbank: t.transpose(
                            pbank[:, oc * 128:(oc + 1) * 128],
                            vtmp[:, oc * W + sc_i * 128: oc * W + (sc_i + 1) * 128],
                            id_sb[:]),
                            after=[tile_reads[-1], tile_reads[-2], ld_id] +
                            ([war_b] if war_b else []), tag="PE")
                    trvs.append(trv)
                bi = (tt * 4 + sc_i) * 4
                if "nocp" in phases:
                    cp = em.o("vector", lambda v, pbank=pbank: v.tensor_copy(
                        v_tm[:, 0:256], pbank[:, 0:256]),
                        after=trvs + [ones_tok], tag="DVE")
                else:
                    cp = em.o("vector", lambda v, bi=bi, pbank=pbank: v.tensor_copy(
                        v_tm.ap().rearrange("p (b c) -> p b c", c=65)[:, bi:bi + 4, 0:64],
                        pbank.ap().rearrange("p (h c) -> p h c", c=64)),
                        after=trvs + [ones_tok], tag="DVE")
                tr_evs.append(cp)
                ptbf_free[sc_i % 2] = em.echo([cp], "scalar", "ACT")
            qkv_toks.extend(tr_evs)
            ht_war[tt % 2] = em.echo(tile_reads, "vector", "DVE")
            if 1 not in phases:
                qkv_toks.extend(tile_reads)
        ph1 = em.phase_barrier()

        war_out = {0: None, 1: None}
        w_free = {0: None, 1: None}
        po_free = None
        pb_free = None
        for h in range(4):
            qrow = (h % 2) * 64
            qcolb = (h // 2) * S
            for m in range(NT):
                qT = q_sb[qrow:qrow + 64, qcolb + m * W: qcolb + (m + 1) * W]
                nj = 4 * m + 4
                exp_toks = {}
                qk_toks = {}
                pv_prev = None
                for j in range(nj):
                    sl = j % 2
                    dep = [ph1[0]] if (h == 0 and m == 0) else []
                    dep = list(ph1) if (h == 0 and m == 0 and j == 0) else []
                    prev_exp = exp_toks.get(j - 2)
                    qk = em.o("tensor", lambda t, j=j, sl=sl, qT=qT, qrow=qrow, qcolb=qcolb: t.matmul(
                        p_s[sl][:, 0:W],
                        k_sb[qrow:qrow + 64, qcolb + j * 128: qcolb + j * 128 + 128],
                        qT, start=True, stop=True),
                        after=dep + ([prev_exp] if prev_exp else []), tag="PE")
                    qk_toks[j] = qk
                    d = j - 4 * m
                    if d >= 0:
                        mk = em.o("vector", lambda v, sl=sl, d=d: v.tensor_add(
                            p_s[sl][:, 0:W], p_s[sl][:, 0:W], mask_sb[:, d * W:(d + 1) * W]),
                            after=[qk, ld_m], tag="DVE")
                        edep = [mk]
                    else:
                        edep = [qk]
                    wfree = w_free.get(j % 2)
                    ex = em.o("scalar", lambda sc, j=j, sl=sl: sc.activation(
                        w_sb[sl][:, 0:W], p_s[sl][:, 0:W], AF.Exp, scale=0.125),
                        after=edep + ([wfree] if wfree else []), tag="ACT")
                    exp_toks[j] = ex
                    blk = ((m * 4 + ... ) if False else 0)
                    blk = (j * 4 + h) * 65
                    pv = em.o("tensor", lambda t, j=j, sl=sl, blk=blk, m=m, nj=nj: t.matmul(
                        p_o[0:65, 0:W], v_tm[:, blk:blk + 65], w_sb[sl][:, 0:W],
                        start=(j == 0), stop=(j == nj - 1)),
                        after=[ex] + ([po_free] if (j == 0 and po_free) else []), tag="PE")
                    w_free[j % 2] = pv
                    pv_prev = pv
                se = em.o("scalar", lambda sc: sc.activation(
                    row_sb[0:1, 0:W], p_o[64:65, 0:W], AF.Identity),
                    after=[pv_prev], tag="ACT")
                rc = em.o("vector", lambda v: v.reciprocal(rc_sb[0:1, 0:W], row_sb[0:1, 0:W]),
                          after=[se], tag="DVE")
                bcmm = em.o("tensor", lambda t: t.matmul(
                    p_b[0:64, 0:W], or_sb[0:1, 0:64], rc_sb[0:1, 0:W], start=True, stop=True),
                    after=[rc, ld_or] + ([pb_free] if pb_free else []), tag="PE")
                bev = em.o("vector", lambda v: v.tensor_copy(bc_sb[:], p_b[0:64, 0:W]),
                           after=[bcmm], tag="DVE")
                pb_free = bev
                oslot = m % 2
                war = war_out[oslot]
                oev = em.o("vector", lambda v, oslot=oslot: v.tensor_mul(
                    out_sb[oslot][:], p_o[0:64, 0:W], bc_sb[:]),
                    after=[bev, pv_prev] + ([war] if war else []), tag="DVE")
                po_free = oev
                ek = em.echo([oev], "scalar", "ACT")
                st = em.o("gpsimd", lambda e, h=h, m=m, oslot=oslot: e.dma_start(
                    out=out1T[h * 64:(h + 1) * 64, m * W:(m + 1) * W],
                    in_=out_sb[oslot][:]),
                    after=[ek], tag="ST", inc=16)
                war_out[oslot] = st



def emit_phase1(nc, em, G):
    W = 512
    NT = S // W
    SC = 128
    xT = G["xT"]
    wq, wk, wv = G["wq"], G["wk"], G["wv"]
    aq, av, bq, bv = G["aq"], G["av"], G["bq"], G["bv"]
    cq, sq, phv, pho = G["cq"], G["sq"], G["phv"], G["pho"]
    rot, mask0, idt = G["rot"], G["mask0"], G["idt"]
    ones_col_bf, ones_row_f, eps = G["ones_col_bf"], G["ones_row_f"], G["eps"]
    out0T = G["out0d"]

    with ExitStack() as stk:
        en = stk.enter_context

        def alloc(name, shape, dt):
            return nc.alloc_sbuf_tensor(name, list(shape), dt)

        alq = alloc("alq", [128, 2 * S], BF16)
        pkt = alloc("pkt", [128, 2 * S], BF16)
        pvt = alloc("pvt", [128, 2 * S], BF16)
        fk = alloc("fk", [128, 2 * NS], BF16)
        fvtm = alloc("fvtm", [128, 16 * 4 * 65], BF16)
        wq_sb = alloc("wq_sb", [128, 8 * 256], BF16)
        wk_sb = alloc("wk_sb", [128, 8 * 256], BF16)
        wv_sb = alloc("wv_sb", [128, 8 * 256], BF16)
        aq_sb = alloc("aq_sb", [128, 8 * LORA], BF16)
        av_sb = alloc("av_sb", [128, 8 * LORA], BF16)
        bq_sb = alloc("bq_sb", [LORA, 256], BF16)
        bv_sb = alloc("bv_sb", [LORA, 256], BF16)
        rot_sb = alloc("rot_sb", [128, 128], F32)
        phv_sb = alloc("phv_sb", [128, W], F32)
        pho_sb = alloc("pho_sb", [64, W], F32)
        mask_sb = alloc("mask_sb", [128, W], F32)
        id_sb = alloc("id_sb", [128, 128], BF16)
        consts = {
            "ones_col_bf": alloc("c_ones_col", [128, 1], BF16),
            "ones_row_f": alloc("c_ones_row", [1, 128], F32),
            "eps": alloc("c_eps", [1, 1], F32),
        }
        xt = alloc("xt", [128, 8 * W], F32)
        ln_scr = {
            "xbf": alloc("ln_xbf", [128, 8 * W], BF16),
            "rows": [alloc(f"ln_r{i}", [1, W], F32) for i in range(4)],
            "bc_r": alloc("ln_bcr", [128, W], F32),
            "bc_mr": alloc("ln_bcmr", [128, W], F32),
        }
        ub = ln_scr["xbf"]
        bc_r = ln_scr["bc_r"]
        tmpf = xt
        tmpf2 = ln_scr["bc_mr"]
        tlora = alloc("tlora", [LORA, W], BF16)
        qf = alloc("qf", [128, 2 * W], F32)
        ksh = alloc("ksh", [128, 2 * W], BF16)
        vsh = alloc("vsh", [128, 2 * W], BF16)
        fvtmp = alloc("fvtmp", [128, 2 * SC], BF16)
        tab = alloc("tab", [128, 2 * W], F32)
        w_sb = [alloc(f"w_sb{i}", [128, W], BF16) for i in range(2)]
        out_sb = [alloc(f"out_sb{i}", [64, W], BF16) for i in range(2)]
        bcw_bf = alloc("bcw_bf", [64, W], BF16)
        p0 = en(nc.psum_tensor("p0", [128, W], F32))
        p1 = en(nc.psum_tensor("p1", [128, W], F32))
        pa = en(nc.psum_tensor("pa", [128, W], F32))
        pb = en(nc.psum_tensor("pb", [128, W], F32))
        p_o = en(nc.psum_tensor("p_o", [128, W], F32))
        p_d = en(nc.psum_tensor("p_d", [1, W], F32))
        pl = en(nc.psum_tensor("pl", [LORA, W], F32))
        pt_bf = en(nc.psum_tensor("pt_bf", [128, 256], BF16))
        ps = {"stat_a": pa, "stat_b": pb}

        def dma_in(dst, src, q="sync"):
            return em.o(q, lambda e, dst=dst, src=src: e.dma_start(out=dst, in_=src),
                        tag="LD" if q == "sync" else "LDG", inc=16)

        ld_wq = dma_in(wq_sb.ap().rearrange("p (c o) -> p c o", c=8),
                       wq.rearrange("(c p) o -> p c o", p=128), q="gpsimd")
        ld_wk = dma_in(wk_sb.ap().rearrange("p (c o) -> p c o", c=8),
                       wk.rearrange("(c p) o -> p c o", p=128), q="gpsimd")
        ld_wv = dma_in(wv_sb.ap().rearrange("p (c o) -> p c o", c=8),
                       wv.rearrange("(c p) o -> p c o", p=128), q="gpsimd")
        ld_aq = dma_in(aq_sb.ap().rearrange("p (c o) -> p c o", c=8),
                       aq.rearrange("(c p) o -> p c o", p=128), q="gpsimd")
        ld_av = dma_in(av_sb.ap().rearrange("p (c o) -> p c o", c=8),
                       av.rearrange("(c p) o -> p c o", p=128), q="gpsimd")
        ld_bq = dma_in(bq_sb[:], bq, q="gpsimd")
        ld_bv = dma_in(bv_sb[:], bv, q="gpsimd")
        ld_rot = dma_in(rot_sb[:], rot)
        ld_phv = dma_in(phv_sb[:], phv)
        ld_pho = dma_in(pho_sb[:], pho)
        ld_m = dma_in(mask_sb[:], mask0)
        ld_id = dma_in(id_sb[:], idt)
        ld_oc = dma_in(consts["ones_col_bf"][:], ones_col_bf)
        ld_or = dma_in(consts["ones_row_f"][:], ones_row_f)
        ld_eps = dma_in(consts["eps"][:], eps)
        const_deps = [ld_oc, ld_or, ld_eps]
        wkh = em.o("vector", lambda v: v.tensor_scalar(
            wk_sb[:], wk_sb[:], 0.5, None, ALU.mult), after=[ld_wk], tag="DVE")
        ones_tok = em.o("vector", lambda v: v.memset(fvtm[:], 1.0), tag="DVE")

        xt_war = None
        tab_war = None
        ub_war = None
        dve_war = None
        fvtmp_war = []
        ptbf_war = None
        shm_prev = {}
        for tt in range(NT):
            t0 = tt * W
            ldx = em.o("sync", lambda e, t0=t0: e.dma_start(
                out=xt.ap().rearrange("p (c t) -> p c t", c=8),
                in_=xT[:, t0:t0 + W].rearrange("(c p) t -> p c t", p=128)),
                after=[xt_war] if xt_war else [], tag="LD", inc=16)
            ldt = []
            for i, tsrc in enumerate((cq, sq)):
                ldt.append(em.o("sync", lambda e, i=i, tsrc=tsrc, t0=t0: e.dma_start(
                    out=tab[:, i * W:(i + 1) * W], in_=tsrc[:, t0:t0 + W]),
                    after=[tab_war] if tab_war else [], tag="LD", inc=16))
            (l1, l2), _, bc_m = emit_ln_stats(
                em, consts, ps, lambda c: xt[:, c * W:(c + 1) * W], W,
                [ldx] + const_deps + ([ub_war] if ub_war else [])
                + ([dve_war] if dve_war else []), "", ln_scr,
                raw_mean=True)
            u_toks = []
            for c in range(8):
                u_toks.append(em.o("vector", lambda v, c=c: v.tensor_sub(
                    ub[:, c * W:(c + 1) * W], xt[:, c * W:(c + 1) * W], bc_m[:, 0:W]),
                    after=[l2], tag="DVE"))
            evs = {}
            rope_toks = []
            proj_last = None

            def lora_pass(asb, lda, prev_read):
                mmt = None
                for c in range(8):
                    mmt = em.o("tensor", lambda t, c=c, asb=asb: t.matmul(
                        pl[:, 0:W], asb[:, c * LORA:(c + 1) * LORA],
                        ub[:, c * W:(c + 1) * W], start=(c == 0), stop=(c == 7)),
                        after=([u_toks[c], lda] + prev_read) if c == 0 else [u_toks[c]],
                        tag="PE")
                return em.o("vector", lambda v: v.tensor_copy(tlora[:, 0:W], pl[:, 0:W]),
                            after=[mmt], tag="DVE")

            def proj(nm, wsb, bsb, ldwt, ldb, tl, oc, bank_dep):
                pp = p0 if oc == 0 else p1
                mm = None
                for c in range(8):
                    mm = em.o("tensor", lambda t, c=c, oc=oc, wsb=wsb, pp=pp: t.matmul(
                        pp[:, 0:W], wsb[:, c * 256 + oc * 128: c * 256 + (oc + 1) * 128],
                        ub[:, c * W:(c + 1) * W],
                        start=(c == 0), stop=(c == 7 and tl is None)),
                        after=([ldwt, u_toks[c]] + bank_dep) if c == 0 else [u_toks[c]],
                        tag="PE")
                if tl is not None:
                    mm = em.o("tensor", lambda t, oc=oc, bsb=bsb, pp=pp: t.matmul(
                        pp[:, 0:W], bsb[0:LORA, oc * 128:(oc + 1) * 128], tlora[:, 0:W],
                        start=False, stop=True), after=[tl, ldb], tag="PE")
                return mm, pp

            tlq = lora_pass(aq_sb, ld_aq, [])
            qmm = {}
            for oc in range(2):
                mm, pp = proj("q", wq_sb, bq_sb, ld_wq, ld_bq, tlq, oc, [])
                ev = em.o("vector", lambda v, oc=oc, pp=pp: v.tensor_mul(
                    qf[:, oc * W:(oc + 1) * W], pp[:, 0:W], bc_r[:, 0:W]),
                    after=[mm, l1], tag="DVE")
                evs[("q", oc)] = ev
                qmm[oc] = mm

            def rope(src_oc_ap, oc, dst_ap, psh, bank_dep, src_dep, war):
                shm = em.o("tensor", lambda t, src_oc_ap=src_oc_ap, psh=psh: t.matmul(
                    psh[:, 0:W], rot_sb[:], src_oc_ap, start=True, stop=True),
                    after=[src_dep, ld_rot] + bank_dep, tag="PE")
                t1 = em.o("vector", lambda v, src_oc_ap=src_oc_ap: v.tensor_mul(
                    tmpf[:, 0:W], src_oc_ap, tab[:, 0:W]),
                    after=[src_dep] + war + ldt, tag="DVE")
                t2 = em.o("vector", lambda v, psh=psh: v.tensor_mul(
                    tmpf2[:, 0:W], psh[:, 0:W], tab[:, W:2 * W]),
                    after=[shm, src_dep] + war + ldt, tag="DVE")
                t3 = em.o("vector", lambda v, dst_ap=dst_ap: v.tensor_add(
                    dst_ap, tmpf[:, 0:W], tmpf2[:, 0:W]), after=[t1, t2], tag="DVE")
                return shm, t2, t3

            rope_war = []
            for oc in range(2):
                psh = pa if oc == 0 else pb
                bank_dep = [l1 if oc == 0 else l2] + (shm_prev.get(oc) or [])
                shm, t2, t3 = rope(qf[:, oc * W:(oc + 1) * W], oc,
                                   alq[:, oc * S + t0: oc * S + t0 + W],
                                   psh, bank_dep, evs[("q", oc)], rope_war)
                rope_toks.append(t3)
                rope_war = [t3]
                evs[("ropeq", oc)] = (shm, t2, t3)

            for oc in range(2):
                mm, pp = proj("k", wk_sb, None, wkh, None, None, oc,
                              [evs[("q", oc)]])
                ev = em.o("vector", lambda v, oc=oc, pp=pp: v.tensor_mul(
                    qf[:, oc * W:(oc + 1) * W], pp[:, 0:W], bc_r[:, 0:W]),
                    after=[mm, evs[("ropeq", oc)][0], evs[("ropeq", oc)][2]], tag="DVE")
                evs[("k", oc)] = ev
            for oc in range(2):
                psh = pa if oc == 0 else pb
                bank_dep = [evs[("ropeq", oc)][1]]
                shm, t2, t3 = rope(qf[:, oc * W:(oc + 1) * W], oc,
                                   ksh[:, oc * W:(oc + 1) * W],
                                   psh, bank_dep, evs[("k", oc)], rope_war)
                rope_toks.append(t3)
                rope_war = [t3]
                evs[("ropek", oc)] = (shm, t2, t3)
                shm_prev[oc] = [t2]

            bcp = em.o("vector", lambda v: v.tensor_mul(
                bc_r[:, 0:W], bc_r[:, 0:W], phv_sb[:]),
                after=[l1, ld_phv, evs[("k", 1)]], tag="DVE")
            tlv = lora_pass(av_sb, ld_av, [qmm[1], tlq])
            for oc in range(2):
                mm, pp = proj("v", wv_sb, bv_sb, ld_wv, ld_bv, tlv, oc,
                              [evs[("k", oc)]])
                ev = em.o("vector", lambda v, oc=oc, pp=pp: v.tensor_mul(
                    vsh[:, oc * W:(oc + 1) * W], pp[:, 0:W], bc_r[:, 0:W]),
                    after=[mm, bcp], tag="DVE")
                evs[("v", oc)] = ev
                if oc == 1:
                    proj_last = mm
            ub_war = proj_last

            comp_last = None
            for oc in range(2):
                for (srcb, isk) in ((ksh, True), (vsh, False)):
                    sdep = evs[("ropek", oc)][2] if isk else evs[("v", oc)]
                    base = oc * W

                    def s0(j, srcb=srcb, base=base):
                        return srcb[:, base + j: base + W: 4]
                    if isk:
                        dsts = fk[:, oc * NS + tt * SC: oc * NS + (tt + 1) * SC]
                    else:
                        dsts = fvtmp[:, oc * SC:(oc + 1) * SC]
                    a1 = em.o("vector", lambda v, s0=s0: v.tensor_add(
                        tmpf[:, 0:SC], s0(0), s0(1)),
                        after=[sdep] + ([comp_last] if comp_last else rope_war),
                        tag="DVE")
                    a2 = em.o("vector", lambda v, s0=s0: v.tensor_add(
                        tmpf[:, 0:SC], tmpf[:, 0:SC], s0(2)), after=[a1], tag="DVE")
                    a3 = em.o("vector", lambda v, s0=s0, dsts=dsts: v.tensor_add(
                        dsts, tmpf[:, 0:SC], s0(3)),
                        after=[a2] + (fvtmp_war if not isk else []), tag="DVE")
                    comp_last = a3
                    if not isk:
                        evs[("fv", oc)] = a3
                for (srcb, dstb, sdep) in ((ksh, pkt, evs[("ropek", oc)][2]),
                                           (vsh, pvt, evs[("v", oc)])):
                    base = oc * W
                    rbase = oc * S + t0

                    def s0(j, srcb=srcb, base=base):
                        return srcb[:, base + j: base + W: 4]

                    def d0(j, dstb=dstb, rbase=rbase):
                        return dstb[:, rbase + j: rbase + W: 4]
                    cprev = em.o("vector", lambda v, s0=s0, d0=d0: v.tensor_copy(
                        d0(0), s0(0)), after=[sdep], tag="DVE")
                    for j in range(1, 4):
                        cprev = em.o("vector", lambda v, j=j, s0=s0, d0=d0: v.tensor_add(
                            d0(j), d0(j - 1), s0(j)), after=[cprev], tag="DVE")
                    comp_last = cprev
            fvtmp_war = []
            trvs = []
            for oc in range(2):
                trv = em.o("tensor", lambda t, oc=oc: t.transpose(
                    pt_bf[:, oc * 128:(oc + 1) * 128], fvtmp[:, oc * SC:(oc + 1) * SC],
                    id_sb[:]),
                    after=[evs[("fv", oc)], ld_id] + ([ptbf_war] if ptbf_war else []),
                    tag="PE")
                fvtmp_war.append(trv)
                trvs.append(trv)
            cps = []
            for oc in range(2):
                for hh in range(2):
                    h = 2 * oc + hh
                    cp = em.o("vector", lambda v, tt=tt, h=h, oc=oc, hh=hh: v.tensor_copy(
                        fvtm[:, (tt * 4 + h) * 65:(tt * 4 + h) * 65 + 64],
                        pt_bf[:, oc * 128 + hh * 64: oc * 128 + hh * 64 + 64]),
                        after=trvs + [ones_tok], tag="DVE")
                    cps.append(cp)
            ptbf_war = em.echo(cps, "scalar", "ACT")
            comp_last = cps[-1]
            dve_war = comp_last
            tab_war = em.echo([evs[("ropek", 1)][1]], "scalar", "ACT")
            xt_war = em.echo([comp_last], "scalar", "ACT")
        ph1 = em.phase_barrier()

        rows = ln_scr["rows"]
        wd_row, se_row, rc_row = rows[0], rows[1], rows[2]
        bcw = xt
        t2f_ap = lambda: xt[0:64, W:2 * W]
        bcf_ap = lambda: ln_scr["bc_r"][0:64, 0:W]
        war_out = {0: None, 1: None}
        w_free = {0: None, 1: None}
        po_free = None
        pd_free = None
        pa_free = None
        pb_free = None
        t2v_prev = None
        bfm_prev = None
        oc_sb = consts["ones_col_bf"]
        or_sb = consts["ones_row_f"]
        for h in range(4):
            qrow = (h % 2) * 64
            fcolb = (h // 2) * NS
            qcolb = (h // 2) * S
            for m in range(NT):
                qT = alq[qrow:qrow + 64, qcolb + m * W: qcolb + (m + 1) * W]
                nj = m + 1
                exp_toks = {}
                pv_prev = None
                for j in range(nj):
                    sl = j % 2
                    dep = list(ph1) if (h == 0 and m == 0 and j == 0) else []
                    prev_exp = exp_toks.get(j - 2)
                    qk = em.o("tensor", lambda t, j=j, sl=sl, qT=qT, qrow=qrow, fcolb=fcolb: t.matmul(
                        (p0 if sl == 0 else p1)[:, 0:W],
                        fk[qrow:qrow + 64, fcolb + j * 128: fcolb + j * 128 + 128],
                        qT, start=True, stop=True),
                        after=dep + ([prev_exp] if prev_exp else []), tag="PE")
                    if j == m:
                        mk = em.o("vector", lambda v, sl=sl: v.tensor_add(
                            (p0 if sl == 0 else p1)[:, 0:W],
                            (p0 if sl == 0 else p1)[:, 0:W], mask_sb[:]),
                            after=[qk, ld_m], tag="DVE")
                        edep = [mk]
                    else:
                        edep = [qk]
                    wfree = w_free.get(sl)
                    ex = em.o("scalar", lambda sc, sl=sl: sc.activation(
                        w_sb[sl][:, 0:W], (p0 if sl == 0 else p1)[:, 0:W],
                        AF.Exp, scale=0.125),
                        after=edep + ([wfree] if wfree else []), tag="ACT")
                    exp_toks[j] = ex
                    blk = (j * 4 + h) * 65
                    pv = em.o("tensor", lambda t, sl=sl, blk=blk, j=j, nj=nj: t.matmul(
                        p_o[0:65, 0:W], fvtm[:, blk:blk + 65], w_sb[sl][:, 0:W],
                        start=(j == 0), stop=(j == nj - 1)),
                        after=[ex] + ([po_free] if (j == 0 and po_free) else []),
                        tag="PE")
                    w_free[sl] = pv
                    pv_prev = pv
                pr = em.o("vector", lambda v, qT=qT, qrow=qrow, qcolb=qcolb, m=m: v.tensor_mul(
                    vsh[qrow:qrow + 64, 0:W], qT,
                    pkt[qrow:qrow + 64, qcolb + m * W: qcolb + (m + 1) * W]),
                    after=(list(ph1) if (h == 0 and m == 0) else [])
                    + ([pd_free] if pd_free else []), tag="DVE")
                dmm = em.o("tensor", lambda t, qrow=qrow: t.matmul(
                    p_d[0:1, 0:W], oc_sb[qrow:qrow + 64, 0:1], vsh[qrow:qrow + 64, 0:W],
                    start=True, stop=True),
                    after=[pr] + ([pd_free] if pd_free else []), tag="PE")
                wd = em.o("scalar", lambda sc: sc.activation(
                    wd_row[0:1, 0:W], p_d[0:1, 0:W], AF.Exp, scale=0.125),
                    after=[dmm], tag="ACT")
                pd_free = wd
                sadd = em.o("vector", lambda v: v.tensor_add(
                    p_o[64:65, 0:W], p_o[64:65, 0:W], wd_row[0:1, 0:W]),
                    after=[wd, pv_prev], tag="DVE")
                bcmm = em.o("tensor", lambda t: t.matmul(
                    pa[0:64, 0:W], or_sb[0:1, 0:64], wd_row[0:1, 0:W],
                    start=True, stop=True),
                    after=[wd] + ([pa_free] if pa_free else []), tag="PE")
                bev = em.o("vector", lambda v: v.tensor_copy(bcw_bf[:], pa[0:64, 0:W]),
                           after=[bcmm] + ([t2v_prev] if t2v_prev else []), tag="DVE")
                pa_free = bev
                pvc = em.o("vector", lambda v, qrow=qrow, qcolb=qcolb, m=m: v.tensor_copy(
                    vsh[0:64, W:2 * W],
                    pvt[qrow:qrow + 64, qcolb + m * W: qcolb + (m + 1) * W]),
                    after=[bev], tag="DVE")
                t2v = em.o("vector", lambda v: v.tensor_mul(
                    t2f_ap(), bcw_bf[:], vsh[0:64, W:2 * W]),
                    after=[pvc], tag="DVE")
                t2v_prev = t2v
                acc = em.o("vector", lambda v: v.tensor_add(
                    p_o[0:64, 0:W], p_o[0:64, 0:W], t2f_ap()),
                    after=[t2v, pv_prev], tag="DVE")
                se = em.o("scalar", lambda sc: sc.activation(
                    se_row[0:1, 0:W], p_o[64:65, 0:W], AF.Identity),
                    after=[sadd], tag="ACT")
                rc = em.o("vector", lambda v: v.reciprocal(rc_row[0:1, 0:W], se_row[0:1, 0:W]),
                          after=[se] + ([bfm_prev] if bfm_prev else []), tag="DVE")
                bfm = em.o("tensor", lambda t: t.matmul(
                    pb[0:64, 0:W], or_sb[0:1, 0:64], rc_row[0:1, 0:W],
                    start=True, stop=True),
                    after=[rc] + ([pb_free] if pb_free else []), tag="PE")
                bfm_prev = bfm
                bcf = em.o("vector", lambda v: v.tensor_mul(
                    bcf_ap(), pb[0:64, 0:W], pho_sb[:]),
                    after=[bfm, ld_pho], tag="DVE")
                pb_free = bcf
                oslot = m % 2
                war = war_out[oslot]
                oev = em.o("vector", lambda v, oslot=oslot: v.tensor_mul(
                    out_sb[oslot][:], p_o[0:64, 0:W], bcf_ap()),
                    after=[bcf, acc] + ([war] if war else []), tag="DVE")
                po_free = oev
                ek = em.echo([oev], "scalar", "ACT")
                st = em.o("gpsimd", lambda e, h=h, m=m, oslot=oslot: e.dma_start(
                    out=out0T[h * 64:(h + 1) * 64, m * W:(m + 1) * W],
                    in_=out_sb[oslot][:]),
                    after=[ek], tag="ST", inc=16)
                war_out[oslot] = st



def emit_partial_oproj(nc, em, G, bar, pfx, src_ap, wkey, akey, bkey, cc_dst, bnd):
    W = 512
    PAD = 8 if bnd else 0
    lora = akey is not None
    with ExitStack() as stk:
        en = stk.enter_context

        def alloc(name, shape, dt):
            return nc.alloc_sbuf_tensor(pfx + name, list(shape), dt)

        ot = [alloc(f"ot{i}", [128, 2 * W], BF16) for i in range(2)]
        w_sb = alloc("w", [128, 2 * D], BF16)
        y0 = [alloc(f"y{i}", [128, 8 * W], F32) for i in range(2)]
        p0 = en(nc.psum_tensor(pfx + "p0", [128, W], F32))
        p1 = en(nc.psum_tensor(pfx + "p1", [128, W], F32))
        ld_w = em.o("gpsimd", lambda g: g.dma_start(
            out=w_sb.ap().rearrange("p (c o) -> p c o", c=2),
            in_=G[wkey].rearrange("(c p) o -> p c o", p=128)),
            after=bar, tag="LDG", inc=16)
        wdeps = [ld_w]
        if lora:
            a_sb = alloc("a", [128, 2 * LORA], BF16)
            b_sb = alloc("b", [LORA, D], BF16)
            tl = alloc("tl", [LORA, W], BF16)
            pl = en(nc.psum_tensor(pfx + "pl", [LORA, W], F32))
            ld_a = em.o("gpsimd", lambda g: g.dma_start(
                out=a_sb.ap().rearrange("p (c o) -> p c o", c=2),
                in_=G[akey].rearrange("(c p) o -> p c o", p=128)),
                after=bar, tag="LDG", inc=16)
            ld_b = em.o("gpsimd", lambda g: g.dma_start(
                out=b_sb[:], in_=G[bkey]), after=bar, tag="LDG", inc=16)
            wdeps += [ld_a, ld_b]
        ld_war = {0: None, 1: None}
        st_war = {0: [], 1: []}
        bank_free = {0: None, 1: None}
        tl_prev = None
        for m in range(16):
            sl = m % 2
            buf = ot[sl]
            ld = em.o("sync", lambda e, m=m, buf=buf: e.dma_start(
                out=buf.ap().rearrange("p (c t) -> p c t", c=2),
                in_=src_ap[:, m * W:(m + 1) * W].rearrange("(c p) t -> p c t", p=128)),
                after=(bar if m < 2 else []) +
                ([ld_war[sl]] if ld_war[sl] else []), tag="LD", inc=16)
            tlt = None
            if lora:
                mmt = None
                for fc in range(2):
                    mmt = em.o("tensor", lambda t, fc=fc, buf=buf: t.matmul(
                        pl[:, 0:W], a_sb[:, fc * LORA:(fc + 1) * LORA],
                        buf[:, fc * W:(fc + 1) * W],
                        start=(fc == 0), stop=(fc == 1)),
                        after=([ld, ld_a] + ([tl_prev] if tl_prev else []))
                        if fc == 0 else (), tag="PE")
                tlt = em.o("vector", lambda v: v.tensor_copy(tl[:, 0:W], pl[:, 0:W]),
                           after=[mmt], tag="DVE")
                tl_prev = tlt
            evs = []
            last_fc_mm = None
            for oc in range(8):
                pp = p0 if oc % 2 == 0 else p1
                prev = bank_free[oc % 2]
                mm = None
                for fc in range(2):
                    mm = em.o("tensor", lambda t, fc=fc, oc=oc, buf=buf, pp=pp: t.matmul(
                        pp[:, 0:W], w_sb[:, fc * D + oc * 128: fc * D + (oc + 1) * 128],
                        buf[:, fc * W:(fc + 1) * W],
                        start=(fc == 0), stop=(fc == 1 and not lora)),
                        after=([ld] + wdeps + ([prev] if prev else []))
                        if fc == 0 else (), tag="PE")
                last_fc_mm = mm
                if lora:
                    mm = em.o("tensor", lambda t, oc=oc, pp=pp: t.matmul(
                        pp[:, 0:W], b_sb[:, oc * 128:(oc + 1) * 128], tl[:, 0:W],
                        start=False, stop=True), after=[tlt], tag="PE")
                ev = em.o("vector", lambda v, oc=oc, pp=pp, sl=sl: v.tensor_copy(
                    y0[sl][:, oc * W:(oc + 1) * W], pp[:, 0:W]),
                    after=[mm] + (st_war[sl] if oc == 0 else []), tag="DVE")
                bank_free[oc % 2] = ev
                evs.append(ev)
            ld_war[sl] = last_fc_mm
            ek = em.echo(evs, "scalar", "ACT")
            qq = m // 4
            st = em.o("gpsimd", lambda e, m=m, qq=qq, sl=sl: e.dma_start(
                out=cc_dst[qq * D:(qq + 1) * D,
                           PAD + (m % 4) * W: PAD + (m % 4) * W + W].rearrange(
                    "(c p) t -> p c t", p=128),
                in_=y0[sl].ap().rearrange("p (c t) -> p c t", c=8)),
                after=[ek], tag="ST", inc=16)
            st_war[sl] = [st]
            if bnd and m % 4 == 3:
                q2 = (qq + 1) % 4
                stb = em.o("gpsimd", lambda e, q2=q2, sl=sl: e.dma_start(
                    out=cc_dst[q2 * D:(q2 + 1) * D, 0:1].rearrange(
                        "(c p) t -> p c t", p=128),
                    in_=y0[sl].ap().rearrange("p (c t) -> p c t", c=8)[:, :, W - 1:W],
                    allow_slow_non_contiguous=True),
                    after=[ek], tag="ST", inc=16)
                st_war[sl].append(stb)



def declare_dram(nc):
    G = {}

    def inp(name, shape, dt):
        G[name] = nc.dram_tensor(name, list(shape), dt, kind="ExternalInput").ap()

    def intl(name, shape, dt):
        h = nc.dram_tensor(name, list(shape), dt, kind="Internal")
        G[name + "_h"] = h
        G[name] = h.ap()

    inp("xT", [D, S], F32)
    inp("wq", [D, 256], F32)
    inp("wk", [D, 256], F32)
    inp("wv", [D, 256], F32)
    inp("aq", [D, LORA], F32)
    inp("av", [D, LORA], F32)
    inp("bq", [LORA, 256], F32)
    inp("bv", [LORA, 256], F32)
    inp("cq", [128, S], F32)
    inp("sq", [128, S], F32)
    inp("phv", [128, 512], F32)
    inp("pho", [64, 512], F32)
    inp("rot", [128, 128], F32)
    inp("mask0", [128, 512], F32)
    inp("idt", [128, 128], BF16)
    inp("ones_col_bf", [128, 1], BF16)
    inp("ones_row_f", [1, 128], F32)
    inp("eps", [1, 1], F32)
    inp("woP", [256, D], F32)
    inp("aoP", [256, LORA], F32)
    inp("boF", [LORA, D], F32)
    inp("owP", [256, D], F32)
    inp("xqT", [D, TQ + 4], F32)
    inp("wrT", [D, D], F32)
    inp("wkT", [D, D], F32)
    inp("wvT", [D, D], F32)
    inp("mu_r", [128, 8], F32)
    inp("mu_k", [128, 8], F32)
    inp("bmask", [128, 1], F32)
    inp("qw", [D, 256], F32)
    inp("kw", [D, 256], F32)
    inp("vw", [D, 256], F32)
    inp("qb", [128, 2], F32)
    inp("kb", [128, 2], F32)
    inp("vb", [128, 2], F32)
    inp("masks", [128, 4 * 512], F32)
    inp("ob", [128, 8], F32)
    inp("w1", [E, D, HID], F32)
    inp("b1c", [128, E * 32], F32)
    inp("w2", [E, HID, D], F32)
    inp("b2c", [128, 32], F32)
    inp("wr", [D, E], F32)
    inp("brc", [4, 1], F32)
    inp("id4", [4, 4], F32)
    inp("id128", [128, 128], F32)
    inp("emat", [4, 512], F32)
    G["x4T"] = nc.dram_tensor("x4T", [D, TQ], F32, kind="ExternalOutput").ap()
    intl("out0d", [256, S], BF16)
    intl("out1d", [256, S], BF16)
    intl("x2scr", [D, TQ], F32)
    intl("x3scr", [D, TQ], F32)
    intl("cc1i", [4 * D, 8 + TQ], F32)
    intl("cc1o", [D, 8 + TQ], F32)
    intl("cc2i", [D, TQ], BF16)
    intl("cc2o", [4 * D, TQ], BF16)
    intl("cc3i", [4 * D, TQ], F32)
    intl("cc3o", [D, TQ], F32)
    return G


def build_fused():
    nc = bass.Bass("TRN2", target_bir_lowering=False)
    G = declare_dram(nc)
    em = Emitter()

    def reset_phase():
        bar = em.phase_barrier()
        nc.sbuf_base = sb0
        nc.psum_base = ps0
        return bar

    def collective(kind, op, in_h, out_h, tag):
        em.o("gpsimd", lambda g: g.collective_compute(
            kind, op, replica_groups=RG,
            ins=[in_h.ap().opt()], outs=[out_h.ap().opt()]),
            after=[em.dma_barrier("ST")], tag=tag, inc=1)
        return (tag, 1, "__none__")

    with ExitStack() as stk:
        en = stk.enter_context
        sems = {s: en(nc.semaphore(s)) for s in ALL_SEMS}
        sb0, ps0 = nc.sbuf_base, nc.psum_base

        emit_phase1(nc, em, G)
        bar = reset_phase()
        emit_partial_oproj(nc, em, G, bar, "b1_", G["out0d"],
                           "woP", "aoP", "boF", G["cc1i"], bnd=True)
        cc0 = collective("ReduceScatter", ALU.add, G["cc1i_h"], G["cc1o_h"], "CC0")
        bar = reset_phase()
        emit_phase2(nc, em, G, bar, cc0)
        cc1 = collective("AllGather", ALU.bypass, G["cc2i_h"], G["cc2o_h"], "CC1")
        bar = reset_phase()
        emit_phase3(nc, em, G, bar, cc1)
        bar = reset_phase()
        emit_partial_oproj(nc, em, G, bar, "b3_", G["out1d"],
                           "owP", None, None, G["cc3i"], bnd=False)
        cc2 = collective("ReduceScatter", ALU.add, G["cc3i_h"], G["cc3o_h"], "CC2")
        bar = reset_phase()
        emit_phase4(nc, em, G, bar, cc2)
        em.o("sync", lambda e: e.nop(), after=[em.dma_barrier("ST")])
        block = en(nc.Block())
        em.build(block, sems)
    return nc


_F_CACHE = {}


def kernel(**inputs):
    g = {k: np.asarray(v) for k, v in inputs.items()}
    if "nc" not in _F_CACHE:
        _F_CACHE["nc"] = build_fused()
    nc = _F_CACHE["nc"]
    bf = ml_dtypes.bfloat16
    x = g["L0_attn_hidden_states"].astype(np.float32)
    wo = np.ascontiguousarray(g["L0_attn_Wo"]).astype(np.float32)
    ao = np.ascontiguousarray(g["L0_attn_Ao"]).astype(np.float32)
    owf = np.ascontiguousarray(g["L1_attn_O_w"].reshape(D, D)).astype(np.float32)
    ob = np.ascontiguousarray(g["L1_attn_O_b"].reshape(8, 128).T).astype(np.float32)
    b1c = np.zeros((128, E * 32), np.float32)
    for e in range(E):
        b1c[:, e * 32:(e + 1) * 32] = g["L1_ffn_b1"][e].reshape(32, 128).T
    b2c = np.zeros((128, 32), np.float32)
    for dc in range(8):
        for e in range(E):
            b2c[:, dc * 4 + e] = g["L1_ffn_b2"][e][dc * 128:(dc + 1) * 128]
    sl_ = np.arange(128)[:, None]
    ql_ = np.arange(512)[None, :]
    masks = np.concatenate(
        [np.where(128 * d + sl_ <= ql_, 0.0, -1e30).astype(np.float32)
         for d in range(4)], axis=1)
    in_common = {
        "boF": np.ascontiguousarray(g["L0_attn_Bo"]).astype(np.float32),
        "wrT": np.ascontiguousarray(g["L0_ffn_W_r"].T).astype(np.float32),
        "wkT": np.ascontiguousarray(g["L0_ffn_W_k"].T).astype(np.float32),
        "wvT": np.ascontiguousarray(g["L0_ffn_W_v"].T).astype(np.float32),
        "mu_r": np.ascontiguousarray(g["L0_ffn_mu_r"].reshape(8, 128).T).astype(np.float32),
        "mu_k": np.ascontiguousarray(g["L0_ffn_mu_k"].reshape(8, 128).T).astype(np.float32),
        "masks": masks,
        "ob": ob, "b1c": b1c, "b2c": b2c,
        "w1": np.ascontiguousarray(g["L1_ffn_w1"]).astype(np.float32),
        "w2": np.ascontiguousarray(g["L1_ffn_w2"]).astype(np.float32),
        "wr": np.ascontiguousarray(g["L1_ffn_wr"]).astype(np.float32),
        "brc": np.ascontiguousarray(g["L1_ffn_br"].reshape(4, 1)).astype(np.float32),
        "id4": CONSTS["ident4_f"], "id128": CONSTS["ident128_f"],
        "emat": CONSTS["emat"],
        "cq": CONSTS["cosP_q"], "sq": CONSTS["ssP_q"],
        "phv": np.ascontiguousarray(CONSTS["phase_h"][:, 0:512]),
        "pho": np.ascontiguousarray(CONSTS["phase_o"][0:64, 0:512]),
        "rot": CONSTS["rot32"], "mask0": CONSTS["mask_l0"],
        "idt": np.eye(128).astype(bf),
        "ones_col_bf": CONSTS["ones_col_bf"],
        "ones_row_f": CONSTS["ones_row_f"], "eps": CONSTS["eps"],
    }
    in_maps = []
    for c in range(NC):
        b, gq = c // 4, c % 4
        t0 = gq * TQ
        hset = list(range(4 * gq, 4 * gq + 4))

        def bcol(bmat):
            v = bmat[hset].reshape(256)
            return np.ascontiguousarray(v.reshape(2, 128).T).astype(np.float32)

        xT = np.ascontiguousarray(x[b].T).astype(np.float32)
        xq = np.ascontiguousarray(x[b].T[:, t0:t0 + TQ]).astype(np.float32)
        if t0 == 0:
            xbnd = np.zeros((D, 4), np.float32)
        else:
            xbnd = np.tile(np.ascontiguousarray(x[b].T[:, t0 - 1:t0]),
                           (1, 4)).astype(np.float32)
        in_maps.append({
            "xT": xT,
            "xqT": np.concatenate([xq, xbnd], axis=1),
            "bmask": np.full((128, 1), 0.0 if gq == 0 else 1.0, np.float32),
            "wq": np.ascontiguousarray(g["L0_attn_Wq"][:, 256 * gq:256 * gq + 256]).astype(np.float32),
            "wk": np.ascontiguousarray(g["L0_attn_Wk"][:, 256 * gq:256 * gq + 256]).astype(np.float32),
            "wv": np.ascontiguousarray(g["L0_attn_Wv"][:, 256 * gq:256 * gq + 256]).astype(np.float32),
            "aq": np.ascontiguousarray(g["L0_attn_Aq"]).astype(np.float32),
            "av": np.ascontiguousarray(g["L0_attn_Av"]).astype(np.float32),
            "bq": np.ascontiguousarray(g["L0_attn_Bq"][:, 256 * gq:256 * gq + 256]).astype(np.float32),
            "bv": np.ascontiguousarray(g["L0_attn_Bv"][:, 256 * gq:256 * gq + 256]).astype(np.float32),
            "woP": np.ascontiguousarray(wo[256 * gq:256 * (gq + 1)]).astype(np.float32),
            "aoP": np.ascontiguousarray(ao[256 * gq:256 * (gq + 1)]).astype(np.float32),
            "owP": np.ascontiguousarray(owf[256 * gq:256 * (gq + 1)]).astype(np.float32),
            "qw": np.ascontiguousarray(g["L1_attn_Q_w"][hset].transpose(1, 0, 2).reshape(D, 256)).astype(np.float32),
            "kw": np.ascontiguousarray(g["L1_attn_K_w"][hset].transpose(1, 0, 2).reshape(D, 256)).astype(np.float32),
            "vw": np.ascontiguousarray(g["L1_attn_V_w"][hset].transpose(1, 0, 2).reshape(D, 256)).astype(np.float32),
            "qb": bcol(g["L1_attn_Q_b"]), "kb": bcol(g["L1_attn_K_b"]),
            "vb": bcol(g["L1_attn_V_b"]),
            **in_common,
        })
    res = run_bass_kernel_spmd(nc, in_maps, core_ids=list(range(NC)))
    x4 = np.zeros((B, S, D), np.float32)
    for c in range(NC):
        b, gq = c // 4, c % 4
        t0 = gq * TQ
        x4[b, t0:t0 + TQ] = res.results[c]["x4T"].T
    return x4.astype(np.float32)


# version 03
# speedup vs baseline: 1.1792x; 1.1792x over previous
import math
import numpy as np
import ml_dtypes
import concourse.bass as bass
import concourse.mybir as mybir
from concourse.bass_utils import run_bass_kernel_spmd
from contextlib import ExitStack

F32 = mybir.dt.float32
BF16 = mybir.dt.bfloat16
AF = mybir.ActivationFunctionType
ALU = mybir.AluOpType
AX = mybir.AxisListType

B, S, D, NH, HD, KF, LORA = 2, 8192, 1024, 16, 64, 4, 64
NS = S // KF
E, HID = 4, 4096
NC = 8
TQ = S // 4
EPS = 1e-5
ENGINES = ("sync", "tensor", "vector", "scalar", "gpsimd")



def _rope_tables():
    inv = 1.0 / (10000.0 ** (np.arange(0, HD, 2, dtype=np.float64) / HD))
    pos = np.arange(S, dtype=np.float64)
    fr = pos[:, None] * inv[None, :]
    emb = np.concatenate([fr, fr], axis=1)
    cos = np.cos(emb).T.astype(np.float32)
    sin = np.sin(emb).T.astype(np.float32)
    return cos, sin


def _hadamard():
    H4 = np.array([[1, 1, 1, 1], [1, -1, 1, -1], [1, 1, -1, -1], [1, -1, -1, 1]],
                  dtype=np.float32)
    CDMA = np.tile(H4, (1, HD // 4))
    return CDMA


def _phaseT():
    CDMA = _hadamard()
    reps = CDMA.T
    return np.tile(reps, (1, S // 4)).astype(np.float32)


def host_consts():
    c = {}
    cos, sin = _rope_tables()
    phase = _phaseT()
    ss = sin.copy()
    ss[:32, :] *= -1.0
    var_scale = math.sqrt(KF)
    c["cosP_q"] = np.tile(cos * phase, (2, 1))
    c["ssP_q"] = np.tile(ss * phase, (2, 1))
    c["cosP_k"] = np.tile(cos * phase / var_scale, (2, 1))
    c["ssP_k"] = np.tile(ss * phase / var_scale, (2, 1))
    c["phase_h"] = np.tile(phase / var_scale, (2, 1))
    c["phase2"] = (phase[:, 0:1] * 0.0 + phase[:, 0:4])
    c["phase_o"] = np.tile(phase * var_scale, (2, 1))
    P = np.zeros((128, 128), dtype=np.float32)
    for h in range(2):
        for d in range(64):
            P[h * 64 + (d + 32) % 64, h * 64 + d] = 1.0
    c["rot32"] = P
    sl = np.arange(128)[:, None]
    ql = np.arange(512)[None, :]
    c["mask_l0"] = np.where(sl < (ql // 4), 0.0, -1e30).astype(np.float32)
    masks = []
    for d in range(4):
        masks.append(np.where(128 * d + sl <= ql, 0.0, -1e30).astype(np.float32))
    c["mask_l1"] = np.stack(masks)
    c["ident64_bf"] = np.eye(64).astype(ml_dtypes.bfloat16)
    c["ident128_f"] = np.eye(128).astype(np.float32)
    c["ident4_f"] = np.eye(4).astype(np.float32)
    c["ones_col_bf"] = np.ones((128, 1), dtype=ml_dtypes.bfloat16)
    c["ones_row_f"] = np.ones((1, 128), dtype=np.float32)
    c["eps"] = np.full((1, 1), EPS, dtype=np.float32)
    em = np.zeros((4, 512), dtype=np.float32)
    for e in range(4):
        em[e, 128 * e:128 * (e + 1)] = 1.0
    c["emat"] = em
    return c


CONSTS = host_consts()


ECHO_SEM = {"tensor": "PE_E", "vector": "DVE_E", "scalar": "ACT_E", "gpsimd": "GP_E"}
DMA_TAGS = {"LD": 4, "LDG": 4, "ST": 4}
CC_SEMS = ["CC0", "CC1", "CC2"]
ALL_SEMS = (["PE", "ACT", "DVE", "GP"] + list(ECHO_SEM.values())
            + [f"{t}{i}" for t, n in DMA_TAGS.items() for i in range(n)]
            + CC_SEMS)
RG = [[0, 1, 2, 3], [4, 5, 6, 7]]


class Emitter:

    def __init__(self):
        self.ops = {e: [] for e in ENGINES}
        self.cnt = {}
        self.hwm = {e: {} for e in ENGINES}
        self.echoed = {e: (0, 0) for e in ENGINES}
        self.dma_idx = {t: 0 for t in DMA_TAGS}
        self.dma_cnt = {}

    def _emit_wait(self, eng, sem, val):
        if self.hwm[eng].get(sem, 0) < val:
            self.ops[eng].append(("w", sem, val))
            self.hwm[eng][sem] = val

    def _wait(self, eng, tok):
        if tok is None:
            return
        sem, val, peng = tok
        if sem in DMA_TAGS:
            for i in range(DMA_TAGS[sem]):
                ps = f"{sem}{i}"
                c = self.dma_cnt.get(ps, 0)
                if c:
                    self._emit_wait(eng, ps, c)
            return
        if peng == eng or peng in ("__none__", "sync", "tensor"):
            self._emit_wait(eng, sem, val)
            return
        cov, esem_val = self.echoed[peng]
        if cov < val:
            esem = ECHO_SEM[peng]
            self._emit_wait(peng, sem, val)
            self.cnt[esem] = self.cnt.get(esem, 0) + 1
            self.ops[peng].append(("o", lambda e: e.nop(), (esem, self.cnt[esem], peng), 1))
            self.echoed[peng] = (val, self.cnt[esem])
            esem_val = self.cnt[esem]
        self._emit_wait(eng, ECHO_SEM[peng], esem_val)

    def o(self, eng, fn, after=(), tag=None, inc=1):
        for tok in after:
            self._wait(eng, tok)
        tokout = None
        if tag is not None and tag in DMA_TAGS:
            i = self.dma_idx[tag] % DMA_TAGS[tag]
            self.dma_idx[tag] += 1
            ps = f"{tag}{i}"
            prev = self.dma_cnt.get(ps, 0)
            if prev:
                self._emit_wait(eng, ps, prev)
            self.dma_cnt[ps] = prev + 16
            tokout = (tag, self.dma_idx[tag], eng)
            self.ops[eng].append(("o", fn, (ps, self.dma_cnt[ps], eng), 16))
            return tokout
        if tag is not None:
            self.cnt[tag] = self.cnt.get(tag, 0) + inc
            tokout = (tag, self.cnt[tag], eng)
        self.ops[eng].append(("o", fn, tokout, inc))
        return tokout

    def dma_barrier(self, tag):
        return (tag, self.dma_idx.get(tag, 0), "__dma__")

    def engine_tok(self, eng):
        sem = {"tensor": "PE", "vector": "DVE", "scalar": "ACT", "gpsimd": "GP"}[eng]
        v = self.cnt.get(sem, 0)
        return (sem, v, eng) if v else None

    def phase_barrier(self):
        toks = [self.dma_barrier(t) for t in DMA_TAGS]
        toks += [self.engine_tok(e) for e in ("tensor", "vector", "scalar", "gpsimd")]
        return [t for t in toks if t is not None]

    def echo(self, toks, via, tag):
        return self.o(via, lambda e: e.nop(), after=toks, tag=tag)

    def barrier_tok(self, tag):
        if tag not in self.cnt:
            return None
        return (tag, self.cnt[tag], "__none__")

    def build(self, block, sems):
        for e in ENGINES:
            ops = self.ops[e]
            if not ops:
                continue
            dec = getattr(block, e)

            def mk(ops_list):
                def _(eng):
                    for op in ops_list:
                        if op[0] == "w":
                            eng.wait_ge(sems[op[1]], op[2])
                        else:
                            _, fn, tokout, inc = op
                            inst = fn(eng)
                            if tokout is not None:
                                inst.then_inc(sems[tokout[0]], inc)
                return _
            dec(mk(ops))


def emit_ln_stats(em, sb, ps, xsrc_fn, W, deps, tagpfx, ln_scr, raw_mean=False):
    xbf = ln_scr["xbf"]
    rows = ln_scr["rows"]
    pa, pb = ps["stat_a"], ps["stat_b"]
    casts = []
    for c in range(8):
        xa = xsrc_fn(c)
        casts.append(em.o("vector", lambda v, xa=xa, c=c: v.tensor_copy(xbf[:, c * W:(c + 1) * W], xa),
                          after=deps, tag="DVE"))
    mean_mms = []
    for c in range(8):
        mean_mms.append(em.o("tensor", lambda t, c=c: t.matmul(
            pa[0:1, 0:W], sb["ones_col_bf"][:], xbf[:, c * W:(c + 1) * W],
            start=(c == 0), stop=(c == 7)), after=[casts[c]], tag="PE"))
    mm_s = None
    sqs = []
    for c in range(8):
        xa = xsrc_fn(c)
        sq = em.o("scalar", lambda s, xa=xa, c=c: s.activation(
            xbf[:, c * W:(c + 1) * W], xa, AF.Square),
            after=[mean_mms[c]] + list(deps), tag="ACT")
        sqs.append(sq)
    for c in range(8):
        mm_s = em.o("tensor", lambda t, c=c: t.matmul(
            pb[0:1, 0:W], sb["ones_col_bf"][:], xbf[:, c * W:(c + 1) * W],
            start=(c == 0), stop=(c == 7)), after=[sqs[c]], tag="PE")
    mm_m = mean_mms[-1]
    m_row, ms_row, t_row, rstd_row = rows
    d1 = em.o("vector", lambda v: v.tensor_scalar(m_row[0:1, 0:W], pa[0:1, 0:W], 1.0 / D, None, ALU.mult),
              after=[mm_m], tag="DVE")
    d2 = em.o("vector", lambda v: v.tensor_scalar(ms_row[0:1, 0:W], pb[0:1, 0:W], 1.0 / D, None, ALU.mult),
              after=[mm_s], tag="DVE")
    d3 = em.o("vector", lambda v: v.tensor_mul(t_row[0:1, 0:W], m_row[0:1, 0:W], m_row[0:1, 0:W]),
              after=[d1], tag="DVE")
    d4 = em.o("vector", lambda v: v.tensor_sub(ms_row[0:1, 0:W], ms_row[0:1, 0:W], t_row[0:1, 0:W]),
              after=[d2, d3], tag="DVE")
    a1 = em.o("scalar", lambda s: s.activation(ms_row[0:1, 0:W], ms_row[0:1, 0:W], AF.Ln,
                                               bias=sb["eps"][0:1, 0:1]),
              after=[d4], tag="ACT")
    a2 = em.o("scalar", lambda s: s.activation(rstd_row[0:1, 0:W], ms_row[0:1, 0:W], AF.Exp, scale=-0.5),
              after=[a1], tag="ACT")
    if raw_mean:
        bc_src, bc_deps = m_row, [d1, d2]
    else:
        d5 = em.o("vector", lambda v: v.tensor_mul(t_row[0:1, 0:W], m_row[0:1, 0:W], rstd_row[0:1, 0:W]),
                  after=[a2], tag="DVE")
        bc_src, bc_deps = t_row, [d5, d2]
    bm_r = em.o("tensor", lambda t: t.matmul(pa[:, 0:W], sb["ones_row_f"][:], rstd_row[0:1, 0:W],
                                             start=True, stop=True),
                after=[a2, d1], tag="PE")
    bm_mr = em.o("tensor", lambda t: t.matmul(pb[:, 0:W], sb["ones_row_f"][:], bc_src[0:1, 0:W],
                                              start=True, stop=True),
                 after=bc_deps, tag="PE")
    bc_r, bc_mr = ln_scr["bc_r"], ln_scr["bc_mr"]
    e1 = em.o("vector", lambda v: v.tensor_copy(bc_r[:, 0:W], pa[:, 0:W]), after=[bm_r], tag="DVE")
    e2 = em.o("vector", lambda v: v.tensor_copy(bc_mr[:, 0:W], pb[:, 0:W]), after=[bm_mr], tag="DVE")
    return (e1, e2), bc_r, bc_mr



def emit_phase2(nc, em, G, bar, cc_tok):
    T = TQ + 4
    xT = G["xqT"]
    wrT, wkT, wvT = G["wrT"], G["wkT"], G["wvT"]
    mu_r, mu_k = G["mu_r"], G["mu_k"]
    ones_col_bf, ones_row_f, eps = G["ones_col_bf"], G["ones_row_f"], G["eps"]
    cc1o, bmask = G["cc1o"], G["bmask"]
    x2T, h3T = G["x2scr"], G["cc2i"]
    with ExitStack() as stk:
        en = stk.enter_context

        def sbuf(name, shape, dt):
            return nc.alloc_sbuf_tensor("m_" + name, list(shape), dt)

        xb = sbuf("xb", [128, 8 * T], F32)
        h2 = sbuf("h2", [128, 8 * 516], BF16)
        wr_sb = sbuf("wr_sb", [128, 8 * D], BF16)
        wk_sb = sbuf("wk_sb", [128, 8 * D], BF16)
        wv_sb = sbuf("wv_sb", [128, 8 * D], BF16)
        mur_sb = sbuf("mur_sb", [128, 8], F32)
        muk_sb = sbuf("muk_sb", [128, 8], F32)
        omr_sb = sbuf("omr_sb", [128, 8], F32)
        omk_sb = sbuf("omk_sb", [128, 8], F32)
        bmask_sb = sbuf("bmask_sb", [128, 1], F32)
        W = 512
        y0t = sbuf("y0t", [128, 8 * W], F32)
        y0b = sbuf("y0b", [128, 8 * 4], F32)
        xrb = sbuf("xrb", [128, 8 * W], BF16)
        xkb = sbuf("xkb", [128, 8 * W], BF16)
        rfb = sbuf("rfb", [128, 8 * W], BF16)
        kact = sbuf("kact", [128, 8 * W], BF16)
        krelu = sbuf("krelu", [128, 2 * W], F32)
        tmix = krelu
        tmp = sbuf("tmp", [128, W], F32)

        ln_scr = {
            "xbf": sbuf("ln_xbf", [128, 8 * W], BF16),
            "rows": [sbuf(f"ln_r{i}", [1, W], F32) for i in range(4)],
            "bc_r": sbuf("ln_bcr", [128, W], F32),
            "bc_mr": sbuf("ln_bcmr", [128, W], F32),
        }
        h3t = ln_scr["xbf"]
        consts = {
            "ones_col_bf": sbuf("c_ones_col", [128, 1], BF16),
            "ones_row_f": sbuf("c_ones_row", [1, 128], F32),
            "eps": sbuf("c_eps", [1, 1], F32),
        }
        p0 = en(nc.psum_tensor("m_p0", [128, W], F32))
        p1 = en(nc.psum_tensor("m_p1", [128, W], F32))
        pa = en(nc.psum_tensor("m_pa", [128, W], F32))
        pb = en(nc.psum_tensor("m_pb", [128, W], F32))
        ps = {"stat_a": pa, "stat_b": pb}

        def dma_in(dst, src, q="sync", slow=False, after=()):
            return em.o(q, lambda e, dst=dst, src=src: e.dma_start(
                out=dst, in_=src, allow_slow_non_contiguous=slow),
                after=after, tag="LD" if q == "sync" else "LDG", inc=16)

        ld_x = dma_in(xb.ap().rearrange("p (c t) -> p c t", c=8),
                      xT.rearrange("(c p) t -> p c t", p=128), after=bar)
        ld_wr = dma_in(wr_sb.ap().rearrange("p (c o) -> p c o", c=8),
                       wrT.rearrange("(c p) o -> p c o", p=128), q="gpsimd", after=bar)
        ld_wk = dma_in(wk_sb.ap().rearrange("p (c o) -> p c o", c=8),
                       wkT.rearrange("(c p) o -> p c o", p=128), q="gpsimd", after=bar)
        ld_wv = dma_in(wv_sb.ap().rearrange("p (c o) -> p c o", c=8),
                       wvT.rearrange("(c p) o -> p c o", p=128), q="gpsimd", after=bar)
        ld_mur = dma_in(mur_sb[:], mu_r, after=bar)
        ld_muk = dma_in(muk_sb[:], mu_k, after=bar)
        ld_bm = dma_in(bmask_sb[:], bmask, after=bar)
        ld_oc = dma_in(consts["ones_col_bf"][:], ones_col_bf, after=bar)
        ld_or = dma_in(consts["ones_row_f"][:], ones_row_f, after=bar)
        ld_eps = dma_in(consts["eps"][:], eps, after=bar)
        ld_y0b = dma_in(y0b.ap().rearrange("p (c j) -> p c j", j=4)[:, :, 0:1],
                        cc1o[:, 0:1].rearrange("(c p) t -> p c t", p=128),
                        slow=True, after=bar + [cc_tok])
        yb_toks = [ld_y0b]
        for j in range(1, 4):
            yb_toks.append(em.o("vector", lambda v, j=j: v.tensor_copy(
                y0b[:, j::4], y0b[:, 0::4]), after=[ld_y0b] + bar, tag="DVE"))
        yb_mask = em.o("vector", lambda v: v.tensor_scalar(
            y0b[:], y0b[:], bmask_sb[:, 0:1], None, ALU.mult),
            after=yb_toks + [ld_bm], tag="DVE")

        om1 = em.o("vector", lambda v: v.tensor_scalar(omr_sb[:], mur_sb[:], -1.0, 1.0, ALU.mult, ALU.add),
                   after=[ld_mur], tag="DVE")
        om2 = em.o("vector", lambda v: v.tensor_scalar(omk_sb[:], muk_sb[:], -1.0, 1.0, ALU.mult, ALU.add),
                   after=[ld_muk], tag="DVE")

        bank_free = {0: None, 1: None}

        def proj_tile(dst_write, rhs_fn, w_sb, Wd, t0, lora=None, deps=(),
                      extra_evac_dep=()):
            toks = []
            tl = None
            if lora is not None:
                a_sb, b_sb, plp = lora
                for c in range(8):
                    tl = em.o("tensor", lambda t, c=c: t.matmul(
                        plp[:, 0:Wd], a_sb[:, c * LORA:(c + 1) * LORA], rhs_fn(c),
                        start=(c == 0), stop=(c == 7)), after=deps, tag="PE")
                tl = em.o("vector", lambda v: v.tensor_copy(tlora[:, 0:Wd], plp[:, 0:Wd]),
                          after=[tl], tag="DVE")
            for oc in range(8):
                pp = p0 if oc % 2 == 0 else p1
                mm = None
                prev = bank_free[oc % 2]
                dep0 = list(deps) + ([prev] if prev else [])
                for c in range(8):
                    mm = em.o("tensor", lambda t, c=c, oc=oc, pp=pp: t.matmul(
                        pp[:, 0:Wd], w_sb[:, c * D + oc * 128: c * D + (oc + 1) * 128],
                        rhs_fn(c), start=(c == 0), stop=(c == 7 and lora is None)),
                        after=dep0 if c == 0 else (), tag="PE")
                if lora is not None:
                    mm = em.o("tensor", lambda t, oc=oc, pp=pp: t.matmul(
                        pp[:, 0:Wd], bo_sb[:, oc * 128:(oc + 1) * 128], tlora[:, 0:Wd],
                        start=False, stop=True), after=[tl], tag="PE")
                ev = dst_write(oc, pp, [mm] + list(extra_evac_dep))
                bank_free[oc % 2] = ev
                toks.append(ev)
            return toks

        def x_chunk(c, off, Wd):
            return xb[:, c * T + off: c * T + off + Wd]

        def h2_chunk(c, off, Wd):
            return h2[:, c * 516 + off: c * 516 + off + Wd]

        def do_tile(t0, Wd, ysrc_chunk, deps_extra=(), h2_off=4):
            deps = [ld_x] + list(deps_extra)
            y_toks = []
            prev = None
            for oc in range(8):
                tk = em.o("vector", lambda v, oc=oc: v.tensor_add(
                    x_chunk(oc, t0, Wd), ysrc_chunk(oc), x_chunk(oc, t0, Wd)),
                    after=deps + ([prev] if prev else []), tag="DVE")
                prev = tk
                y_toks.append(tk)
            (l1, l2), bc_r, bc_mr = emit_ln_stats(
                em, consts, ps, lambda c: x_chunk(c, t0, Wd), Wd,
                y_toks + [ld_oc, ld_or, ld_eps], "", ln_scr)
            h_toks = []
            prev_hh = None
            for c in range(8):
                hm = em.o("vector", lambda v, c=c: v.tensor_mul(
                    tmp[:, 0:Wd], x_chunk(c, t0, Wd), bc_r[:, 0:Wd]),
                    after=[l1] + ([prev_hh] if prev_hh else []), tag="DVE")
                hh = em.o("vector", lambda v, c=c: v.tensor_sub(
                    h2_chunk(c, h2_off, Wd), tmp[:, 0:Wd], bc_mr[:, 0:Wd]),
                    after=[l2, hm], tag="DVE")
                prev_hh = hh
                h_toks.append(hh)
            return h_toks, y_toks

        bt, _ = do_tile(TQ, 4, lambda c: y0b[:, c * 4:(c + 1) * 4],
                        deps_extra=[yb_mask], h2_off=0)

        ffn_weight_deps = [ld_wr, ld_wk, ld_wv, om1, om2]
        prev_ffn_reads = []
        prev_h3_out = []
        prev_y_toks = []
        for tt in range(4):
            t0 = tt * W
            ldt = em.o("sync", lambda e, t0=t0: e.dma_start(
                out=y0t.ap().rearrange("p (c t) -> p c t", c=8),
                in_=cc1o[:, 8 + t0:8 + t0 + W].rearrange("(c p) t -> p c t", p=128)),
                after=prev_y_toks + [cc_tok] + bar, tag="LD", inc=16)
            h_toks, prev_y_toks = do_tile(t0, W, lambda c: y0t[:, c * W:(c + 1) * W],
                                          deps_extra=[ldt] + bt + prev_h3_out)
            mix_toks = []
            for c in range(8):
                for (mu_s, om_s, dstb) in ((mur_sb, omr_sb, xrb), (muk_sb, omk_sb, xkb)):
                    m1 = em.o("vector", lambda v, c=c, om_s=om_s: v.tensor_scalar(
                        tmix[:, 0:W], h2_chunk(c, 3, W), om_s[:, c:c + 1], None, ALU.mult),
                        after=h_toks + (bt if tt == 0 else []) + prev_ffn_reads + prev_h3_out
                        + (mix_toks[-1:] if mix_toks else []), tag="DVE")
                    m2 = em.o("vector", lambda v, c=c, mu_s=mu_s, dstb=dstb: v.scalar_tensor_tensor(
                        dstb[:, c * W:(c + 1) * W], h2_chunk(c, 4, W), mu_s[:, c:c + 1],
                        tmix[:, 0:W], ALU.mult, ALU.add), after=[m1], tag="DVE")
                    mix_toks.append(m2)
            for c in range(8):
                em.o("vector", lambda v, c=c: v.tensor_copy(h2_chunk(c, 3, 1), h2_chunk(c, 3 + W, 1)),
                     after=mix_toks, tag="DVE")

            def evac_r(oc, pp, after):
                return em.o("scalar", lambda s, oc=oc, pp=pp: s.activation(
                    rfb[:, oc * W:(oc + 1) * W], pp[:, 0:W], AF.Sigmoid),
                    after=after, tag="ACT")

            r_toks = proj_tile(evac_r, lambda c: xrb[:, c * W:(c + 1) * W],
                               wr_sb, W, t0, deps=mix_toks + ffn_weight_deps)

            krelu_last = {0: None, 1: None}

            def evac_k(oc, pp, after):
                prev_sq = krelu_last.get(oc % 2)
                t1 = em.o("vector", lambda v, oc=oc, pp=pp: v.tensor_scalar(
                    krelu[:, (oc % 2) * W:(oc % 2 + 1) * W], pp[:, 0:W], 0.0, None, ALU.max),
                    after=list(after) + ([prev_sq] if prev_sq else []), tag="DVE")
                sq = em.o("scalar", lambda s, oc=oc: s.activation(
                    kact[:, oc * W:(oc + 1) * W], krelu[:, (oc % 2) * W:(oc % 2 + 1) * W], AF.Square),
                    after=[t1], tag="ACT")
                krelu_last[oc % 2] = sq
                return sq

            k_toks = proj_tile(evac_k, lambda c: xkb[:, c * W:(c + 1) * W],
                               wk_sb, W, t0, deps=mix_toks + ffn_weight_deps)

            fprev = [None]

            def evac_f(oc, pp, after):
                t1 = em.o("vector", lambda v, oc=oc, pp=pp: v.tensor_mul(
                    tmp[:, 0:W], pp[:, 0:W], rfb[:, oc * W:(oc + 1) * W]),
                    after=list(after) + r_toks + ([fprev[0]] if fprev[0] else []), tag="DVE")
                fadd = em.o("vector", lambda v, oc=oc, t0=t0: v.tensor_add(
                    x_chunk(oc, t0, W), x_chunk(oc, t0, W), tmp[:, 0:W]),
                    after=[t1], tag="DVE")
                fprev[0] = fadd
                return fadd

            f_toks = proj_tile(evac_f, lambda c: kact[:, c * W:(c + 1) * W],
                               wv_sb, W, t0, deps=k_toks)
            prev_ffn_reads = f_toks

            (l1, l2), bc_r, bc_mr = emit_ln_stats(
                em, consts, ps, lambda c: x_chunk(c, t0, W), W, f_toks, "", ln_scr)
            out_toks = []
            prev_hh = None
            for c in range(8):
                hm = em.o("vector", lambda v, c=c, t0=t0: v.tensor_mul(
                    tmp[:, 0:W], x_chunk(c, t0, W), bc_r[:, 0:W]),
                    after=[l1] + prev_h3_out + ([prev_hh] if prev_hh else []), tag="DVE")
                hh = em.o("vector", lambda v, c=c: v.tensor_sub(
                    h3t[:, c * W:(c + 1) * W], tmp[:, 0:W], bc_mr[:, 0:W]),
                    after=[l2, hm], tag="DVE")
                prev_hh = hh
                out_toks.append(hh)
            ek1 = em.echo(out_toks, "scalar", "ACT")
            st1 = em.o("gpsimd", lambda e, t0=t0: e.dma_start(
                out=h3T[:, t0:t0 + W].rearrange("(c p) t -> p c t", p=128),
                in_=h3t.ap().rearrange("p (c t) -> p c t", c=8)),
                after=[ek1], tag="ST", inc=16)
            ek2 = em.echo(f_toks, "scalar", "ACT")
            st2 = em.o("gpsimd", lambda e, t0=t0: e.dma_start(
                out=x2T[:, t0:t0 + W].rearrange("(c p) t -> p c t", p=128),
                in_=xb.ap().rearrange("p (c t) -> p c t", c=8)[:, :, t0:t0 + W]),
                after=[ek2], tag="ST", inc=16)
            prev_h3_out = [st1]



def emit_phase4(nc, em, G, bar, cc_tok):
    T = TQ
    W = 512
    BIG = 30000.0
    x2T = G["x2scr"]
    cc3o = G["cc3o"]
    ob, w1, b1c, w2, b2c = G["ob"], G["w1"], G["b1c"], G["w2"], G["b2c"]
    wr, brc, id4, id128, emat = G["wr"], G["brc"], G["id4"], G["id128"], G["emat"]
    ones_col_bf, ones_row_f, eps = G["ones_col_bf"], G["ones_row_f"], G["eps"]
    x4T = G["x4T"]
    x3scr = G["x3scr"]

    with ExitStack() as stk:
        en = stk.enter_context

        def alloc(name, shape, dt):
            return nc.alloc_sbuf_tensor("f_" + name, list(shape), dt)

        xf = alloc("xf", [128, 8 * T], BF16)
        gT = alloc("gT", [4, T], F32)
        b1_sb = alloc("b1_sb", [128, E * 32], F32)
        b2_sb = alloc("b2_sb", [128, 32], F32)
        gel_sb = alloc("gel_sb", [128, E * 32], BF16)
        c_sb = alloc("c_sb", [128, 32], F32)
        cT_sb = alloc("cT_sb", [4, 8 * 128], F32)
        csum_sb = alloc("csum_sb", [128, 8], F32)
        rs_sb = alloc("rs_sb", [128, 16], F32)
        id4_sb = alloc("id4_sb", [4, 4], F32)
        id128_sb = alloc("id128_sb", [128, 128], F32)
        emat_sb = alloc("emat_sb", [4, 512], F32)
        wr_sb = alloc("wr_sb", [128, 8 * 4], F32)
        brc_sb = alloc("brc_sb", [4, 1], F32)
        consts = {
            "ones_col_bf": alloc("c_ones_col", [128, 1], BF16),
            "ones_row_f": alloc("c_ones_row", [1, 128], F32),
            "eps": alloc("c_eps", [1, 1], F32),
        }
        p0 = en(nc.psum_tensor("f_p0", [128, W], F32))
        p1 = en(nc.psum_tensor("f_p1", [128, W], F32))
        pa = en(nc.psum_tensor("f_pa", [128, W], F32))
        pb = en(nc.psum_tensor("f_pb", [128, W], F32))
        pc = en(nc.psum_tensor("f_pc", [128, 16], F32))
        pt = en(nc.psum_tensor("f_pt", [128, 128], F32))
        ps = {"stat_a": pa, "stat_b": pb}

        def dma_in(dst, src, q="sync", slow=False, after=()):
            return em.o(q, lambda e, dst=dst, src=src: e.dma_start(
                out=dst, in_=src, allow_slow_non_contiguous=slow),
                after=after, tag="LD" if q == "sync" else "LDG", inc=16)

        ob_sb = alloc("ob_sb", [128, 8], F32)
        ld_ob = dma_in(ob_sb[:], ob, after=bar)
        ld_wr = dma_in(wr_sb.ap().rearrange("p (c o) -> p c o", c=8),
                       wr.rearrange("(c p) o -> p c o", p=128), after=bar)
        ld_br = dma_in(brc_sb[:], brc, after=bar)
        ld_id4 = dma_in(id4_sb[:], id4, after=bar)
        ld_id128 = dma_in(id128_sb[:], id128, after=bar)
        ld_em = dma_in(emat_sb[:], emat, after=bar)
        ld_b1 = dma_in(b1_sb[:], b1c, after=bar)
        ld_b2 = dma_in(b2_sb[:], b2c, after=bar)
        ld_oc = dma_in(consts["ones_col_bf"][:], ones_col_bf, after=bar)
        ld_or = dma_in(consts["ones_row_f"][:], ones_row_f, after=bar)
        ld_eps = dma_in(consts["eps"][:], eps, after=bar)
        const_deps = [ld_oc, ld_or, ld_eps]
        gel_tok = em.o("scalar", lambda sc: sc.activation(gel_sb[:], b1_sb[:], AF.Gelu),
                       after=[ld_b1], tag="ACT")

        markA = nc.sbuf_base
        x3t = alloc("x3t", [128, 8 * W], F32)
        y1t = alloc("y1t", [128, 8 * W], F32)
        ut = alloc("ut", [128, 8 * W], F32)
        lg_sb = alloc("lg_sb", [4, W], F32)
        lgtm = alloc("lgtm", [128, 16], F32)
        m1b = alloc("m1b", [128, 4], F32)
        dbuf = alloc("dbuf", [128, 16], F32)
        sgb = alloc("sgb", [128, 16], F32)
        q1b = alloc("q1b", [128, 16], F32)
        zb = alloc("zb", [128, 16], F32)
        m2b = alloc("m2b", [128, 4], F32)
        d2b = alloc("d2b", [128, 16], F32)
        nrb = alloc("nrb", [128, 16], F32)
        selb = alloc("selb", [128, 16], F32)
        eb = alloc("eb", [128, 16], F32)
        g0b = alloc("g0b", [128, 16], F32)
        nsb = alloc("nsb", [128, 4], F32)
        rcb = alloc("rcb", [128, 4], F32)
        gtm = alloc("gtm", [128, 16], F32)
        ln_scr = {
            "xbf": alloc("ln_xbf", [128, 8 * W], BF16),
            "rows": [alloc(f"ln_r{i}", [1, W], F32) for i in range(4)],
            "bc_r": alloc("ln_bcr", [128, W], F32),
            "bc_mr": alloc("ln_bcmr", [128, W], F32),
        }
        gate_toks = []
        xf_toks = []
        x3st_toks = []
        prev_tile_toks = []
        bank_free = {}
        for tt in range(4):
            t0 = tt * W
            bar_tile = [t for t in (em.engine_tok("tensor"), em.engine_tok("vector"),
                                    em.engine_tok("scalar")) if t] + \
                ([em.dma_barrier("ST")] if em.dma_idx.get("ST") else []) + \
                (bar if tt == 0 else [])
            ldx = em.o("sync", lambda e, t0=t0: e.dma_start(
                out=x3t.ap().rearrange("p (c t) -> p c t", c=8),
                in_=x2T[:, t0:t0 + W].rearrange("(c p) t -> p c t", p=128)),
                after=bar_tile, tag="LD", inc=16)
            ldo = em.o("sync", lambda e, t0=t0: e.dma_start(
                out=y1t.ap().rearrange("p (c t) -> p c t", c=8),
                in_=cc3o[:, t0:t0 + W].rearrange("(c p) t -> p c t", p=128)),
                after=bar_tile + [cc_tok], tag="LD", inc=16)
            proj_toks = []
            prev_pj = None
            for oc in range(8):
                ev = em.o("vector", lambda v, oc=oc: v.scalar_tensor_tensor(
                    x3t[:, oc * W:(oc + 1) * W], y1t[:, oc * W:(oc + 1) * W],
                    ob_sb[:, oc:oc + 1],
                    x3t[:, oc * W:(oc + 1) * W], ALU.add, ALU.add),
                    after=[ldo, ldx, ld_ob] + prev_tile_toks +
                    ([prev_pj] if prev_pj else []), tag="DVE")
                prev_pj = ev
                proj_toks.append(ev)
            st_x3 = em.o("gpsimd", lambda e, t0=t0: e.dma_start(
                out=x3scr[:, t0:t0 + W].rearrange("(c p) t -> p c t", p=128),
                in_=x3t.ap().rearrange("p (c t) -> p c t", c=8)),
                after=[em.echo(proj_toks, "scalar", "ACT")], tag="ST", inc=16)
            x3st_toks.append(st_x3)
            (l1, l2), bc_r, bc_mr = emit_ln_stats(
                em, consts, ps, lambda c: x3t[:, c * W:(c + 1) * W], W,
                proj_toks + const_deps, "", ln_scr)
            u_toks = []
            for c in range(8):
                um = em.o("vector", lambda v, c=c: v.tensor_mul(
                    ut[:, c * W:(c + 1) * W], x3t[:, c * W:(c + 1) * W], bc_r[:, 0:W]),
                    after=[l1], tag="DVE")
                ux = em.o("vector", lambda v, c=c, t0=t0: v.tensor_sub(
                    xf[:, c * T + t0: c * T + t0 + W], ut[:, c * W:(c + 1) * W], bc_mr[:, 0:W]),
                    after=[l2, um], tag="DVE")
                xf_toks.append(ux)
                uc = em.o("vector", lambda v, c=c, t0=t0: v.tensor_copy(
                    ut[:, c * W:(c + 1) * W], xf[:, c * T + t0: c * T + t0 + W]),
                    after=[ux], tag="DVE")
                u_toks.append(uc)
            mm = None
            for c in range(8):
                mm = em.o("tensor", lambda t, c=c: t.matmul(
                    pa[0:4, 0:W], wr_sb[:, c * 4:(c + 1) * 4], ut[:, c * W:(c + 1) * W],
                    start=(c == 0), stop=(c == 7)),
                    after=[u_toks[c], ld_wr] if c == 0 else [u_toks[c]], tag="PE")
            lg = em.o("vector", lambda v: v.tensor_scalar(
                lg_sb[0:4, 0:W], pa[0:4, 0:W], brc_sb[0:4, 0:1], None, ALU.add),
                after=[mm, ld_br], tag="DVE")
            gs = [slice(g * 4, (g + 1) * 4) for g in range(4)]
            tr_toks = []
            for g in range(4):
                tr = em.o("tensor", lambda t, g=g: t.transpose(
                    pt[0:128, g * 4:(g + 1) * 4], lg_sb[0:4, g * 128:(g + 1) * 128],
                    id4_sb[:]),
                    after=[lg, ld_id4] + (tr_toks[-1:] if tr_toks else []) +
                    (gate_toks[-1:] if gate_toks else []), tag="PE")
                tr_toks.append(em.o("vector", lambda v, g=g: v.tensor_copy(
                    lgtm[:, gs[g]], pt[0:128, g * 4:(g + 1) * 4]),
                    after=[tr], tag="DVE"))
            t_m1 = [em.o("vector", lambda v, g=g: v.reduce_max(
                m1b[:, g:g + 1], lgtm[:, gs[g]], axis=AX.X), after=[tr_toks[g]], tag="DVE")
                for g in range(4)]
            t_d = [em.o("vector", lambda v, g=g: v.tensor_scalar(
                dbuf[:, gs[g]], lgtm[:, gs[g]], m1b[:, g:g + 1], None, ALU.subtract),
                after=[t_m1[g]], tag="DVE") for g in range(4)]
            t_sg = [em.o("scalar", lambda sc, g=g: sc.sign(
                sgb[:, gs[g]], dbuf[:, gs[g]], bias=0.0), after=[t_d[g]], tag="ACT")
                for g in range(4)]
            t_q1 = [em.o("vector", lambda v, g=g: v.tensor_scalar(
                q1b[:, gs[g]], sgb[:, gs[g]], 1.0, -BIG, ALU.add, ALU.mult),
                after=[t_sg[g]], tag="DVE") for g in range(4)]
            t_z = [em.o("vector", lambda v, g=g: v.tensor_add(
                zb[:, gs[g]], dbuf[:, gs[g]], q1b[:, gs[g]]),
                after=[t_q1[g], t_d[g]], tag="DVE") for g in range(4)]
            t_m2 = [em.o("vector", lambda v, g=g: v.reduce_max(
                m2b[:, g:g + 1], zb[:, gs[g]], axis=AX.X), after=[t_z[g]], tag="DVE")
                for g in range(4)]
            t_d2 = [em.o("vector", lambda v, g=g: v.tensor_scalar(
                d2b[:, gs[g]], dbuf[:, gs[g]], m2b[:, g:g + 1], None, ALU.subtract),
                after=[t_m2[g]], tag="DVE") for g in range(4)]
            t_s2 = [em.o("scalar", lambda sc, g=g: sc.sign(
                d2b[:, gs[g]], d2b[:, gs[g]], bias=0.0), after=[t_d2[g]], tag="ACT")
                for g in range(4)]
            t_nr = [em.o("vector", lambda v, g=g: v.tensor_scalar(
                nrb[:, gs[g]], d2b[:, gs[g]], -1.0, 0.0, ALU.mult, ALU.max),
                after=[t_s2[g]], tag="DVE") for g in range(4)]
            t_sel = [em.o("vector", lambda v, g=g: v.tensor_scalar(
                selb[:, gs[g]], nrb[:, gs[g]], -1.0, 1.0, ALU.mult, ALU.add),
                after=[t_nr[g]], tag="DVE") for g in range(4)]
            t_e = [em.o("scalar", lambda sc, g=g: sc.activation(
                eb[:, gs[g]], dbuf[:, gs[g]], AF.Exp), after=[t_d[g]], tag="ACT")
                for g in range(4)]
            t_g0 = [em.o("vector", lambda v, g=g: v.tensor_mul(
                g0b[:, gs[g]], eb[:, gs[g]], selb[:, gs[g]]),
                after=[t_e[g], t_sel[g]], tag="DVE") for g in range(4)]
            t_ns = [em.o("vector", lambda v, g=g: v.reduce_sum(
                nsb[:, g:g + 1], g0b[:, gs[g]], axis=AX.X), after=[t_g0[g]], tag="DVE")
                for g in range(4)]
            t_rc = [em.o("vector", lambda v, g=g: v.reciprocal(
                rcb[:, g:g + 1], nsb[:, g:g + 1]), after=[t_ns[g]], tag="DVE")
                for g in range(4)]
            t_gg = [em.o("vector", lambda v, g=g: v.tensor_scalar(
                gtm[:, gs[g]], g0b[:, gs[g]], rcb[:, g:g + 1], None, ALU.mult),
                after=[t_rc[g]], tag="DVE") for g in range(4)]
            for g in range(4):
                trb = em.o("tensor", lambda t, g=g: t.transpose(
                    pt[0:4, 0:128], gtm[:, gs[g]], id128_sb[:]),
                    after=[t_gg[g], ld_id128, tr_toks[-1]] +
                    (gate_toks[-1:] if gate_toks else []), tag="PE")
                gate_toks.append(em.o("vector", lambda v, g=g, t0=t0: v.tensor_copy(
                    gT[0:4, t0 + g * 128: t0 + (g + 1) * 128], pt[0:4, 0:128]),
                    after=[trb], tag="DVE"))
            prev_tile_toks = proj_toks + [lg]
        barA = em.phase_barrier()
        nc.sbuf_base = markA

        markB = nc.sbuf_base
        w1b = [alloc(f"w1b{i}", [128, 8 * 512], BF16) for i in range(2)]
        w2b = [alloc(f"w2b{i}", [128, 32 * 128], BF16) for i in range(2)]
        h1 = alloc("h1", [128, 32 * 1024], BF16)
        bct = alloc("bct", [128, 2 * W], F32)
        yt = alloc("yt", [128, 2 * W], F32)
        HB = 512
        acc_toks = []
        c_toks = []
        war_yt = {0: None, 1: None}
        ld_prev = {0: None, 1: None}
        ld2_prev = {0: None, 1: None}
        bct_prev = [None]
        for e in range(E):
            for half in range(2):
                h0 = half * 1024
                st1_toks = []
                for hb in range(8):
                    buf = w1b[hb % 2]
                    war = ld_prev[hb % 2]
                    ldw = em.o("gpsimd", lambda g, e=e, hb=hb, buf=buf: g.dma_start(
                        out=buf.ap().rearrange("p (c t) -> p c t", c=8),
                        in_=w1[e, :, hb * HB:(hb + 1) * HB].rearrange("(c p) t -> p c t", p=128)),
                        after=([war] if war else []) + barA, tag="LDG", inc=16)
                    hb_toks = []
                    for hc in range(4):
                        for tg in range(2):
                            slot1 = (hc * 2 + tg) % 2
                            pp = p0 if slot1 == 0 else p1
                            mm = None
                            for c in range(8):
                                mm = em.o("tensor", lambda t, c=c, hc=hc, tg=tg, buf=buf, pp=pp, h0=h0: t.matmul(
                                    pp[:, 0:W], buf[:, c * HB + hc * 128:c * HB + (hc + 1) * 128],
                                    xf[:, c * T + h0 + tg * W: c * T + h0 + (tg + 1) * W],
                                    start=(c == 0), stop=(c == 7)),
                                    after=[ldw] + barA +
                                    ([bank_free[slot1]] if bank_free.get(slot1) else [])
                                    if c == 0 else (), tag="PE")
                            evg = em.o("scalar", lambda sc, hb=hb, hc=hc, tg=tg, pp=pp, e=e: sc.activation(
                                h1[:, (hb * 4 + hc) * 1024 + tg * W: (hb * 4 + hc) * 1024 + (tg + 1) * W],
                                pp[:, 0:W], AF.Gelu,
                                bias=b1_sb[:, e * 32 + hb * 4 + hc: e * 32 + hb * 4 + hc + 1]),
                                after=[mm, ld_b1], tag="ACT")
                            bank_free[slot1] = evg
                            hb_toks.append(evg)
                            st1_toks.append(evg)
                    ld_prev[hb % 2] = em.echo(hb_toks, "scalar", "ACT")
                st1_bar = em.echo(st1_toks, "scalar", "ACT")
                btoks = []
                for tg in range(2):
                    bmm = em.o("tensor", lambda t, e=e, h0=h0, tg=tg: t.matmul(
                        pa[:, 0:W], emat_sb[:, e * 128:(e + 1) * 128],
                        gT[0:4, h0 + tg * W: h0 + (tg + 1) * W],
                        start=True, stop=True),
                        after=[ld_em] + barA + ([bct_prev[0]] if bct_prev[0] else [])
                        + btoks[-1:], tag="PE")
                    btoks.append(em.o("vector", lambda v, tg=tg: v.tensor_copy(
                        bct[:, tg * W:(tg + 1) * W], pa[:, 0:W]),
                        after=[bmm], tag="DVE"))
                bct_prev[0] = em.echo(btoks, "scalar", "ACT")
                for dc in range(8):
                    buf = w2b[dc % 2]
                    war = ld2_prev[dc % 2]
                    ldw2 = em.o("gpsimd", lambda g, e=e, dc=dc, buf=buf: g.dma_start(
                        out=buf.ap().rearrange("p (c t) -> p c t", c=32),
                        in_=w2[e, :, dc * 128:(dc + 1) * 128].rearrange("(c p) t -> p c t", p=128)),
                        after=([war] if war else []) + barA, tag="LDG", inc=16)
                    dc_reads = []
                    for tg in range(2):
                        pp = p0 if tg == 0 else p1
                        mm = None
                        for hc in range(32):
                            mm = em.o("tensor", lambda t, hc=hc, tg=tg, buf=buf, pp=pp: t.matmul(
                                pp[:, 0:W], buf[:, hc * 128:(hc + 1) * 128],
                                h1[:, hc * 1024 + tg * W: hc * 1024 + (tg + 1) * W],
                                start=(hc == 0), stop=(hc == 31)),
                                after=[ldw2, st1_bar] +
                                ([bank_free[tg]] if bank_free.get(tg) else [])
                                if hc == 0 else (), tag="PE")
                        slot = tg
                        tv = em.o("vector", lambda v, pp=pp, slot=slot: v.tensor_mul(
                            yt[:, slot * W:(slot + 1) * W], pp[:, 0:W], bct[:, slot * W:(slot + 1) * W]),
                            after=[mm, bct_prev[0]] + ([war_yt[slot]] if war_yt[slot] else []),
                            tag="DVE")
                        bank_free[slot] = tv
                        tv_e = em.echo([tv], "scalar", "ACT")
                        acc = em.o("gpsimd", lambda g, dc=dc, h0=h0, tg=tg, slot=slot: g.dma_start(
                            out=x3scr[dc * 128:(dc + 1) * 128, h0 + tg * W: h0 + (tg + 1) * W],
                            in_=yt[:, slot * W:(slot + 1) * W], accum_op=ALU.add),
                            after=[tv_e] + x3st_toks, tag="ST", inc=16)
                        acc_toks.append(acc)
                        war_yt[slot] = acc
                        dc_reads.append(tv)
                    if half == 0:
                        mmc = None
                        for hc in range(32):
                            mmc = em.o("tensor", lambda t, hc=hc, dc=dc, e=e, buf=buf: t.matmul(
                                pc[:, 4 + e:4 + e + 1], buf[:, hc * 128:(hc + 1) * 128],
                                gel_sb[:, e * 32 + hc:e * 32 + hc + 1],
                                start=(hc == 0), stop=(hc == 31)),
                                after=[ldw2, gel_tok] + (c_toks[-1:] if c_toks else [])
                                if hc == 0 else (), tag="PE")
                        c_toks.append(em.o("vector", lambda v, dc=dc, e=e: v.tensor_copy(
                            c_sb[:, dc * 4 + e:dc * 4 + e + 1], pc[:, 4 + e:4 + e + 1]),
                            after=[mmc], tag="DVE"))
                        dc_reads.append(c_toks[-1])
                    ld2_prev[dc % 2] = em.echo(dc_reads, "scalar", "ACT")
        barB = em.phase_barrier()
        nc.sbuf_base = markB

        x3c = alloc("x3c", [128, 8 * W], F32)
        cbar = em.echo(c_toks, "vector", "DVE")
        ct_toks = []
        for dc in range(8):
            trc = em.o("tensor", lambda t, dc=dc: t.transpose(
                pt[0:4, 0:128], c_sb[:, dc * 4:(dc + 1) * 4], id128_sb[:]),
                after=[cbar, ld_id128] + barB + (ct_toks[-2:-1] if ct_toks else []) +
                (gate_toks[-1:] if gate_toks else []), tag="PE")
            ct_toks.append(em.o("vector", lambda v, dc=dc: v.tensor_copy(
                cT_sb[0:4, dc * 128:(dc + 1) * 128], pt[0:4, 0:128]),
                after=[trc], tag="DVE"))
            rs1 = em.o("vector", lambda v, dc=dc: v.reduce_sum(
                rs_sb[:, dc:dc + 1], c_sb[:, dc * 4:(dc + 1) * 4], axis=AX.X),
                after=[cbar], tag="DVE")
            rs2 = em.o("vector", lambda v, dc=dc: v.reduce_sum(
                rs_sb[:, 8 + dc:8 + dc + 1], b2_sb[:, dc * 4:(dc + 1) * 4], axis=AX.X),
                after=[ld_b2], tag="DVE")
            ct_toks.append(em.o("vector", lambda v, dc=dc: v.tensor_add(
                csum_sb[:, dc:dc + 1], rs_sb[:, dc:dc + 1], rs_sb[:, 8 + dc:8 + dc + 1]),
                after=[rs1, rs2, rs1], tag="DVE"))
        prev_out = []
        for tt in range(4):
            t0 = tt * W
            ldx3 = em.o("sync", lambda e, t0=t0: e.dma_start(
                out=x3c.ap().rearrange("p (c t) -> p c t", c=8),
                in_=x3scr[:, t0:t0 + W].rearrange("(c p) t -> p c t", p=128)),
                after=barB + prev_out + [em.dma_barrier("ST")], tag="LD", inc=16)
            out_toks = []
            prev_o1 = None
            for dc in range(8):
                mmt = em.o("tensor", lambda t, dc=dc, t0=t0: t.matmul(
                    pb[:, 0:W], cT_sb[0:4, dc * 128:(dc + 1) * 128], gT[0:4, t0:t0 + W],
                    start=True, stop=True),
                    after=ct_toks + [ldx3] + ([prev_o1] if prev_o1 else []), tag="PE")
                o1_ = em.o("vector", lambda v, dc=dc: v.scalar_tensor_tensor(
                    x3c[:, dc * W:(dc + 1) * W], pb[:, 0:W], -1.0,
                    x3c[:, dc * W:(dc + 1) * W], ALU.mult, ALU.add),
                    after=[mmt, ldx3], tag="DVE")
                o2_ = em.o("vector", lambda v, dc=dc: v.tensor_scalar(
                    x3c[:, dc * W:(dc + 1) * W], x3c[:, dc * W:(dc + 1) * W],
                    csum_sb[:, dc:dc + 1], None, ALU.add),
                    after=[o1_], tag="DVE")
                prev_o1 = o1_
                out_toks.append(o2_)
            ek = em.echo(out_toks, "scalar", "ACT")
            sto = em.o("gpsimd", lambda e, t0=t0: e.dma_start(
                out=x4T[:, t0:t0 + W].rearrange("(c p) t -> p c t", p=128),
                in_=x3c.ap().rearrange("p (c t) -> p c t", c=8)),
                after=[ek], tag="ST", inc=16)
            prev_out = [sto]



def emit_phase3(nc, em, G, bar, cc_tok, phases=(1, 2), nt_limit=None):
    W = 512
    NT = S // W
    cc2o = G["cc2o"]
    qw, kw, vw = G["qw"], G["kw"], G["vw"]
    qb, kb, vb = G["qb"], G["kb"], G["vb"]
    masks = G["masks"]
    id64 = G["idt"]
    ones_row_f = G["ones_row_f"]
    out1T = G["out1d"]

    with ExitStack() as stk:
        en = stk.enter_context

        def alloc(name, shape, dt):
            return nc.alloc_sbuf_tensor("a_" + name, list(shape), dt)

        q_sb = alloc("q_sb", [128, 2 * S], BF16)
        k_sb = alloc("k_sb", [128, 2 * S], BF16)
        v_tm = alloc("v_tm", [128, 64 * 4 * 65], BF16)
        ht = [alloc(f"ht{i}", [128, 8 * W], BF16) for i in range(2)]
        vtmp = alloc("vtmp", [128, 2 * W], BF16)
        qw_sb = alloc("qw_sb", [128, 8 * 256], BF16)
        kw_sb = alloc("kw_sb", [128, 8 * 256], BF16)
        vw_sb = alloc("vw_sb", [128, 8 * 256], BF16)
        qb_sb = alloc("qb_sb", [128, 2], F32)
        kb_sb = alloc("kb_sb", [128, 2], F32)
        vb_sb = alloc("vb_sb", [128, 2], F32)
        mask_sb = alloc("mask_sb", [128, 4 * W], F32)
        id_sb = alloc("id_sb", [128, 128], BF16)
        or_sb = alloc("or_sb", [1, 128], F32)
        w_sb = [alloc(f"w_sb{i}", [128, W], BF16) for i in range(2)]
        row_sb = alloc("row_sb", [1, W], F32)
        rc_sb = alloc("rc_sb", [1, W], F32)
        bc_sb = alloc("bc_sb", [64, W], F32)
        out_sb = [alloc(f"out_sb{i}", [64, W], BF16) for i in range(2)]
        p_s = [en(nc.psum_tensor(f"a_p_s{i}", [128, W], F32)) for i in range(2)]
        p_o = en(nc.psum_tensor("a_p_o", [128, W], F32))
        p_b = en(nc.psum_tensor("a_p_b", [64, W], F32))
        pt_bf2 = [en(nc.psum_tensor(f"a_pt_bf{i}", [128, 256], BF16)) for i in range(2)]

        def dma_in(dst, src, q="sync", after=()):
            return em.o(q, lambda e, dst=dst, src=src: e.dma_start(out=dst, in_=src),
                        after=after, tag="LD" if q == "sync" else "LDG", inc=16)

        ld_qw = dma_in(qw_sb.ap().rearrange("p (c o) -> p c o", c=8),
                       qw.rearrange("(c p) o -> p c o", p=128), q="gpsimd", after=bar)
        ld_kw = dma_in(kw_sb.ap().rearrange("p (c o) -> p c o", c=8),
                       kw.rearrange("(c p) o -> p c o", p=128), q="gpsimd", after=bar)
        ld_vw = dma_in(vw_sb.ap().rearrange("p (c o) -> p c o", c=8),
                       vw.rearrange("(c p) o -> p c o", p=128), q="gpsimd", after=bar)
        ld_qb = dma_in(qb_sb[:], qb, after=bar)
        ld_kb = dma_in(kb_sb[:], kb, after=bar)
        ld_vb = dma_in(vb_sb[:], vb, after=bar)
        ld_m = dma_in(mask_sb[:], masks, after=bar)
        ld_id = dma_in(id_sb[:], id64, after=bar)
        ld_or = dma_in(or_sb[:], ones_row_f, after=bar)

        ones_tok = em.o("vector", lambda v: v.memset(v_tm[:], 1.0), after=bar,
                        tag="DVE") \
            if 1 in phases else None

        qkv_toks = []
        ht_war = {0: None, 1: None}
        pbank_free = {0: None, 1: None}
        ptbf_free = {0: None, 1: None}
        for tt in range(nt_limit if nt_limit else NT):
            hb = ht[tt % 2]
            war = ht_war[tt % 2]
            ldh = em.o("sync", lambda e, tt=tt, hb=hb: e.dma_start(
                out=hb.ap().rearrange("p (c t) -> p c t", c=8),
                in_=cc2o[(tt // 4) * D:((tt // 4) + 1) * D,
                         (tt % 4) * W:((tt % 4) + 1) * W].rearrange(
                    "(c p) t -> p c t", p=128)),
                after=([war] if war else []) + [cc_tok] + bar, tag="LD", inc=16)
            tile_reads = []
            for (wsb, bsb, ldwt, ldbt, dst) in (
                    (qw_sb, qb_sb, ld_qw, ld_qb, q_sb),
                    (kw_sb, kb_sb, ld_kw, ld_kb, k_sb),
                    (vw_sb, vb_sb, ld_vw, ld_vb, None)):
                for oc in range(2):
                    pp = p_s[oc]
                    mm = None
                    for c in range(8):
                        mm = em.o("tensor", lambda t, c=c, oc=oc, wsb=wsb, hb=hb, pp=pp: t.matmul(
                            pp[:, 0:W], wsb[:, c * 256 + oc * 128: c * 256 + (oc + 1) * 128],
                            hb[:, c * W:(c + 1) * W], start=(c == 0), stop=(c == 7)),
                            after=[ldwt, ldh] +
                            ([pbank_free[oc]] if pbank_free.get(oc) else [])
                            if c == 0 else (), tag="PE")
                    if dst is not None:
                        ev = em.o("scalar", lambda sc, oc=oc, tt=tt, pp=pp, dst=dst, bsb=bsb: sc.activation(
                            dst[:, oc * S + tt * W: oc * S + (tt + 1) * W], pp[:, 0:W],
                            AF.Identity, bias=bsb[:, oc:oc + 1]),
                            after=[mm, ldbt], tag="ACT")
                    else:
                        ev = em.o("scalar", lambda sc, oc=oc, pp=pp, bsb=bsb: sc.activation(
                            vtmp[:, oc * W:(oc + 1) * W], pp[:, 0:W],
                            AF.Identity, bias=bsb[:, oc:oc + 1]),
                            after=[mm, ldbt] + [t for t in ptbf_free.values() if t],
                            tag="ACT")
                    pbank_free[oc] = ev
                    tile_reads.append(ev)
                    qkv_toks.append(ev)
            tr_evs = []
            for sc_i in (range(4) if 1 in phases else ()):
                pbank = pt_bf2[sc_i % 2]
                war_b = ptbf_free.get(sc_i % 2)
                trvs = []
                for oc in range(2):
                    if "notr" in phases:
                        trv = em.o("tensor", lambda t: t.nop(),
                                   after=[tile_reads[-1], tile_reads[-2], ld_id] +
                                   ([war_b] if war_b else []), tag="PE")
                    else:
                        trv = em.o("tensor", lambda t, oc=oc, sc_i=sc_i, pbank=pbank: t.transpose(
                            pbank[:, oc * 128:(oc + 1) * 128],
                            vtmp[:, oc * W + sc_i * 128: oc * W + (sc_i + 1) * 128],
                            id_sb[:]),
                            after=[tile_reads[-1], tile_reads[-2], ld_id] +
                            ([war_b] if war_b else []), tag="PE")
                    trvs.append(trv)
                bi = (tt * 4 + sc_i) * 4
                if "nocp" in phases:
                    cp = em.o("vector", lambda v, pbank=pbank: v.tensor_copy(
                        v_tm[:, 0:256], pbank[:, 0:256]),
                        after=trvs + [ones_tok], tag="DVE")
                else:
                    cp = em.o("vector", lambda v, bi=bi, pbank=pbank: v.tensor_copy(
                        v_tm.ap().rearrange("p (b c) -> p b c", c=65)[:, bi:bi + 4, 0:64],
                        pbank.ap().rearrange("p (h c) -> p h c", c=64)),
                        after=trvs + [ones_tok], tag="DVE")
                tr_evs.append(cp)
                ptbf_free[sc_i % 2] = em.echo([cp], "scalar", "ACT")
            qkv_toks.extend(tr_evs)
            ht_war[tt % 2] = em.echo(tile_reads, "vector", "DVE")
            if 1 not in phases:
                qkv_toks.extend(tile_reads)
        ph1 = em.phase_barrier()

        war_out = {0: None, 1: None}
        w_free = {0: None, 1: None}
        po_free = None
        pb_free = None
        for h in range(4):
            qrow = (h % 2) * 64
            qcolb = (h // 2) * S
            for m in range(NT):
                qT = q_sb[qrow:qrow + 64, qcolb + m * W: qcolb + (m + 1) * W]
                nj = 4 * m + 4
                exp_toks = {}
                qk_toks = {}
                pv_prev = None
                for j in range(nj):
                    sl = j % 2
                    dep = [ph1[0]] if (h == 0 and m == 0) else []
                    dep = list(ph1) if (h == 0 and m == 0 and j == 0) else []
                    prev_exp = exp_toks.get(j - 2)
                    qk = em.o("tensor", lambda t, j=j, sl=sl, qT=qT, qrow=qrow, qcolb=qcolb: t.matmul(
                        p_s[sl][:, 0:W],
                        k_sb[qrow:qrow + 64, qcolb + j * 128: qcolb + j * 128 + 128],
                        qT, start=True, stop=True),
                        after=dep + ([prev_exp] if prev_exp else []), tag="PE")
                    qk_toks[j] = qk
                    d = j - 4 * m
                    if d >= 0:
                        mk = em.o("vector", lambda v, sl=sl, d=d: v.tensor_add(
                            p_s[sl][:, 0:W], p_s[sl][:, 0:W], mask_sb[:, d * W:(d + 1) * W]),
                            after=[qk, ld_m], tag="DVE")
                        edep = [mk]
                    else:
                        edep = [qk]
                    wfree = w_free.get(j % 2)
                    ex = em.o("scalar", lambda sc, j=j, sl=sl: sc.activation(
                        w_sb[sl][:, 0:W], p_s[sl][:, 0:W], AF.Exp, scale=0.125),
                        after=edep + ([wfree] if wfree else []), tag="ACT")
                    exp_toks[j] = ex
                    blk = ((m * 4 + ... ) if False else 0)
                    blk = (j * 4 + h) * 65
                    pv = em.o("tensor", lambda t, j=j, sl=sl, blk=blk, m=m, nj=nj: t.matmul(
                        p_o[0:65, 0:W], v_tm[:, blk:blk + 65], w_sb[sl][:, 0:W],
                        start=(j == 0), stop=(j == nj - 1)),
                        after=[ex] + ([po_free] if (j == 0 and po_free) else []), tag="PE")
                    w_free[j % 2] = pv
                    pv_prev = pv
                se = em.o("scalar", lambda sc: sc.activation(
                    row_sb[0:1, 0:W], p_o[64:65, 0:W], AF.Identity),
                    after=[pv_prev], tag="ACT")
                rc = em.o("vector", lambda v: v.reciprocal(rc_sb[0:1, 0:W], row_sb[0:1, 0:W]),
                          after=[se], tag="DVE")
                bcmm = em.o("tensor", lambda t: t.matmul(
                    p_b[0:64, 0:W], or_sb[0:1, 0:64], rc_sb[0:1, 0:W], start=True, stop=True),
                    after=[rc, ld_or] + ([pb_free] if pb_free else []), tag="PE")
                bev = em.o("vector", lambda v: v.tensor_copy(bc_sb[:], p_b[0:64, 0:W]),
                           after=[bcmm], tag="DVE")
                pb_free = bev
                oslot = m % 2
                war = war_out[oslot]
                oev = em.o("vector", lambda v, oslot=oslot: v.tensor_mul(
                    out_sb[oslot][:], p_o[0:64, 0:W], bc_sb[:]),
                    after=[bev, pv_prev] + ([war] if war else []), tag="DVE")
                po_free = oev
                ek = em.echo([oev], "scalar", "ACT")
                st = em.o("gpsimd", lambda e, h=h, m=m, oslot=oslot: e.dma_start(
                    out=out1T[h * 64:(h + 1) * 64, m * W:(m + 1) * W],
                    in_=out_sb[oslot][:]),
                    after=[ek], tag="ST", inc=16)
                war_out[oslot] = st



def emit_phase1(nc, em, G):
    W = 512
    NT = S // W
    SC = 128
    xT = G["xT"]
    wq, wk, wv = G["wq"], G["wk"], G["wv"]
    aq, av, bq, bv = G["aq"], G["av"], G["bq"], G["bv"]
    cq, sq, phv, pho = G["cq"], G["sq"], G["phv"], G["pho"]
    rot, mask0, idt = G["rot"], G["mask0"], G["idt"]
    ones_col_bf, ones_row_f, eps = G["ones_col_bf"], G["ones_row_f"], G["eps"]
    out0T = G["out0d"]

    with ExitStack() as stk:
        en = stk.enter_context

        def alloc(name, shape, dt):
            return nc.alloc_sbuf_tensor(name, list(shape), dt)

        alq = alloc("alq", [128, 2 * S], BF16)
        pkt = alloc("pkt", [128, 2 * S], BF16)
        pvt = alloc("pvt", [128, 2 * S], BF16)
        fk = alloc("fk", [128, 2 * NS], BF16)
        fvtm = alloc("fvtm", [128, 16 * 4 * 65], BF16)
        wq_sb = alloc("wq_sb", [128, 8 * 256], BF16)
        wk_sb = alloc("wk_sb", [128, 8 * 256], BF16)
        wv_sb = alloc("wv_sb", [128, 8 * 256], BF16)
        aq_sb = alloc("aq_sb", [128, 8 * LORA], BF16)
        av_sb = alloc("av_sb", [128, 8 * LORA], BF16)
        bq_sb = alloc("bq_sb", [LORA, 256], BF16)
        bv_sb = alloc("bv_sb", [LORA, 256], BF16)
        rot_sb = alloc("rot_sb", [128, 128], F32)
        phv_sb = alloc("phv_sb", [128, W], F32)
        pho_sb = alloc("pho_sb", [64, W], F32)
        mask_sb = alloc("mask_sb", [128, W], F32)
        id_sb = alloc("id_sb", [128, 128], BF16)
        consts = {
            "ones_col_bf": alloc("c_ones_col", [128, 1], BF16),
            "ones_row_f": alloc("c_ones_row", [1, 128], F32),
            "eps": alloc("c_eps", [1, 1], F32),
        }
        xt = alloc("xt", [128, 8 * W], F32)
        ln_scr = {
            "xbf": alloc("ln_xbf", [128, 8 * W], BF16),
            "rows": [alloc(f"ln_r{i}", [1, W], F32) for i in range(4)],
            "bc_r": alloc("ln_bcr", [128, W], F32),
            "bc_mr": alloc("ln_bcmr", [128, W], F32),
        }
        ub = ln_scr["xbf"]
        bc_r = ln_scr["bc_r"]
        tmpf = xt
        tmpf2 = ln_scr["bc_mr"]
        tlora = alloc("tlora", [LORA, W], BF16)
        qf = alloc("qf", [128, 2 * W], F32)
        ksh = alloc("ksh", [128, 2 * W], BF16)
        vsh = alloc("vsh", [128, 2 * W], BF16)
        fvtmp = alloc("fvtmp", [128, 2 * SC], BF16)
        tab = alloc("tab", [128, 2 * W], F32)
        w_sb = [alloc(f"w_sb{i}", [128, W], BF16) for i in range(2)]
        out_sb = [alloc(f"out_sb{i}", [64, W], BF16) for i in range(2)]
        bcw_bf = alloc("bcw_bf", [64, W], BF16)
        p0 = en(nc.psum_tensor("p0", [128, W], F32))
        p1 = en(nc.psum_tensor("p1", [128, W], F32))
        pa = en(nc.psum_tensor("pa", [128, W], F32))
        pb = en(nc.psum_tensor("pb", [128, W], F32))
        p_o = en(nc.psum_tensor("p_o", [128, W], F32))
        p_d = en(nc.psum_tensor("p_d", [1, W], F32))
        pl = en(nc.psum_tensor("pl", [LORA, W], F32))
        pt_bf = en(nc.psum_tensor("pt_bf", [128, 256], BF16))
        ps = {"stat_a": pa, "stat_b": pb}

        def dma_in(dst, src, q="sync"):
            return em.o(q, lambda e, dst=dst, src=src: e.dma_start(out=dst, in_=src),
                        tag="LD" if q == "sync" else "LDG", inc=16)

        ld_wq = dma_in(wq_sb.ap().rearrange("p (c o) -> p c o", c=8),
                       wq.rearrange("(c p) o -> p c o", p=128), q="gpsimd")
        ld_wk = dma_in(wk_sb.ap().rearrange("p (c o) -> p c o", c=8),
                       wk.rearrange("(c p) o -> p c o", p=128), q="gpsimd")
        ld_wv = dma_in(wv_sb.ap().rearrange("p (c o) -> p c o", c=8),
                       wv.rearrange("(c p) o -> p c o", p=128), q="gpsimd")
        ld_aq = dma_in(aq_sb.ap().rearrange("p (c o) -> p c o", c=8),
                       aq.rearrange("(c p) o -> p c o", p=128), q="gpsimd")
        ld_av = dma_in(av_sb.ap().rearrange("p (c o) -> p c o", c=8),
                       av.rearrange("(c p) o -> p c o", p=128), q="gpsimd")
        ld_bq = dma_in(bq_sb[:], bq, q="gpsimd")
        ld_bv = dma_in(bv_sb[:], bv, q="gpsimd")
        ld_rot = dma_in(rot_sb[:], rot)
        ld_phv = dma_in(phv_sb[:], phv)
        ld_pho = dma_in(pho_sb[:], pho)
        ld_m = dma_in(mask_sb[:], mask0)
        ld_id = dma_in(id_sb[:], idt)
        ld_oc = dma_in(consts["ones_col_bf"][:], ones_col_bf)
        ld_or = dma_in(consts["ones_row_f"][:], ones_row_f)
        ld_eps = dma_in(consts["eps"][:], eps)
        const_deps = [ld_oc, ld_or, ld_eps]
        wkh = em.o("vector", lambda v: v.tensor_scalar(
            wk_sb[:], wk_sb[:], 0.5, None, ALU.mult), after=[ld_wk], tag="DVE")
        ones_tok = em.o("vector", lambda v: v.memset(fvtm[:], 1.0), tag="DVE")

        xt_war = None
        tab_war = None
        ub_war = None
        dve_war = None
        fvtmp_war = []
        ptbf_war = None
        shm_prev = {}
        for tt in range(NT):
            t0 = tt * W
            ldx = em.o("sync", lambda e, t0=t0: e.dma_start(
                out=xt.ap().rearrange("p (c t) -> p c t", c=8),
                in_=xT[:, t0:t0 + W].rearrange("(c p) t -> p c t", p=128)),
                after=[xt_war] if xt_war else [], tag="LD", inc=16)
            ldt = []
            for i, tsrc in enumerate((cq, sq)):
                ldt.append(em.o("sync", lambda e, i=i, tsrc=tsrc, t0=t0: e.dma_start(
                    out=tab[:, i * W:(i + 1) * W], in_=tsrc[:, t0:t0 + W]),
                    after=[tab_war] if tab_war else [], tag="LD", inc=16))
            (l1, l2), _, bc_m = emit_ln_stats(
                em, consts, ps, lambda c: xt[:, c * W:(c + 1) * W], W,
                [ldx] + const_deps + ([ub_war] if ub_war else [])
                + ([dve_war] if dve_war else []), "", ln_scr,
                raw_mean=True)
            u_toks = []
            for c in range(8):
                u_toks.append(em.o("vector", lambda v, c=c: v.tensor_sub(
                    ub[:, c * W:(c + 1) * W], xt[:, c * W:(c + 1) * W], bc_m[:, 0:W]),
                    after=[l2], tag="DVE"))
            evs = {}
            rope_toks = []
            proj_last = None

            def lora_pass(asb, lda, prev_read):
                mmt = None
                for c in range(8):
                    mmt = em.o("tensor", lambda t, c=c, asb=asb: t.matmul(
                        pl[:, 0:W], asb[:, c * LORA:(c + 1) * LORA],
                        ub[:, c * W:(c + 1) * W], start=(c == 0), stop=(c == 7)),
                        after=([u_toks[c], lda] + prev_read) if c == 0 else [u_toks[c]],
                        tag="PE")
                return em.o("vector", lambda v: v.tensor_copy(tlora[:, 0:W], pl[:, 0:W]),
                            after=[mmt], tag="DVE")

            def proj(nm, wsb, bsb, ldwt, ldb, tl, oc, bank_dep):
                pp = p0 if oc == 0 else p1
                mm = None
                for c in range(8):
                    mm = em.o("tensor", lambda t, c=c, oc=oc, wsb=wsb, pp=pp: t.matmul(
                        pp[:, 0:W], wsb[:, c * 256 + oc * 128: c * 256 + (oc + 1) * 128],
                        ub[:, c * W:(c + 1) * W],
                        start=(c == 0), stop=(c == 7 and tl is None)),
                        after=([ldwt, u_toks[c]] + bank_dep) if c == 0 else [u_toks[c]],
                        tag="PE")
                if tl is not None:
                    mm = em.o("tensor", lambda t, oc=oc, bsb=bsb, pp=pp: t.matmul(
                        pp[:, 0:W], bsb[0:LORA, oc * 128:(oc + 1) * 128], tlora[:, 0:W],
                        start=False, stop=True), after=[tl, ldb], tag="PE")
                return mm, pp

            tlq = lora_pass(aq_sb, ld_aq, [])
            qmm = {}
            for oc in range(2):
                mm, pp = proj("q", wq_sb, bq_sb, ld_wq, ld_bq, tlq, oc, [])
                ev = em.o("vector", lambda v, oc=oc, pp=pp: v.tensor_mul(
                    qf[:, oc * W:(oc + 1) * W], pp[:, 0:W], bc_r[:, 0:W]),
                    after=[mm, l1], tag="DVE")
                evs[("q", oc)] = ev
                qmm[oc] = mm

            def rope(src_oc_ap, oc, dst_ap, psh, bank_dep, src_dep, war):
                shm = em.o("tensor", lambda t, src_oc_ap=src_oc_ap, psh=psh: t.matmul(
                    psh[:, 0:W], rot_sb[:], src_oc_ap, start=True, stop=True),
                    after=[src_dep, ld_rot] + bank_dep, tag="PE")
                t1 = em.o("vector", lambda v, src_oc_ap=src_oc_ap: v.tensor_mul(
                    tmpf[:, 0:W], src_oc_ap, tab[:, 0:W]),
                    after=[src_dep] + war + ldt, tag="DVE")
                t2 = em.o("vector", lambda v, psh=psh: v.tensor_mul(
                    tmpf2[:, 0:W], psh[:, 0:W], tab[:, W:2 * W]),
                    after=[shm, src_dep] + war + ldt, tag="DVE")
                t3 = em.o("vector", lambda v, dst_ap=dst_ap: v.tensor_add(
                    dst_ap, tmpf[:, 0:W], tmpf2[:, 0:W]), after=[t1, t2], tag="DVE")
                return shm, t2, t3

            rope_war = []
            for oc in range(2):
                psh = pa if oc == 0 else pb
                bank_dep = [l1 if oc == 0 else l2] + (shm_prev.get(oc) or [])
                shm, t2, t3 = rope(qf[:, oc * W:(oc + 1) * W], oc,
                                   alq[:, oc * S + t0: oc * S + t0 + W],
                                   psh, bank_dep, evs[("q", oc)], rope_war)
                rope_toks.append(t3)
                rope_war = [t3]
                evs[("ropeq", oc)] = (shm, t2, t3)

            for oc in range(2):
                mm, pp = proj("k", wk_sb, None, wkh, None, None, oc,
                              [evs[("q", oc)]])
                ev = em.o("vector", lambda v, oc=oc, pp=pp: v.tensor_mul(
                    qf[:, oc * W:(oc + 1) * W], pp[:, 0:W], bc_r[:, 0:W]),
                    after=[mm, evs[("ropeq", oc)][0], evs[("ropeq", oc)][2]], tag="DVE")
                evs[("k", oc)] = ev
            for oc in range(2):
                psh = pa if oc == 0 else pb
                bank_dep = [evs[("ropeq", oc)][1]]
                shm, t2, t3 = rope(qf[:, oc * W:(oc + 1) * W], oc,
                                   ksh[:, oc * W:(oc + 1) * W],
                                   psh, bank_dep, evs[("k", oc)], rope_war)
                rope_toks.append(t3)
                rope_war = [t3]
                evs[("ropek", oc)] = (shm, t2, t3)
                shm_prev[oc] = [t2]

            bcp = em.o("vector", lambda v: v.tensor_mul(
                bc_r[:, 0:W], bc_r[:, 0:W], phv_sb[:]),
                after=[l1, ld_phv, evs[("k", 1)]], tag="DVE")
            tlv = lora_pass(av_sb, ld_av, [qmm[1], tlq])
            for oc in range(2):
                mm, pp = proj("v", wv_sb, bv_sb, ld_wv, ld_bv, tlv, oc,
                              [evs[("k", oc)]])
                ev = em.o("vector", lambda v, oc=oc, pp=pp: v.tensor_mul(
                    vsh[:, oc * W:(oc + 1) * W], pp[:, 0:W], bc_r[:, 0:W]),
                    after=[mm, bcp], tag="DVE")
                evs[("v", oc)] = ev
                if oc == 1:
                    proj_last = mm
            ub_war = proj_last

            comp_last = None
            for oc in range(2):
                for (srcb, isk) in ((ksh, True), (vsh, False)):
                    sdep = evs[("ropek", oc)][2] if isk else evs[("v", oc)]
                    base = oc * W

                    def s0(j, srcb=srcb, base=base):
                        return srcb[:, base + j: base + W: 4]
                    if isk:
                        dsts = fk[:, oc * NS + tt * SC: oc * NS + (tt + 1) * SC]
                    else:
                        dsts = fvtmp[:, oc * SC:(oc + 1) * SC]
                    a1 = em.o("vector", lambda v, s0=s0: v.tensor_add(
                        tmpf[:, 0:SC], s0(0), s0(1)),
                        after=[sdep] + ([comp_last] if comp_last else rope_war),
                        tag="DVE")
                    a2 = em.o("vector", lambda v, s0=s0: v.tensor_add(
                        tmpf[:, 0:SC], tmpf[:, 0:SC], s0(2)), after=[a1], tag="DVE")
                    a3 = em.o("vector", lambda v, s0=s0, dsts=dsts: v.tensor_add(
                        dsts, tmpf[:, 0:SC], s0(3)),
                        after=[a2] + (fvtmp_war if not isk else []), tag="DVE")
                    comp_last = a3
                    if not isk:
                        evs[("fv", oc)] = a3
                for (srcb, dstb, sdep) in ((ksh, pkt, evs[("ropek", oc)][2]),
                                           (vsh, pvt, evs[("v", oc)])):
                    base = oc * W
                    rbase = oc * S + t0

                    def s0(j, srcb=srcb, base=base):
                        return srcb[:, base + j: base + W: 4]

                    def d0(j, dstb=dstb, rbase=rbase):
                        return dstb[:, rbase + j: rbase + W: 4]
                    cprev = em.o("vector", lambda v, s0=s0, d0=d0: v.tensor_copy(
                        d0(0), s0(0)), after=[sdep], tag="DVE")
                    for j in range(1, 4):
                        cprev = em.o("vector", lambda v, j=j, s0=s0, d0=d0: v.tensor_add(
                            d0(j), d0(j - 1), s0(j)), after=[cprev], tag="DVE")
                    comp_last = cprev
            fvtmp_war = []
            trvs = []
            for oc in range(2):
                trv = em.o("tensor", lambda t, oc=oc: t.transpose(
                    pt_bf[:, oc * 128:(oc + 1) * 128], fvtmp[:, oc * SC:(oc + 1) * SC],
                    id_sb[:]),
                    after=[evs[("fv", oc)], ld_id] + ([ptbf_war] if ptbf_war else []),
                    tag="PE")
                fvtmp_war.append(trv)
                trvs.append(trv)
            cps = []
            for oc in range(2):
                for hh in range(2):
                    h = 2 * oc + hh
                    cp = em.o("vector", lambda v, tt=tt, h=h, oc=oc, hh=hh: v.tensor_copy(
                        fvtm[:, (tt * 4 + h) * 65:(tt * 4 + h) * 65 + 64],
                        pt_bf[:, oc * 128 + hh * 64: oc * 128 + hh * 64 + 64]),
                        after=trvs + [ones_tok], tag="DVE")
                    cps.append(cp)
            ptbf_war = em.echo(cps, "scalar", "ACT")
            comp_last = cps[-1]
            dve_war = comp_last
            tab_war = em.echo([evs[("ropek", 1)][1]], "scalar", "ACT")
            xt_war = em.echo([comp_last], "scalar", "ACT")
        ph1 = em.phase_barrier()

        rows = ln_scr["rows"]
        wd_row, se_row, rc_row = rows[0], rows[1], rows[2]
        bcw = xt
        t2f_ap = lambda: xt[0:64, W:2 * W]
        bcf_ap = lambda: ln_scr["bc_r"][0:64, 0:W]
        war_out = {0: None, 1: None}
        w_free = {0: None, 1: None}
        po_free = None
        pd_free = None
        pa_free = None
        pb_free = None
        t2v_prev = None
        bfm_prev = None
        oc_sb = consts["ones_col_bf"]
        or_sb = consts["ones_row_f"]
        for h in range(4):
            qrow = (h % 2) * 64
            fcolb = (h // 2) * NS
            qcolb = (h // 2) * S
            for m in range(NT):
                qT = alq[qrow:qrow + 64, qcolb + m * W: qcolb + (m + 1) * W]
                nj = m + 1
                exp_toks = {}
                pv_prev = None
                for j in range(nj):
                    sl = j % 2
                    dep = list(ph1) if (h == 0 and m == 0 and j == 0) else []
                    prev_exp = exp_toks.get(j - 2)
                    qk = em.o("tensor", lambda t, j=j, sl=sl, qT=qT, qrow=qrow, fcolb=fcolb: t.matmul(
                        (p0 if sl == 0 else p1)[:, 0:W],
                        fk[qrow:qrow + 64, fcolb + j * 128: fcolb + j * 128 + 128],
                        qT, start=True, stop=True),
                        after=dep + ([prev_exp] if prev_exp else []), tag="PE")
                    if j == m:
                        mk = em.o("vector", lambda v, sl=sl: v.tensor_add(
                            (p0 if sl == 0 else p1)[:, 0:W],
                            (p0 if sl == 0 else p1)[:, 0:W], mask_sb[:]),
                            after=[qk, ld_m], tag="DVE")
                        edep = [mk]
                    else:
                        edep = [qk]
                    wfree = w_free.get(sl)
                    ex = em.o("scalar", lambda sc, sl=sl: sc.activation(
                        w_sb[sl][:, 0:W], (p0 if sl == 0 else p1)[:, 0:W],
                        AF.Exp, scale=0.125),
                        after=edep + ([wfree] if wfree else []), tag="ACT")
                    exp_toks[j] = ex
                    blk = (j * 4 + h) * 65
                    pv = em.o("tensor", lambda t, sl=sl, blk=blk, j=j, nj=nj: t.matmul(
                        p_o[0:65, 0:W], fvtm[:, blk:blk + 65], w_sb[sl][:, 0:W],
                        start=(j == 0), stop=(j == nj - 1)),
                        after=[ex] + ([po_free] if (j == 0 and po_free) else []),
                        tag="PE")
                    w_free[sl] = pv
                    pv_prev = pv
                pr = em.o("vector", lambda v, qT=qT, qrow=qrow, qcolb=qcolb, m=m: v.tensor_mul(
                    vsh[qrow:qrow + 64, 0:W], qT,
                    pkt[qrow:qrow + 64, qcolb + m * W: qcolb + (m + 1) * W]),
                    after=(list(ph1) if (h == 0 and m == 0) else [])
                    + ([pd_free] if pd_free else []), tag="DVE")
                dmm = em.o("tensor", lambda t, qrow=qrow: t.matmul(
                    p_d[0:1, 0:W], oc_sb[qrow:qrow + 64, 0:1], vsh[qrow:qrow + 64, 0:W],
                    start=True, stop=True),
                    after=[pr] + ([pd_free] if pd_free else []), tag="PE")
                wd = em.o("scalar", lambda sc: sc.activation(
                    wd_row[0:1, 0:W], p_d[0:1, 0:W], AF.Exp, scale=0.125),
                    after=[dmm], tag="ACT")
                pd_free = wd
                sadd = em.o("vector", lambda v: v.tensor_add(
                    p_o[64:65, 0:W], p_o[64:65, 0:W], wd_row[0:1, 0:W]),
                    after=[wd, pv_prev], tag="DVE")
                bcmm = em.o("tensor", lambda t: t.matmul(
                    pa[0:64, 0:W], or_sb[0:1, 0:64], wd_row[0:1, 0:W],
                    start=True, stop=True),
                    after=[wd] + ([pa_free] if pa_free else []), tag="PE")
                bev = em.o("vector", lambda v: v.tensor_copy(bcw_bf[:], pa[0:64, 0:W]),
                           after=[bcmm] + ([t2v_prev] if t2v_prev else []), tag="DVE")
                pa_free = bev
                pvc = em.o("vector", lambda v, qrow=qrow, qcolb=qcolb, m=m: v.tensor_copy(
                    vsh[0:64, W:2 * W],
                    pvt[qrow:qrow + 64, qcolb + m * W: qcolb + (m + 1) * W]),
                    after=[bev], tag="DVE")
                t2v = em.o("vector", lambda v: v.tensor_mul(
                    t2f_ap(), bcw_bf[:], vsh[0:64, W:2 * W]),
                    after=[pvc], tag="DVE")
                t2v_prev = t2v
                acc = em.o("vector", lambda v: v.tensor_add(
                    p_o[0:64, 0:W], p_o[0:64, 0:W], t2f_ap()),
                    after=[t2v, pv_prev], tag="DVE")
                se = em.o("scalar", lambda sc: sc.activation(
                    se_row[0:1, 0:W], p_o[64:65, 0:W], AF.Identity),
                    after=[sadd], tag="ACT")
                rc = em.o("vector", lambda v: v.reciprocal(rc_row[0:1, 0:W], se_row[0:1, 0:W]),
                          after=[se] + ([bfm_prev] if bfm_prev else []), tag="DVE")
                bfm = em.o("tensor", lambda t: t.matmul(
                    pb[0:64, 0:W], or_sb[0:1, 0:64], rc_row[0:1, 0:W],
                    start=True, stop=True),
                    after=[rc] + ([pb_free] if pb_free else []), tag="PE")
                bfm_prev = bfm
                bcf = em.o("vector", lambda v: v.tensor_mul(
                    bcf_ap(), pb[0:64, 0:W], pho_sb[:]),
                    after=[bfm, ld_pho], tag="DVE")
                pb_free = bcf
                oslot = m % 2
                war = war_out[oslot]
                oev = em.o("vector", lambda v, oslot=oslot: v.tensor_mul(
                    out_sb[oslot][:], p_o[0:64, 0:W], bcf_ap()),
                    after=[bcf, acc] + ([war] if war else []), tag="DVE")
                po_free = oev
                ek = em.echo([oev], "scalar", "ACT")
                st = em.o("gpsimd", lambda e, h=h, m=m, oslot=oslot: e.dma_start(
                    out=out0T[h * 64:(h + 1) * 64, m * W:(m + 1) * W],
                    in_=out_sb[oslot][:]),
                    after=[ek], tag="ST", inc=16)
                war_out[oslot] = st



def emit_partial_oproj(nc, em, G, bar, pfx, src_ap, wkey, akey, bkey, cc_dst, bnd):
    W = 512
    PAD = 8 if bnd else 0
    lora = akey is not None
    with ExitStack() as stk:
        en = stk.enter_context

        def alloc(name, shape, dt):
            return nc.alloc_sbuf_tensor(pfx + name, list(shape), dt)

        ot = [alloc(f"ot{i}", [128, 2 * W], BF16) for i in range(2)]
        w_sb = alloc("w", [128, 2 * D], BF16)
        y0 = [alloc(f"y{i}", [128, 8 * W], F32) for i in range(2)]
        p0 = en(nc.psum_tensor(pfx + "p0", [128, W], F32))
        p1 = en(nc.psum_tensor(pfx + "p1", [128, W], F32))
        ld_w = em.o("gpsimd", lambda g: g.dma_start(
            out=w_sb.ap().rearrange("p (c o) -> p c o", c=2),
            in_=G[wkey].rearrange("(c p) o -> p c o", p=128)),
            after=bar, tag="LDG", inc=16)
        wdeps = [ld_w]
        if lora:
            a_sb = alloc("a", [128, 2 * LORA], BF16)
            b_sb = alloc("b", [LORA, D], BF16)
            tl = alloc("tl", [LORA, W], BF16)
            pl = en(nc.psum_tensor(pfx + "pl", [LORA, W], F32))
            ld_a = em.o("gpsimd", lambda g: g.dma_start(
                out=a_sb.ap().rearrange("p (c o) -> p c o", c=2),
                in_=G[akey].rearrange("(c p) o -> p c o", p=128)),
                after=bar, tag="LDG", inc=16)
            ld_b = em.o("gpsimd", lambda g: g.dma_start(
                out=b_sb[:], in_=G[bkey]), after=bar, tag="LDG", inc=16)
            wdeps += [ld_a, ld_b]
        ld_war = {0: None, 1: None}
        st_war = {0: [], 1: []}
        bank_free = {0: None, 1: None}
        tl_prev = None
        for m in range(16):
            sl = m % 2
            buf = ot[sl]
            ld = em.o("sync", lambda e, m=m, buf=buf: e.dma_start(
                out=buf.ap().rearrange("p (c t) -> p c t", c=2),
                in_=src_ap[:, m * W:(m + 1) * W].rearrange("(c p) t -> p c t", p=128)),
                after=(bar if m < 2 else []) +
                ([ld_war[sl]] if ld_war[sl] else []), tag="LD", inc=16)
            tlt = None
            if lora:
                mmt = None
                for fc in range(2):
                    mmt = em.o("tensor", lambda t, fc=fc, buf=buf: t.matmul(
                        pl[:, 0:W], a_sb[:, fc * LORA:(fc + 1) * LORA],
                        buf[:, fc * W:(fc + 1) * W],
                        start=(fc == 0), stop=(fc == 1)),
                        after=([ld, ld_a] + ([tl_prev] if tl_prev else []))
                        if fc == 0 else (), tag="PE")
                tlt = em.o("vector", lambda v: v.tensor_copy(tl[:, 0:W], pl[:, 0:W]),
                           after=[mmt], tag="DVE")
                tl_prev = tlt
            evs = []
            last_fc_mm = None
            for oc in range(8):
                pp = p0 if oc % 2 == 0 else p1
                prev = bank_free[oc % 2]
                mm = None
                for fc in range(2):
                    mm = em.o("tensor", lambda t, fc=fc, oc=oc, buf=buf, pp=pp: t.matmul(
                        pp[:, 0:W], w_sb[:, fc * D + oc * 128: fc * D + (oc + 1) * 128],
                        buf[:, fc * W:(fc + 1) * W],
                        start=(fc == 0), stop=(fc == 1 and not lora)),
                        after=([ld] + wdeps + ([prev] if prev else []))
                        if fc == 0 else (), tag="PE")
                last_fc_mm = mm
                if lora:
                    mm = em.o("tensor", lambda t, oc=oc, pp=pp: t.matmul(
                        pp[:, 0:W], b_sb[:, oc * 128:(oc + 1) * 128], tl[:, 0:W],
                        start=False, stop=True), after=[tlt], tag="PE")
                ev = em.o("vector", lambda v, oc=oc, pp=pp, sl=sl: v.tensor_copy(
                    y0[sl][:, oc * W:(oc + 1) * W], pp[:, 0:W]),
                    after=[mm] + (st_war[sl] if oc == 0 else []), tag="DVE")
                bank_free[oc % 2] = ev
                evs.append(ev)
            ld_war[sl] = last_fc_mm
            ek = em.echo(evs, "scalar", "ACT")
            qq = m // 4
            st = em.o("gpsimd", lambda e, m=m, qq=qq, sl=sl: e.dma_start(
                out=cc_dst[qq * D:(qq + 1) * D,
                           PAD + (m % 4) * W: PAD + (m % 4) * W + W].rearrange(
                    "(c p) t -> p c t", p=128),
                in_=y0[sl].ap().rearrange("p (c t) -> p c t", c=8)),
                after=[ek], tag="ST", inc=16)
            st_war[sl] = [st]
            if bnd and m % 4 == 3:
                q2 = (qq + 1) % 4
                stb = em.o("gpsimd", lambda e, q2=q2, sl=sl: e.dma_start(
                    out=cc_dst[q2 * D:(q2 + 1) * D, 0:1].rearrange(
                        "(c p) t -> p c t", p=128),
                    in_=y0[sl].ap().rearrange("p (c t) -> p c t", c=8)[:, :, W - 1:W],
                    allow_slow_non_contiguous=True),
                    after=[ek], tag="ST", inc=16)
                st_war[sl].append(stb)



def declare_dram(nc):
    G = {}

    def inp(name, shape, dt):
        G[name] = nc.dram_tensor(name, list(shape), dt, kind="ExternalInput").ap()

    def intl(name, shape, dt):
        h = nc.dram_tensor(name, list(shape), dt, kind="Internal")
        G[name + "_h"] = h
        G[name] = h.ap()

    inp("xT", [D, S], F32)
    inp("wq", [D, 256], BF16)
    inp("wk", [D, 256], BF16)
    inp("wv", [D, 256], BF16)
    inp("aq", [D, LORA], BF16)
    inp("av", [D, LORA], BF16)
    inp("bq", [LORA, 256], BF16)
    inp("bv", [LORA, 256], BF16)
    inp("cq", [128, S], F32)
    inp("sq", [128, S], F32)
    inp("phv", [128, 512], F32)
    inp("pho", [64, 512], F32)
    inp("rot", [128, 128], F32)
    inp("mask0", [128, 512], F32)
    inp("idt", [128, 128], BF16)
    inp("ones_col_bf", [128, 1], BF16)
    inp("ones_row_f", [1, 128], F32)
    inp("eps", [1, 1], F32)
    inp("woP", [256, D], BF16)
    inp("aoP", [256, LORA], BF16)
    inp("boF", [LORA, D], BF16)
    inp("owP", [256, D], BF16)
    inp("xqT", [D, TQ + 4], F32)
    inp("wrT", [D, D], BF16)
    inp("wkT", [D, D], BF16)
    inp("wvT", [D, D], BF16)
    inp("mu_r", [128, 8], F32)
    inp("mu_k", [128, 8], F32)
    inp("bmask", [128, 1], F32)
    inp("qw", [D, 256], BF16)
    inp("kw", [D, 256], BF16)
    inp("vw", [D, 256], BF16)
    inp("qb", [128, 2], F32)
    inp("kb", [128, 2], F32)
    inp("vb", [128, 2], F32)
    inp("masks", [128, 4 * 512], F32)
    inp("ob", [128, 8], F32)
    inp("w1", [E, D, HID], BF16)
    inp("b1c", [128, E * 32], F32)
    inp("w2", [E, HID, D], BF16)
    inp("b2c", [128, 32], F32)
    inp("wr", [D, E], F32)
    inp("brc", [4, 1], F32)
    inp("id4", [4, 4], F32)
    inp("id128", [128, 128], F32)
    inp("emat", [4, 512], F32)
    G["x4T"] = nc.dram_tensor("x4T", [D, TQ], F32, kind="ExternalOutput").ap()
    intl("out0d", [256, S], BF16)
    intl("out1d", [256, S], BF16)
    intl("x2scr", [D, TQ], F32)
    intl("x3scr", [D, TQ], F32)
    intl("cc1i", [4 * D, 8 + TQ], F32)
    intl("cc1o", [D, 8 + TQ], F32)
    intl("cc2i", [D, TQ], BF16)
    intl("cc2o", [4 * D, TQ], BF16)
    intl("cc3i", [4 * D, TQ], F32)
    intl("cc3o", [D, TQ], F32)
    return G


def build_fused():
    nc = bass.Bass("TRN2", target_bir_lowering=False)
    G = declare_dram(nc)
    em = Emitter()

    def reset_phase():
        bar = em.phase_barrier()
        nc.sbuf_base = sb0
        nc.psum_base = ps0
        return bar

    def collective(kind, op, in_h, out_h, tag):
        em.o("gpsimd", lambda g: g.collective_compute(
            kind, op, replica_groups=RG,
            ins=[in_h.ap().opt()], outs=[out_h.ap().opt()]),
            after=[em.dma_barrier("ST")], tag=tag, inc=1)
        return (tag, 1, "__none__")

    with ExitStack() as stk:
        en = stk.enter_context
        sems = {s: en(nc.semaphore(s)) for s in ALL_SEMS}
        sb0, ps0 = nc.sbuf_base, nc.psum_base

        emit_phase1(nc, em, G)
        bar = reset_phase()
        emit_partial_oproj(nc, em, G, bar, "b1_", G["out0d"],
                           "woP", "aoP", "boF", G["cc1i"], bnd=True)
        cc0 = collective("ReduceScatter", ALU.add, G["cc1i_h"], G["cc1o_h"], "CC0")
        bar = reset_phase()
        emit_phase2(nc, em, G, bar, cc0)
        cc1 = collective("AllGather", ALU.bypass, G["cc2i_h"], G["cc2o_h"], "CC1")
        bar = reset_phase()
        emit_phase3(nc, em, G, bar, cc1)
        bar = reset_phase()
        emit_partial_oproj(nc, em, G, bar, "b3_", G["out1d"],
                           "owP", None, None, G["cc3i"], bnd=False)
        cc2 = collective("ReduceScatter", ALU.add, G["cc3i_h"], G["cc3o_h"], "CC2")
        bar = reset_phase()
        emit_phase4(nc, em, G, bar, cc2)
        em.o("sync", lambda e: e.nop(), after=[em.dma_barrier("ST")])
        block = en(nc.Block())
        em.build(block, sems)
    return nc


_F_CACHE = {}


def kernel(**inputs):
    g = {k: np.asarray(v) for k, v in inputs.items()}
    if "nc" not in _F_CACHE:
        _F_CACHE["nc"] = build_fused()
    nc = _F_CACHE["nc"]
    bf = ml_dtypes.bfloat16
    x = g["L0_attn_hidden_states"].astype(np.float32)
    wo = np.ascontiguousarray(g["L0_attn_Wo"]).astype(np.float32)
    ao = np.ascontiguousarray(g["L0_attn_Ao"]).astype(np.float32)
    owf = np.ascontiguousarray(g["L1_attn_O_w"].reshape(D, D)).astype(np.float32)
    ob = np.ascontiguousarray(g["L1_attn_O_b"].reshape(8, 128).T).astype(np.float32)
    b1c = np.zeros((128, E * 32), np.float32)
    for e in range(E):
        b1c[:, e * 32:(e + 1) * 32] = g["L1_ffn_b1"][e].reshape(32, 128).T
    b2c = np.zeros((128, 32), np.float32)
    for dc in range(8):
        for e in range(E):
            b2c[:, dc * 4 + e] = g["L1_ffn_b2"][e][dc * 128:(dc + 1) * 128]
    sl_ = np.arange(128)[:, None]
    ql_ = np.arange(512)[None, :]
    masks = np.concatenate(
        [np.where(128 * d + sl_ <= ql_, 0.0, -1e30).astype(np.float32)
         for d in range(4)], axis=1)
    in_common = {
        "boF": np.ascontiguousarray(g["L0_attn_Bo"]).astype(bf),
        "wrT": np.ascontiguousarray(g["L0_ffn_W_r"].T).astype(bf),
        "wkT": np.ascontiguousarray(g["L0_ffn_W_k"].T).astype(bf),
        "wvT": np.ascontiguousarray(g["L0_ffn_W_v"].T).astype(bf),
        "mu_r": np.ascontiguousarray(g["L0_ffn_mu_r"].reshape(8, 128).T).astype(np.float32),
        "mu_k": np.ascontiguousarray(g["L0_ffn_mu_k"].reshape(8, 128).T).astype(np.float32),
        "masks": masks,
        "ob": ob, "b1c": b1c, "b2c": b2c,
        "w1": np.ascontiguousarray(g["L1_ffn_w1"]).astype(bf),
        "w2": np.ascontiguousarray(g["L1_ffn_w2"]).astype(bf),
        "wr": np.ascontiguousarray(g["L1_ffn_wr"]).astype(np.float32),
        "brc": np.ascontiguousarray(g["L1_ffn_br"].reshape(4, 1)).astype(np.float32),
        "id4": CONSTS["ident4_f"], "id128": CONSTS["ident128_f"],
        "emat": CONSTS["emat"],
        "cq": CONSTS["cosP_q"], "sq": CONSTS["ssP_q"],
        "phv": np.ascontiguousarray(CONSTS["phase_h"][:, 0:512]),
        "pho": np.ascontiguousarray(CONSTS["phase_o"][0:64, 0:512]),
        "rot": CONSTS["rot32"], "mask0": CONSTS["mask_l0"],
        "idt": np.eye(128).astype(bf),
        "ones_col_bf": CONSTS["ones_col_bf"],
        "ones_row_f": CONSTS["ones_row_f"], "eps": CONSTS["eps"],
    }
    in_maps = []
    for c in range(NC):
        b, gq = c // 4, c % 4
        t0 = gq * TQ
        hset = list(range(4 * gq, 4 * gq + 4))

        def bcol(bmat):
            v = bmat[hset].reshape(256)
            return np.ascontiguousarray(v.reshape(2, 128).T).astype(np.float32)

        xT = np.ascontiguousarray(x[b].T).astype(np.float32)
        xq = np.ascontiguousarray(x[b].T[:, t0:t0 + TQ]).astype(np.float32)
        if t0 == 0:
            xbnd = np.zeros((D, 4), np.float32)
        else:
            xbnd = np.tile(np.ascontiguousarray(x[b].T[:, t0 - 1:t0]),
                           (1, 4)).astype(np.float32)
        in_maps.append({
            "xT": xT,
            "xqT": np.concatenate([xq, xbnd], axis=1),
            "bmask": np.full((128, 1), 0.0 if gq == 0 else 1.0, np.float32),
            "wq": np.ascontiguousarray(g["L0_attn_Wq"][:, 256 * gq:256 * gq + 256]).astype(bf),
            "wk": np.ascontiguousarray(g["L0_attn_Wk"][:, 256 * gq:256 * gq + 256]).astype(bf),
            "wv": np.ascontiguousarray(g["L0_attn_Wv"][:, 256 * gq:256 * gq + 256]).astype(bf),
            "aq": np.ascontiguousarray(g["L0_attn_Aq"]).astype(bf),
            "av": np.ascontiguousarray(g["L0_attn_Av"]).astype(bf),
            "bq": np.ascontiguousarray(g["L0_attn_Bq"][:, 256 * gq:256 * gq + 256]).astype(bf),
            "bv": np.ascontiguousarray(g["L0_attn_Bv"][:, 256 * gq:256 * gq + 256]).astype(bf),
            "woP": np.ascontiguousarray(wo[256 * gq:256 * (gq + 1)]).astype(bf),
            "aoP": np.ascontiguousarray(ao[256 * gq:256 * (gq + 1)]).astype(bf),
            "owP": np.ascontiguousarray(owf[256 * gq:256 * (gq + 1)]).astype(bf),
            "qw": np.ascontiguousarray(g["L1_attn_Q_w"][hset].transpose(1, 0, 2).reshape(D, 256)).astype(bf),
            "kw": np.ascontiguousarray(g["L1_attn_K_w"][hset].transpose(1, 0, 2).reshape(D, 256)).astype(bf),
            "vw": np.ascontiguousarray(g["L1_attn_V_w"][hset].transpose(1, 0, 2).reshape(D, 256)).astype(bf),
            "qb": bcol(g["L1_attn_Q_b"]), "kb": bcol(g["L1_attn_K_b"]),
            "vb": bcol(g["L1_attn_V_b"]),
            **in_common,
        })
    res = run_bass_kernel_spmd(nc, in_maps, core_ids=list(range(NC)))
    x4 = np.zeros((B, S, D), np.float32)
    for c in range(NC):
        b, gq = c // 4, c % 4
        t0 = gq * TQ
        x4[b, t0:t0 + TQ] = res.results[c]["x4T"].T
    return x4.astype(np.float32)


# version 04
# speedup vs baseline: 1.3346x; 1.1318x over previous
import math
import numpy as np
import ml_dtypes
import concourse.bass as bass
import concourse.mybir as mybir
from concourse.bass_utils import run_bass_kernel_spmd
from contextlib import ExitStack

F32 = mybir.dt.float32
BF16 = mybir.dt.bfloat16
AF = mybir.ActivationFunctionType
ALU = mybir.AluOpType
AX = mybir.AxisListType

B, S, D, NH, HD, KF, LORA = 2, 8192, 1024, 16, 64, 4, 64
NS = S // KF
E, HID = 4, 4096
NC = 8
TQ = S // 4
EPS = 1e-5
ENGINES = ("sync", "tensor", "vector", "scalar", "gpsimd")



def _rope_tables():
    inv = 1.0 / (10000.0 ** (np.arange(0, HD, 2, dtype=np.float64) / HD))
    pos = np.arange(S, dtype=np.float64)
    fr = pos[:, None] * inv[None, :]
    emb = np.concatenate([fr, fr], axis=1)
    cos = np.cos(emb).T.astype(np.float32)
    sin = np.sin(emb).T.astype(np.float32)
    return cos, sin


def _hadamard():
    H4 = np.array([[1, 1, 1, 1], [1, -1, 1, -1], [1, 1, -1, -1], [1, -1, -1, 1]],
                  dtype=np.float32)
    CDMA = np.tile(H4, (1, HD // 4))
    return CDMA


def _phaseT():
    CDMA = _hadamard()
    reps = CDMA.T
    return np.tile(reps, (1, S // 4)).astype(np.float32)


def host_consts():
    c = {}
    cos, sin = _rope_tables()
    phase = _phaseT()
    ss = sin.copy()
    ss[:32, :] *= -1.0
    var_scale = math.sqrt(KF)
    c["cosP_q"] = np.tile(cos * phase, (2, 1))
    c["ssP_q"] = np.tile(ss * phase, (2, 1))
    c["cosP_k"] = np.tile(cos * phase / var_scale, (2, 1))
    c["ssP_k"] = np.tile(ss * phase / var_scale, (2, 1))
    c["phase_h"] = np.tile(phase / var_scale, (2, 1))
    c["phase2"] = (phase[:, 0:1] * 0.0 + phase[:, 0:4])
    c["phase_o"] = np.tile(phase * var_scale, (2, 1))
    P = np.zeros((128, 128), dtype=np.float32)
    for h in range(2):
        for d in range(64):
            P[h * 64 + (d + 32) % 64, h * 64 + d] = 1.0
    c["rot32"] = P
    sl = np.arange(128)[:, None]
    ql = np.arange(512)[None, :]
    c["mask_l0"] = np.where(sl < (ql // 4), 0.0, -1e30).astype(np.float32)
    masks = []
    for d in range(4):
        masks.append(np.where(128 * d + sl <= ql, 0.0, -1e30).astype(np.float32))
    c["mask_l1"] = np.stack(masks)
    c["ident64_bf"] = np.eye(64).astype(ml_dtypes.bfloat16)
    c["ident128_f"] = np.eye(128).astype(np.float32)
    c["ident4_f"] = np.eye(4).astype(np.float32)
    c["ones_col_bf"] = np.ones((128, 1), dtype=ml_dtypes.bfloat16)
    c["ones_row_f"] = np.ones((1, 128), dtype=np.float32)
    c["eps"] = np.full((1, 1), EPS, dtype=np.float32)
    em = np.zeros((4, 512), dtype=np.float32)
    for e in range(4):
        em[e, 128 * e:128 * (e + 1)] = 1.0
    c["emat"] = em
    return c


CONSTS = host_consts()


ECHO_SEM = {"tensor": "PE_E", "vector": "DVE_E", "scalar": "ACT_E", "gpsimd": "GP_E"}
DMA_TAGS = {"LD": 4, "LDG": 4, "ST": 4}
CC_SEMS = ["CC0", "CC1", "CC2"]
ALL_SEMS = (["PE", "ACT", "DVE", "GP"] + list(ECHO_SEM.values())
            + [f"{t}{i}" for t, n in DMA_TAGS.items() for i in range(n)]
            + CC_SEMS)
RG = [[0, 1, 2, 3], [4, 5, 6, 7]]


class Emitter:

    def __init__(self):
        self.ops = {e: [] for e in ENGINES}
        self.cnt = {}
        self.hwm = {e: {} for e in ENGINES}
        self.echoed = {e: (0, 0) for e in ENGINES}
        self.dma_idx = {t: 0 for t in DMA_TAGS}
        self.dma_cnt = {}

    def _emit_wait(self, eng, sem, val):
        if self.hwm[eng].get(sem, 0) < val:
            self.ops[eng].append(("w", sem, val))
            self.hwm[eng][sem] = val

    def _wait(self, eng, tok):
        if tok is None:
            return
        sem, val, peng = tok
        if sem in DMA_TAGS:
            for i in range(DMA_TAGS[sem]):
                ps = f"{sem}{i}"
                c = self.dma_cnt.get(ps, 0)
                if c:
                    self._emit_wait(eng, ps, c)
            return
        if peng == eng or peng in ("__none__", "sync", "tensor"):
            self._emit_wait(eng, sem, val)
            return
        cov, esem_val = self.echoed[peng]
        if cov < val:
            esem = ECHO_SEM[peng]
            self._emit_wait(peng, sem, val)
            self.cnt[esem] = self.cnt.get(esem, 0) + 1
            self.ops[peng].append(("o", lambda e: e.nop(), (esem, self.cnt[esem], peng), 1))
            self.echoed[peng] = (val, self.cnt[esem])
            esem_val = self.cnt[esem]
        self._emit_wait(eng, ECHO_SEM[peng], esem_val)

    def o(self, eng, fn, after=(), tag=None, inc=1):
        for tok in after:
            self._wait(eng, tok)
        tokout = None
        if tag is not None and tag in DMA_TAGS:
            i = self.dma_idx[tag] % DMA_TAGS[tag]
            self.dma_idx[tag] += 1
            ps = f"{tag}{i}"
            prev = self.dma_cnt.get(ps, 0)
            if prev:
                self._emit_wait(eng, ps, prev)
            self.dma_cnt[ps] = prev + 16
            tokout = (tag, self.dma_idx[tag], eng)
            self.ops[eng].append(("o", fn, (ps, self.dma_cnt[ps], eng), 16))
            return tokout
        if tag is not None:
            self.cnt[tag] = self.cnt.get(tag, 0) + inc
            tokout = (tag, self.cnt[tag], eng)
        self.ops[eng].append(("o", fn, tokout, inc))
        return tokout

    def dma_barrier(self, tag):
        return (tag, self.dma_idx.get(tag, 0), "__dma__")

    def engine_tok(self, eng):
        sem = {"tensor": "PE", "vector": "DVE", "scalar": "ACT", "gpsimd": "GP"}[eng]
        v = self.cnt.get(sem, 0)
        return (sem, v, eng) if v else None

    def phase_barrier(self):
        toks = [self.dma_barrier(t) for t in DMA_TAGS]
        toks += [self.engine_tok(e) for e in ("tensor", "vector", "scalar", "gpsimd")]
        return [t for t in toks if t is not None]

    def echo(self, toks, via, tag):
        return self.o(via, lambda e: e.nop(), after=toks, tag=tag)

    def barrier_tok(self, tag):
        if tag not in self.cnt:
            return None
        return (tag, self.cnt[tag], "__none__")

    def build(self, block, sems):
        for e in ENGINES:
            ops = self.ops[e]
            if not ops:
                continue
            dec = getattr(block, e)

            def mk(ops_list):
                def _(eng):
                    for op in ops_list:
                        if op[0] == "w":
                            eng.wait_ge(sems[op[1]], op[2])
                        else:
                            _, fn, tokout, inc = op
                            inst = fn(eng)
                            if tokout is not None:
                                inst.then_inc(sems[tokout[0]], inc)
                return _
            dec(mk(ops))


def emit_ln_stats(em, sb, ps, xsrc_fn, W, deps, tagpfx, ln_scr, raw_mean=False):
    xbf = ln_scr["xbf"]
    rows = ln_scr["rows"]
    pa, pb = ps["stat_a"], ps["stat_b"]
    casts = []
    for c in range(8):
        xa = xsrc_fn(c)
        casts.append(em.o("vector", lambda v, xa=xa, c=c: v.tensor_copy(xbf[:, c * W:(c + 1) * W], xa),
                          after=deps, tag="DVE"))
    mean_mms = []
    for c in range(8):
        mean_mms.append(em.o("tensor", lambda t, c=c: t.matmul(
            pa[0:1, 0:W], sb["ones_col_bf"][:], xbf[:, c * W:(c + 1) * W],
            start=(c == 0), stop=(c == 7)), after=[casts[c]], tag="PE"))
    mm_s = None
    sqs = []
    for c in range(8):
        xa = xsrc_fn(c)
        sq = em.o("scalar", lambda s, xa=xa, c=c: s.activation(
            xbf[:, c * W:(c + 1) * W], xa, AF.Square),
            after=[mean_mms[c]] + list(deps), tag="ACT")
        sqs.append(sq)
    for c in range(8):
        mm_s = em.o("tensor", lambda t, c=c: t.matmul(
            pb[0:1, 0:W], sb["ones_col_bf"][:], xbf[:, c * W:(c + 1) * W],
            start=(c == 0), stop=(c == 7)), after=[sqs[c]], tag="PE")
    mm_m = mean_mms[-1]
    m_row, ms_row, t_row, rstd_row = rows
    d1 = em.o("vector", lambda v: v.tensor_scalar(m_row[0:1, 0:W], pa[0:1, 0:W], 1.0 / D, None, ALU.mult),
              after=[mm_m], tag="DVE")
    d2 = em.o("vector", lambda v: v.tensor_scalar(ms_row[0:1, 0:W], pb[0:1, 0:W], 1.0 / D, None, ALU.mult),
              after=[mm_s], tag="DVE")
    d3 = em.o("vector", lambda v: v.tensor_mul(t_row[0:1, 0:W], m_row[0:1, 0:W], m_row[0:1, 0:W]),
              after=[d1], tag="DVE")
    d4 = em.o("vector", lambda v: v.tensor_sub(ms_row[0:1, 0:W], ms_row[0:1, 0:W], t_row[0:1, 0:W]),
              after=[d2, d3], tag="DVE")
    a1 = em.o("scalar", lambda s: s.activation(ms_row[0:1, 0:W], ms_row[0:1, 0:W], AF.Ln,
                                               bias=sb["eps"][0:1, 0:1]),
              after=[d4], tag="ACT")
    a2 = em.o("scalar", lambda s: s.activation(rstd_row[0:1, 0:W], ms_row[0:1, 0:W], AF.Exp, scale=-0.5),
              after=[a1], tag="ACT")
    if raw_mean:
        bc_src, bc_deps = m_row, [d1, d2]
    else:
        d5 = em.o("vector", lambda v: v.tensor_mul(t_row[0:1, 0:W], m_row[0:1, 0:W], rstd_row[0:1, 0:W]),
                  after=[a2], tag="DVE")
        bc_src, bc_deps = t_row, [d5, d2]
    bm_r = em.o("tensor", lambda t: t.matmul(pa[:, 0:W], sb["ones_row_f"][:], rstd_row[0:1, 0:W],
                                             start=True, stop=True),
                after=[a2, d1], tag="PE")
    bm_mr = em.o("tensor", lambda t: t.matmul(pb[:, 0:W], sb["ones_row_f"][:], bc_src[0:1, 0:W],
                                              start=True, stop=True),
                 after=bc_deps, tag="PE")
    bc_r, bc_mr = ln_scr["bc_r"], ln_scr["bc_mr"]
    e1 = em.o("vector", lambda v: v.tensor_copy(bc_r[:, 0:W], pa[:, 0:W]), after=[bm_r], tag="DVE")
    e2 = em.o("vector", lambda v: v.tensor_copy(bc_mr[:, 0:W], pb[:, 0:W]), after=[bm_mr], tag="DVE")
    return (e1, e2), bc_r, bc_mr



def emit_phase2(nc, em, G, bar, cc_tok):
    T = TQ + 4
    xT = G["xqT"]
    wrT, wkT, wvT = G["wrT"], G["wkT"], G["wvT"]
    mu_r, mu_k = G["mu_r"], G["mu_k"]
    ones_col_bf, ones_row_f, eps = G["ones_col_bf"], G["ones_row_f"], G["eps"]
    cc1o, bmask = G["cc1o"], G["bmask"]
    x2T, h3T = G["x2scr"], G["cc2i"]
    with ExitStack() as stk:
        en = stk.enter_context

        def sbuf(name, shape, dt):
            return nc.alloc_sbuf_tensor("m_" + name, list(shape), dt)

        xb = sbuf("xb", [128, 8 * T], F32)
        h2 = sbuf("h2", [128, 8 * 516], BF16)
        wr_sb = sbuf("wr_sb", [128, 8 * D], BF16)
        wk_sb = sbuf("wk_sb", [128, 8 * D], BF16)
        wv_sb = sbuf("wv_sb", [128, 8 * D], BF16)
        mur_sb = sbuf("mur_sb", [128, 8], F32)
        muk_sb = sbuf("muk_sb", [128, 8], F32)
        omr_sb = sbuf("omr_sb", [128, 8], F32)
        omk_sb = sbuf("omk_sb", [128, 8], F32)
        bmask_sb = sbuf("bmask_sb", [128, 1], F32)
        W = 512
        y0t = sbuf("y0t", [128, 8 * W], F32)
        y0b = sbuf("y0b", [128, 8 * 4], F32)
        xrb = sbuf("xrb", [128, 8 * W], BF16)
        xkb = sbuf("xkb", [128, 8 * W], BF16)
        rfb = sbuf("rfb", [128, 8 * W], BF16)
        kact = sbuf("kact", [128, 8 * W], BF16)
        krelu = sbuf("krelu", [128, 2 * W], F32)
        tmix = krelu
        tmp = sbuf("tmp", [128, W], F32)

        ln_scr = {
            "xbf": sbuf("ln_xbf", [128, 8 * W], BF16),
            "rows": [sbuf(f"ln_r{i}", [1, W], F32) for i in range(4)],
            "bc_r": sbuf("ln_bcr", [128, W], F32),
            "bc_mr": sbuf("ln_bcmr", [128, W], F32),
        }
        h3t = ln_scr["xbf"]
        consts = {
            "ones_col_bf": sbuf("c_ones_col", [128, 1], BF16),
            "ones_row_f": sbuf("c_ones_row", [1, 128], F32),
            "eps": sbuf("c_eps", [1, 1], F32),
        }
        p0 = en(nc.psum_tensor("m_p0", [128, W], F32))
        p1 = en(nc.psum_tensor("m_p1", [128, W], F32))
        pa = en(nc.psum_tensor("m_pa", [128, W], F32))
        pb = en(nc.psum_tensor("m_pb", [128, W], F32))
        ps = {"stat_a": pa, "stat_b": pb}

        def dma_in(dst, src, q="sync", slow=False, after=()):
            return em.o(q, lambda e, dst=dst, src=src: e.dma_start(
                out=dst, in_=src, allow_slow_non_contiguous=slow),
                after=after, tag="LD" if q == "sync" else "LDG", inc=16)

        ld_x = dma_in(xb.ap().rearrange("p (c t) -> p c t", c=8),
                      xT.rearrange("(c p) t -> p c t", p=128), after=bar)
        ld_wr = dma_in(wr_sb.ap().rearrange("p (c o) -> p c o", c=8),
                       wrT.rearrange("(c p) o -> p c o", p=128), q="gpsimd", after=bar)
        ld_wk = dma_in(wk_sb.ap().rearrange("p (c o) -> p c o", c=8),
                       wkT.rearrange("(c p) o -> p c o", p=128), q="gpsimd", after=bar)
        ld_wv = dma_in(wv_sb.ap().rearrange("p (c o) -> p c o", c=8),
                       wvT.rearrange("(c p) o -> p c o", p=128), q="gpsimd", after=bar)
        ld_mur = dma_in(mur_sb[:], mu_r, after=bar)
        ld_muk = dma_in(muk_sb[:], mu_k, after=bar)
        ld_bm = dma_in(bmask_sb[:], bmask, after=bar)
        ld_oc = dma_in(consts["ones_col_bf"][:], ones_col_bf, after=bar)
        ld_or = dma_in(consts["ones_row_f"][:], ones_row_f, after=bar)
        ld_eps = dma_in(consts["eps"][:], eps, after=bar)
        ld_y0b = dma_in(y0b.ap().rearrange("p (c j) -> p c j", j=4)[:, :, 0:1],
                        cc1o[:, 0:1].rearrange("(c p) t -> p c t", p=128),
                        slow=True, after=bar + [cc_tok])
        yb_toks = [ld_y0b]
        for j in range(1, 4):
            yb_toks.append(em.o("vector", lambda v, j=j: v.tensor_copy(
                y0b[:, j::4], y0b[:, 0::4]), after=[ld_y0b] + bar, tag="DVE"))
        yb_mask = em.o("vector", lambda v: v.tensor_scalar(
            y0b[:], y0b[:], bmask_sb[:, 0:1], None, ALU.mult),
            after=yb_toks + [ld_bm], tag="DVE")

        om1 = em.o("vector", lambda v: v.tensor_scalar(omr_sb[:], mur_sb[:], -1.0, 1.0, ALU.mult, ALU.add),
                   after=[ld_mur], tag="DVE")
        om2 = em.o("vector", lambda v: v.tensor_scalar(omk_sb[:], muk_sb[:], -1.0, 1.0, ALU.mult, ALU.add),
                   after=[ld_muk], tag="DVE")

        bank_free = {0: None, 1: None}

        def proj_tile(dst_write, rhs_fn, w_sb, Wd, t0, lora=None, deps=(),
                      extra_evac_dep=()):
            toks = []
            tl = None
            if lora is not None:
                a_sb, b_sb, plp = lora
                for c in range(8):
                    tl = em.o("tensor", lambda t, c=c: t.matmul(
                        plp[:, 0:Wd], a_sb[:, c * LORA:(c + 1) * LORA], rhs_fn(c),
                        start=(c == 0), stop=(c == 7)), after=deps, tag="PE")
                tl = em.o("vector", lambda v: v.tensor_copy(tlora[:, 0:Wd], plp[:, 0:Wd]),
                          after=[tl], tag="DVE")
            for oc in range(8):
                pp = p0 if oc % 2 == 0 else p1
                mm = None
                prev = bank_free[oc % 2]
                dep0 = list(deps) + ([prev] if prev else [])
                for c in range(8):
                    mm = em.o("tensor", lambda t, c=c, oc=oc, pp=pp: t.matmul(
                        pp[:, 0:Wd], w_sb[:, c * D + oc * 128: c * D + (oc + 1) * 128],
                        rhs_fn(c), start=(c == 0), stop=(c == 7 and lora is None)),
                        after=dep0 if c == 0 else (), tag="PE")
                if lora is not None:
                    mm = em.o("tensor", lambda t, oc=oc, pp=pp: t.matmul(
                        pp[:, 0:Wd], bo_sb[:, oc * 128:(oc + 1) * 128], tlora[:, 0:Wd],
                        start=False, stop=True), after=[tl], tag="PE")
                ev = dst_write(oc, pp, [mm] + list(extra_evac_dep))
                bank_free[oc % 2] = ev
                toks.append(ev)
            return toks

        def x_chunk(c, off, Wd):
            return xb[:, c * T + off: c * T + off + Wd]

        def h2_chunk(c, off, Wd):
            return h2[:, c * 516 + off: c * 516 + off + Wd]

        def do_tile(t0, Wd, ysrc_chunk, deps_extra=(), h2_off=4):
            deps = [ld_x] + list(deps_extra)
            y_toks = []
            prev = None
            for oc in range(8):
                tk = em.o("vector", lambda v, oc=oc: v.tensor_add(
                    x_chunk(oc, t0, Wd), ysrc_chunk(oc), x_chunk(oc, t0, Wd)),
                    after=deps + ([prev] if prev else []), tag="DVE")
                prev = tk
                y_toks.append(tk)
            (l1, l2), bc_r, bc_mr = emit_ln_stats(
                em, consts, ps, lambda c: x_chunk(c, t0, Wd), Wd,
                y_toks + [ld_oc, ld_or, ld_eps], "", ln_scr)
            h_toks = []
            prev_hh = None
            for c in range(8):
                hm = em.o("vector", lambda v, c=c: v.tensor_mul(
                    tmp[:, 0:Wd], x_chunk(c, t0, Wd), bc_r[:, 0:Wd]),
                    after=[l1] + ([prev_hh] if prev_hh else []), tag="DVE")
                hh = em.o("vector", lambda v, c=c: v.tensor_sub(
                    h2_chunk(c, h2_off, Wd), tmp[:, 0:Wd], bc_mr[:, 0:Wd]),
                    after=[l2, hm], tag="DVE")
                prev_hh = hh
                h_toks.append(hh)
            return h_toks, y_toks

        bt, _ = do_tile(TQ, 4, lambda c: y0b[:, c * 4:(c + 1) * 4],
                        deps_extra=[yb_mask], h2_off=0)

        ffn_weight_deps = [ld_wr, ld_wk, ld_wv, om1, om2]
        prev_ffn_reads = []
        prev_h3_out = []
        prev_y_toks = []
        for tt in range(4):
            t0 = tt * W
            ldt = em.o("sync", lambda e, t0=t0: e.dma_start(
                out=y0t.ap().rearrange("p (c t) -> p c t", c=8),
                in_=cc1o[:, 8 + t0:8 + t0 + W].rearrange("(c p) t -> p c t", p=128)),
                after=prev_y_toks + [cc_tok] + bar, tag="LD", inc=16)
            h_toks, prev_y_toks = do_tile(t0, W, lambda c: y0t[:, c * W:(c + 1) * W],
                                          deps_extra=[ldt] + bt + prev_h3_out)
            mix_toks = []
            for c in range(8):
                for (mu_s, om_s, dstb) in ((mur_sb, omr_sb, xrb), (muk_sb, omk_sb, xkb)):
                    m1 = em.o("vector", lambda v, c=c, om_s=om_s: v.tensor_scalar(
                        tmix[:, 0:W], h2_chunk(c, 3, W), om_s[:, c:c + 1], None, ALU.mult),
                        after=h_toks + (bt if tt == 0 else []) + prev_ffn_reads + prev_h3_out
                        + (mix_toks[-1:] if mix_toks else []), tag="DVE")
                    m2 = em.o("vector", lambda v, c=c, mu_s=mu_s, dstb=dstb: v.scalar_tensor_tensor(
                        dstb[:, c * W:(c + 1) * W], h2_chunk(c, 4, W), mu_s[:, c:c + 1],
                        tmix[:, 0:W], ALU.mult, ALU.add), after=[m1], tag="DVE")
                    mix_toks.append(m2)
            for c in range(8):
                em.o("vector", lambda v, c=c: v.tensor_copy(h2_chunk(c, 3, 1), h2_chunk(c, 3 + W, 1)),
                     after=mix_toks, tag="DVE")

            def evac_r(oc, pp, after):
                return em.o("scalar", lambda s, oc=oc, pp=pp: s.activation(
                    rfb[:, oc * W:(oc + 1) * W], pp[:, 0:W], AF.Sigmoid),
                    after=after, tag="ACT")

            r_toks = proj_tile(evac_r, lambda c: xrb[:, c * W:(c + 1) * W],
                               wr_sb, W, t0, deps=mix_toks + ffn_weight_deps)

            krelu_last = {0: None, 1: None}

            def evac_k(oc, pp, after):
                prev_sq = krelu_last.get(oc % 2)
                t1 = em.o("vector", lambda v, oc=oc, pp=pp: v.tensor_scalar(
                    krelu[:, (oc % 2) * W:(oc % 2 + 1) * W], pp[:, 0:W], 0.0, None, ALU.max),
                    after=list(after) + ([prev_sq] if prev_sq else []), tag="DVE")
                sq = em.o("scalar", lambda s, oc=oc: s.activation(
                    kact[:, oc * W:(oc + 1) * W], krelu[:, (oc % 2) * W:(oc % 2 + 1) * W], AF.Square),
                    after=[t1], tag="ACT")
                krelu_last[oc % 2] = sq
                return sq

            k_toks = proj_tile(evac_k, lambda c: xkb[:, c * W:(c + 1) * W],
                               wk_sb, W, t0, deps=mix_toks + ffn_weight_deps)

            fprev = [None]

            def evac_f(oc, pp, after):
                t1 = em.o("vector", lambda v, oc=oc, pp=pp: v.tensor_mul(
                    tmp[:, 0:W], pp[:, 0:W], rfb[:, oc * W:(oc + 1) * W]),
                    after=list(after) + r_toks + ([fprev[0]] if fprev[0] else []), tag="DVE")
                fadd = em.o("vector", lambda v, oc=oc, t0=t0: v.tensor_add(
                    x_chunk(oc, t0, W), x_chunk(oc, t0, W), tmp[:, 0:W]),
                    after=[t1], tag="DVE")
                fprev[0] = fadd
                return fadd

            f_toks = proj_tile(evac_f, lambda c: kact[:, c * W:(c + 1) * W],
                               wv_sb, W, t0, deps=k_toks)
            prev_ffn_reads = f_toks

            (l1, l2), bc_r, bc_mr = emit_ln_stats(
                em, consts, ps, lambda c: x_chunk(c, t0, W), W, f_toks, "", ln_scr)
            out_toks = []
            prev_hh = None
            for c in range(8):
                hm = em.o("vector", lambda v, c=c, t0=t0: v.tensor_mul(
                    tmp[:, 0:W], x_chunk(c, t0, W), bc_r[:, 0:W]),
                    after=[l1] + prev_h3_out + ([prev_hh] if prev_hh else []), tag="DVE")
                hh = em.o("vector", lambda v, c=c: v.tensor_sub(
                    h3t[:, c * W:(c + 1) * W], tmp[:, 0:W], bc_mr[:, 0:W]),
                    after=[l2, hm], tag="DVE")
                prev_hh = hh
                out_toks.append(hh)
            ek1 = em.echo(out_toks, "scalar", "ACT")
            st1 = em.o("gpsimd", lambda e, t0=t0: e.dma_start(
                out=h3T[:, t0:t0 + W].rearrange("(c p) t -> p c t", p=128),
                in_=h3t.ap().rearrange("p (c t) -> p c t", c=8)),
                after=[ek1], tag="ST", inc=16)
            ek2 = em.echo(f_toks, "scalar", "ACT")
            st2 = em.o("gpsimd", lambda e, t0=t0: e.dma_start(
                out=x2T[:, t0:t0 + W].rearrange("(c p) t -> p c t", p=128),
                in_=xb.ap().rearrange("p (c t) -> p c t", c=8)[:, :, t0:t0 + W]),
                after=[ek2], tag="ST", inc=16)
            prev_h3_out = [st1]



def emit_phase4(nc, em, G, bar, cc_tok):
    T = TQ
    W = 512
    BIG = 30000.0
    x2T = G["x2scr"]
    cc3o = G["cc3o"]
    ob, w1, b1c, w2, b2c = G["ob"], G["w1"], G["b1c"], G["w2"], G["b2c"]
    wr, brc, id4, id128, emat = G["wr"], G["brc"], G["id4"], G["id128"], G["emat"]
    ones_col_bf, ones_row_f, eps = G["ones_col_bf"], G["ones_row_f"], G["eps"]
    x4T = G["x4T"]
    x3scr = G["x3scr"]

    with ExitStack() as stk:
        en = stk.enter_context

        def alloc(name, shape, dt):
            return nc.alloc_sbuf_tensor("f_" + name, list(shape), dt)

        xf = alloc("xf", [128, 8 * T], BF16)
        gT = alloc("gT", [4, T], F32)
        b1_sb = alloc("b1_sb", [128, E * 32], F32)
        b2_sb = alloc("b2_sb", [128, 32], F32)
        gel_sb = alloc("gel_sb", [128, E * 32], BF16)
        c_sb = alloc("c_sb", [128, 32], F32)
        cT_sb = alloc("cT_sb", [4, 8 * 128], F32)
        csum_sb = alloc("csum_sb", [128, 8], F32)
        rs_sb = alloc("rs_sb", [128, 16], F32)
        id4_sb = alloc("id4_sb", [4, 4], F32)
        id128_sb = alloc("id128_sb", [128, 128], F32)
        emat_sb = alloc("emat_sb", [4, 512], F32)
        wr_sb = alloc("wr_sb", [128, 8 * 4], F32)
        brc_sb = alloc("brc_sb", [4, 1], F32)
        consts = {
            "ones_col_bf": alloc("c_ones_col", [128, 1], BF16),
            "ones_row_f": alloc("c_ones_row", [1, 128], F32),
            "eps": alloc("c_eps", [1, 1], F32),
        }
        p0 = en(nc.psum_tensor("f_p0", [128, W], F32))
        p1 = en(nc.psum_tensor("f_p1", [128, W], F32))
        pa = en(nc.psum_tensor("f_pa", [128, W], F32))
        pb = en(nc.psum_tensor("f_pb", [128, W], F32))
        pc = en(nc.psum_tensor("f_pc", [128, 16], F32))
        pt = en(nc.psum_tensor("f_pt", [128, 128], F32))
        ps = {"stat_a": pa, "stat_b": pb}

        def dma_in(dst, src, q="sync", slow=False, after=()):
            return em.o(q, lambda e, dst=dst, src=src: e.dma_start(
                out=dst, in_=src, allow_slow_non_contiguous=slow),
                after=after, tag="LD" if q == "sync" else "LDG", inc=16)

        ob_sb = alloc("ob_sb", [128, 8], F32)
        ld_ob = dma_in(ob_sb[:], ob, after=bar)
        ld_wr = dma_in(wr_sb.ap().rearrange("p (c o) -> p c o", c=8),
                       wr.rearrange("(c p) o -> p c o", p=128), after=bar)
        ld_br = dma_in(brc_sb[:], brc, after=bar)
        ld_id4 = dma_in(id4_sb[:], id4, after=bar)
        ld_id128 = dma_in(id128_sb[:], id128, after=bar)
        ld_em = dma_in(emat_sb[:], emat, after=bar)
        ld_b1 = dma_in(b1_sb[:], b1c, after=bar)
        ld_b2 = dma_in(b2_sb[:], b2c, after=bar)
        ld_oc = dma_in(consts["ones_col_bf"][:], ones_col_bf, after=bar)
        ld_or = dma_in(consts["ones_row_f"][:], ones_row_f, after=bar)
        ld_eps = dma_in(consts["eps"][:], eps, after=bar)
        const_deps = [ld_oc, ld_or, ld_eps]
        gel_tok = em.o("scalar", lambda sc: sc.activation(gel_sb[:], b1_sb[:], AF.Gelu),
                       after=[ld_b1], tag="ACT")

        markA = nc.sbuf_base
        x3t = alloc("x3t", [128, 8 * W], F32)
        y1t = alloc("y1t", [128, 8 * W], F32)
        ut = alloc("ut", [128, 8 * W], F32)
        lg_sb = alloc("lg_sb", [4, W], F32)
        lgtm = alloc("lgtm", [128, 16], F32)
        m1b = alloc("m1b", [128, 4], F32)
        dbuf = alloc("dbuf", [128, 16], F32)
        sgb = alloc("sgb", [128, 16], F32)
        q1b = alloc("q1b", [128, 16], F32)
        zb = alloc("zb", [128, 16], F32)
        m2b = alloc("m2b", [128, 4], F32)
        d2b = alloc("d2b", [128, 16], F32)
        nrb = alloc("nrb", [128, 16], F32)
        selb = alloc("selb", [128, 16], F32)
        eb = alloc("eb", [128, 16], F32)
        g0b = alloc("g0b", [128, 16], F32)
        nsb = alloc("nsb", [128, 4], F32)
        rcb = alloc("rcb", [128, 4], F32)
        gtm = alloc("gtm", [128, 16], F32)
        ln_scr = {
            "xbf": alloc("ln_xbf", [128, 8 * W], BF16),
            "rows": [alloc(f"ln_r{i}", [1, W], F32) for i in range(4)],
            "bc_r": alloc("ln_bcr", [128, W], F32),
            "bc_mr": alloc("ln_bcmr", [128, W], F32),
        }
        gate_toks = []
        xf_toks = []
        x3st_toks = []
        prev_tile_toks = []
        bank_free = {}
        for tt in range(4):
            t0 = tt * W
            bar_tile = [t for t in (em.engine_tok("tensor"), em.engine_tok("vector"),
                                    em.engine_tok("scalar")) if t] + \
                ([em.dma_barrier("ST")] if em.dma_idx.get("ST") else []) + \
                (bar if tt == 0 else [])
            ldx = em.o("sync", lambda e, t0=t0: e.dma_start(
                out=x3t.ap().rearrange("p (c t) -> p c t", c=8),
                in_=x2T[:, t0:t0 + W].rearrange("(c p) t -> p c t", p=128)),
                after=bar_tile, tag="LD", inc=16)
            ldo = em.o("sync", lambda e, t0=t0: e.dma_start(
                out=y1t.ap().rearrange("p (c t) -> p c t", c=8),
                in_=cc3o[:, t0:t0 + W].rearrange("(c p) t -> p c t", p=128)),
                after=bar_tile + [cc_tok], tag="LD", inc=16)
            proj_toks = []
            prev_pj = None
            for oc in range(8):
                ev = em.o("vector", lambda v, oc=oc: v.scalar_tensor_tensor(
                    x3t[:, oc * W:(oc + 1) * W], y1t[:, oc * W:(oc + 1) * W],
                    ob_sb[:, oc:oc + 1],
                    x3t[:, oc * W:(oc + 1) * W], ALU.add, ALU.add),
                    after=[ldo, ldx, ld_ob] + prev_tile_toks +
                    ([prev_pj] if prev_pj else []), tag="DVE")
                prev_pj = ev
                proj_toks.append(ev)
            st_x3 = em.o("gpsimd", lambda e, t0=t0: e.dma_start(
                out=x3scr[:, t0:t0 + W].rearrange("(c p) t -> p c t", p=128),
                in_=x3t.ap().rearrange("p (c t) -> p c t", c=8)),
                after=[em.echo(proj_toks, "scalar", "ACT")], tag="ST", inc=16)
            x3st_toks.append(st_x3)
            (l1, l2), bc_r, bc_mr = emit_ln_stats(
                em, consts, ps, lambda c: x3t[:, c * W:(c + 1) * W], W,
                proj_toks + const_deps, "", ln_scr)
            u_toks = []
            for c in range(8):
                um = em.o("vector", lambda v, c=c: v.tensor_mul(
                    ut[:, c * W:(c + 1) * W], x3t[:, c * W:(c + 1) * W], bc_r[:, 0:W]),
                    after=[l1], tag="DVE")
                ux = em.o("vector", lambda v, c=c, t0=t0: v.tensor_sub(
                    xf[:, c * T + t0: c * T + t0 + W], ut[:, c * W:(c + 1) * W], bc_mr[:, 0:W]),
                    after=[l2, um], tag="DVE")
                xf_toks.append(ux)
                uc = em.o("vector", lambda v, c=c, t0=t0: v.tensor_copy(
                    ut[:, c * W:(c + 1) * W], xf[:, c * T + t0: c * T + t0 + W]),
                    after=[ux], tag="DVE")
                u_toks.append(uc)
            mm = None
            for c in range(8):
                mm = em.o("tensor", lambda t, c=c: t.matmul(
                    pa[0:4, 0:W], wr_sb[:, c * 4:(c + 1) * 4], ut[:, c * W:(c + 1) * W],
                    start=(c == 0), stop=(c == 7)),
                    after=[u_toks[c], ld_wr] if c == 0 else [u_toks[c]], tag="PE")
            lg = em.o("vector", lambda v: v.tensor_scalar(
                lg_sb[0:4, 0:W], pa[0:4, 0:W], brc_sb[0:4, 0:1], None, ALU.add),
                after=[mm, ld_br], tag="DVE")
            gs = [slice(g * 4, (g + 1) * 4) for g in range(4)]
            tr_toks = []
            for g in range(4):
                tr = em.o("tensor", lambda t, g=g: t.transpose(
                    pt[0:128, g * 4:(g + 1) * 4], lg_sb[0:4, g * 128:(g + 1) * 128],
                    id4_sb[:]),
                    after=[lg, ld_id4] + (tr_toks[-1:] if tr_toks else []) +
                    (gate_toks[-1:] if gate_toks else []), tag="PE")
                tr_toks.append(em.o("vector", lambda v, g=g: v.tensor_copy(
                    lgtm[:, gs[g]], pt[0:128, g * 4:(g + 1) * 4]),
                    after=[tr], tag="DVE"))
            t_m1 = [em.o("vector", lambda v, g=g: v.reduce_max(
                m1b[:, g:g + 1], lgtm[:, gs[g]], axis=AX.X), after=[tr_toks[g]], tag="DVE")
                for g in range(4)]
            t_d = [em.o("vector", lambda v, g=g: v.tensor_scalar(
                dbuf[:, gs[g]], lgtm[:, gs[g]], m1b[:, g:g + 1], None, ALU.subtract),
                after=[t_m1[g]], tag="DVE") for g in range(4)]
            t_sg = [em.o("scalar", lambda sc, g=g: sc.sign(
                sgb[:, gs[g]], dbuf[:, gs[g]], bias=0.0), after=[t_d[g]], tag="ACT")
                for g in range(4)]
            t_q1 = [em.o("vector", lambda v, g=g: v.tensor_scalar(
                q1b[:, gs[g]], sgb[:, gs[g]], 1.0, -BIG, ALU.add, ALU.mult),
                after=[t_sg[g]], tag="DVE") for g in range(4)]
            t_z = [em.o("vector", lambda v, g=g: v.tensor_add(
                zb[:, gs[g]], dbuf[:, gs[g]], q1b[:, gs[g]]),
                after=[t_q1[g], t_d[g]], tag="DVE") for g in range(4)]
            t_m2 = [em.o("vector", lambda v, g=g: v.reduce_max(
                m2b[:, g:g + 1], zb[:, gs[g]], axis=AX.X), after=[t_z[g]], tag="DVE")
                for g in range(4)]
            t_d2 = [em.o("vector", lambda v, g=g: v.tensor_scalar(
                d2b[:, gs[g]], dbuf[:, gs[g]], m2b[:, g:g + 1], None, ALU.subtract),
                after=[t_m2[g]], tag="DVE") for g in range(4)]
            t_s2 = [em.o("scalar", lambda sc, g=g: sc.sign(
                d2b[:, gs[g]], d2b[:, gs[g]], bias=0.0), after=[t_d2[g]], tag="ACT")
                for g in range(4)]
            t_nr = [em.o("vector", lambda v, g=g: v.tensor_scalar(
                nrb[:, gs[g]], d2b[:, gs[g]], -1.0, 0.0, ALU.mult, ALU.max),
                after=[t_s2[g]], tag="DVE") for g in range(4)]
            t_sel = [em.o("vector", lambda v, g=g: v.tensor_scalar(
                selb[:, gs[g]], nrb[:, gs[g]], -1.0, 1.0, ALU.mult, ALU.add),
                after=[t_nr[g]], tag="DVE") for g in range(4)]
            t_e = [em.o("scalar", lambda sc, g=g: sc.activation(
                eb[:, gs[g]], dbuf[:, gs[g]], AF.Exp), after=[t_d[g]], tag="ACT")
                for g in range(4)]
            t_g0 = [em.o("vector", lambda v, g=g: v.tensor_mul(
                g0b[:, gs[g]], eb[:, gs[g]], selb[:, gs[g]]),
                after=[t_e[g], t_sel[g]], tag="DVE") for g in range(4)]
            t_ns = [em.o("vector", lambda v, g=g: v.reduce_sum(
                nsb[:, g:g + 1], g0b[:, gs[g]], axis=AX.X), after=[t_g0[g]], tag="DVE")
                for g in range(4)]
            t_rc = [em.o("vector", lambda v, g=g: v.reciprocal(
                rcb[:, g:g + 1], nsb[:, g:g + 1]), after=[t_ns[g]], tag="DVE")
                for g in range(4)]
            t_gg = [em.o("vector", lambda v, g=g: v.tensor_scalar(
                gtm[:, gs[g]], g0b[:, gs[g]], rcb[:, g:g + 1], None, ALU.mult),
                after=[t_rc[g]], tag="DVE") for g in range(4)]
            for g in range(4):
                trb = em.o("tensor", lambda t, g=g: t.transpose(
                    pt[0:4, 0:128], gtm[:, gs[g]], id128_sb[:]),
                    after=[t_gg[g], ld_id128, tr_toks[-1]] +
                    (gate_toks[-1:] if gate_toks else []), tag="PE")
                gate_toks.append(em.o("vector", lambda v, g=g, t0=t0: v.tensor_copy(
                    gT[0:4, t0 + g * 128: t0 + (g + 1) * 128], pt[0:4, 0:128]),
                    after=[trb], tag="DVE"))
            prev_tile_toks = proj_toks + [lg]
        barA = em.phase_barrier()
        nc.sbuf_base = markA

        markB = nc.sbuf_base
        w1b = [alloc(f"w1b{i}", [128, 8 * 512], BF16) for i in range(2)]
        w2b = [alloc(f"w2b{i}", [128, 32 * 128], BF16) for i in range(2)]
        h1 = alloc("h1", [128, 32 * 1024], BF16)
        bct = alloc("bct", [128, 2 * W], F32)
        yt = alloc("yt", [128, 2 * W], F32)
        HB = 512
        acc_toks = []
        c_toks = []
        war_yt = {0: None, 1: None}
        ld_prev = {0: None, 1: None}
        ld2_prev = {0: None, 1: None}
        bct_prev = [None]
        for e in range(E):
            for half in range(2):
                h0 = half * 1024
                st1_toks = []
                for hb in range(8):
                    buf = w1b[hb % 2]
                    war = ld_prev[hb % 2]
                    ldw = em.o("gpsimd", lambda g, e=e, hb=hb, buf=buf: g.dma_start(
                        out=buf.ap().rearrange("p (c t) -> p c t", c=8),
                        in_=w1[e, :, hb * HB:(hb + 1) * HB].rearrange("(c p) t -> p c t", p=128)),
                        after=([war] if war else []) + barA, tag="LDG", inc=16)
                    hb_toks = []
                    for hc in range(4):
                        for tg in range(2):
                            slot1 = (hc * 2 + tg) % 2
                            pp = p0 if slot1 == 0 else p1
                            mm = None
                            for c in range(8):
                                mm = em.o("tensor", lambda t, c=c, hc=hc, tg=tg, buf=buf, pp=pp, h0=h0: t.matmul(
                                    pp[:, 0:W], buf[:, c * HB + hc * 128:c * HB + (hc + 1) * 128],
                                    xf[:, c * T + h0 + tg * W: c * T + h0 + (tg + 1) * W],
                                    start=(c == 0), stop=(c == 7)),
                                    after=[ldw] + barA +
                                    ([bank_free[slot1]] if bank_free.get(slot1) else [])
                                    if c == 0 else (), tag="PE")
                            evg = em.o("scalar", lambda sc, hb=hb, hc=hc, tg=tg, pp=pp, e=e: sc.activation(
                                h1[:, (hb * 4 + hc) * 1024 + tg * W: (hb * 4 + hc) * 1024 + (tg + 1) * W],
                                pp[:, 0:W], AF.Gelu,
                                bias=b1_sb[:, e * 32 + hb * 4 + hc: e * 32 + hb * 4 + hc + 1]),
                                after=[mm, ld_b1], tag="ACT")
                            bank_free[slot1] = evg
                            hb_toks.append(evg)
                            st1_toks.append(evg)
                    ld_prev[hb % 2] = em.echo(hb_toks, "scalar", "ACT")
                st1_bar = em.echo(st1_toks, "scalar", "ACT")
                btoks = []
                for tg in range(2):
                    bmm = em.o("tensor", lambda t, e=e, h0=h0, tg=tg: t.matmul(
                        pa[:, 0:W], emat_sb[:, e * 128:(e + 1) * 128],
                        gT[0:4, h0 + tg * W: h0 + (tg + 1) * W],
                        start=True, stop=True),
                        after=[ld_em] + barA + ([bct_prev[0]] if bct_prev[0] else [])
                        + btoks[-1:], tag="PE")
                    btoks.append(em.o("vector", lambda v, tg=tg: v.tensor_copy(
                        bct[:, tg * W:(tg + 1) * W], pa[:, 0:W]),
                        after=[bmm], tag="DVE"))
                bct_prev[0] = em.echo(btoks, "scalar", "ACT")
                for dc in range(8):
                    buf = w2b[dc % 2]
                    war = ld2_prev[dc % 2]
                    ldw2 = em.o("gpsimd", lambda g, e=e, dc=dc, buf=buf: g.dma_start(
                        out=buf.ap().rearrange("p (c t) -> p c t", c=32),
                        in_=w2[e, :, dc * 128:(dc + 1) * 128].rearrange("(c p) t -> p c t", p=128)),
                        after=([war] if war else []) + barA, tag="LDG", inc=16)
                    dc_reads = []
                    for tg in range(2):
                        pp = p0 if tg == 0 else p1
                        mm = None
                        for hc in range(32):
                            mm = em.o("tensor", lambda t, hc=hc, tg=tg, buf=buf, pp=pp: t.matmul(
                                pp[:, 0:W], buf[:, hc * 128:(hc + 1) * 128],
                                h1[:, hc * 1024 + tg * W: hc * 1024 + (tg + 1) * W],
                                start=(hc == 0), stop=(hc == 31)),
                                after=[ldw2, st1_bar] +
                                ([bank_free[tg]] if bank_free.get(tg) else [])
                                if hc == 0 else (), tag="PE")
                        slot = tg
                        tv = em.o("vector", lambda v, pp=pp, slot=slot: v.tensor_mul(
                            yt[:, slot * W:(slot + 1) * W], pp[:, 0:W], bct[:, slot * W:(slot + 1) * W]),
                            after=[mm, bct_prev[0]] + ([war_yt[slot]] if war_yt[slot] else []),
                            tag="DVE")
                        bank_free[slot] = tv
                        tv_e = em.echo([tv], "scalar", "ACT")
                        acc = em.o("gpsimd", lambda g, dc=dc, h0=h0, tg=tg, slot=slot: g.dma_start(
                            out=x3scr[dc * 128:(dc + 1) * 128, h0 + tg * W: h0 + (tg + 1) * W],
                            in_=yt[:, slot * W:(slot + 1) * W], accum_op=ALU.add),
                            after=[tv_e] + x3st_toks, tag="ST", inc=16)
                        acc_toks.append(acc)
                        war_yt[slot] = acc
                        dc_reads.append(tv)
                    if half == 0:
                        mmc = None
                        for hc in range(32):
                            mmc = em.o("tensor", lambda t, hc=hc, dc=dc, e=e, buf=buf: t.matmul(
                                pc[:, 4 + e:4 + e + 1], buf[:, hc * 128:(hc + 1) * 128],
                                gel_sb[:, e * 32 + hc:e * 32 + hc + 1],
                                start=(hc == 0), stop=(hc == 31)),
                                after=[ldw2, gel_tok] + (c_toks[-1:] if c_toks else [])
                                if hc == 0 else (), tag="PE")
                        c_toks.append(em.o("vector", lambda v, dc=dc, e=e: v.tensor_copy(
                            c_sb[:, dc * 4 + e:dc * 4 + e + 1], pc[:, 4 + e:4 + e + 1]),
                            after=[mmc], tag="DVE"))
                        dc_reads.append(c_toks[-1])
                    ld2_prev[dc % 2] = em.echo(dc_reads, "scalar", "ACT")
        barB = em.phase_barrier()
        nc.sbuf_base = markB

        x3c = alloc("x3c", [128, 8 * W], F32)
        cbar = em.echo(c_toks, "vector", "DVE")
        ct_toks = []
        for dc in range(8):
            trc = em.o("tensor", lambda t, dc=dc: t.transpose(
                pt[0:4, 0:128], c_sb[:, dc * 4:(dc + 1) * 4], id128_sb[:]),
                after=[cbar, ld_id128] + barB + (ct_toks[-2:-1] if ct_toks else []) +
                (gate_toks[-1:] if gate_toks else []), tag="PE")
            ct_toks.append(em.o("vector", lambda v, dc=dc: v.tensor_copy(
                cT_sb[0:4, dc * 128:(dc + 1) * 128], pt[0:4, 0:128]),
                after=[trc], tag="DVE"))
            rs1 = em.o("vector", lambda v, dc=dc: v.reduce_sum(
                rs_sb[:, dc:dc + 1], c_sb[:, dc * 4:(dc + 1) * 4], axis=AX.X),
                after=[cbar], tag="DVE")
            rs2 = em.o("vector", lambda v, dc=dc: v.reduce_sum(
                rs_sb[:, 8 + dc:8 + dc + 1], b2_sb[:, dc * 4:(dc + 1) * 4], axis=AX.X),
                after=[ld_b2], tag="DVE")
            ct_toks.append(em.o("vector", lambda v, dc=dc: v.tensor_add(
                csum_sb[:, dc:dc + 1], rs_sb[:, dc:dc + 1], rs_sb[:, 8 + dc:8 + dc + 1]),
                after=[rs1, rs2, rs1], tag="DVE"))
        prev_out = []
        for tt in range(4):
            t0 = tt * W
            ldx3 = em.o("sync", lambda e, t0=t0: e.dma_start(
                out=x3c.ap().rearrange("p (c t) -> p c t", c=8),
                in_=x3scr[:, t0:t0 + W].rearrange("(c p) t -> p c t", p=128)),
                after=barB + prev_out + [em.dma_barrier("ST")], tag="LD", inc=16)
            out_toks = []
            prev_o1 = None
            for dc in range(8):
                mmt = em.o("tensor", lambda t, dc=dc, t0=t0: t.matmul(
                    pb[:, 0:W], cT_sb[0:4, dc * 128:(dc + 1) * 128], gT[0:4, t0:t0 + W],
                    start=True, stop=True),
                    after=ct_toks + [ldx3] + ([prev_o1] if prev_o1 else []), tag="PE")
                o1_ = em.o("vector", lambda v, dc=dc: v.scalar_tensor_tensor(
                    x3c[:, dc * W:(dc + 1) * W], pb[:, 0:W], -1.0,
                    x3c[:, dc * W:(dc + 1) * W], ALU.mult, ALU.add),
                    after=[mmt, ldx3], tag="DVE")
                o2_ = em.o("vector", lambda v, dc=dc: v.tensor_scalar(
                    x3c[:, dc * W:(dc + 1) * W], x3c[:, dc * W:(dc + 1) * W],
                    csum_sb[:, dc:dc + 1], None, ALU.add),
                    after=[o1_], tag="DVE")
                prev_o1 = o1_
                out_toks.append(o2_)
            ek = em.echo(out_toks, "scalar", "ACT")
            sto = em.o("gpsimd", lambda e, t0=t0: e.dma_start(
                out=x4T[:, t0:t0 + W].rearrange("(c p) t -> p c t", p=128),
                in_=x3c.ap().rearrange("p (c t) -> p c t", c=8)),
                after=[ek], tag="ST", inc=16)
            prev_out = [sto]



def emit_phase3(nc, em, G, bar, cc_tok, phases=(1, 2), nt_limit=None):
    W = 512
    NT = S // W
    cc2o = G["cc2o"]
    qw, kw, vw = G["qw"], G["kw"], G["vw"]
    qb, kb, vb = G["qb"], G["kb"], G["vb"]
    masks = G["masks"]
    id64 = G["idt"]
    ones_row_f = G["ones_row_f"]
    out1T = G["out1d"]

    with ExitStack() as stk:
        en = stk.enter_context

        def alloc(name, shape, dt):
            return nc.alloc_sbuf_tensor("a_" + name, list(shape), dt)

        q_sb = alloc("q_sb", [128, 2 * S], BF16)
        k_sb = alloc("k_sb", [128, 2 * S], BF16)
        v_tm = alloc("v_tm", [128, 64 * 4 * 65], BF16)
        ht = [alloc(f"ht{i}", [128, 8 * W], BF16) for i in range(2)]
        vtmp = alloc("vtmp", [128, 2 * W], BF16)
        qw_sb = alloc("qw_sb", [128, 8 * 256], BF16)
        kw_sb = alloc("kw_sb", [128, 8 * 256], BF16)
        vw_sb = alloc("vw_sb", [128, 8 * 256], BF16)
        qb_sb = alloc("qb_sb", [128, 2], F32)
        kb_sb = alloc("kb_sb", [128, 2], F32)
        vb_sb = alloc("vb_sb", [128, 2], F32)
        mask_sb = alloc("mask_sb", [128, 4 * W], F32)
        id_sb = alloc("id_sb", [128, 128], BF16)
        or_sb = alloc("or_sb", [1, 128], F32)
        w_sb = [alloc(f"w_sb{i}", [128, W], BF16) for i in range(4)]
        row_sb = alloc("row_sb", [1, W], F32)
        rc_sb = alloc("rc_sb", [1, W], F32)
        bc_sb = alloc("bc_sb", [64, W], F32)
        out_sb = [alloc(f"out_sb{i}", [64, W], BF16) for i in range(2)]
        p_s = [en(nc.psum_tensor(f"a_p_s{i}", [128, W], F32)) for i in range(4)]
        p_o = en(nc.psum_tensor("a_p_o", [128, W], F32))
        p_b = en(nc.psum_tensor("a_p_b", [64, W], F32))
        pt_bf2 = [en(nc.psum_tensor(f"a_pt_bf{i}", [128, 256], BF16)) for i in range(2)]

        def dma_in(dst, src, q="sync", after=()):
            return em.o(q, lambda e, dst=dst, src=src: e.dma_start(out=dst, in_=src),
                        after=after, tag="LD" if q == "sync" else "LDG", inc=16)

        ld_qw = dma_in(qw_sb.ap().rearrange("p (c o) -> p c o", c=8),
                       qw.rearrange("(c p) o -> p c o", p=128), q="gpsimd", after=bar)
        ld_kw = dma_in(kw_sb.ap().rearrange("p (c o) -> p c o", c=8),
                       kw.rearrange("(c p) o -> p c o", p=128), q="gpsimd", after=bar)
        ld_vw = dma_in(vw_sb.ap().rearrange("p (c o) -> p c o", c=8),
                       vw.rearrange("(c p) o -> p c o", p=128), q="gpsimd", after=bar)
        ld_qb = dma_in(qb_sb[:], qb, after=bar)
        ld_kb = dma_in(kb_sb[:], kb, after=bar)
        ld_vb = dma_in(vb_sb[:], vb, after=bar)
        ld_m = dma_in(mask_sb[:], masks, after=bar)
        ld_id = dma_in(id_sb[:], id64, after=bar)
        ld_or = dma_in(or_sb[:], ones_row_f, after=bar)

        ones_tok = em.o("vector", lambda v: v.memset(v_tm[:], 1.0), after=bar,
                        tag="DVE") \
            if 1 in phases else None

        qkv_toks = []
        ht_war = {0: None, 1: None}
        pbank_free = {0: None, 1: None}
        ptbf_free = {0: None, 1: None}
        for tt in range(nt_limit if nt_limit else NT):
            hb = ht[tt % 2]
            war = ht_war[tt % 2]
            ldh = em.o("sync", lambda e, tt=tt, hb=hb: e.dma_start(
                out=hb.ap().rearrange("p (c t) -> p c t", c=8),
                in_=cc2o[(tt // 4) * D:((tt // 4) + 1) * D,
                         (tt % 4) * W:((tt % 4) + 1) * W].rearrange(
                    "(c p) t -> p c t", p=128)),
                after=([war] if war else []) + [cc_tok] + bar, tag="LD", inc=16)
            tile_reads = []
            for (wsb, bsb, ldwt, ldbt, dst) in (
                    (qw_sb, qb_sb, ld_qw, ld_qb, q_sb),
                    (kw_sb, kb_sb, ld_kw, ld_kb, k_sb),
                    (vw_sb, vb_sb, ld_vw, ld_vb, None)):
                for oc in range(2):
                    pp = p_s[oc]
                    mm = None
                    for c in range(8):
                        mm = em.o("tensor", lambda t, c=c, oc=oc, wsb=wsb, hb=hb, pp=pp: t.matmul(
                            pp[:, 0:W], wsb[:, c * 256 + oc * 128: c * 256 + (oc + 1) * 128],
                            hb[:, c * W:(c + 1) * W], start=(c == 0), stop=(c == 7)),
                            after=[ldwt, ldh] +
                            ([pbank_free[oc]] if pbank_free.get(oc) else [])
                            if c == 0 else (), tag="PE")
                    if dst is not None:
                        ev = em.o("scalar", lambda sc, oc=oc, tt=tt, pp=pp, dst=dst, bsb=bsb: sc.activation(
                            dst[:, oc * S + tt * W: oc * S + (tt + 1) * W], pp[:, 0:W],
                            AF.Identity, bias=bsb[:, oc:oc + 1]),
                            after=[mm, ldbt], tag="ACT")
                    else:
                        ev = em.o("scalar", lambda sc, oc=oc, pp=pp, bsb=bsb: sc.activation(
                            vtmp[:, oc * W:(oc + 1) * W], pp[:, 0:W],
                            AF.Identity, bias=bsb[:, oc:oc + 1]),
                            after=[mm, ldbt] + [t for t in ptbf_free.values() if t],
                            tag="ACT")
                    pbank_free[oc] = ev
                    tile_reads.append(ev)
                    qkv_toks.append(ev)
            tr_evs = []
            for sc_i in (range(4) if 1 in phases else ()):
                pbank = pt_bf2[sc_i % 2]
                war_b = ptbf_free.get(sc_i % 2)
                trvs = []
                for oc in range(2):
                    if "notr" in phases:
                        trv = em.o("tensor", lambda t: t.nop(),
                                   after=[tile_reads[-1], tile_reads[-2], ld_id] +
                                   ([war_b] if war_b else []), tag="PE")
                    else:
                        trv = em.o("tensor", lambda t, oc=oc, sc_i=sc_i, pbank=pbank: t.transpose(
                            pbank[:, oc * 128:(oc + 1) * 128],
                            vtmp[:, oc * W + sc_i * 128: oc * W + (sc_i + 1) * 128],
                            id_sb[:]),
                            after=[tile_reads[-1], tile_reads[-2], ld_id] +
                            ([war_b] if war_b else []), tag="PE")
                    trvs.append(trv)
                bi = (tt * 4 + sc_i) * 4
                if "nocp" in phases:
                    cp = em.o("vector", lambda v, pbank=pbank: v.tensor_copy(
                        v_tm[:, 0:256], pbank[:, 0:256]),
                        after=trvs + [ones_tok], tag="DVE")
                else:
                    cp = em.o("vector", lambda v, bi=bi, pbank=pbank: v.tensor_copy(
                        v_tm.ap().rearrange("p (b c) -> p b c", c=65)[:, bi:bi + 4, 0:64],
                        pbank.ap().rearrange("p (h c) -> p h c", c=64)),
                        after=trvs + [ones_tok], tag="DVE")
                tr_evs.append(cp)
                ptbf_free[sc_i % 2] = em.echo([cp], "scalar", "ACT")
            qkv_toks.extend(tr_evs)
            ht_war[tt % 2] = em.echo(tile_reads, "vector", "DVE")
            if 1 not in phases:
                qkv_toks.extend(tile_reads)
        ph1 = em.phase_barrier()

        DELAY = 2
        war_out = {0: None, 1: None}
        w_free = {i: None for i in range(4)}
        po_free = None
        pb_free = None
        for h in range(4):
            qrow = (h % 2) * 64
            qcolb = (h // 2) * S
            for m in range(NT):
                qT = q_sb[qrow:qrow + 64, qcolb + m * W: qcolb + (m + 1) * W]
                nj = 4 * m + 4
                exp_toks = {}
                pv_prev = None
                for jx in range(nj + DELAY):
                    if jx < nj:
                        j = jx
                        sl = j % 4
                        dep = list(ph1) if (h == 0 and m == 0 and j == 0) else []
                        prev_exp = exp_toks.get(j - 4)
                        qk = em.o("tensor", lambda t, j=j, sl=sl, qT=qT, qrow=qrow, qcolb=qcolb: t.matmul(
                            p_s[sl][:, 0:W],
                            k_sb[qrow:qrow + 64, qcolb + j * 128: qcolb + j * 128 + 128],
                            qT, start=True, stop=True),
                            after=dep + ([prev_exp] if prev_exp else []), tag="PE")
                        d = j - 4 * m
                        if d >= 0:
                            mk = em.o("vector", lambda v, sl=sl, d=d: v.tensor_add(
                                p_s[sl][:, 0:W], p_s[sl][:, 0:W], mask_sb[:, d * W:(d + 1) * W]),
                                after=[qk, ld_m], tag="DVE")
                            edep = [mk]
                        else:
                            edep = [qk]
                        wfree = w_free.get(sl)
                        ex = em.o("scalar", lambda sc, sl=sl: sc.activation(
                            w_sb[sl][:, 0:W], p_s[sl][:, 0:W], AF.Exp, scale=0.125),
                            after=edep + ([wfree] if wfree else []), tag="ACT")
                        exp_toks[j] = ex
                    jj = jx - DELAY
                    if jj >= 0:
                        sl2 = jj % 4
                        blk = (jj * 4 + h) * 65
                        pv = em.o("tensor", lambda t, sl2=sl2, blk=blk, jj=jj, nj=nj: t.matmul(
                            p_o[0:65, 0:W], v_tm[:, blk:blk + 65], w_sb[sl2][:, 0:W],
                            start=(jj == 0), stop=(jj == nj - 1)),
                            after=[exp_toks[jj]] + ([po_free] if (jj == 0 and po_free) else []),
                            tag="PE")
                        w_free[sl2] = pv
                        pv_prev = pv
                se = em.o("scalar", lambda sc: sc.activation(
                    row_sb[0:1, 0:W], p_o[64:65, 0:W], AF.Identity),
                    after=[pv_prev], tag="ACT")
                rc = em.o("vector", lambda v: v.reciprocal(rc_sb[0:1, 0:W], row_sb[0:1, 0:W]),
                          after=[se], tag="DVE")
                bcmm = em.o("tensor", lambda t: t.matmul(
                    p_b[0:64, 0:W], or_sb[0:1, 0:64], rc_sb[0:1, 0:W], start=True, stop=True),
                    after=[rc, ld_or] + ([pb_free] if pb_free else []), tag="PE")
                bev = em.o("vector", lambda v: v.tensor_copy(bc_sb[:], p_b[0:64, 0:W]),
                           after=[bcmm], tag="DVE")
                pb_free = bev
                oslot = m % 2
                war = war_out[oslot]
                oev = em.o("vector", lambda v, oslot=oslot: v.tensor_mul(
                    out_sb[oslot][:], p_o[0:64, 0:W], bc_sb[:]),
                    after=[bev, pv_prev] + ([war] if war else []), tag="DVE")
                po_free = oev
                ek = em.echo([oev], "scalar", "ACT")
                st = em.o("gpsimd", lambda e, h=h, m=m, oslot=oslot: e.dma_start(
                    out=out1T[h * 64:(h + 1) * 64, m * W:(m + 1) * W],
                    in_=out_sb[oslot][:]),
                    after=[ek], tag="ST", inc=16)
                war_out[oslot] = st



def emit_phase1(nc, em, G):
    W = 512
    NT = S // W
    SC = 128
    xT = G["xT"]
    wq, wk, wv = G["wq"], G["wk"], G["wv"]
    aq, av, bq, bv = G["aq"], G["av"], G["bq"], G["bv"]
    cq, sq, phv, pho = G["cq"], G["sq"], G["phv"], G["pho"]
    rot, mask0, idt = G["rot"], G["mask0"], G["idt"]
    ones_col_bf, ones_row_f, eps = G["ones_col_bf"], G["ones_row_f"], G["eps"]
    out0T = G["out0d"]

    with ExitStack() as stk:
        en = stk.enter_context

        def alloc(name, shape, dt):
            return nc.alloc_sbuf_tensor(name, list(shape), dt)

        alq = alloc("alq", [128, 2 * S], BF16)
        pkt = alloc("pkt", [128, 2 * S], BF16)
        pvt = alloc("pvt", [128, 2 * S], BF16)
        fk = alloc("fk", [128, 2 * NS], BF16)
        fvtm = alloc("fvtm", [128, 16 * 4 * 65], BF16)
        wq_sb = alloc("wq_sb", [128, 8 * 256], BF16)
        wk_sb = alloc("wk_sb", [128, 8 * 256], BF16)
        wv_sb = alloc("wv_sb", [128, 8 * 256], BF16)
        aq_sb = alloc("aq_sb", [128, 8 * LORA], BF16)
        av_sb = alloc("av_sb", [128, 8 * LORA], BF16)
        bq_sb = alloc("bq_sb", [LORA, 256], BF16)
        bv_sb = alloc("bv_sb", [LORA, 256], BF16)
        rot_sb = alloc("rot_sb", [128, 128], F32)
        phv_sb = alloc("phv_sb", [128, W], F32)
        pho_sb = alloc("pho_sb", [64, W], F32)
        mask_sb = alloc("mask_sb", [128, W], F32)
        id_sb = alloc("id_sb", [128, 128], BF16)
        consts = {
            "ones_col_bf": alloc("c_ones_col", [128, 1], BF16),
            "ones_row_f": alloc("c_ones_row", [1, 128], F32),
            "eps": alloc("c_eps", [1, 1], F32),
        }
        xt = alloc("xt", [128, 8 * W], F32)
        ln_scr = {
            "xbf": alloc("ln_xbf", [128, 8 * W], BF16),
            "rows": [alloc(f"ln_r{i}", [1, W], F32) for i in range(4)],
            "bc_r": alloc("ln_bcr", [128, W], F32),
            "bc_mr": alloc("ln_bcmr", [128, W], F32),
        }
        ub = ln_scr["xbf"]
        bc_r = ln_scr["bc_r"]
        tmpf = xt
        tmpf2 = ln_scr["bc_mr"]
        tlora = alloc("tlora", [LORA, W], BF16)
        qf = alloc("qf", [128, 2 * W], F32)
        ksh = alloc("ksh", [128, 2 * W], BF16)
        vsh = alloc("vsh", [128, 2 * W], BF16)
        fvtmp = alloc("fvtmp", [128, 2 * SC], BF16)
        tab = alloc("tab", [128, 2 * W], F32)
        w_sb = [alloc(f"w_sb{i}", [128, W], BF16) for i in range(2)]
        out_sb = [alloc(f"out_sb{i}", [64, W], BF16) for i in range(2)]
        bcw_bf = alloc("bcw_bf", [64, W], BF16)
        p0 = en(nc.psum_tensor("p0", [128, W], F32))
        p1 = en(nc.psum_tensor("p1", [128, W], F32))
        pa = en(nc.psum_tensor("pa", [128, W], F32))
        pb = en(nc.psum_tensor("pb", [128, W], F32))
        p_o = en(nc.psum_tensor("p_o", [128, W], F32))
        p_d = en(nc.psum_tensor("p_d", [1, W], F32))
        pl = en(nc.psum_tensor("pl", [LORA, W], F32))
        pt_bf = en(nc.psum_tensor("pt_bf", [128, 256], BF16))
        ps = {"stat_a": pa, "stat_b": pb}

        def dma_in(dst, src, q="sync"):
            return em.o(q, lambda e, dst=dst, src=src: e.dma_start(out=dst, in_=src),
                        tag="LD" if q == "sync" else "LDG", inc=16)

        ld_wq = dma_in(wq_sb.ap().rearrange("p (c o) -> p c o", c=8),
                       wq.rearrange("(c p) o -> p c o", p=128), q="gpsimd")
        ld_wk = dma_in(wk_sb.ap().rearrange("p (c o) -> p c o", c=8),
                       wk.rearrange("(c p) o -> p c o", p=128), q="gpsimd")
        ld_wv = dma_in(wv_sb.ap().rearrange("p (c o) -> p c o", c=8),
                       wv.rearrange("(c p) o -> p c o", p=128), q="gpsimd")
        ld_aq = dma_in(aq_sb.ap().rearrange("p (c o) -> p c o", c=8),
                       aq.rearrange("(c p) o -> p c o", p=128), q="gpsimd")
        ld_av = dma_in(av_sb.ap().rearrange("p (c o) -> p c o", c=8),
                       av.rearrange("(c p) o -> p c o", p=128), q="gpsimd")
        ld_bq = dma_in(bq_sb[:], bq, q="gpsimd")
        ld_bv = dma_in(bv_sb[:], bv, q="gpsimd")
        ld_rot = dma_in(rot_sb[:], rot)
        ld_phv = dma_in(phv_sb[:], phv)
        ld_pho = dma_in(pho_sb[:], pho)
        ld_m = dma_in(mask_sb[:], mask0)
        ld_id = dma_in(id_sb[:], idt)
        ld_oc = dma_in(consts["ones_col_bf"][:], ones_col_bf)
        ld_or = dma_in(consts["ones_row_f"][:], ones_row_f)
        ld_eps = dma_in(consts["eps"][:], eps)
        const_deps = [ld_oc, ld_or, ld_eps]
        wkh = em.o("vector", lambda v: v.tensor_scalar(
            wk_sb[:], wk_sb[:], 0.5, None, ALU.mult), after=[ld_wk], tag="DVE")
        ones_tok = em.o("vector", lambda v: v.memset(fvtm[:], 1.0), tag="DVE")

        xt_war = None
        tab_war = None
        ub_war = None
        dve_war = None
        fvtmp_war = []
        ptbf_war = None
        shm_prev = {}
        for tt in range(NT):
            t0 = tt * W
            ldx = em.o("sync", lambda e, t0=t0: e.dma_start(
                out=xt.ap().rearrange("p (c t) -> p c t", c=8),
                in_=xT[:, t0:t0 + W].rearrange("(c p) t -> p c t", p=128)),
                after=[xt_war] if xt_war else [], tag="LD", inc=16)
            ldt = []
            for i, tsrc in enumerate((cq, sq)):
                ldt.append(em.o("sync", lambda e, i=i, tsrc=tsrc, t0=t0: e.dma_start(
                    out=tab[:, i * W:(i + 1) * W], in_=tsrc[:, t0:t0 + W]),
                    after=[tab_war] if tab_war else [], tag="LD", inc=16))
            (l1, l2), _, bc_m = emit_ln_stats(
                em, consts, ps, lambda c: xt[:, c * W:(c + 1) * W], W,
                [ldx] + const_deps + ([ub_war] if ub_war else [])
                + ([dve_war] if dve_war else []), "", ln_scr,
                raw_mean=True)
            u_toks = []
            for c in range(8):
                u_toks.append(em.o("vector", lambda v, c=c: v.tensor_sub(
                    ub[:, c * W:(c + 1) * W], xt[:, c * W:(c + 1) * W], bc_m[:, 0:W]),
                    after=[l2], tag="DVE"))
            evs = {}
            rope_toks = []
            proj_last = None

            def lora_pass(asb, lda, prev_read):
                mmt = None
                for c in range(8):
                    mmt = em.o("tensor", lambda t, c=c, asb=asb: t.matmul(
                        pl[:, 0:W], asb[:, c * LORA:(c + 1) * LORA],
                        ub[:, c * W:(c + 1) * W], start=(c == 0), stop=(c == 7)),
                        after=([u_toks[c], lda] + prev_read) if c == 0 else [u_toks[c]],
                        tag="PE")
                return em.o("vector", lambda v: v.tensor_copy(tlora[:, 0:W], pl[:, 0:W]),
                            after=[mmt], tag="DVE")

            def proj(nm, wsb, bsb, ldwt, ldb, tl, oc, bank_dep):
                pp = p0 if oc == 0 else p1
                mm = None
                for c in range(8):
                    mm = em.o("tensor", lambda t, c=c, oc=oc, wsb=wsb, pp=pp: t.matmul(
                        pp[:, 0:W], wsb[:, c * 256 + oc * 128: c * 256 + (oc + 1) * 128],
                        ub[:, c * W:(c + 1) * W],
                        start=(c == 0), stop=(c == 7 and tl is None)),
                        after=([ldwt, u_toks[c]] + bank_dep) if c == 0 else [u_toks[c]],
                        tag="PE")
                if tl is not None:
                    mm = em.o("tensor", lambda t, oc=oc, bsb=bsb, pp=pp: t.matmul(
                        pp[:, 0:W], bsb[0:LORA, oc * 128:(oc + 1) * 128], tlora[:, 0:W],
                        start=False, stop=True), after=[tl, ldb], tag="PE")
                return mm, pp

            tlq = lora_pass(aq_sb, ld_aq, [])
            qmm = {}
            for oc in range(2):
                mm, pp = proj("q", wq_sb, bq_sb, ld_wq, ld_bq, tlq, oc, [])
                ev = em.o("vector", lambda v, oc=oc, pp=pp: v.tensor_mul(
                    qf[:, oc * W:(oc + 1) * W], pp[:, 0:W], bc_r[:, 0:W]),
                    after=[mm, l1], tag="DVE")
                evs[("q", oc)] = ev
                qmm[oc] = mm

            def rope(src_oc_ap, oc, dst_ap, psh, bank_dep, src_dep, war):
                shm = em.o("tensor", lambda t, src_oc_ap=src_oc_ap, psh=psh: t.matmul(
                    psh[:, 0:W], rot_sb[:], src_oc_ap, start=True, stop=True),
                    after=[src_dep, ld_rot] + bank_dep, tag="PE")
                t1 = em.o("vector", lambda v, src_oc_ap=src_oc_ap: v.tensor_mul(
                    tmpf[:, 0:W], src_oc_ap, tab[:, 0:W]),
                    after=[src_dep] + war + ldt, tag="DVE")
                t2 = em.o("vector", lambda v, psh=psh: v.tensor_mul(
                    tmpf2[:, 0:W], psh[:, 0:W], tab[:, W:2 * W]),
                    after=[shm, src_dep] + war + ldt, tag="DVE")
                t3 = em.o("vector", lambda v, dst_ap=dst_ap: v.tensor_add(
                    dst_ap, tmpf[:, 0:W], tmpf2[:, 0:W]), after=[t1, t2], tag="DVE")
                return shm, t2, t3

            rope_war = []
            for oc in range(2):
                psh = pa if oc == 0 else pb
                bank_dep = [l1 if oc == 0 else l2] + (shm_prev.get(oc) or [])
                shm, t2, t3 = rope(qf[:, oc * W:(oc + 1) * W], oc,
                                   alq[:, oc * S + t0: oc * S + t0 + W],
                                   psh, bank_dep, evs[("q", oc)], rope_war)
                rope_toks.append(t3)
                rope_war = [t3]
                evs[("ropeq", oc)] = (shm, t2, t3)

            for oc in range(2):
                mm, pp = proj("k", wk_sb, None, wkh, None, None, oc,
                              [evs[("q", oc)]])
                ev = em.o("vector", lambda v, oc=oc, pp=pp: v.tensor_mul(
                    qf[:, oc * W:(oc + 1) * W], pp[:, 0:W], bc_r[:, 0:W]),
                    after=[mm, evs[("ropeq", oc)][0], evs[("ropeq", oc)][2]], tag="DVE")
                evs[("k", oc)] = ev
            for oc in range(2):
                psh = pa if oc == 0 else pb
                bank_dep = [evs[("ropeq", oc)][1]]
                shm, t2, t3 = rope(qf[:, oc * W:(oc + 1) * W], oc,
                                   ksh[:, oc * W:(oc + 1) * W],
                                   psh, bank_dep, evs[("k", oc)], rope_war)
                rope_toks.append(t3)
                rope_war = [t3]
                evs[("ropek", oc)] = (shm, t2, t3)
                shm_prev[oc] = [t2]

            bcp = em.o("vector", lambda v: v.tensor_mul(
                bc_r[:, 0:W], bc_r[:, 0:W], phv_sb[:]),
                after=[l1, ld_phv, evs[("k", 1)]], tag="DVE")
            tlv = lora_pass(av_sb, ld_av, [qmm[1], tlq])
            for oc in range(2):
                mm, pp = proj("v", wv_sb, bv_sb, ld_wv, ld_bv, tlv, oc,
                              [evs[("k", oc)]])
                ev = em.o("vector", lambda v, oc=oc, pp=pp: v.tensor_mul(
                    vsh[:, oc * W:(oc + 1) * W], pp[:, 0:W], bc_r[:, 0:W]),
                    after=[mm, bcp], tag="DVE")
                evs[("v", oc)] = ev
                if oc == 1:
                    proj_last = mm
            ub_war = proj_last

            comp_last = None
            for oc in range(2):
                for (srcb, isk) in ((ksh, True), (vsh, False)):
                    sdep = evs[("ropek", oc)][2] if isk else evs[("v", oc)]
                    base = oc * W

                    def s0(j, srcb=srcb, base=base):
                        return srcb[:, base + j: base + W: 4]
                    if isk:
                        dsts = fk[:, oc * NS + tt * SC: oc * NS + (tt + 1) * SC]
                    else:
                        dsts = fvtmp[:, oc * SC:(oc + 1) * SC]
                    a1 = em.o("vector", lambda v, s0=s0: v.tensor_add(
                        tmpf[:, 0:SC], s0(0), s0(1)),
                        after=[sdep] + ([comp_last] if comp_last else rope_war),
                        tag="DVE")
                    a2 = em.o("vector", lambda v, s0=s0: v.tensor_add(
                        tmpf[:, 0:SC], tmpf[:, 0:SC], s0(2)), after=[a1], tag="DVE")
                    a3 = em.o("vector", lambda v, s0=s0, dsts=dsts: v.tensor_add(
                        dsts, tmpf[:, 0:SC], s0(3)),
                        after=[a2] + (fvtmp_war if not isk else []), tag="DVE")
                    comp_last = a3
                    if not isk:
                        evs[("fv", oc)] = a3
                for (srcb, dstb, sdep) in ((ksh, pkt, evs[("ropek", oc)][2]),
                                           (vsh, pvt, evs[("v", oc)])):
                    base = oc * W
                    rbase = oc * S + t0

                    def s0(j, srcb=srcb, base=base):
                        return srcb[:, base + j: base + W: 4]

                    def d0(j, dstb=dstb, rbase=rbase):
                        return dstb[:, rbase + j: rbase + W: 4]
                    cprev = em.o("vector", lambda v, s0=s0, d0=d0: v.tensor_copy(
                        d0(0), s0(0)), after=[sdep], tag="DVE")
                    for j in range(1, 4):
                        cprev = em.o("vector", lambda v, j=j, s0=s0, d0=d0: v.tensor_add(
                            d0(j), d0(j - 1), s0(j)), after=[cprev], tag="DVE")
                    comp_last = cprev
            fvtmp_war = []
            trvs = []
            for oc in range(2):
                trv = em.o("tensor", lambda t, oc=oc: t.transpose(
                    pt_bf[:, oc * 128:(oc + 1) * 128], fvtmp[:, oc * SC:(oc + 1) * SC],
                    id_sb[:]),
                    after=[evs[("fv", oc)], ld_id] + ([ptbf_war] if ptbf_war else []),
                    tag="PE")
                fvtmp_war.append(trv)
                trvs.append(trv)
            cps = []
            for oc in range(2):
                for hh in range(2):
                    h = 2 * oc + hh
                    cp = em.o("vector", lambda v, tt=tt, h=h, oc=oc, hh=hh: v.tensor_copy(
                        fvtm[:, (tt * 4 + h) * 65:(tt * 4 + h) * 65 + 64],
                        pt_bf[:, oc * 128 + hh * 64: oc * 128 + hh * 64 + 64]),
                        after=trvs + [ones_tok], tag="DVE")
                    cps.append(cp)
            ptbf_war = em.echo(cps, "scalar", "ACT")
            comp_last = cps[-1]
            dve_war = comp_last
            tab_war = em.echo([evs[("ropek", 1)][1]], "scalar", "ACT")
            xt_war = em.echo([comp_last], "scalar", "ACT")
        ph1 = em.phase_barrier()

        rows = ln_scr["rows"]
        wd_row, se_row, rc_row = rows[0], rows[1], rows[2]
        bcw = xt
        t2f_ap = lambda: xt[0:64, W:2 * W]
        bcf_ap = lambda: ln_scr["bc_r"][0:64, 0:W]
        war_out = {0: None, 1: None}
        w_free = {0: None, 1: None}
        po_free = None
        pd_free = None
        pa_free = None
        pb_free = None
        t2v_prev = None
        bfm_prev = None
        oc_sb = consts["ones_col_bf"]
        or_sb = consts["ones_row_f"]
        for h in range(4):
            qrow = (h % 2) * 64
            fcolb = (h // 2) * NS
            qcolb = (h // 2) * S
            for m in range(NT):
                qT = alq[qrow:qrow + 64, qcolb + m * W: qcolb + (m + 1) * W]
                nj = m + 1
                exp_toks = {}
                pv_prev = None
                for jx in range(nj + 1):
                    if jx < nj:
                        j = jx
                        sl = j % 2
                        dep = list(ph1) if (h == 0 and m == 0 and j == 0) else []
                        prev_exp = exp_toks.get(j - 2)
                        qk = em.o("tensor", lambda t, j=j, sl=sl, qT=qT, qrow=qrow, fcolb=fcolb: t.matmul(
                            (p0 if sl == 0 else p1)[:, 0:W],
                            fk[qrow:qrow + 64, fcolb + j * 128: fcolb + j * 128 + 128],
                            qT, start=True, stop=True),
                            after=dep + ([prev_exp] if prev_exp else []), tag="PE")
                        if j == m:
                            mk = em.o("vector", lambda v, sl=sl: v.tensor_add(
                                (p0 if sl == 0 else p1)[:, 0:W],
                                (p0 if sl == 0 else p1)[:, 0:W], mask_sb[:]),
                                after=[qk, ld_m], tag="DVE")
                            edep = [mk]
                        else:
                            edep = [qk]
                        wfree = w_free.get(sl)
                        ex = em.o("scalar", lambda sc, sl=sl: sc.activation(
                            w_sb[sl][:, 0:W], (p0 if sl == 0 else p1)[:, 0:W],
                            AF.Exp, scale=0.125),
                            after=edep + ([wfree] if wfree else []), tag="ACT")
                        exp_toks[j] = ex
                    jj = jx - 1
                    if jj >= 0:
                        sl2 = jj % 2
                        blk = (jj * 4 + h) * 65
                        pv = em.o("tensor", lambda t, sl2=sl2, blk=blk, jj=jj, nj=nj: t.matmul(
                            p_o[0:65, 0:W], fvtm[:, blk:blk + 65], w_sb[sl2][:, 0:W],
                            start=(jj == 0), stop=(jj == nj - 1)),
                            after=[exp_toks[jj]] + ([po_free] if (jj == 0 and po_free) else []),
                            tag="PE")
                        w_free[sl2] = pv
                        pv_prev = pv
                pr = em.o("vector", lambda v, qT=qT, qrow=qrow, qcolb=qcolb, m=m: v.tensor_mul(
                    vsh[qrow:qrow + 64, 0:W], qT,
                    pkt[qrow:qrow + 64, qcolb + m * W: qcolb + (m + 1) * W]),
                    after=(list(ph1) if (h == 0 and m == 0) else [])
                    + ([pd_free] if pd_free else []), tag="DVE")
                dmm = em.o("tensor", lambda t, qrow=qrow: t.matmul(
                    p_d[0:1, 0:W], oc_sb[qrow:qrow + 64, 0:1], vsh[qrow:qrow + 64, 0:W],
                    start=True, stop=True),
                    after=[pr] + ([pd_free] if pd_free else []), tag="PE")
                wd = em.o("scalar", lambda sc: sc.activation(
                    wd_row[0:1, 0:W], p_d[0:1, 0:W], AF.Exp, scale=0.125),
                    after=[dmm], tag="ACT")
                pd_free = wd
                sadd = em.o("vector", lambda v: v.tensor_add(
                    p_o[64:65, 0:W], p_o[64:65, 0:W], wd_row[0:1, 0:W]),
                    after=[wd, pv_prev], tag="DVE")
                bcmm = em.o("tensor", lambda t: t.matmul(
                    pa[0:64, 0:W], or_sb[0:1, 0:64], wd_row[0:1, 0:W],
                    start=True, stop=True),
                    after=[wd] + ([pa_free] if pa_free else []), tag="PE")
                bev = em.o("vector", lambda v: v.tensor_copy(bcw_bf[:], pa[0:64, 0:W]),
                           after=[bcmm] + ([t2v_prev] if t2v_prev else []), tag="DVE")
                pa_free = bev
                pvc = em.o("vector", lambda v, qrow=qrow, qcolb=qcolb, m=m: v.tensor_copy(
                    vsh[0:64, W:2 * W],
                    pvt[qrow:qrow + 64, qcolb + m * W: qcolb + (m + 1) * W]),
                    after=[bev], tag="DVE")
                t2v = em.o("vector", lambda v: v.tensor_mul(
                    t2f_ap(), bcw_bf[:], vsh[0:64, W:2 * W]),
                    after=[pvc], tag="DVE")
                t2v_prev = t2v
                acc = em.o("vector", lambda v: v.tensor_add(
                    p_o[0:64, 0:W], p_o[0:64, 0:W], t2f_ap()),
                    after=[t2v, pv_prev], tag="DVE")
                se = em.o("scalar", lambda sc: sc.activation(
                    se_row[0:1, 0:W], p_o[64:65, 0:W], AF.Identity),
                    after=[sadd], tag="ACT")
                rc = em.o("vector", lambda v: v.reciprocal(rc_row[0:1, 0:W], se_row[0:1, 0:W]),
                          after=[se] + ([bfm_prev] if bfm_prev else []), tag="DVE")
                bfm = em.o("tensor", lambda t: t.matmul(
                    pb[0:64, 0:W], or_sb[0:1, 0:64], rc_row[0:1, 0:W],
                    start=True, stop=True),
                    after=[rc] + ([pb_free] if pb_free else []), tag="PE")
                bfm_prev = bfm
                bcf = em.o("vector", lambda v: v.tensor_mul(
                    bcf_ap(), pb[0:64, 0:W], pho_sb[:]),
                    after=[bfm, ld_pho], tag="DVE")
                pb_free = bcf
                oslot = m % 2
                war = war_out[oslot]
                oev = em.o("vector", lambda v, oslot=oslot: v.tensor_mul(
                    out_sb[oslot][:], p_o[0:64, 0:W], bcf_ap()),
                    after=[bcf, acc] + ([war] if war else []), tag="DVE")
                po_free = oev
                ek = em.echo([oev], "scalar", "ACT")
                st = em.o("gpsimd", lambda e, h=h, m=m, oslot=oslot: e.dma_start(
                    out=out0T[h * 64:(h + 1) * 64, m * W:(m + 1) * W],
                    in_=out_sb[oslot][:]),
                    after=[ek], tag="ST", inc=16)
                war_out[oslot] = st



def emit_partial_oproj(nc, em, G, bar, pfx, src_ap, wkey, akey, bkey, cc_dst, bnd):
    W = 512
    PAD = 8 if bnd else 0
    lora = akey is not None
    with ExitStack() as stk:
        en = stk.enter_context

        def alloc(name, shape, dt):
            return nc.alloc_sbuf_tensor(pfx + name, list(shape), dt)

        ot = [alloc(f"ot{i}", [128, 2 * W], BF16) for i in range(2)]
        w_sb = alloc("w", [128, 2 * D], BF16)
        y0 = [alloc(f"y{i}", [128, 8 * W], F32) for i in range(2)]
        p0 = en(nc.psum_tensor(pfx + "p0", [128, W], F32))
        p1 = en(nc.psum_tensor(pfx + "p1", [128, W], F32))
        ld_w = em.o("gpsimd", lambda g: g.dma_start(
            out=w_sb.ap().rearrange("p (c o) -> p c o", c=2),
            in_=G[wkey].rearrange("(c p) o -> p c o", p=128)),
            after=bar, tag="LDG", inc=16)
        wdeps = [ld_w]
        if lora:
            a_sb = alloc("a", [128, 2 * LORA], BF16)
            b_sb = alloc("b", [LORA, D], BF16)
            tl = alloc("tl", [LORA, W], BF16)
            pl = en(nc.psum_tensor(pfx + "pl", [LORA, W], F32))
            ld_a = em.o("gpsimd", lambda g: g.dma_start(
                out=a_sb.ap().rearrange("p (c o) -> p c o", c=2),
                in_=G[akey].rearrange("(c p) o -> p c o", p=128)),
                after=bar, tag="LDG", inc=16)
            ld_b = em.o("gpsimd", lambda g: g.dma_start(
                out=b_sb[:], in_=G[bkey]), after=bar, tag="LDG", inc=16)
            wdeps += [ld_a, ld_b]
        ld_war = {0: None, 1: None}
        st_war = {0: [], 1: []}
        bank_free = {0: None, 1: None}
        tl_prev = None
        for m in range(16):
            sl = m % 2
            buf = ot[sl]
            ld = em.o("sync", lambda e, m=m, buf=buf: e.dma_start(
                out=buf.ap().rearrange("p (c t) -> p c t", c=2),
                in_=src_ap[:, m * W:(m + 1) * W].rearrange("(c p) t -> p c t", p=128)),
                after=(bar if m < 2 else []) +
                ([ld_war[sl]] if ld_war[sl] else []), tag="LD", inc=16)
            tlt = None
            if lora:
                mmt = None
                for fc in range(2):
                    mmt = em.o("tensor", lambda t, fc=fc, buf=buf: t.matmul(
                        pl[:, 0:W], a_sb[:, fc * LORA:(fc + 1) * LORA],
                        buf[:, fc * W:(fc + 1) * W],
                        start=(fc == 0), stop=(fc == 1)),
                        after=([ld, ld_a] + ([tl_prev] if tl_prev else []))
                        if fc == 0 else (), tag="PE")
                tlt = em.o("vector", lambda v: v.tensor_copy(tl[:, 0:W], pl[:, 0:W]),
                           after=[mmt], tag="DVE")
                tl_prev = tlt
            evs = []
            last_fc_mm = None
            for oc in range(8):
                pp = p0 if oc % 2 == 0 else p1
                prev = bank_free[oc % 2]
                mm = None
                for fc in range(2):
                    mm = em.o("tensor", lambda t, fc=fc, oc=oc, buf=buf, pp=pp: t.matmul(
                        pp[:, 0:W], w_sb[:, fc * D + oc * 128: fc * D + (oc + 1) * 128],
                        buf[:, fc * W:(fc + 1) * W],
                        start=(fc == 0), stop=(fc == 1 and not lora)),
                        after=([ld] + wdeps + ([prev] if prev else []))
                        if fc == 0 else (), tag="PE")
                last_fc_mm = mm
                if lora:
                    mm = em.o("tensor", lambda t, oc=oc, pp=pp: t.matmul(
                        pp[:, 0:W], b_sb[:, oc * 128:(oc + 1) * 128], tl[:, 0:W],
                        start=False, stop=True), after=[tlt], tag="PE")
                ev = em.o("vector", lambda v, oc=oc, pp=pp, sl=sl: v.tensor_copy(
                    y0[sl][:, oc * W:(oc + 1) * W], pp[:, 0:W]),
                    after=[mm] + (st_war[sl] if oc == 0 else []), tag="DVE")
                bank_free[oc % 2] = ev
                evs.append(ev)
            ld_war[sl] = last_fc_mm
            ek = em.echo(evs, "scalar", "ACT")
            qq = m // 4
            st = em.o("gpsimd", lambda e, m=m, qq=qq, sl=sl: e.dma_start(
                out=cc_dst[qq * D:(qq + 1) * D,
                           PAD + (m % 4) * W: PAD + (m % 4) * W + W].rearrange(
                    "(c p) t -> p c t", p=128),
                in_=y0[sl].ap().rearrange("p (c t) -> p c t", c=8)),
                after=[ek], tag="ST", inc=16)
            st_war[sl] = [st]
            if bnd and m % 4 == 3:
                q2 = (qq + 1) % 4
                stb = em.o("gpsimd", lambda e, q2=q2, sl=sl: e.dma_start(
                    out=cc_dst[q2 * D:(q2 + 1) * D, 0:1].rearrange(
                        "(c p) t -> p c t", p=128),
                    in_=y0[sl].ap().rearrange("p (c t) -> p c t", c=8)[:, :, W - 1:W],
                    allow_slow_non_contiguous=True),
                    after=[ek], tag="ST", inc=16)
                st_war[sl].append(stb)



def declare_dram(nc):
    G = {}

    def inp(name, shape, dt):
        G[name] = nc.dram_tensor(name, list(shape), dt, kind="ExternalInput").ap()

    def intl(name, shape, dt):
        h = nc.dram_tensor(name, list(shape), dt, kind="Internal")
        G[name + "_h"] = h
        G[name] = h.ap()

    inp("xT", [D, S], F32)
    inp("wq", [D, 256], BF16)
    inp("wk", [D, 256], BF16)
    inp("wv", [D, 256], BF16)
    inp("aq", [D, LORA], BF16)
    inp("av", [D, LORA], BF16)
    inp("bq", [LORA, 256], BF16)
    inp("bv", [LORA, 256], BF16)
    inp("cq", [128, S], F32)
    inp("sq", [128, S], F32)
    inp("phv", [128, 512], F32)
    inp("pho", [64, 512], F32)
    inp("rot", [128, 128], F32)
    inp("mask0", [128, 512], F32)
    inp("idt", [128, 128], BF16)
    inp("ones_col_bf", [128, 1], BF16)
    inp("ones_row_f", [1, 128], F32)
    inp("eps", [1, 1], F32)
    inp("woP", [256, D], BF16)
    inp("aoP", [256, LORA], BF16)
    inp("boF", [LORA, D], BF16)
    inp("owP", [256, D], BF16)
    inp("xqT", [D, TQ + 4], F32)
    inp("wrT", [D, D], BF16)
    inp("wkT", [D, D], BF16)
    inp("wvT", [D, D], BF16)
    inp("mu_r", [128, 8], F32)
    inp("mu_k", [128, 8], F32)
    inp("bmask", [128, 1], F32)
    inp("qw", [D, 256], BF16)
    inp("kw", [D, 256], BF16)
    inp("vw", [D, 256], BF16)
    inp("qb", [128, 2], F32)
    inp("kb", [128, 2], F32)
    inp("vb", [128, 2], F32)
    inp("masks", [128, 4 * 512], F32)
    inp("ob", [128, 8], F32)
    inp("w1", [E, D, HID], BF16)
    inp("b1c", [128, E * 32], F32)
    inp("w2", [E, HID, D], BF16)
    inp("b2c", [128, 32], F32)
    inp("wr", [D, E], F32)
    inp("brc", [4, 1], F32)
    inp("id4", [4, 4], F32)
    inp("id128", [128, 128], F32)
    inp("emat", [4, 512], F32)
    G["x4T"] = nc.dram_tensor("x4T", [D, TQ], F32, kind="ExternalOutput").ap()
    intl("out0d", [256, S], BF16)
    intl("out1d", [256, S], BF16)
    intl("x2scr", [D, TQ], F32)
    intl("x3scr", [D, TQ], F32)
    intl("cc1i", [4 * D, 8 + TQ], F32)
    intl("cc1o", [D, 8 + TQ], F32)
    intl("cc2i", [D, TQ], BF16)
    intl("cc2o", [4 * D, TQ], BF16)
    intl("cc3i", [4 * D, TQ], F32)
    intl("cc3o", [D, TQ], F32)
    return G


def build_fused():
    nc = bass.Bass("TRN2", target_bir_lowering=False)
    G = declare_dram(nc)
    em = Emitter()

    def reset_phase():
        bar = em.phase_barrier()
        nc.sbuf_base = sb0
        nc.psum_base = ps0
        return bar

    def collective(kind, op, in_h, out_h, tag):
        em.o("gpsimd", lambda g: g.collective_compute(
            kind, op, replica_groups=RG,
            ins=[in_h.ap().opt()], outs=[out_h.ap().opt()]),
            after=[em.dma_barrier("ST")], tag=tag, inc=1)
        return (tag, 1, "__none__")

    with ExitStack() as stk:
        en = stk.enter_context
        sems = {s: en(nc.semaphore(s)) for s in ALL_SEMS}
        sb0, ps0 = nc.sbuf_base, nc.psum_base

        emit_phase1(nc, em, G)
        bar = reset_phase()
        emit_partial_oproj(nc, em, G, bar, "b1_", G["out0d"],
                           "woP", "aoP", "boF", G["cc1i"], bnd=True)
        cc0 = collective("ReduceScatter", ALU.add, G["cc1i_h"], G["cc1o_h"], "CC0")
        bar = reset_phase()
        emit_phase2(nc, em, G, bar, cc0)
        cc1 = collective("AllGather", ALU.bypass, G["cc2i_h"], G["cc2o_h"], "CC1")
        bar = reset_phase()
        emit_phase3(nc, em, G, bar, cc1)
        bar = reset_phase()
        emit_partial_oproj(nc, em, G, bar, "b3_", G["out1d"],
                           "owP", None, None, G["cc3i"], bnd=False)
        cc2 = collective("ReduceScatter", ALU.add, G["cc3i_h"], G["cc3o_h"], "CC2")
        bar = reset_phase()
        emit_phase4(nc, em, G, bar, cc2)
        em.o("sync", lambda e: e.nop(), after=[em.dma_barrier("ST")])
        block = en(nc.Block())
        em.build(block, sems)
    return nc


_F_CACHE = {}


def kernel(**inputs):
    g = {k: np.asarray(v) for k, v in inputs.items()}
    if "nc" not in _F_CACHE:
        _F_CACHE["nc"] = build_fused()
    nc = _F_CACHE["nc"]
    bf = ml_dtypes.bfloat16
    x = g["L0_attn_hidden_states"].astype(np.float32)
    wo = np.ascontiguousarray(g["L0_attn_Wo"]).astype(np.float32)
    ao = np.ascontiguousarray(g["L0_attn_Ao"]).astype(np.float32)
    owf = np.ascontiguousarray(g["L1_attn_O_w"].reshape(D, D)).astype(np.float32)
    ob = np.ascontiguousarray(g["L1_attn_O_b"].reshape(8, 128).T).astype(np.float32)
    b1c = np.zeros((128, E * 32), np.float32)
    for e in range(E):
        b1c[:, e * 32:(e + 1) * 32] = g["L1_ffn_b1"][e].reshape(32, 128).T
    b2c = np.zeros((128, 32), np.float32)
    for dc in range(8):
        for e in range(E):
            b2c[:, dc * 4 + e] = g["L1_ffn_b2"][e][dc * 128:(dc + 1) * 128]
    sl_ = np.arange(128)[:, None]
    ql_ = np.arange(512)[None, :]
    masks = np.concatenate(
        [np.where(128 * d + sl_ <= ql_, 0.0, -1e30).astype(np.float32)
         for d in range(4)], axis=1)
    in_common = {
        "boF": np.ascontiguousarray(g["L0_attn_Bo"]).astype(bf),
        "wrT": np.ascontiguousarray(g["L0_ffn_W_r"].T).astype(bf),
        "wkT": np.ascontiguousarray(g["L0_ffn_W_k"].T).astype(bf),
        "wvT": np.ascontiguousarray(g["L0_ffn_W_v"].T).astype(bf),
        "mu_r": np.ascontiguousarray(g["L0_ffn_mu_r"].reshape(8, 128).T).astype(np.float32),
        "mu_k": np.ascontiguousarray(g["L0_ffn_mu_k"].reshape(8, 128).T).astype(np.float32),
        "masks": masks,
        "ob": ob, "b1c": b1c, "b2c": b2c,
        "w1": np.ascontiguousarray(g["L1_ffn_w1"]).astype(bf),
        "w2": np.ascontiguousarray(g["L1_ffn_w2"]).astype(bf),
        "wr": np.ascontiguousarray(g["L1_ffn_wr"]).astype(np.float32),
        "brc": np.ascontiguousarray(g["L1_ffn_br"].reshape(4, 1)).astype(np.float32),
        "id4": CONSTS["ident4_f"], "id128": CONSTS["ident128_f"],
        "emat": CONSTS["emat"],
        "cq": CONSTS["cosP_q"], "sq": CONSTS["ssP_q"],
        "phv": np.ascontiguousarray(CONSTS["phase_h"][:, 0:512]),
        "pho": np.ascontiguousarray(CONSTS["phase_o"][0:64, 0:512]),
        "rot": CONSTS["rot32"], "mask0": CONSTS["mask_l0"],
        "idt": np.eye(128).astype(bf),
        "ones_col_bf": CONSTS["ones_col_bf"],
        "ones_row_f": CONSTS["ones_row_f"], "eps": CONSTS["eps"],
    }
    in_maps = []
    for c in range(NC):
        b, gq = c // 4, c % 4
        t0 = gq * TQ
        hset = list(range(4 * gq, 4 * gq + 4))

        def bcol(bmat):
            v = bmat[hset].reshape(256)
            return np.ascontiguousarray(v.reshape(2, 128).T).astype(np.float32)

        xT = np.ascontiguousarray(x[b].T).astype(np.float32)
        xq = np.ascontiguousarray(x[b].T[:, t0:t0 + TQ]).astype(np.float32)
        if t0 == 0:
            xbnd = np.zeros((D, 4), np.float32)
        else:
            xbnd = np.tile(np.ascontiguousarray(x[b].T[:, t0 - 1:t0]),
                           (1, 4)).astype(np.float32)
        in_maps.append({
            "xT": xT,
            "xqT": np.concatenate([xq, xbnd], axis=1),
            "bmask": np.full((128, 1), 0.0 if gq == 0 else 1.0, np.float32),
            "wq": np.ascontiguousarray(g["L0_attn_Wq"][:, 256 * gq:256 * gq + 256]).astype(bf),
            "wk": np.ascontiguousarray(g["L0_attn_Wk"][:, 256 * gq:256 * gq + 256]).astype(bf),
            "wv": np.ascontiguousarray(g["L0_attn_Wv"][:, 256 * gq:256 * gq + 256]).astype(bf),
            "aq": np.ascontiguousarray(g["L0_attn_Aq"]).astype(bf),
            "av": np.ascontiguousarray(g["L0_attn_Av"]).astype(bf),
            "bq": np.ascontiguousarray(g["L0_attn_Bq"][:, 256 * gq:256 * gq + 256]).astype(bf),
            "bv": np.ascontiguousarray(g["L0_attn_Bv"][:, 256 * gq:256 * gq + 256]).astype(bf),
            "woP": np.ascontiguousarray(wo[256 * gq:256 * (gq + 1)]).astype(bf),
            "aoP": np.ascontiguousarray(ao[256 * gq:256 * (gq + 1)]).astype(bf),
            "owP": np.ascontiguousarray(owf[256 * gq:256 * (gq + 1)]).astype(bf),
            "qw": np.ascontiguousarray(g["L1_attn_Q_w"][hset].transpose(1, 0, 2).reshape(D, 256)).astype(bf),
            "kw": np.ascontiguousarray(g["L1_attn_K_w"][hset].transpose(1, 0, 2).reshape(D, 256)).astype(bf),
            "vw": np.ascontiguousarray(g["L1_attn_V_w"][hset].transpose(1, 0, 2).reshape(D, 256)).astype(bf),
            "qb": bcol(g["L1_attn_Q_b"]), "kb": bcol(g["L1_attn_K_b"]),
            "vb": bcol(g["L1_attn_V_b"]),
            **in_common,
        })
    res = run_bass_kernel_spmd(nc, in_maps, core_ids=list(range(NC)))
    x4 = np.zeros((B, S, D), np.float32)
    for c in range(NC):
        b, gq = c // 4, c % 4
        t0 = gq * TQ
        x4[b, t0:t0 + TQ] = res.results[c]["x4T"].T
    return x4.astype(np.float32)


# version 05
# speedup vs baseline: 1.3577x; 1.0174x over previous
import math
import numpy as np
import ml_dtypes
import concourse.bass as bass
import concourse.mybir as mybir
from concourse.bass_utils import run_bass_kernel_spmd
from contextlib import ExitStack

F32 = mybir.dt.float32
BF16 = mybir.dt.bfloat16
AF = mybir.ActivationFunctionType
ALU = mybir.AluOpType
AX = mybir.AxisListType

B, S, D, NH, HD, KF, LORA = 2, 8192, 1024, 16, 64, 4, 64
NS = S // KF
E, HID = 4, 4096
NC = 8
TQ = S // 4
EPS = 1e-5
ENGINES = ("sync", "tensor", "vector", "scalar", "gpsimd")



def _rope_tables():
    inv = 1.0 / (10000.0 ** (np.arange(0, HD, 2, dtype=np.float64) / HD))
    pos = np.arange(S, dtype=np.float64)
    fr = pos[:, None] * inv[None, :]
    emb = np.concatenate([fr, fr], axis=1)
    cos = np.cos(emb).T.astype(np.float32)
    sin = np.sin(emb).T.astype(np.float32)
    return cos, sin


def _hadamard():
    H4 = np.array([[1, 1, 1, 1], [1, -1, 1, -1], [1, 1, -1, -1], [1, -1, -1, 1]],
                  dtype=np.float32)
    CDMA = np.tile(H4, (1, HD // 4))
    return CDMA


def _phaseT():
    CDMA = _hadamard()
    reps = CDMA.T
    return np.tile(reps, (1, S // 4)).astype(np.float32)


def host_consts():
    c = {}
    cos, sin = _rope_tables()
    phase = _phaseT()
    ss = sin.copy()
    ss[:32, :] *= -1.0
    var_scale = math.sqrt(KF)
    c["cosP_q"] = np.tile(cos * phase, (2, 1))
    c["ssP_q"] = np.tile(ss * phase, (2, 1))
    c["cosP_k"] = np.tile(cos * phase / var_scale, (2, 1))
    c["ssP_k"] = np.tile(ss * phase / var_scale, (2, 1))
    c["phase_h"] = np.tile(phase / var_scale, (2, 1))
    c["phase2"] = (phase[:, 0:1] * 0.0 + phase[:, 0:4])
    c["phase_o"] = np.tile(phase * var_scale, (2, 1))
    P = np.zeros((128, 128), dtype=np.float32)
    for h in range(2):
        for d in range(64):
            P[h * 64 + (d + 32) % 64, h * 64 + d] = 1.0
    c["rot32"] = P
    sl = np.arange(128)[:, None]
    ql = np.arange(512)[None, :]
    c["mask_l0"] = np.where(sl < (ql // 4), 0.0, -1e30).astype(np.float32)
    masks = []
    for d in range(4):
        masks.append(np.where(128 * d + sl <= ql, 0.0, -1e30).astype(np.float32))
    c["mask_l1"] = np.stack(masks)
    c["ident64_bf"] = np.eye(64).astype(ml_dtypes.bfloat16)
    c["ident128_f"] = np.eye(128).astype(np.float32)
    c["ident4_f"] = np.eye(4).astype(np.float32)
    c["ones_col_bf"] = np.ones((128, 1), dtype=ml_dtypes.bfloat16)
    c["ones_row_f"] = np.ones((1, 128), dtype=np.float32)
    c["eps"] = np.full((1, 1), EPS, dtype=np.float32)
    em = np.zeros((4, 512), dtype=np.float32)
    for e in range(4):
        em[e, 128 * e:128 * (e + 1)] = 1.0
    c["emat"] = em
    return c


CONSTS = host_consts()


ECHO_SEM = {"tensor": "PE_E", "vector": "DVE_E", "scalar": "ACT_E", "gpsimd": "GP_E"}
DMA_TAGS = {"LD": 4, "LDG": 4, "ST": 4}
CC_SEMS = ["CC0", "CC1", "CC2"]
ALL_SEMS = (["PE", "ACT", "DVE", "GP"] + list(ECHO_SEM.values())
            + [f"{t}{i}" for t, n in DMA_TAGS.items() for i in range(n)]
            + CC_SEMS)
RG = [[0, 1, 2, 3], [4, 5, 6, 7]]


class Emitter:

    def __init__(self):
        self.ops = {e: [] for e in ENGINES}
        self.cnt = {}
        self.hwm = {e: {} for e in ENGINES}
        self.echoed = {e: (0, 0) for e in ENGINES}
        self.dma_idx = {t: 0 for t in DMA_TAGS}
        self.dma_cnt = {}

    def _emit_wait(self, eng, sem, val):
        if self.hwm[eng].get(sem, 0) < val:
            self.ops[eng].append(("w", sem, val))
            self.hwm[eng][sem] = val

    def _wait(self, eng, tok):
        if tok is None:
            return
        sem, val, peng = tok
        if sem in DMA_TAGS:
            for i in range(DMA_TAGS[sem]):
                ps = f"{sem}{i}"
                c = self.dma_cnt.get(ps, 0)
                if c:
                    self._emit_wait(eng, ps, c)
            return
        if peng == eng or peng in ("__none__", "sync", "tensor"):
            self._emit_wait(eng, sem, val)
            return
        cov, esem_val = self.echoed[peng]
        if cov < val:
            esem = ECHO_SEM[peng]
            self._emit_wait(peng, sem, val)
            self.cnt[esem] = self.cnt.get(esem, 0) + 1
            self.ops[peng].append(("o", lambda e: e.nop(), (esem, self.cnt[esem], peng), 1))
            self.echoed[peng] = (val, self.cnt[esem])
            esem_val = self.cnt[esem]
        self._emit_wait(eng, ECHO_SEM[peng], esem_val)

    def o(self, eng, fn, after=(), tag=None, inc=1):
        for tok in after:
            self._wait(eng, tok)
        tokout = None
        if tag is not None and tag in DMA_TAGS:
            i = self.dma_idx[tag] % DMA_TAGS[tag]
            self.dma_idx[tag] += 1
            ps = f"{tag}{i}"
            prev = self.dma_cnt.get(ps, 0)
            if prev:
                self._emit_wait(eng, ps, prev)
            self.dma_cnt[ps] = prev + 16
            tokout = (tag, self.dma_idx[tag], eng)
            self.ops[eng].append(("o", fn, (ps, self.dma_cnt[ps], eng), 16))
            return tokout
        if tag is not None:
            self.cnt[tag] = self.cnt.get(tag, 0) + inc
            tokout = (tag, self.cnt[tag], eng)
        self.ops[eng].append(("o", fn, tokout, inc))
        return tokout

    def dma_barrier(self, tag):
        return (tag, self.dma_idx.get(tag, 0), "__dma__")

    def engine_tok(self, eng):
        sem = {"tensor": "PE", "vector": "DVE", "scalar": "ACT", "gpsimd": "GP"}[eng]
        v = self.cnt.get(sem, 0)
        return (sem, v, eng) if v else None

    def phase_barrier(self):
        toks = [self.dma_barrier(t) for t in DMA_TAGS]
        toks += [self.engine_tok(e) for e in ("tensor", "vector", "scalar", "gpsimd")]
        return [t for t in toks if t is not None]

    def echo(self, toks, via, tag):
        return self.o(via, lambda e: e.nop(), after=toks, tag=tag)

    def barrier_tok(self, tag):
        if tag not in self.cnt:
            return None
        return (tag, self.cnt[tag], "__none__")

    def build(self, block, sems):
        for e in ENGINES:
            ops = self.ops[e]
            if not ops:
                continue
            dec = getattr(block, e)

            def mk(ops_list):
                def _(eng):
                    for op in ops_list:
                        if op[0] == "w":
                            eng.wait_ge(sems[op[1]], op[2])
                        else:
                            _, fn, tokout, inc = op
                            inst = fn(eng)
                            if tokout is not None:
                                inst.then_inc(sems[tokout[0]], inc)
                return _
            dec(mk(ops))


def emit_ln_stats(em, sb, ps, xsrc_fn, W, deps, tagpfx, ln_scr, raw_mean=False):
    xbf = ln_scr["xbf"]
    rows = ln_scr["rows"]
    pa, pb = ps["stat_a"], ps["stat_b"]
    casts = []
    for c in range(8):
        xa = xsrc_fn(c)
        casts.append(em.o("vector", lambda v, xa=xa, c=c: v.tensor_copy(xbf[:, c * W:(c + 1) * W], xa),
                          after=deps, tag="DVE"))
    mean_mms = []
    for c in range(8):
        mean_mms.append(em.o("tensor", lambda t, c=c: t.matmul(
            pa[0:1, 0:W], sb["ones_col_bf"][:], xbf[:, c * W:(c + 1) * W],
            start=(c == 0), stop=(c == 7)), after=[casts[c]], tag="PE"))
    mm_s = None
    sqs = []
    for c in range(8):
        xa = xsrc_fn(c)
        sq = em.o("scalar", lambda s, xa=xa, c=c: s.activation(
            xbf[:, c * W:(c + 1) * W], xa, AF.Square),
            after=[mean_mms[c]] + list(deps), tag="ACT")
        sqs.append(sq)
    for c in range(8):
        mm_s = em.o("tensor", lambda t, c=c: t.matmul(
            pb[0:1, 0:W], sb["ones_col_bf"][:], xbf[:, c * W:(c + 1) * W],
            start=(c == 0), stop=(c == 7)), after=[sqs[c]], tag="PE")
    mm_m = mean_mms[-1]
    m_row, ms_row, t_row, rstd_row = rows
    d1 = em.o("vector", lambda v: v.tensor_scalar(m_row[0:1, 0:W], pa[0:1, 0:W], 1.0 / D, None, ALU.mult),
              after=[mm_m], tag="DVE")
    d2 = em.o("vector", lambda v: v.tensor_scalar(ms_row[0:1, 0:W], pb[0:1, 0:W], 1.0 / D, None, ALU.mult),
              after=[mm_s], tag="DVE")
    d3 = em.o("vector", lambda v: v.tensor_mul(t_row[0:1, 0:W], m_row[0:1, 0:W], m_row[0:1, 0:W]),
              after=[d1], tag="DVE")
    d4 = em.o("vector", lambda v: v.tensor_sub(ms_row[0:1, 0:W], ms_row[0:1, 0:W], t_row[0:1, 0:W]),
              after=[d2, d3], tag="DVE")
    a1 = em.o("scalar", lambda s: s.activation(ms_row[0:1, 0:W], ms_row[0:1, 0:W], AF.Ln,
                                               bias=sb["eps"][0:1, 0:1]),
              after=[d4], tag="ACT")
    a2 = em.o("scalar", lambda s: s.activation(rstd_row[0:1, 0:W], ms_row[0:1, 0:W], AF.Exp, scale=-0.5),
              after=[a1], tag="ACT")
    if raw_mean:
        bc_src, bc_deps = m_row, [d1, d2]
    else:
        d5 = em.o("vector", lambda v: v.tensor_mul(t_row[0:1, 0:W], m_row[0:1, 0:W], rstd_row[0:1, 0:W]),
                  after=[a2], tag="DVE")
        bc_src, bc_deps = t_row, [d5, d2]
    bm_r = em.o("tensor", lambda t: t.matmul(pa[:, 0:W], sb["ones_row_f"][:], rstd_row[0:1, 0:W],
                                             start=True, stop=True),
                after=[a2, d1], tag="PE")
    bm_mr = em.o("tensor", lambda t: t.matmul(pb[:, 0:W], sb["ones_row_f"][:], bc_src[0:1, 0:W],
                                              start=True, stop=True),
                 after=bc_deps, tag="PE")
    bc_r, bc_mr = ln_scr["bc_r"], ln_scr["bc_mr"]
    e1 = em.o("vector", lambda v: v.tensor_copy(bc_r[:, 0:W], pa[:, 0:W]), after=[bm_r], tag="DVE")
    e2 = em.o("vector", lambda v: v.tensor_copy(bc_mr[:, 0:W], pb[:, 0:W]), after=[bm_mr], tag="DVE")
    return (e1, e2), bc_r, bc_mr



def emit_phase2(nc, em, G, bar, cc_tok):
    T = TQ + 4
    xT = G["xqT"]
    wrT, wkT, wvT = G["wrT"], G["wkT"], G["wvT"]
    mu_r, mu_k = G["mu_r"], G["mu_k"]
    ones_col_bf, ones_row_f, eps = G["ones_col_bf"], G["ones_row_f"], G["eps"]
    cc1o, bmask = G["cc1o"], G["bmask"]
    x2T, h3T = G["x2scr"], G["cc2i"]
    with ExitStack() as stk:
        en = stk.enter_context

        def sbuf(name, shape, dt):
            return nc.alloc_sbuf_tensor("m_" + name, list(shape), dt)

        xb = sbuf("xb", [128, 8 * T], F32)
        h2 = sbuf("h2", [128, 8 * 516], BF16)
        wr_sb = sbuf("wr_sb", [128, 8 * D], BF16)
        wk_sb = sbuf("wk_sb", [128, 8 * D], BF16)
        wv_sb = sbuf("wv_sb", [128, 8 * D], BF16)
        mur_sb = sbuf("mur_sb", [128, 8], F32)
        muk_sb = sbuf("muk_sb", [128, 8], F32)
        omr_sb = sbuf("omr_sb", [128, 8], F32)
        omk_sb = sbuf("omk_sb", [128, 8], F32)
        bmask_sb = sbuf("bmask_sb", [128, 1], F32)
        W = 512
        y0t = sbuf("y0t", [128, 8 * W], F32)
        y0b = sbuf("y0b", [128, 8 * 4], F32)
        xrb = sbuf("xrb", [128, 8 * W], BF16)
        xkb = sbuf("xkb", [128, 8 * W], BF16)
        rfb = sbuf("rfb", [128, 8 * W], BF16)
        kact = sbuf("kact", [128, 8 * W], BF16)
        krelu = sbuf("krelu", [128, 2 * W], F32)
        tmix = krelu
        tmp = sbuf("tmp", [128, W], F32)

        ln_scr = {
            "xbf": sbuf("ln_xbf", [128, 8 * W], BF16),
            "rows": [sbuf(f"ln_r{i}", [1, W], F32) for i in range(4)],
            "bc_r": sbuf("ln_bcr", [128, W], F32),
            "bc_mr": sbuf("ln_bcmr", [128, W], F32),
        }
        h3t = ln_scr["xbf"]
        consts = {
            "ones_col_bf": sbuf("c_ones_col", [128, 1], BF16),
            "ones_row_f": sbuf("c_ones_row", [1, 128], F32),
            "eps": sbuf("c_eps", [1, 1], F32),
        }
        p0 = en(nc.psum_tensor("m_p0", [128, W], F32))
        p1 = en(nc.psum_tensor("m_p1", [128, W], F32))
        pa = en(nc.psum_tensor("m_pa", [128, W], F32))
        pb = en(nc.psum_tensor("m_pb", [128, W], F32))
        ps = {"stat_a": pa, "stat_b": pb}

        def dma_in(dst, src, q="sync", slow=False, after=()):
            return em.o(q, lambda e, dst=dst, src=src: e.dma_start(
                out=dst, in_=src, allow_slow_non_contiguous=slow),
                after=after, tag="LD" if q == "sync" else "LDG", inc=16)

        ld_x = dma_in(xb.ap().rearrange("p (c t) -> p c t", c=8),
                      xT.rearrange("(c p) t -> p c t", p=128), after=bar)
        ld_wr = dma_in(wr_sb.ap().rearrange("p (c o) -> p c o", c=8),
                       wrT.rearrange("(c p) o -> p c o", p=128), q="gpsimd", after=bar)
        ld_wk = dma_in(wk_sb.ap().rearrange("p (c o) -> p c o", c=8),
                       wkT.rearrange("(c p) o -> p c o", p=128), q="gpsimd", after=bar)
        ld_wv = dma_in(wv_sb.ap().rearrange("p (c o) -> p c o", c=8),
                       wvT.rearrange("(c p) o -> p c o", p=128), q="gpsimd", after=bar)
        ld_mur = dma_in(mur_sb[:], mu_r, after=bar)
        ld_muk = dma_in(muk_sb[:], mu_k, after=bar)
        ld_bm = dma_in(bmask_sb[:], bmask, after=bar)
        ld_oc = dma_in(consts["ones_col_bf"][:], ones_col_bf, after=bar)
        ld_or = dma_in(consts["ones_row_f"][:], ones_row_f, after=bar)
        ld_eps = dma_in(consts["eps"][:], eps, after=bar)
        ld_y0b = dma_in(y0b.ap().rearrange("p (c j) -> p c j", j=4)[:, :, 0:1],
                        cc1o[:, 0:1].rearrange("(c p) t -> p c t", p=128),
                        slow=True, after=bar + [cc_tok])
        yb_toks = [ld_y0b]
        for j in range(1, 4):
            yb_toks.append(em.o("vector", lambda v, j=j: v.tensor_copy(
                y0b[:, j::4], y0b[:, 0::4]), after=[ld_y0b] + bar, tag="DVE"))
        yb_mask = em.o("vector", lambda v: v.tensor_scalar(
            y0b[:], y0b[:], bmask_sb[:, 0:1], None, ALU.mult),
            after=yb_toks + [ld_bm], tag="DVE")

        om1 = em.o("vector", lambda v: v.tensor_scalar(omr_sb[:], mur_sb[:], -1.0, 1.0, ALU.mult, ALU.add),
                   after=[ld_mur], tag="DVE")
        om2 = em.o("vector", lambda v: v.tensor_scalar(omk_sb[:], muk_sb[:], -1.0, 1.0, ALU.mult, ALU.add),
                   after=[ld_muk], tag="DVE")

        bank_free = {0: None, 1: None}

        def proj_tile(dst_write, rhs_fn, w_sb, Wd, t0, lora=None, deps=(),
                      extra_evac_dep=()):
            toks = []
            tl = None
            if lora is not None:
                a_sb, b_sb, plp = lora
                for c in range(8):
                    tl = em.o("tensor", lambda t, c=c: t.matmul(
                        plp[:, 0:Wd], a_sb[:, c * LORA:(c + 1) * LORA], rhs_fn(c),
                        start=(c == 0), stop=(c == 7)), after=deps, tag="PE")
                tl = em.o("vector", lambda v: v.tensor_copy(tlora[:, 0:Wd], plp[:, 0:Wd]),
                          after=[tl], tag="DVE")
            for oc in range(8):
                pp = p0 if oc % 2 == 0 else p1
                mm = None
                prev = bank_free[oc % 2]
                dep0 = list(deps) + ([prev] if prev else [])
                for c in range(8):
                    mm = em.o("tensor", lambda t, c=c, oc=oc, pp=pp: t.matmul(
                        pp[:, 0:Wd], w_sb[:, c * D + oc * 128: c * D + (oc + 1) * 128],
                        rhs_fn(c), start=(c == 0), stop=(c == 7 and lora is None)),
                        after=dep0 if c == 0 else (), tag="PE")
                if lora is not None:
                    mm = em.o("tensor", lambda t, oc=oc, pp=pp: t.matmul(
                        pp[:, 0:Wd], bo_sb[:, oc * 128:(oc + 1) * 128], tlora[:, 0:Wd],
                        start=False, stop=True), after=[tl], tag="PE")
                ev = dst_write(oc, pp, [mm] + list(extra_evac_dep))
                bank_free[oc % 2] = ev
                toks.append(ev)
            return toks

        def x_chunk(c, off, Wd):
            return xb[:, c * T + off: c * T + off + Wd]

        def h2_chunk(c, off, Wd):
            return h2[:, c * 516 + off: c * 516 + off + Wd]

        def do_tile(t0, Wd, ysrc_chunk, deps_extra=(), h2_off=4):
            deps = [ld_x] + list(deps_extra)
            y_toks = []
            prev = None
            for oc in range(8):
                tk = em.o("vector", lambda v, oc=oc: v.tensor_add(
                    x_chunk(oc, t0, Wd), ysrc_chunk(oc), x_chunk(oc, t0, Wd)),
                    after=deps + ([prev] if prev else []), tag="DVE")
                prev = tk
                y_toks.append(tk)
            (l1, l2), bc_r, bc_mr = emit_ln_stats(
                em, consts, ps, lambda c: x_chunk(c, t0, Wd), Wd,
                y_toks + [ld_oc, ld_or, ld_eps], "", ln_scr)
            h_toks = []
            prev_hh = None
            for c in range(8):
                hm = em.o("vector", lambda v, c=c: v.tensor_mul(
                    tmp[:, 0:Wd], x_chunk(c, t0, Wd), bc_r[:, 0:Wd]),
                    after=[l1] + ([prev_hh] if prev_hh else []), tag="DVE")
                hh = em.o("vector", lambda v, c=c: v.tensor_sub(
                    h2_chunk(c, h2_off, Wd), tmp[:, 0:Wd], bc_mr[:, 0:Wd]),
                    after=[l2, hm], tag="DVE")
                prev_hh = hh
                h_toks.append(hh)
            return h_toks, y_toks

        bt, _ = do_tile(TQ, 4, lambda c: y0b[:, c * 4:(c + 1) * 4],
                        deps_extra=[yb_mask], h2_off=0)

        ffn_weight_deps = [ld_wr, ld_wk, ld_wv, om1, om2]
        prev_ffn_reads = []
        prev_h3_out = []
        prev_y_toks = []
        for tt in range(4):
            t0 = tt * W
            ldt = em.o("sync", lambda e, t0=t0: e.dma_start(
                out=y0t.ap().rearrange("p (c t) -> p c t", c=8),
                in_=cc1o[:, 8 + t0:8 + t0 + W].rearrange("(c p) t -> p c t", p=128)),
                after=prev_y_toks + [cc_tok] + bar, tag="LD", inc=16)
            h_toks, prev_y_toks = do_tile(t0, W, lambda c: y0t[:, c * W:(c + 1) * W],
                                          deps_extra=[ldt] + bt + prev_h3_out)
            mix_toks = []
            for c in range(8):
                for (mu_s, om_s, dstb) in ((mur_sb, omr_sb, xrb), (muk_sb, omk_sb, xkb)):
                    m1 = em.o("vector", lambda v, c=c, om_s=om_s: v.tensor_scalar(
                        tmix[:, 0:W], h2_chunk(c, 3, W), om_s[:, c:c + 1], None, ALU.mult),
                        after=h_toks + (bt if tt == 0 else []) + prev_ffn_reads + prev_h3_out
                        + (mix_toks[-1:] if mix_toks else []), tag="DVE")
                    m2 = em.o("vector", lambda v, c=c, mu_s=mu_s, dstb=dstb: v.scalar_tensor_tensor(
                        dstb[:, c * W:(c + 1) * W], h2_chunk(c, 4, W), mu_s[:, c:c + 1],
                        tmix[:, 0:W], ALU.mult, ALU.add), after=[m1], tag="DVE")
                    mix_toks.append(m2)
            for c in range(8):
                em.o("vector", lambda v, c=c: v.tensor_copy(h2_chunk(c, 3, 1), h2_chunk(c, 3 + W, 1)),
                     after=mix_toks, tag="DVE")

            def evac_r(oc, pp, after):
                return em.o("scalar", lambda s, oc=oc, pp=pp: s.activation(
                    rfb[:, oc * W:(oc + 1) * W], pp[:, 0:W], AF.Sigmoid),
                    after=after, tag="ACT")

            r_toks = proj_tile(evac_r, lambda c: xrb[:, c * W:(c + 1) * W],
                               wr_sb, W, t0, deps=mix_toks + ffn_weight_deps)

            krelu_last = {0: None, 1: None}

            def evac_k(oc, pp, after):
                prev_sq = krelu_last.get(oc % 2)
                t1 = em.o("vector", lambda v, oc=oc, pp=pp: v.tensor_scalar(
                    krelu[:, (oc % 2) * W:(oc % 2 + 1) * W], pp[:, 0:W], 0.0, None, ALU.max),
                    after=list(after) + ([prev_sq] if prev_sq else []), tag="DVE")
                sq = em.o("scalar", lambda s, oc=oc: s.activation(
                    kact[:, oc * W:(oc + 1) * W], krelu[:, (oc % 2) * W:(oc % 2 + 1) * W], AF.Square),
                    after=[t1], tag="ACT")
                krelu_last[oc % 2] = sq
                return sq

            k_toks = proj_tile(evac_k, lambda c: xkb[:, c * W:(c + 1) * W],
                               wk_sb, W, t0, deps=mix_toks + ffn_weight_deps)

            fprev = [None]

            def evac_f(oc, pp, after):
                t1 = em.o("vector", lambda v, oc=oc, pp=pp: v.tensor_mul(
                    tmp[:, 0:W], pp[:, 0:W], rfb[:, oc * W:(oc + 1) * W]),
                    after=list(after) + r_toks + ([fprev[0]] if fprev[0] else []), tag="DVE")
                fadd = em.o("vector", lambda v, oc=oc, t0=t0: v.tensor_add(
                    x_chunk(oc, t0, W), x_chunk(oc, t0, W), tmp[:, 0:W]),
                    after=[t1], tag="DVE")
                fprev[0] = fadd
                return fadd

            f_toks = proj_tile(evac_f, lambda c: kact[:, c * W:(c + 1) * W],
                               wv_sb, W, t0, deps=k_toks)
            prev_ffn_reads = f_toks

            (l1, l2), bc_r, bc_mr = emit_ln_stats(
                em, consts, ps, lambda c: x_chunk(c, t0, W), W, f_toks, "", ln_scr)
            out_toks = []
            prev_hh = None
            for c in range(8):
                hm = em.o("vector", lambda v, c=c, t0=t0: v.tensor_mul(
                    tmp[:, 0:W], x_chunk(c, t0, W), bc_r[:, 0:W]),
                    after=[l1] + prev_h3_out + ([prev_hh] if prev_hh else []), tag="DVE")
                hh = em.o("vector", lambda v, c=c: v.tensor_sub(
                    h3t[:, c * W:(c + 1) * W], tmp[:, 0:W], bc_mr[:, 0:W]),
                    after=[l2, hm], tag="DVE")
                prev_hh = hh
                out_toks.append(hh)
            ek1 = em.echo(out_toks, "scalar", "ACT")
            st1 = em.o("gpsimd", lambda e, t0=t0: e.dma_start(
                out=h3T[:, t0:t0 + W].rearrange("(c p) t -> p c t", p=128),
                in_=h3t.ap().rearrange("p (c t) -> p c t", c=8)),
                after=[ek1], tag="ST", inc=16)
            ek2 = em.echo(f_toks, "scalar", "ACT")
            st2 = em.o("gpsimd", lambda e, t0=t0: e.dma_start(
                out=x2T[:, t0:t0 + W].rearrange("(c p) t -> p c t", p=128),
                in_=xb.ap().rearrange("p (c t) -> p c t", c=8)[:, :, t0:t0 + W]),
                after=[ek2], tag="ST", inc=16)
            prev_h3_out = [st1]



def emit_phase4(nc, em, G, bar, cc_tok):
    T = TQ
    W = 512
    BIG = 30000.0
    x2T = G["x2scr"]
    cc3o = G["cc3o"]
    ob, w1, b1c, w2, b2c = G["ob"], G["w1"], G["b1c"], G["w2"], G["b2c"]
    wr, brc, id4, id128, emat = G["wr"], G["brc"], G["id4"], G["id128"], G["emat"]
    ones_col_bf, ones_row_f, eps = G["ones_col_bf"], G["ones_row_f"], G["eps"]
    x4T = G["x4T"]
    x3scr = G["x3scr"]

    with ExitStack() as stk:
        en = stk.enter_context

        def alloc(name, shape, dt):
            return nc.alloc_sbuf_tensor("f_" + name, list(shape), dt)

        xf = alloc("xf", [128, 8 * T], BF16)
        gT = alloc("gT", [4, T], F32)
        b1_sb = alloc("b1_sb", [128, E * 32], F32)
        b2_sb = alloc("b2_sb", [128, 32], F32)
        gel_sb = alloc("gel_sb", [128, E * 32], BF16)
        c_sb = alloc("c_sb", [128, 32], F32)
        cT_sb = alloc("cT_sb", [4, 8 * 128], F32)
        csum_sb = alloc("csum_sb", [128, 8], F32)
        rs_sb = alloc("rs_sb", [128, 16], F32)
        id4_sb = alloc("id4_sb", [4, 4], F32)
        id128_sb = alloc("id128_sb", [128, 128], F32)
        emat_sb = alloc("emat_sb", [4, 512], F32)
        wr_sb = alloc("wr_sb", [128, 8 * 4], F32)
        brc_sb = alloc("brc_sb", [4, 1], F32)
        consts = {
            "ones_col_bf": alloc("c_ones_col", [128, 1], BF16),
            "ones_row_f": alloc("c_ones_row", [1, 128], F32),
            "eps": alloc("c_eps", [1, 1], F32),
        }
        p0 = en(nc.psum_tensor("f_p0", [128, W], F32))
        p1 = en(nc.psum_tensor("f_p1", [128, W], F32))
        pa = en(nc.psum_tensor("f_pa", [128, W], F32))
        pb = en(nc.psum_tensor("f_pb", [128, W], F32))
        pc = en(nc.psum_tensor("f_pc", [128, 16], F32))
        pt = en(nc.psum_tensor("f_pt", [128, 128], F32))
        ps = {"stat_a": pa, "stat_b": pb}

        def dma_in(dst, src, q="sync", slow=False, after=()):
            return em.o(q, lambda e, dst=dst, src=src: e.dma_start(
                out=dst, in_=src, allow_slow_non_contiguous=slow),
                after=after, tag="LD" if q == "sync" else "LDG", inc=16)

        ob_sb = alloc("ob_sb", [128, 8], F32)
        ld_ob = dma_in(ob_sb[:], ob, after=bar)
        ld_wr = dma_in(wr_sb.ap().rearrange("p (c o) -> p c o", c=8),
                       wr.rearrange("(c p) o -> p c o", p=128), after=bar)
        ld_br = dma_in(brc_sb[:], brc, after=bar)
        ld_id4 = dma_in(id4_sb[:], id4, after=bar)
        ld_id128 = dma_in(id128_sb[:], id128, after=bar)
        ld_em = dma_in(emat_sb[:], emat, after=bar)
        ld_b1 = dma_in(b1_sb[:], b1c, after=bar)
        ld_b2 = dma_in(b2_sb[:], b2c, after=bar)
        ld_oc = dma_in(consts["ones_col_bf"][:], ones_col_bf, after=bar)
        ld_or = dma_in(consts["ones_row_f"][:], ones_row_f, after=bar)
        ld_eps = dma_in(consts["eps"][:], eps, after=bar)
        const_deps = [ld_oc, ld_or, ld_eps]
        gel_tok = em.o("scalar", lambda sc: sc.activation(gel_sb[:], b1_sb[:], AF.Gelu),
                       after=[ld_b1], tag="ACT")

        markA = nc.sbuf_base
        x3t = alloc("x3t", [128, 8 * W], F32)
        y1t = alloc("y1t", [128, 8 * W], F32)
        ut = alloc("ut", [128, 8 * W], F32)
        lg_sb = alloc("lg_sb", [4, W], F32)
        lgtm = alloc("lgtm", [128, 16], F32)
        m1b = alloc("m1b", [128, 4], F32)
        dbuf = alloc("dbuf", [128, 16], F32)
        sgb = alloc("sgb", [128, 16], F32)
        q1b = alloc("q1b", [128, 16], F32)
        zb = alloc("zb", [128, 16], F32)
        m2b = alloc("m2b", [128, 4], F32)
        d2b = alloc("d2b", [128, 16], F32)
        nrb = alloc("nrb", [128, 16], F32)
        selb = alloc("selb", [128, 16], F32)
        eb = alloc("eb", [128, 16], F32)
        g0b = alloc("g0b", [128, 16], F32)
        nsb = alloc("nsb", [128, 4], F32)
        rcb = alloc("rcb", [128, 4], F32)
        gtm = alloc("gtm", [128, 16], F32)
        ln_scr = {
            "xbf": alloc("ln_xbf", [128, 8 * W], BF16),
            "rows": [alloc(f"ln_r{i}", [1, W], F32) for i in range(4)],
            "bc_r": alloc("ln_bcr", [128, W], F32),
            "bc_mr": alloc("ln_bcmr", [128, W], F32),
        }
        gate_toks = []
        xf_toks = []
        x3st_toks = []
        prev_tile_toks = []
        bank_free = {}
        for tt in range(4):
            t0 = tt * W
            bar_tile = [t for t in (em.engine_tok("tensor"), em.engine_tok("vector"),
                                    em.engine_tok("scalar")) if t] + \
                ([em.dma_barrier("ST")] if em.dma_idx.get("ST") else []) + \
                (bar if tt == 0 else [])
            ldx = em.o("sync", lambda e, t0=t0: e.dma_start(
                out=x3t.ap().rearrange("p (c t) -> p c t", c=8),
                in_=x2T[:, t0:t0 + W].rearrange("(c p) t -> p c t", p=128)),
                after=bar_tile, tag="LD", inc=16)
            ldo = em.o("sync", lambda e, t0=t0: e.dma_start(
                out=y1t.ap().rearrange("p (c t) -> p c t", c=8),
                in_=cc3o[:, t0:t0 + W].rearrange("(c p) t -> p c t", p=128)),
                after=bar_tile + [cc_tok], tag="LD", inc=16)
            proj_toks = []
            prev_pj = None
            for oc in range(8):
                ev = em.o("vector", lambda v, oc=oc: v.scalar_tensor_tensor(
                    x3t[:, oc * W:(oc + 1) * W], y1t[:, oc * W:(oc + 1) * W],
                    ob_sb[:, oc:oc + 1],
                    x3t[:, oc * W:(oc + 1) * W], ALU.add, ALU.add),
                    after=[ldo, ldx, ld_ob] + prev_tile_toks +
                    ([prev_pj] if prev_pj else []), tag="DVE")
                prev_pj = ev
                proj_toks.append(ev)
            st_x3 = em.o("gpsimd", lambda e, t0=t0: e.dma_start(
                out=x3scr[:, t0:t0 + W].rearrange("(c p) t -> p c t", p=128),
                in_=x3t.ap().rearrange("p (c t) -> p c t", c=8)),
                after=[em.echo(proj_toks, "scalar", "ACT")], tag="ST", inc=16)
            x3st_toks.append(st_x3)
            (l1, l2), bc_r, bc_mr = emit_ln_stats(
                em, consts, ps, lambda c: x3t[:, c * W:(c + 1) * W], W,
                proj_toks + const_deps, "", ln_scr)
            u_toks = []
            for c in range(8):
                um = em.o("vector", lambda v, c=c: v.tensor_mul(
                    ut[:, c * W:(c + 1) * W], x3t[:, c * W:(c + 1) * W], bc_r[:, 0:W]),
                    after=[l1], tag="DVE")
                ux = em.o("vector", lambda v, c=c, t0=t0: v.tensor_sub(
                    xf[:, c * T + t0: c * T + t0 + W], ut[:, c * W:(c + 1) * W], bc_mr[:, 0:W]),
                    after=[l2, um], tag="DVE")
                xf_toks.append(ux)
                uc = em.o("vector", lambda v, c=c, t0=t0: v.tensor_copy(
                    ut[:, c * W:(c + 1) * W], xf[:, c * T + t0: c * T + t0 + W]),
                    after=[ux], tag="DVE")
                u_toks.append(uc)
            mm = None
            for c in range(8):
                mm = em.o("tensor", lambda t, c=c: t.matmul(
                    pa[0:4, 0:W], wr_sb[:, c * 4:(c + 1) * 4], ut[:, c * W:(c + 1) * W],
                    start=(c == 0), stop=(c == 7)),
                    after=[u_toks[c], ld_wr] if c == 0 else [u_toks[c]], tag="PE")
            lg = em.o("vector", lambda v: v.tensor_scalar(
                lg_sb[0:4, 0:W], pa[0:4, 0:W], brc_sb[0:4, 0:1], None, ALU.add),
                after=[mm, ld_br], tag="DVE")
            gs = [slice(g * 4, (g + 1) * 4) for g in range(4)]
            tr_toks = []
            for g in range(4):
                tr = em.o("tensor", lambda t, g=g: t.transpose(
                    pt[0:128, g * 4:(g + 1) * 4], lg_sb[0:4, g * 128:(g + 1) * 128],
                    id4_sb[:]),
                    after=[lg, ld_id4] + (tr_toks[-1:] if tr_toks else []) +
                    (gate_toks[-1:] if gate_toks else []), tag="PE")
                tr_toks.append(em.o("vector", lambda v, g=g: v.tensor_copy(
                    lgtm[:, gs[g]], pt[0:128, g * 4:(g + 1) * 4]),
                    after=[tr], tag="DVE"))
            t_m1 = [em.o("vector", lambda v, g=g: v.reduce_max(
                m1b[:, g:g + 1], lgtm[:, gs[g]], axis=AX.X), after=[tr_toks[g]], tag="DVE")
                for g in range(4)]
            t_d = [em.o("vector", lambda v, g=g: v.tensor_scalar(
                dbuf[:, gs[g]], lgtm[:, gs[g]], m1b[:, g:g + 1], None, ALU.subtract),
                after=[t_m1[g]], tag="DVE") for g in range(4)]
            t_sg = [em.o("scalar", lambda sc, g=g: sc.sign(
                sgb[:, gs[g]], dbuf[:, gs[g]], bias=0.0), after=[t_d[g]], tag="ACT")
                for g in range(4)]
            t_q1 = [em.o("vector", lambda v, g=g: v.tensor_scalar(
                q1b[:, gs[g]], sgb[:, gs[g]], 1.0, -BIG, ALU.add, ALU.mult),
                after=[t_sg[g]], tag="DVE") for g in range(4)]
            t_z = [em.o("vector", lambda v, g=g: v.tensor_add(
                zb[:, gs[g]], dbuf[:, gs[g]], q1b[:, gs[g]]),
                after=[t_q1[g], t_d[g]], tag="DVE") for g in range(4)]
            t_m2 = [em.o("vector", lambda v, g=g: v.reduce_max(
                m2b[:, g:g + 1], zb[:, gs[g]], axis=AX.X), after=[t_z[g]], tag="DVE")
                for g in range(4)]
            t_d2 = [em.o("vector", lambda v, g=g: v.tensor_scalar(
                d2b[:, gs[g]], dbuf[:, gs[g]], m2b[:, g:g + 1], None, ALU.subtract),
                after=[t_m2[g]], tag="DVE") for g in range(4)]
            t_s2 = [em.o("scalar", lambda sc, g=g: sc.sign(
                d2b[:, gs[g]], d2b[:, gs[g]], bias=0.0), after=[t_d2[g]], tag="ACT")
                for g in range(4)]
            t_nr = [em.o("vector", lambda v, g=g: v.tensor_scalar(
                nrb[:, gs[g]], d2b[:, gs[g]], -1.0, 0.0, ALU.mult, ALU.max),
                after=[t_s2[g]], tag="DVE") for g in range(4)]
            t_sel = [em.o("vector", lambda v, g=g: v.tensor_scalar(
                selb[:, gs[g]], nrb[:, gs[g]], -1.0, 1.0, ALU.mult, ALU.add),
                after=[t_nr[g]], tag="DVE") for g in range(4)]
            t_e = [em.o("scalar", lambda sc, g=g: sc.activation(
                eb[:, gs[g]], dbuf[:, gs[g]], AF.Exp), after=[t_d[g]], tag="ACT")
                for g in range(4)]
            t_g0 = [em.o("vector", lambda v, g=g: v.tensor_mul(
                g0b[:, gs[g]], eb[:, gs[g]], selb[:, gs[g]]),
                after=[t_e[g], t_sel[g]], tag="DVE") for g in range(4)]
            t_ns = [em.o("vector", lambda v, g=g: v.reduce_sum(
                nsb[:, g:g + 1], g0b[:, gs[g]], axis=AX.X), after=[t_g0[g]], tag="DVE")
                for g in range(4)]
            t_rc = [em.o("vector", lambda v, g=g: v.reciprocal(
                rcb[:, g:g + 1], nsb[:, g:g + 1]), after=[t_ns[g]], tag="DVE")
                for g in range(4)]
            t_gg = [em.o("vector", lambda v, g=g: v.tensor_scalar(
                gtm[:, gs[g]], g0b[:, gs[g]], rcb[:, g:g + 1], None, ALU.mult),
                after=[t_rc[g]], tag="DVE") for g in range(4)]
            for g in range(4):
                trb = em.o("tensor", lambda t, g=g: t.transpose(
                    pt[0:4, 0:128], gtm[:, gs[g]], id128_sb[:]),
                    after=[t_gg[g], ld_id128, tr_toks[-1]] +
                    (gate_toks[-1:] if gate_toks else []), tag="PE")
                gate_toks.append(em.o("vector", lambda v, g=g, t0=t0: v.tensor_copy(
                    gT[0:4, t0 + g * 128: t0 + (g + 1) * 128], pt[0:4, 0:128]),
                    after=[trb], tag="DVE"))
            prev_tile_toks = proj_toks + [lg]
        barA = em.phase_barrier()
        nc.sbuf_base = markA

        markB = nc.sbuf_base
        w1b = [alloc(f"w1b{i}", [128, 8 * 512], BF16) for i in range(2)]
        w2b = [alloc(f"w2b{i}", [128, 32 * 128], BF16) for i in range(2)]
        h1 = alloc("h1", [128, 32 * 1024], BF16)
        bct = alloc("bct", [128, 2 * W], F32)
        yt = alloc("yt", [128, 2 * W], F32)
        HB = 512
        acc_toks = []
        c_toks = []
        war_yt = {0: None, 1: None}
        ld_prev = {0: None, 1: None}
        ld2_prev = {0: None, 1: None}
        bct_prev = [None]
        for e in range(E):
            for half in range(2):
                h0 = half * 1024
                st1_toks = []
                for hb in range(8):
                    buf = w1b[hb % 2]
                    war = ld_prev[hb % 2]
                    ldw = em.o("gpsimd", lambda g, e=e, hb=hb, buf=buf: g.dma_start(
                        out=buf[:], in_=w1[e, hb]),
                        after=([war] if war else []) + barA, tag="LDG", inc=16)
                    hb_toks = []
                    for hc in range(4):
                        for tg in range(2):
                            slot1 = (hc * 2 + tg) % 2
                            pp = p0 if slot1 == 0 else p1
                            mm = None
                            for c in range(8):
                                mm = em.o("tensor", lambda t, c=c, hc=hc, tg=tg, buf=buf, pp=pp, h0=h0: t.matmul(
                                    pp[:, 0:W], buf[:, c * HB + hc * 128:c * HB + (hc + 1) * 128],
                                    xf[:, c * T + h0 + tg * W: c * T + h0 + (tg + 1) * W],
                                    start=(c == 0), stop=(c == 7)),
                                    after=[ldw] + barA +
                                    ([bank_free[slot1]] if bank_free.get(slot1) else [])
                                    if c == 0 else (), tag="PE")
                            evg = em.o("scalar", lambda sc, hb=hb, hc=hc, tg=tg, pp=pp, e=e: sc.activation(
                                h1[:, (hb * 4 + hc) * 1024 + tg * W: (hb * 4 + hc) * 1024 + (tg + 1) * W],
                                pp[:, 0:W], AF.Gelu,
                                bias=b1_sb[:, e * 32 + hb * 4 + hc: e * 32 + hb * 4 + hc + 1]),
                                after=[mm, ld_b1], tag="ACT")
                            bank_free[slot1] = evg
                            hb_toks.append(evg)
                            st1_toks.append(evg)
                    ld_prev[hb % 2] = em.echo(hb_toks, "scalar", "ACT")
                st1_bar = em.echo(st1_toks, "scalar", "ACT")
                btoks = []
                for tg in range(2):
                    bmm = em.o("tensor", lambda t, e=e, h0=h0, tg=tg: t.matmul(
                        pa[:, 0:W], emat_sb[:, e * 128:(e + 1) * 128],
                        gT[0:4, h0 + tg * W: h0 + (tg + 1) * W],
                        start=True, stop=True),
                        after=[ld_em] + barA + ([bct_prev[0]] if bct_prev[0] else [])
                        + btoks[-1:], tag="PE")
                    btoks.append(em.o("vector", lambda v, tg=tg: v.tensor_copy(
                        bct[:, tg * W:(tg + 1) * W], pa[:, 0:W]),
                        after=[bmm], tag="DVE"))
                bct_prev[0] = em.echo(btoks, "scalar", "ACT")
                for dc in range(8):
                    buf = w2b[dc % 2]
                    war = ld2_prev[dc % 2]
                    ldw2 = em.o("gpsimd", lambda g, e=e, dc=dc, buf=buf: g.dma_start(
                        out=buf[:], in_=w2[e, dc]),
                        after=([war] if war else []) + barA, tag="LDG", inc=16)
                    dc_reads = []
                    for tg in range(2):
                        pp = p0 if tg == 0 else p1
                        mm = None
                        for hc in range(32):
                            mm = em.o("tensor", lambda t, hc=hc, tg=tg, buf=buf, pp=pp: t.matmul(
                                pp[:, 0:W], buf[:, hc * 128:(hc + 1) * 128],
                                h1[:, hc * 1024 + tg * W: hc * 1024 + (tg + 1) * W],
                                start=(hc == 0), stop=(hc == 31)),
                                after=[ldw2, st1_bar] +
                                ([bank_free[tg]] if bank_free.get(tg) else [])
                                if hc == 0 else (), tag="PE")
                        slot = tg
                        tv = em.o("vector", lambda v, pp=pp, slot=slot: v.tensor_mul(
                            yt[:, slot * W:(slot + 1) * W], pp[:, 0:W], bct[:, slot * W:(slot + 1) * W]),
                            after=[mm, bct_prev[0]] + ([war_yt[slot]] if war_yt[slot] else []),
                            tag="DVE")
                        bank_free[slot] = tv
                        tv_e = em.echo([tv], "scalar", "ACT")
                        acc = em.o("gpsimd", lambda g, dc=dc, h0=h0, tg=tg, slot=slot: g.dma_start(
                            out=x3scr[dc * 128:(dc + 1) * 128, h0 + tg * W: h0 + (tg + 1) * W],
                            in_=yt[:, slot * W:(slot + 1) * W], accum_op=ALU.add),
                            after=[tv_e] + x3st_toks, tag="ST", inc=16)
                        acc_toks.append(acc)
                        war_yt[slot] = acc
                        dc_reads.append(tv)
                    if half == 0:
                        mmc = None
                        for hc in range(32):
                            mmc = em.o("tensor", lambda t, hc=hc, dc=dc, e=e, buf=buf: t.matmul(
                                pc[:, 4 + e:4 + e + 1], buf[:, hc * 128:(hc + 1) * 128],
                                gel_sb[:, e * 32 + hc:e * 32 + hc + 1],
                                start=(hc == 0), stop=(hc == 31)),
                                after=[ldw2, gel_tok] + (c_toks[-1:] if c_toks else [])
                                if hc == 0 else (), tag="PE")
                        c_toks.append(em.o("vector", lambda v, dc=dc, e=e: v.tensor_copy(
                            c_sb[:, dc * 4 + e:dc * 4 + e + 1], pc[:, 4 + e:4 + e + 1]),
                            after=[mmc], tag="DVE"))
                        dc_reads.append(c_toks[-1])
                    ld2_prev[dc % 2] = em.echo(dc_reads, "scalar", "ACT")
        barB = em.phase_barrier()
        nc.sbuf_base = markB

        x3c = alloc("x3c", [128, 8 * W], F32)
        cbar = em.echo(c_toks, "vector", "DVE")
        ct_toks = []
        for dc in range(8):
            trc = em.o("tensor", lambda t, dc=dc: t.transpose(
                pt[0:4, 0:128], c_sb[:, dc * 4:(dc + 1) * 4], id128_sb[:]),
                after=[cbar, ld_id128] + barB + (ct_toks[-2:-1] if ct_toks else []) +
                (gate_toks[-1:] if gate_toks else []), tag="PE")
            ct_toks.append(em.o("vector", lambda v, dc=dc: v.tensor_copy(
                cT_sb[0:4, dc * 128:(dc + 1) * 128], pt[0:4, 0:128]),
                after=[trc], tag="DVE"))
            rs1 = em.o("vector", lambda v, dc=dc: v.reduce_sum(
                rs_sb[:, dc:dc + 1], c_sb[:, dc * 4:(dc + 1) * 4], axis=AX.X),
                after=[cbar], tag="DVE")
            rs2 = em.o("vector", lambda v, dc=dc: v.reduce_sum(
                rs_sb[:, 8 + dc:8 + dc + 1], b2_sb[:, dc * 4:(dc + 1) * 4], axis=AX.X),
                after=[ld_b2], tag="DVE")
            ct_toks.append(em.o("vector", lambda v, dc=dc: v.tensor_add(
                csum_sb[:, dc:dc + 1], rs_sb[:, dc:dc + 1], rs_sb[:, 8 + dc:8 + dc + 1]),
                after=[rs1, rs2, rs1], tag="DVE"))
        prev_out = []
        for tt in range(4):
            t0 = tt * W
            ldx3 = em.o("sync", lambda e, t0=t0: e.dma_start(
                out=x3c.ap().rearrange("p (c t) -> p c t", c=8),
                in_=x3scr[:, t0:t0 + W].rearrange("(c p) t -> p c t", p=128)),
                after=barB + prev_out + [em.dma_barrier("ST")], tag="LD", inc=16)
            out_toks = []
            prev_o1 = None
            for dc in range(8):
                mmt = em.o("tensor", lambda t, dc=dc, t0=t0: t.matmul(
                    pb[:, 0:W], cT_sb[0:4, dc * 128:(dc + 1) * 128], gT[0:4, t0:t0 + W],
                    start=True, stop=True),
                    after=ct_toks + [ldx3] + ([prev_o1] if prev_o1 else []), tag="PE")
                o1_ = em.o("vector", lambda v, dc=dc: v.scalar_tensor_tensor(
                    x3c[:, dc * W:(dc + 1) * W], pb[:, 0:W], -1.0,
                    x3c[:, dc * W:(dc + 1) * W], ALU.mult, ALU.add),
                    after=[mmt, ldx3], tag="DVE")
                o2_ = em.o("vector", lambda v, dc=dc: v.tensor_scalar(
                    x3c[:, dc * W:(dc + 1) * W], x3c[:, dc * W:(dc + 1) * W],
                    csum_sb[:, dc:dc + 1], None, ALU.add),
                    after=[o1_], tag="DVE")
                prev_o1 = o1_
                out_toks.append(o2_)
            ek = em.echo(out_toks, "scalar", "ACT")
            sto = em.o("gpsimd", lambda e, t0=t0: e.dma_start(
                out=x4T[:, t0:t0 + W].rearrange("(c p) t -> p c t", p=128),
                in_=x3c.ap().rearrange("p (c t) -> p c t", c=8)),
                after=[ek], tag="ST", inc=16)
            prev_out = [sto]



def emit_phase3(nc, em, G, bar, cc_tok, phases=(1, 2), nt_limit=None):
    W = 512
    NT = S // W
    cc2o = G["cc2o"]
    qw, kw, vw = G["qw"], G["kw"], G["vw"]
    qb, kb, vb = G["qb"], G["kb"], G["vb"]
    masks = G["masks"]
    id64 = G["idt"]
    ones_row_f = G["ones_row_f"]
    out1T = G["out1d"]

    with ExitStack() as stk:
        en = stk.enter_context

        def alloc(name, shape, dt):
            return nc.alloc_sbuf_tensor("a_" + name, list(shape), dt)

        q_sb = alloc("q_sb", [128, 2 * S], BF16)
        k_sb = alloc("k_sb", [128, 2 * S], BF16)
        v_tm = alloc("v_tm", [128, 64 * 4 * 65], BF16)
        ht = [alloc(f"ht{i}", [128, 8 * W], BF16) for i in range(2)]
        vtmp = alloc("vtmp", [128, 2 * W], BF16)
        qw_sb = alloc("qw_sb", [128, 8 * 256], BF16)
        kw_sb = alloc("kw_sb", [128, 8 * 256], BF16)
        vw_sb = alloc("vw_sb", [128, 8 * 256], BF16)
        qb_sb = alloc("qb_sb", [128, 2], F32)
        kb_sb = alloc("kb_sb", [128, 2], F32)
        vb_sb = alloc("vb_sb", [128, 2], F32)
        mask_sb = alloc("mask_sb", [128, 4 * W], F32)
        id_sb = alloc("id_sb", [128, 128], BF16)
        or_sb = alloc("or_sb", [1, 128], F32)
        w_sb = [alloc(f"w_sb{i}", [128, W], BF16) for i in range(4)]
        row_sb = alloc("row_sb", [1, W], F32)
        rc_sb = alloc("rc_sb", [1, W], F32)
        bc_sb = alloc("bc_sb", [64, W], F32)
        out_sb = [alloc(f"out_sb{i}", [64, W], BF16) for i in range(2)]
        p_s = [en(nc.psum_tensor(f"a_p_s{i}", [128, W], F32)) for i in range(4)]
        p_o = en(nc.psum_tensor("a_p_o", [128, W], F32))
        p_b = en(nc.psum_tensor("a_p_b", [64, W], F32))
        pt_bf2 = [en(nc.psum_tensor(f"a_pt_bf{i}", [128, 256], BF16)) for i in range(2)]

        def dma_in(dst, src, q="sync", after=()):
            return em.o(q, lambda e, dst=dst, src=src: e.dma_start(out=dst, in_=src),
                        after=after, tag="LD" if q == "sync" else "LDG", inc=16)

        ld_qw = dma_in(qw_sb.ap().rearrange("p (c o) -> p c o", c=8),
                       qw.rearrange("(c p) o -> p c o", p=128), q="gpsimd", after=bar)
        ld_kw = dma_in(kw_sb.ap().rearrange("p (c o) -> p c o", c=8),
                       kw.rearrange("(c p) o -> p c o", p=128), q="gpsimd", after=bar)
        ld_vw = dma_in(vw_sb.ap().rearrange("p (c o) -> p c o", c=8),
                       vw.rearrange("(c p) o -> p c o", p=128), q="gpsimd", after=bar)
        ld_qb = dma_in(qb_sb[:], qb, after=bar)
        ld_kb = dma_in(kb_sb[:], kb, after=bar)
        ld_vb = dma_in(vb_sb[:], vb, after=bar)
        ld_m = dma_in(mask_sb[:], masks, after=bar)
        ld_id = dma_in(id_sb[:], id64, after=bar)
        ld_or = dma_in(or_sb[:], ones_row_f, after=bar)

        ones_tok = em.o("vector", lambda v: v.memset(v_tm[:], 1.0), after=bar,
                        tag="DVE") \
            if 1 in phases else None

        qkv_toks = []
        ht_war = {0: None, 1: None}
        pbank_free = {0: None, 1: None}
        ptbf_free = {0: None, 1: None}
        for tt in range(nt_limit if nt_limit else NT):
            hb = ht[tt % 2]
            war = ht_war[tt % 2]
            ldh = em.o("sync", lambda e, tt=tt, hb=hb: e.dma_start(
                out=hb.ap().rearrange("p (c t) -> p c t", c=8),
                in_=cc2o[(tt // 4) * D:((tt // 4) + 1) * D,
                         (tt % 4) * W:((tt % 4) + 1) * W].rearrange(
                    "(c p) t -> p c t", p=128)),
                after=([war] if war else []) + [cc_tok] + bar, tag="LD", inc=16)
            tile_reads = []
            for (wsb, bsb, ldwt, ldbt, dst) in (
                    (qw_sb, qb_sb, ld_qw, ld_qb, q_sb),
                    (kw_sb, kb_sb, ld_kw, ld_kb, k_sb),
                    (vw_sb, vb_sb, ld_vw, ld_vb, None)):
                for oc in range(2):
                    pp = p_s[oc]
                    mm = None
                    for c in range(8):
                        mm = em.o("tensor", lambda t, c=c, oc=oc, wsb=wsb, hb=hb, pp=pp: t.matmul(
                            pp[:, 0:W], wsb[:, c * 256 + oc * 128: c * 256 + (oc + 1) * 128],
                            hb[:, c * W:(c + 1) * W], start=(c == 0), stop=(c == 7)),
                            after=[ldwt, ldh] +
                            ([pbank_free[oc]] if pbank_free.get(oc) else [])
                            if c == 0 else (), tag="PE")
                    if dst is not None:
                        ev = em.o("scalar", lambda sc, oc=oc, tt=tt, pp=pp, dst=dst, bsb=bsb: sc.activation(
                            dst[:, oc * S + tt * W: oc * S + (tt + 1) * W], pp[:, 0:W],
                            AF.Identity, bias=bsb[:, oc:oc + 1]),
                            after=[mm, ldbt], tag="ACT")
                    else:
                        ev = em.o("scalar", lambda sc, oc=oc, pp=pp, bsb=bsb: sc.activation(
                            vtmp[:, oc * W:(oc + 1) * W], pp[:, 0:W],
                            AF.Identity, bias=bsb[:, oc:oc + 1]),
                            after=[mm, ldbt] + [t for t in ptbf_free.values() if t],
                            tag="ACT")
                    pbank_free[oc] = ev
                    tile_reads.append(ev)
                    qkv_toks.append(ev)
            tr_evs = []
            for sc_i in (range(4) if 1 in phases else ()):
                pbank = pt_bf2[sc_i % 2]
                war_b = ptbf_free.get(sc_i % 2)
                trvs = []
                for oc in range(2):
                    if "notr" in phases:
                        trv = em.o("tensor", lambda t: t.nop(),
                                   after=[tile_reads[-1], tile_reads[-2], ld_id] +
                                   ([war_b] if war_b else []), tag="PE")
                    else:
                        trv = em.o("tensor", lambda t, oc=oc, sc_i=sc_i, pbank=pbank: t.transpose(
                            pbank[:, oc * 128:(oc + 1) * 128],
                            vtmp[:, oc * W + sc_i * 128: oc * W + (sc_i + 1) * 128],
                            id_sb[:]),
                            after=[tile_reads[-1], tile_reads[-2], ld_id] +
                            ([war_b] if war_b else []), tag="PE")
                    trvs.append(trv)
                bi = (tt * 4 + sc_i) * 4
                if "nocp" in phases:
                    cp = em.o("vector", lambda v, pbank=pbank: v.tensor_copy(
                        v_tm[:, 0:256], pbank[:, 0:256]),
                        after=trvs + [ones_tok], tag="DVE")
                else:
                    cp = em.o("vector", lambda v, bi=bi, pbank=pbank: v.tensor_copy(
                        v_tm.ap().rearrange("p (b c) -> p b c", c=65)[:, bi:bi + 4, 0:64],
                        pbank.ap().rearrange("p (h c) -> p h c", c=64)),
                        after=trvs + [ones_tok], tag="DVE")
                tr_evs.append(cp)
                ptbf_free[sc_i % 2] = em.echo([cp], "scalar", "ACT")
            qkv_toks.extend(tr_evs)
            ht_war[tt % 2] = em.echo(tile_reads, "vector", "DVE")
            if 1 not in phases:
                qkv_toks.extend(tile_reads)
        ph1 = em.phase_barrier()

        DELAY = 2
        war_out = {0: None, 1: None}
        w_free = {i: None for i in range(4)}
        po_free = None
        pb_free = None
        for h in range(4):
            qrow = (h % 2) * 64
            qcolb = (h // 2) * S
            for m in range(NT):
                qT = q_sb[qrow:qrow + 64, qcolb + m * W: qcolb + (m + 1) * W]
                nj = 4 * m + 4
                exp_toks = {}
                pv_prev = None
                for jx in range(nj + DELAY):
                    if jx < nj:
                        j = jx
                        sl = j % 4
                        dep = list(ph1) if (h == 0 and m == 0 and j == 0) else []
                        prev_exp = exp_toks.get(j - 4)
                        qk = em.o("tensor", lambda t, j=j, sl=sl, qT=qT, qrow=qrow, qcolb=qcolb: t.matmul(
                            p_s[sl][:, 0:W],
                            k_sb[qrow:qrow + 64, qcolb + j * 128: qcolb + j * 128 + 128],
                            qT, start=True, stop=True),
                            after=dep + ([prev_exp] if prev_exp else []), tag="PE")
                        d = j - 4 * m
                        if d >= 0:
                            mk = em.o("vector", lambda v, sl=sl, d=d: v.tensor_add(
                                p_s[sl][:, 0:W], p_s[sl][:, 0:W], mask_sb[:, d * W:(d + 1) * W]),
                                after=[qk, ld_m], tag="DVE")
                            edep = [mk]
                        else:
                            edep = [qk]
                        wfree = w_free.get(sl)
                        ex = em.o("scalar", lambda sc, sl=sl: sc.activation(
                            w_sb[sl][:, 0:W], p_s[sl][:, 0:W], AF.Exp, scale=0.125),
                            after=edep + ([wfree] if wfree else []), tag="ACT")
                        exp_toks[j] = ex
                    jj = jx - DELAY
                    if jj >= 0:
                        sl2 = jj % 4
                        blk = (jj * 4 + h) * 65
                        pv = em.o("tensor", lambda t, sl2=sl2, blk=blk, jj=jj, nj=nj: t.matmul(
                            p_o[0:65, 0:W], v_tm[:, blk:blk + 65], w_sb[sl2][:, 0:W],
                            start=(jj == 0), stop=(jj == nj - 1)),
                            after=[exp_toks[jj]] + ([po_free] if (jj == 0 and po_free) else []),
                            tag="PE")
                        w_free[sl2] = pv
                        pv_prev = pv
                se = em.o("scalar", lambda sc: sc.activation(
                    row_sb[0:1, 0:W], p_o[64:65, 0:W], AF.Identity),
                    after=[pv_prev], tag="ACT")
                rc = em.o("vector", lambda v: v.reciprocal(rc_sb[0:1, 0:W], row_sb[0:1, 0:W]),
                          after=[se], tag="DVE")
                bcmm = em.o("tensor", lambda t: t.matmul(
                    p_b[0:64, 0:W], or_sb[0:1, 0:64], rc_sb[0:1, 0:W], start=True, stop=True),
                    after=[rc, ld_or] + ([pb_free] if pb_free else []), tag="PE")
                bev = em.o("vector", lambda v: v.tensor_copy(bc_sb[:], p_b[0:64, 0:W]),
                           after=[bcmm], tag="DVE")
                pb_free = bev
                oslot = m % 2
                war = war_out[oslot]
                oev = em.o("vector", lambda v, oslot=oslot: v.tensor_mul(
                    out_sb[oslot][:], p_o[0:64, 0:W], bc_sb[:]),
                    after=[bev, pv_prev] + ([war] if war else []), tag="DVE")
                po_free = oev
                ek = em.echo([oev], "scalar", "ACT")
                st = em.o("gpsimd", lambda e, h=h, m=m, oslot=oslot: e.dma_start(
                    out=out1T[h * 64:(h + 1) * 64, m * W:(m + 1) * W],
                    in_=out_sb[oslot][:]),
                    after=[ek], tag="ST", inc=16)
                war_out[oslot] = st



def emit_phase1(nc, em, G):
    W = 512
    NT = S // W
    SC = 128
    xT = G["xT"]
    wq, wk, wv = G["wq"], G["wk"], G["wv"]
    aq, av, bq, bv = G["aq"], G["av"], G["bq"], G["bv"]
    cq, sq, phv, pho = G["cq"], G["sq"], G["phv"], G["pho"]
    rot, mask0, idt = G["rot"], G["mask0"], G["idt"]
    ones_col_bf, ones_row_f, eps = G["ones_col_bf"], G["ones_row_f"], G["eps"]
    out0T = G["out0d"]

    with ExitStack() as stk:
        en = stk.enter_context

        def alloc(name, shape, dt):
            return nc.alloc_sbuf_tensor(name, list(shape), dt)

        alq = alloc("alq", [128, 2 * S], BF16)
        pkt = alloc("pkt", [128, 2 * S], BF16)
        pvt = alloc("pvt", [128, 2 * S], BF16)
        fk = alloc("fk", [128, 2 * NS], BF16)
        fvtm = alloc("fvtm", [128, 16 * 4 * 65], BF16)
        wq_sb = alloc("wq_sb", [128, 8 * 256], BF16)
        wk_sb = alloc("wk_sb", [128, 8 * 256], BF16)
        wv_sb = alloc("wv_sb", [128, 8 * 256], BF16)
        aq_sb = alloc("aq_sb", [128, 8 * LORA], BF16)
        av_sb = alloc("av_sb", [128, 8 * LORA], BF16)
        bq_sb = alloc("bq_sb", [LORA, 256], BF16)
        bv_sb = alloc("bv_sb", [LORA, 256], BF16)
        rot_sb = alloc("rot_sb", [128, 128], F32)
        phv_sb = alloc("phv_sb", [128, W], F32)
        pho_sb = alloc("pho_sb", [64, W], F32)
        mask_sb = alloc("mask_sb", [128, W], F32)
        id_sb = alloc("id_sb", [128, 128], BF16)
        consts = {
            "ones_col_bf": alloc("c_ones_col", [128, 1], BF16),
            "ones_row_f": alloc("c_ones_row", [1, 128], F32),
            "eps": alloc("c_eps", [1, 1], F32),
        }
        xt = alloc("xt", [128, 8 * W], F32)
        ln_scr = {
            "xbf": alloc("ln_xbf", [128, 8 * W], BF16),
            "rows": [alloc(f"ln_r{i}", [1, W], F32) for i in range(4)],
            "bc_r": alloc("ln_bcr", [128, W], F32),
            "bc_mr": alloc("ln_bcmr", [128, W], F32),
        }
        ub = ln_scr["xbf"]
        bc_r = ln_scr["bc_r"]
        tmpf = xt
        tmpf2 = ln_scr["bc_mr"]
        tlora = alloc("tlora", [LORA, W], BF16)
        qf = alloc("qf", [128, 2 * W], F32)
        ksh = alloc("ksh", [128, 2 * W], BF16)
        vsh = alloc("vsh", [128, 2 * W], BF16)
        fvtmp = alloc("fvtmp", [128, 2 * SC], BF16)
        tab = alloc("tab", [128, 2 * W], F32)
        w_sb = [alloc(f"w_sb{i}", [128, W], BF16) for i in range(2)]
        out_sb = [alloc(f"out_sb{i}", [64, W], BF16) for i in range(2)]
        bcw_bf = alloc("bcw_bf", [64, W], BF16)
        p0 = en(nc.psum_tensor("p0", [128, W], F32))
        p1 = en(nc.psum_tensor("p1", [128, W], F32))
        pa = en(nc.psum_tensor("pa", [128, W], F32))
        pb = en(nc.psum_tensor("pb", [128, W], F32))
        p_o = en(nc.psum_tensor("p_o", [128, W], F32))
        p_d = en(nc.psum_tensor("p_d", [1, W], F32))
        pl = en(nc.psum_tensor("pl", [LORA, W], F32))
        pt_bf = en(nc.psum_tensor("pt_bf", [128, 256], BF16))
        ps = {"stat_a": pa, "stat_b": pb}

        def dma_in(dst, src, q="sync"):
            return em.o(q, lambda e, dst=dst, src=src: e.dma_start(out=dst, in_=src),
                        tag="LD" if q == "sync" else "LDG", inc=16)

        ld_wq = dma_in(wq_sb.ap().rearrange("p (c o) -> p c o", c=8),
                       wq.rearrange("(c p) o -> p c o", p=128), q="gpsimd")
        ld_wk = dma_in(wk_sb.ap().rearrange("p (c o) -> p c o", c=8),
                       wk.rearrange("(c p) o -> p c o", p=128), q="gpsimd")
        ld_wv = dma_in(wv_sb.ap().rearrange("p (c o) -> p c o", c=8),
                       wv.rearrange("(c p) o -> p c o", p=128), q="gpsimd")
        ld_aq = dma_in(aq_sb.ap().rearrange("p (c o) -> p c o", c=8),
                       aq.rearrange("(c p) o -> p c o", p=128), q="gpsimd")
        ld_av = dma_in(av_sb.ap().rearrange("p (c o) -> p c o", c=8),
                       av.rearrange("(c p) o -> p c o", p=128), q="gpsimd")
        ld_bq = dma_in(bq_sb[:], bq, q="gpsimd")
        ld_bv = dma_in(bv_sb[:], bv, q="gpsimd")
        ld_rot = dma_in(rot_sb[:], rot)
        ld_phv = dma_in(phv_sb[:], phv)
        ld_pho = dma_in(pho_sb[:], pho)
        ld_m = dma_in(mask_sb[:], mask0)
        ld_id = dma_in(id_sb[:], idt)
        ld_oc = dma_in(consts["ones_col_bf"][:], ones_col_bf)
        ld_or = dma_in(consts["ones_row_f"][:], ones_row_f)
        ld_eps = dma_in(consts["eps"][:], eps)
        const_deps = [ld_oc, ld_or, ld_eps]
        wkh = em.o("vector", lambda v: v.tensor_scalar(
            wk_sb[:], wk_sb[:], 0.5, None, ALU.mult), after=[ld_wk], tag="DVE")
        ones_tok = em.o("vector", lambda v: v.memset(fvtm[:], 1.0), tag="DVE")

        xt_war = None
        tab_war = None
        ub_war = None
        dve_war = None
        fvtmp_war = []
        ptbf_war = None
        shm_prev = {}
        for tt in range(NT):
            t0 = tt * W
            ldx = em.o("sync", lambda e, t0=t0: e.dma_start(
                out=xt.ap().rearrange("p (c t) -> p c t", c=8),
                in_=xT[:, t0:t0 + W].rearrange("(c p) t -> p c t", p=128)),
                after=[xt_war] if xt_war else [], tag="LD", inc=16)
            ldt = []
            for i, tsrc in enumerate((cq, sq)):
                ldt.append(em.o("sync", lambda e, i=i, tsrc=tsrc, t0=t0: e.dma_start(
                    out=tab[:, i * W:(i + 1) * W], in_=tsrc[:, t0:t0 + W]),
                    after=[tab_war] if tab_war else [], tag="LD", inc=16))
            (l1, l2), _, bc_m = emit_ln_stats(
                em, consts, ps, lambda c: xt[:, c * W:(c + 1) * W], W,
                [ldx] + const_deps + ([ub_war] if ub_war else [])
                + ([dve_war] if dve_war else []), "", ln_scr,
                raw_mean=True)
            u_toks = []
            for c in range(8):
                u_toks.append(em.o("vector", lambda v, c=c: v.tensor_sub(
                    ub[:, c * W:(c + 1) * W], xt[:, c * W:(c + 1) * W], bc_m[:, 0:W]),
                    after=[l2], tag="DVE"))
            evs = {}
            rope_toks = []
            proj_last = None

            def lora_pass(asb, lda, prev_read):
                mmt = None
                for c in range(8):
                    mmt = em.o("tensor", lambda t, c=c, asb=asb: t.matmul(
                        pl[:, 0:W], asb[:, c * LORA:(c + 1) * LORA],
                        ub[:, c * W:(c + 1) * W], start=(c == 0), stop=(c == 7)),
                        after=([u_toks[c], lda] + prev_read) if c == 0 else [u_toks[c]],
                        tag="PE")
                return em.o("vector", lambda v: v.tensor_copy(tlora[:, 0:W], pl[:, 0:W]),
                            after=[mmt], tag="DVE")

            def proj(nm, wsb, bsb, ldwt, ldb, tl, oc, bank_dep):
                pp = p0 if oc == 0 else p1
                mm = None
                for c in range(8):
                    mm = em.o("tensor", lambda t, c=c, oc=oc, wsb=wsb, pp=pp: t.matmul(
                        pp[:, 0:W], wsb[:, c * 256 + oc * 128: c * 256 + (oc + 1) * 128],
                        ub[:, c * W:(c + 1) * W],
                        start=(c == 0), stop=(c == 7 and tl is None)),
                        after=([ldwt, u_toks[c]] + bank_dep) if c == 0 else [u_toks[c]],
                        tag="PE")
                if tl is not None:
                    mm = em.o("tensor", lambda t, oc=oc, bsb=bsb, pp=pp: t.matmul(
                        pp[:, 0:W], bsb[0:LORA, oc * 128:(oc + 1) * 128], tlora[:, 0:W],
                        start=False, stop=True), after=[tl, ldb], tag="PE")
                return mm, pp

            tlq = lora_pass(aq_sb, ld_aq, [])
            qmm = {}
            for oc in range(2):
                mm, pp = proj("q", wq_sb, bq_sb, ld_wq, ld_bq, tlq, oc, [])
                ev = em.o("vector", lambda v, oc=oc, pp=pp: v.tensor_mul(
                    qf[:, oc * W:(oc + 1) * W], pp[:, 0:W], bc_r[:, 0:W]),
                    after=[mm, l1], tag="DVE")
                evs[("q", oc)] = ev
                qmm[oc] = mm

            def rope(src_oc_ap, oc, dst_ap, psh, bank_dep, src_dep, war):
                shm = em.o("tensor", lambda t, src_oc_ap=src_oc_ap, psh=psh: t.matmul(
                    psh[:, 0:W], rot_sb[:], src_oc_ap, start=True, stop=True),
                    after=[src_dep, ld_rot] + bank_dep, tag="PE")
                t1 = em.o("vector", lambda v, src_oc_ap=src_oc_ap: v.tensor_mul(
                    tmpf[:, 0:W], src_oc_ap, tab[:, 0:W]),
                    after=[src_dep] + war + ldt, tag="DVE")
                t2 = em.o("vector", lambda v, psh=psh: v.tensor_mul(
                    tmpf2[:, 0:W], psh[:, 0:W], tab[:, W:2 * W]),
                    after=[shm, src_dep] + war + ldt, tag="DVE")
                t3 = em.o("vector", lambda v, dst_ap=dst_ap: v.tensor_add(
                    dst_ap, tmpf[:, 0:W], tmpf2[:, 0:W]), after=[t1, t2], tag="DVE")
                return shm, t2, t3

            rope_war = []
            for oc in range(2):
                psh = pa if oc == 0 else pb
                bank_dep = [l1 if oc == 0 else l2] + (shm_prev.get(oc) or [])
                shm, t2, t3 = rope(qf[:, oc * W:(oc + 1) * W], oc,
                                   alq[:, oc * S + t0: oc * S + t0 + W],
                                   psh, bank_dep, evs[("q", oc)], rope_war)
                rope_toks.append(t3)
                rope_war = [t3]
                evs[("ropeq", oc)] = (shm, t2, t3)

            for oc in range(2):
                mm, pp = proj("k", wk_sb, None, wkh, None, None, oc,
                              [evs[("q", oc)]])
                ev = em.o("vector", lambda v, oc=oc, pp=pp: v.tensor_mul(
                    qf[:, oc * W:(oc + 1) * W], pp[:, 0:W], bc_r[:, 0:W]),
                    after=[mm, evs[("ropeq", oc)][0], evs[("ropeq", oc)][2]], tag="DVE")
                evs[("k", oc)] = ev
            for oc in range(2):
                psh = pa if oc == 0 else pb
                bank_dep = [evs[("ropeq", oc)][1]]
                shm, t2, t3 = rope(qf[:, oc * W:(oc + 1) * W], oc,
                                   ksh[:, oc * W:(oc + 1) * W],
                                   psh, bank_dep, evs[("k", oc)], rope_war)
                rope_toks.append(t3)
                rope_war = [t3]
                evs[("ropek", oc)] = (shm, t2, t3)
                shm_prev[oc] = [t2]

            bcp = em.o("vector", lambda v: v.tensor_mul(
                bc_r[:, 0:W], bc_r[:, 0:W], phv_sb[:]),
                after=[l1, ld_phv, evs[("k", 1)]], tag="DVE")
            tlv = lora_pass(av_sb, ld_av, [qmm[1], tlq])
            for oc in range(2):
                mm, pp = proj("v", wv_sb, bv_sb, ld_wv, ld_bv, tlv, oc,
                              [evs[("k", oc)]])
                ev = em.o("vector", lambda v, oc=oc, pp=pp: v.tensor_mul(
                    vsh[:, oc * W:(oc + 1) * W], pp[:, 0:W], bc_r[:, 0:W]),
                    after=[mm, bcp], tag="DVE")
                evs[("v", oc)] = ev
                if oc == 1:
                    proj_last = mm
            ub_war = proj_last

            comp_last = None
            for oc in range(2):
                for (srcb, isk) in ((ksh, True), (vsh, False)):
                    sdep = evs[("ropek", oc)][2] if isk else evs[("v", oc)]
                    base = oc * W

                    def s0(j, srcb=srcb, base=base):
                        return srcb[:, base + j: base + W: 4]
                    if isk:
                        dsts = fk[:, oc * NS + tt * SC: oc * NS + (tt + 1) * SC]
                    else:
                        dsts = fvtmp[:, oc * SC:(oc + 1) * SC]
                    a1 = em.o("vector", lambda v, s0=s0: v.tensor_add(
                        tmpf[:, 0:SC], s0(0), s0(1)),
                        after=[sdep] + ([comp_last] if comp_last else rope_war),
                        tag="DVE")
                    a2 = em.o("vector", lambda v, s0=s0: v.tensor_add(
                        tmpf[:, 0:SC], tmpf[:, 0:SC], s0(2)), after=[a1], tag="DVE")
                    a3 = em.o("vector", lambda v, s0=s0, dsts=dsts: v.tensor_add(
                        dsts, tmpf[:, 0:SC], s0(3)),
                        after=[a2] + (fvtmp_war if not isk else []), tag="DVE")
                    comp_last = a3
                    if not isk:
                        evs[("fv", oc)] = a3
                for (srcb, dstb, sdep) in ((ksh, pkt, evs[("ropek", oc)][2]),
                                           (vsh, pvt, evs[("v", oc)])):
                    base = oc * W
                    rbase = oc * S + t0

                    def s0(j, srcb=srcb, base=base):
                        return srcb[:, base + j: base + W: 4]

                    def d0(j, dstb=dstb, rbase=rbase):
                        return dstb[:, rbase + j: rbase + W: 4]
                    cprev = em.o("vector", lambda v, s0=s0, d0=d0: v.tensor_copy(
                        d0(0), s0(0)), after=[sdep], tag="DVE")
                    for j in range(1, 4):
                        cprev = em.o("vector", lambda v, j=j, s0=s0, d0=d0: v.tensor_add(
                            d0(j), d0(j - 1), s0(j)), after=[cprev], tag="DVE")
                    comp_last = cprev
            fvtmp_war = []
            trvs = []
            for oc in range(2):
                trv = em.o("tensor", lambda t, oc=oc: t.transpose(
                    pt_bf[:, oc * 128:(oc + 1) * 128], fvtmp[:, oc * SC:(oc + 1) * SC],
                    id_sb[:]),
                    after=[evs[("fv", oc)], ld_id] + ([ptbf_war] if ptbf_war else []),
                    tag="PE")
                fvtmp_war.append(trv)
                trvs.append(trv)
            cps = []
            for oc in range(2):
                for hh in range(2):
                    h = 2 * oc + hh
                    cp = em.o("vector", lambda v, tt=tt, h=h, oc=oc, hh=hh: v.tensor_copy(
                        fvtm[:, (tt * 4 + h) * 65:(tt * 4 + h) * 65 + 64],
                        pt_bf[:, oc * 128 + hh * 64: oc * 128 + hh * 64 + 64]),
                        after=trvs + [ones_tok], tag="DVE")
                    cps.append(cp)
            ptbf_war = em.echo(cps, "scalar", "ACT")
            comp_last = cps[-1]
            dve_war = comp_last
            tab_war = em.echo([evs[("ropek", 1)][1]], "scalar", "ACT")
            xt_war = em.echo([comp_last], "scalar", "ACT")
        ph1 = em.phase_barrier()

        rows = ln_scr["rows"]
        wd_row, se_row, rc_row = rows[0], rows[1], rows[2]
        bcw = xt
        t2f_ap = lambda: xt[0:64, W:2 * W]
        bcf_ap = lambda: ln_scr["bc_r"][0:64, 0:W]
        war_out = {0: None, 1: None}
        w_free = {0: None, 1: None}
        po_free = None
        pd_free = None
        pa_free = None
        pb_free = None
        t2v_prev = None
        bfm_prev = None
        oc_sb = consts["ones_col_bf"]
        or_sb = consts["ones_row_f"]
        for h in range(4):
            qrow = (h % 2) * 64
            fcolb = (h // 2) * NS
            qcolb = (h // 2) * S
            for m in range(NT):
                qT = alq[qrow:qrow + 64, qcolb + m * W: qcolb + (m + 1) * W]
                nj = m + 1
                exp_toks = {}
                pv_prev = None
                for jx in range(nj + 1):
                    if jx < nj:
                        j = jx
                        sl = j % 2
                        dep = list(ph1) if (h == 0 and m == 0 and j == 0) else []
                        prev_exp = exp_toks.get(j - 2)
                        qk = em.o("tensor", lambda t, j=j, sl=sl, qT=qT, qrow=qrow, fcolb=fcolb: t.matmul(
                            (p0 if sl == 0 else p1)[:, 0:W],
                            fk[qrow:qrow + 64, fcolb + j * 128: fcolb + j * 128 + 128],
                            qT, start=True, stop=True),
                            after=dep + ([prev_exp] if prev_exp else []), tag="PE")
                        if j == m:
                            mk = em.o("vector", lambda v, sl=sl: v.tensor_add(
                                (p0 if sl == 0 else p1)[:, 0:W],
                                (p0 if sl == 0 else p1)[:, 0:W], mask_sb[:]),
                                after=[qk, ld_m], tag="DVE")
                            edep = [mk]
                        else:
                            edep = [qk]
                        wfree = w_free.get(sl)
                        ex = em.o("scalar", lambda sc, sl=sl: sc.activation(
                            w_sb[sl][:, 0:W], (p0 if sl == 0 else p1)[:, 0:W],
                            AF.Exp, scale=0.125),
                            after=edep + ([wfree] if wfree else []), tag="ACT")
                        exp_toks[j] = ex
                    jj = jx - 1
                    if jj >= 0:
                        sl2 = jj % 2
                        blk = (jj * 4 + h) * 65
                        pv = em.o("tensor", lambda t, sl2=sl2, blk=blk, jj=jj, nj=nj: t.matmul(
                            p_o[0:65, 0:W], fvtm[:, blk:blk + 65], w_sb[sl2][:, 0:W],
                            start=(jj == 0), stop=(jj == nj - 1)),
                            after=[exp_toks[jj]] + ([po_free] if (jj == 0 and po_free) else []),
                            tag="PE")
                        w_free[sl2] = pv
                        pv_prev = pv
                pr = em.o("vector", lambda v, qT=qT, qrow=qrow, qcolb=qcolb, m=m: v.tensor_mul(
                    vsh[qrow:qrow + 64, 0:W], qT,
                    pkt[qrow:qrow + 64, qcolb + m * W: qcolb + (m + 1) * W]),
                    after=(list(ph1) if (h == 0 and m == 0) else [])
                    + ([pd_free] if pd_free else []), tag="DVE")
                dmm = em.o("tensor", lambda t, qrow=qrow: t.matmul(
                    p_d[0:1, 0:W], oc_sb[qrow:qrow + 64, 0:1], vsh[qrow:qrow + 64, 0:W],
                    start=True, stop=True),
                    after=[pr] + ([pd_free] if pd_free else []), tag="PE")
                wd = em.o("scalar", lambda sc: sc.activation(
                    wd_row[0:1, 0:W], p_d[0:1, 0:W], AF.Exp, scale=0.125),
                    after=[dmm], tag="ACT")
                pd_free = wd
                sadd = em.o("vector", lambda v: v.tensor_add(
                    p_o[64:65, 0:W], p_o[64:65, 0:W], wd_row[0:1, 0:W]),
                    after=[wd, pv_prev], tag="DVE")
                bcmm = em.o("tensor", lambda t: t.matmul(
                    pa[0:64, 0:W], or_sb[0:1, 0:64], wd_row[0:1, 0:W],
                    start=True, stop=True),
                    after=[wd] + ([pa_free] if pa_free else []), tag="PE")
                bev = em.o("vector", lambda v: v.tensor_copy(bcw_bf[:], pa[0:64, 0:W]),
                           after=[bcmm] + ([t2v_prev] if t2v_prev else []), tag="DVE")
                pa_free = bev
                pvc = em.o("vector", lambda v, qrow=qrow, qcolb=qcolb, m=m: v.tensor_copy(
                    vsh[0:64, W:2 * W],
                    pvt[qrow:qrow + 64, qcolb + m * W: qcolb + (m + 1) * W]),
                    after=[bev], tag="DVE")
                t2v = em.o("vector", lambda v: v.tensor_mul(
                    t2f_ap(), bcw_bf[:], vsh[0:64, W:2 * W]),
                    after=[pvc], tag="DVE")
                t2v_prev = t2v
                acc = em.o("vector", lambda v: v.tensor_add(
                    p_o[0:64, 0:W], p_o[0:64, 0:W], t2f_ap()),
                    after=[t2v, pv_prev], tag="DVE")
                se = em.o("scalar", lambda sc: sc.activation(
                    se_row[0:1, 0:W], p_o[64:65, 0:W], AF.Identity),
                    after=[sadd], tag="ACT")
                rc = em.o("vector", lambda v: v.reciprocal(rc_row[0:1, 0:W], se_row[0:1, 0:W]),
                          after=[se] + ([bfm_prev] if bfm_prev else []), tag="DVE")
                bfm = em.o("tensor", lambda t: t.matmul(
                    pb[0:64, 0:W], or_sb[0:1, 0:64], rc_row[0:1, 0:W],
                    start=True, stop=True),
                    after=[rc] + ([pb_free] if pb_free else []), tag="PE")
                bfm_prev = bfm
                bcf = em.o("vector", lambda v: v.tensor_mul(
                    bcf_ap(), pb[0:64, 0:W], pho_sb[:]),
                    after=[bfm, ld_pho], tag="DVE")
                pb_free = bcf
                oslot = m % 2
                war = war_out[oslot]
                oev = em.o("vector", lambda v, oslot=oslot: v.tensor_mul(
                    out_sb[oslot][:], p_o[0:64, 0:W], bcf_ap()),
                    after=[bcf, acc] + ([war] if war else []), tag="DVE")
                po_free = oev
                ek = em.echo([oev], "scalar", "ACT")
                st = em.o("gpsimd", lambda e, h=h, m=m, oslot=oslot: e.dma_start(
                    out=out0T[h * 64:(h + 1) * 64, m * W:(m + 1) * W],
                    in_=out_sb[oslot][:]),
                    after=[ek], tag="ST", inc=16)
                war_out[oslot] = st



def emit_partial_oproj(nc, em, G, bar, pfx, src_ap, wkey, akey, bkey, cc_dst, bnd):
    W = 512
    PAD = 8 if bnd else 0
    lora = akey is not None
    with ExitStack() as stk:
        en = stk.enter_context

        def alloc(name, shape, dt):
            return nc.alloc_sbuf_tensor(pfx + name, list(shape), dt)

        ot = [alloc(f"ot{i}", [128, 2 * W], BF16) for i in range(2)]
        w_sb = alloc("w", [128, 2 * D], BF16)
        y0 = [alloc(f"y{i}", [128, 8 * W], F32) for i in range(2)]
        p0 = en(nc.psum_tensor(pfx + "p0", [128, W], F32))
        p1 = en(nc.psum_tensor(pfx + "p1", [128, W], F32))
        ld_w = em.o("gpsimd", lambda g: g.dma_start(
            out=w_sb.ap().rearrange("p (c o) -> p c o", c=2),
            in_=G[wkey].rearrange("(c p) o -> p c o", p=128)),
            after=bar, tag="LDG", inc=16)
        wdeps = [ld_w]
        if lora:
            a_sb = alloc("a", [128, 2 * LORA], BF16)
            b_sb = alloc("b", [LORA, D], BF16)
            tl = alloc("tl", [LORA, W], BF16)
            pl = en(nc.psum_tensor(pfx + "pl", [LORA, W], F32))
            ld_a = em.o("gpsimd", lambda g: g.dma_start(
                out=a_sb.ap().rearrange("p (c o) -> p c o", c=2),
                in_=G[akey].rearrange("(c p) o -> p c o", p=128)),
                after=bar, tag="LDG", inc=16)
            ld_b = em.o("gpsimd", lambda g: g.dma_start(
                out=b_sb[:], in_=G[bkey]), after=bar, tag="LDG", inc=16)
            wdeps += [ld_a, ld_b]
        ld_war = {0: None, 1: None}
        st_war = {0: [], 1: []}
        bank_free = {0: None, 1: None}
        tl_prev = None
        for m in range(16):
            sl = m % 2
            buf = ot[sl]
            ld = em.o("sync", lambda e, m=m, buf=buf: e.dma_start(
                out=buf.ap().rearrange("p (c t) -> p c t", c=2),
                in_=src_ap[:, m * W:(m + 1) * W].rearrange("(c p) t -> p c t", p=128)),
                after=(bar if m < 2 else []) +
                ([ld_war[sl]] if ld_war[sl] else []), tag="LD", inc=16)
            tlt = None
            if lora:
                mmt = None
                for fc in range(2):
                    mmt = em.o("tensor", lambda t, fc=fc, buf=buf: t.matmul(
                        pl[:, 0:W], a_sb[:, fc * LORA:(fc + 1) * LORA],
                        buf[:, fc * W:(fc + 1) * W],
                        start=(fc == 0), stop=(fc == 1)),
                        after=([ld, ld_a] + ([tl_prev] if tl_prev else []))
                        if fc == 0 else (), tag="PE")
                tlt = em.o("vector", lambda v: v.tensor_copy(tl[:, 0:W], pl[:, 0:W]),
                           after=[mmt], tag="DVE")
                tl_prev = tlt
            evs = []
            last_fc_mm = None
            for oc in range(8):
                pp = p0 if oc % 2 == 0 else p1
                prev = bank_free[oc % 2]
                mm = None
                for fc in range(2):
                    mm = em.o("tensor", lambda t, fc=fc, oc=oc, buf=buf, pp=pp: t.matmul(
                        pp[:, 0:W], w_sb[:, fc * D + oc * 128: fc * D + (oc + 1) * 128],
                        buf[:, fc * W:(fc + 1) * W],
                        start=(fc == 0), stop=(fc == 1 and not lora)),
                        after=([ld] + wdeps + ([prev] if prev else []))
                        if fc == 0 else (), tag="PE")
                last_fc_mm = mm
                if lora:
                    mm = em.o("tensor", lambda t, oc=oc, pp=pp: t.matmul(
                        pp[:, 0:W], b_sb[:, oc * 128:(oc + 1) * 128], tl[:, 0:W],
                        start=False, stop=True), after=[tlt], tag="PE")
                ev = em.o("vector", lambda v, oc=oc, pp=pp, sl=sl: v.tensor_copy(
                    y0[sl][:, oc * W:(oc + 1) * W], pp[:, 0:W]),
                    after=[mm] + (st_war[sl] if oc == 0 else []), tag="DVE")
                bank_free[oc % 2] = ev
                evs.append(ev)
            ld_war[sl] = last_fc_mm
            ek = em.echo(evs, "scalar", "ACT")
            qq = m // 4
            st = em.o("gpsimd", lambda e, m=m, qq=qq, sl=sl: e.dma_start(
                out=cc_dst[qq * D:(qq + 1) * D,
                           PAD + (m % 4) * W: PAD + (m % 4) * W + W].rearrange(
                    "(c p) t -> p c t", p=128),
                in_=y0[sl].ap().rearrange("p (c t) -> p c t", c=8)),
                after=[ek], tag="ST", inc=16)
            st_war[sl] = [st]
            if bnd and m % 4 == 3:
                q2 = (qq + 1) % 4
                stb = em.o("gpsimd", lambda e, q2=q2, sl=sl: e.dma_start(
                    out=cc_dst[q2 * D:(q2 + 1) * D, 0:1].rearrange(
                        "(c p) t -> p c t", p=128),
                    in_=y0[sl].ap().rearrange("p (c t) -> p c t", c=8)[:, :, W - 1:W],
                    allow_slow_non_contiguous=True),
                    after=[ek], tag="ST", inc=16)
                st_war[sl].append(stb)



def declare_dram(nc):
    G = {}

    def inp(name, shape, dt):
        G[name] = nc.dram_tensor(name, list(shape), dt, kind="ExternalInput").ap()

    def intl(name, shape, dt):
        h = nc.dram_tensor(name, list(shape), dt, kind="Internal")
        G[name + "_h"] = h
        G[name] = h.ap()

    inp("xT", [D, S], F32)
    inp("wq", [D, 256], BF16)
    inp("wk", [D, 256], BF16)
    inp("wv", [D, 256], BF16)
    inp("aq", [D, LORA], BF16)
    inp("av", [D, LORA], BF16)
    inp("bq", [LORA, 256], BF16)
    inp("bv", [LORA, 256], BF16)
    inp("cq", [128, S], F32)
    inp("sq", [128, S], F32)
    inp("phv", [128, 512], F32)
    inp("pho", [64, 512], F32)
    inp("rot", [128, 128], F32)
    inp("mask0", [128, 512], F32)
    inp("idt", [128, 128], BF16)
    inp("ones_col_bf", [128, 1], BF16)
    inp("ones_row_f", [1, 128], F32)
    inp("eps", [1, 1], F32)
    inp("woP", [256, D], BF16)
    inp("aoP", [256, LORA], BF16)
    inp("boF", [LORA, D], BF16)
    inp("owP", [256, D], BF16)
    inp("xqT", [D, TQ + 4], F32)
    inp("wrT", [D, D], BF16)
    inp("wkT", [D, D], BF16)
    inp("wvT", [D, D], BF16)
    inp("mu_r", [128, 8], F32)
    inp("mu_k", [128, 8], F32)
    inp("bmask", [128, 1], F32)
    inp("qw", [D, 256], BF16)
    inp("kw", [D, 256], BF16)
    inp("vw", [D, 256], BF16)
    inp("qb", [128, 2], F32)
    inp("kb", [128, 2], F32)
    inp("vb", [128, 2], F32)
    inp("masks", [128, 4 * 512], F32)
    inp("ob", [128, 8], F32)
    inp("w1", [E, 8, 128, 4096], BF16)
    inp("b1c", [128, E * 32], F32)
    inp("w2", [E, 8, 128, 4096], BF16)
    inp("b2c", [128, 32], F32)
    inp("wr", [D, E], F32)
    inp("brc", [4, 1], F32)
    inp("id4", [4, 4], F32)
    inp("id128", [128, 128], F32)
    inp("emat", [4, 512], F32)
    G["x4T"] = nc.dram_tensor("x4T", [D, TQ], F32, kind="ExternalOutput").ap()
    intl("out0d", [256, S], BF16)
    intl("out1d", [256, S], BF16)
    intl("x2scr", [D, TQ], F32)
    intl("x3scr", [D, TQ], F32)
    intl("cc1i", [4 * D, 8 + TQ], F32)
    intl("cc1o", [D, 8 + TQ], F32)
    intl("cc2i", [D, TQ], BF16)
    intl("cc2o", [4 * D, TQ], BF16)
    intl("cc3i", [4 * D, TQ], F32)
    intl("cc3o", [D, TQ], F32)
    return G


def build_fused():
    nc = bass.Bass("TRN2", target_bir_lowering=False)
    G = declare_dram(nc)
    em = Emitter()

    def reset_phase():
        bar = em.phase_barrier()
        nc.sbuf_base = sb0
        nc.psum_base = ps0
        return bar

    def collective(kind, op, in_h, out_h, tag):
        em.o("gpsimd", lambda g: g.collective_compute(
            kind, op, replica_groups=RG,
            ins=[in_h.ap().opt()], outs=[out_h.ap().opt()]),
            after=[em.dma_barrier("ST")], tag=tag, inc=1)
        return (tag, 1, "__none__")

    with ExitStack() as stk:
        en = stk.enter_context
        sems = {s: en(nc.semaphore(s)) for s in ALL_SEMS}
        sb0, ps0 = nc.sbuf_base, nc.psum_base

        emit_phase1(nc, em, G)
        bar = reset_phase()
        emit_partial_oproj(nc, em, G, bar, "b1_", G["out0d"],
                           "woP", "aoP", "boF", G["cc1i"], bnd=True)
        cc0 = collective("ReduceScatter", ALU.add, G["cc1i_h"], G["cc1o_h"], "CC0")
        bar = reset_phase()
        emit_phase2(nc, em, G, bar, cc0)
        cc1 = collective("AllGather", ALU.bypass, G["cc2i_h"], G["cc2o_h"], "CC1")
        bar = reset_phase()
        emit_phase3(nc, em, G, bar, cc1)
        bar = reset_phase()
        emit_partial_oproj(nc, em, G, bar, "b3_", G["out1d"],
                           "owP", None, None, G["cc3i"], bnd=False)
        cc2 = collective("ReduceScatter", ALU.add, G["cc3i_h"], G["cc3o_h"], "CC2")
        bar = reset_phase()
        emit_phase4(nc, em, G, bar, cc2)
        em.o("sync", lambda e: e.nop(), after=[em.dma_barrier("ST")])
        block = en(nc.Block())
        em.build(block, sems)
    return nc


_F_CACHE = {}


def kernel(**inputs):
    g = {k: np.asarray(v) for k, v in inputs.items()}
    if "nc" not in _F_CACHE:
        _F_CACHE["nc"] = build_fused()
    nc = _F_CACHE["nc"]
    bf = ml_dtypes.bfloat16
    x = g["L0_attn_hidden_states"].astype(np.float32)
    wo = np.ascontiguousarray(g["L0_attn_Wo"]).astype(np.float32)
    ao = np.ascontiguousarray(g["L0_attn_Ao"]).astype(np.float32)
    owf = np.ascontiguousarray(g["L1_attn_O_w"].reshape(D, D)).astype(np.float32)
    ob = np.ascontiguousarray(g["L1_attn_O_b"].reshape(8, 128).T).astype(np.float32)
    b1c = np.zeros((128, E * 32), np.float32)
    for e in range(E):
        b1c[:, e * 32:(e + 1) * 32] = g["L1_ffn_b1"][e].reshape(32, 128).T
    b2c = np.zeros((128, 32), np.float32)
    for dc in range(8):
        for e in range(E):
            b2c[:, dc * 4 + e] = g["L1_ffn_b2"][e][dc * 128:(dc + 1) * 128]
    sl_ = np.arange(128)[:, None]
    ql_ = np.arange(512)[None, :]
    masks = np.concatenate(
        [np.where(128 * d + sl_ <= ql_, 0.0, -1e30).astype(np.float32)
         for d in range(4)], axis=1)
    in_common = {
        "boF": np.ascontiguousarray(g["L0_attn_Bo"]).astype(bf),
        "wrT": np.ascontiguousarray(g["L0_ffn_W_r"].T).astype(bf),
        "wkT": np.ascontiguousarray(g["L0_ffn_W_k"].T).astype(bf),
        "wvT": np.ascontiguousarray(g["L0_ffn_W_v"].T).astype(bf),
        "mu_r": np.ascontiguousarray(g["L0_ffn_mu_r"].reshape(8, 128).T).astype(np.float32),
        "mu_k": np.ascontiguousarray(g["L0_ffn_mu_k"].reshape(8, 128).T).astype(np.float32),
        "masks": masks,
        "ob": ob, "b1c": b1c, "b2c": b2c,
        "w1": np.ascontiguousarray(np.asarray(g["L1_ffn_w1"]).reshape(E, 8, 128, 8, 512).transpose(0, 3, 2, 1, 4).reshape(E, 8, 128, 4096)).astype(bf),
        "w2": np.ascontiguousarray(np.asarray(g["L1_ffn_w2"]).reshape(E, 32, 128, 8, 128).transpose(0, 3, 2, 1, 4).reshape(E, 8, 128, 4096)).astype(bf),
        "wr": np.ascontiguousarray(g["L1_ffn_wr"]).astype(np.float32),
        "brc": np.ascontiguousarray(g["L1_ffn_br"].reshape(4, 1)).astype(np.float32),
        "id4": CONSTS["ident4_f"], "id128": CONSTS["ident128_f"],
        "emat": CONSTS["emat"],
        "cq": CONSTS["cosP_q"], "sq": CONSTS["ssP_q"],
        "phv": np.ascontiguousarray(CONSTS["phase_h"][:, 0:512]),
        "pho": np.ascontiguousarray(CONSTS["phase_o"][0:64, 0:512]),
        "rot": CONSTS["rot32"], "mask0": CONSTS["mask_l0"],
        "idt": np.eye(128).astype(bf),
        "ones_col_bf": CONSTS["ones_col_bf"],
        "ones_row_f": CONSTS["ones_row_f"], "eps": CONSTS["eps"],
    }
    in_maps = []
    for c in range(NC):
        b, gq = c // 4, c % 4
        t0 = gq * TQ
        hset = list(range(4 * gq, 4 * gq + 4))

        def bcol(bmat):
            v = bmat[hset].reshape(256)
            return np.ascontiguousarray(v.reshape(2, 128).T).astype(np.float32)

        xT = np.ascontiguousarray(x[b].T).astype(np.float32)
        xq = np.ascontiguousarray(x[b].T[:, t0:t0 + TQ]).astype(np.float32)
        if t0 == 0:
            xbnd = np.zeros((D, 4), np.float32)
        else:
            xbnd = np.tile(np.ascontiguousarray(x[b].T[:, t0 - 1:t0]),
                           (1, 4)).astype(np.float32)
        in_maps.append({
            "xT": xT,
            "xqT": np.concatenate([xq, xbnd], axis=1),
            "bmask": np.full((128, 1), 0.0 if gq == 0 else 1.0, np.float32),
            "wq": np.ascontiguousarray(g["L0_attn_Wq"][:, 256 * gq:256 * gq + 256]).astype(bf),
            "wk": np.ascontiguousarray(g["L0_attn_Wk"][:, 256 * gq:256 * gq + 256]).astype(bf),
            "wv": np.ascontiguousarray(g["L0_attn_Wv"][:, 256 * gq:256 * gq + 256]).astype(bf),
            "aq": np.ascontiguousarray(g["L0_attn_Aq"]).astype(bf),
            "av": np.ascontiguousarray(g["L0_attn_Av"]).astype(bf),
            "bq": np.ascontiguousarray(g["L0_attn_Bq"][:, 256 * gq:256 * gq + 256]).astype(bf),
            "bv": np.ascontiguousarray(g["L0_attn_Bv"][:, 256 * gq:256 * gq + 256]).astype(bf),
            "woP": np.ascontiguousarray(wo[256 * gq:256 * (gq + 1)]).astype(bf),
            "aoP": np.ascontiguousarray(ao[256 * gq:256 * (gq + 1)]).astype(bf),
            "owP": np.ascontiguousarray(owf[256 * gq:256 * (gq + 1)]).astype(bf),
            "qw": np.ascontiguousarray(g["L1_attn_Q_w"][hset].transpose(1, 0, 2).reshape(D, 256)).astype(bf),
            "kw": np.ascontiguousarray(g["L1_attn_K_w"][hset].transpose(1, 0, 2).reshape(D, 256)).astype(bf),
            "vw": np.ascontiguousarray(g["L1_attn_V_w"][hset].transpose(1, 0, 2).reshape(D, 256)).astype(bf),
            "qb": bcol(g["L1_attn_Q_b"]), "kb": bcol(g["L1_attn_K_b"]),
            "vb": bcol(g["L1_attn_V_b"]),
            **in_common,
        })
    res = run_bass_kernel_spmd(nc, in_maps, core_ids=list(range(NC)))
    x4 = np.zeros((B, S, D), np.float32)
    for c in range(NC):
        b, gq = c // 4, c % 4
        t0 = gq * TQ
        x4[b, t0:t0 + TQ] = res.results[c]["x4T"].T
    return x4.astype(np.float32)
